# version 01
# baseline (speedup 1.0000x reference)
import contextlib
import numpy as np
import ml_dtypes
import concourse.bass as bass
import concourse.mybir as mybir
from concourse.bass_utils import run_bass_kernel_spmd
from concourse.bass_core_rust import IndirectOffsetOnAxis

F32 = mybir.dt.float32
BF16 = mybir.dt.bfloat16
I32 = mybir.dt.int32
AF = mybir.ActivationFunctionType
ALU = mybir.AluOpType
AX = mybir.AxisListType
NP_BF16 = ml_dtypes.bfloat16

ENGINES = ("sync", "vector", "scalar", "tensor", "gpsimd")
SCALE_ATTN = 128.0 ** -0.5


class Sem:
    def __init__(self, handle, name):
        self.h = handle
        self.n = 0
        self.name = name
        self.eng = None

    def now(self):
        return (self, self.n)


class K:

    def __init__(self, nc, E):
        self.nc = nc
        self.E = E
        self.q = {e: [] for e in ENGINES}
        self._semi = 0
        self.dma_sems = set()

    def sem(self, name=None):
        name = name or f"s{self._semi}"
        self._semi += 1
        h = self.E(self.nc.semaphore(name))
        return Sem(h, name)

    def op(self, eng, emit, waits=(), inc=None, inck=1):
        wl = []
        for w in waits:
            if isinstance(w, Sem):
                wl.append((w, w.n))
            else:
                wl.append(w)
        if inc is not None:
            if inc.eng is None:
                inc.eng = eng
            assert inc.eng == eng, f"sem {inc.name} inc from {eng} and {inc.eng}"
            inc.n += inck
            if inck >= 16:
                self.dma_sems.add(inc)

        def run(e, emit=emit, wl=wl, inc=inc, inck=inck):
            last_wait = None
            for (s, v) in wl:
                if v > 0:
                    e.wait_ge(s.h, v)
            inst = emit(e)
            if inc is not None:
                assert inst is not None, "inc on an op that emitted nothing"
                inst.then_inc(inc.h, inck)
        self.q[eng].append(run)

    def final_barrier(self):
        self.op("sync", lambda e: None,
                waits=[(x, x.n) for x in sorted(self.dma_sems, key=lambda s: s.name)])

    def finish(self, block):
        for name in ENGINES:
            fns = self.q[name]
            if not fns:
                continue

            def body(e, fns=fns):
                for f in fns:
                    f(e)
            getattr(block, name)(body)



def make_const_arrays():
    c = {}
    c["ones_col"] = np.ones((128, 1), NP_BF16)
    c["ones_row"] = np.ones((1, 128), NP_BF16)
    c["ident_bf"] = np.eye(128, dtype=NP_BF16)
    c["ident_f"] = np.eye(128, dtype=np.float32)
    am = np.zeros((128, 4 * 512), np.float32)
    for r in range(4):
        t = np.arange(128)[:, None]
        kk = np.arange(512)[None, :]
        am[:, r * 512:(r + 1) * 512] = np.where(kk <= 128 * r + t, 0.0, -1e38)
    c["am"] = am
    mmk = np.zeros((128, 4 * 512), np.float32)
    for r in range(4):
        kk = np.arange(128)[:, None]
        t = np.arange(512)[None, :]
        mmk[:, r * 512:(r + 1) * 512] = (t >= 128 * r + kk).astype(np.float32)
    c["mm"] = mmk.astype(NP_BF16)
    c["ones_col_f"] = np.ones((128, 1), np.float32)
    c["ones_row_f"] = np.ones((1, 128), np.float32)
    j = np.arange(128)[:, None]
    t = np.arange(128)[None, :]
    c["strictL_f"] = (j < t).astype(np.float32)
    c["tri_incl_f"] = (j <= t).astype(np.float32)
    c["tri_revx_f"] = (j > t).astype(np.float32)
    return c

CONST_SPECS = [
    ("ones_col", [128, 1], BF16), ("ones_row", [1, 128], BF16),
    ("ident_bf", [128, 128], BF16), ("ident_f", [128, 128], F32),
    ("am", [128, 2048], F32), ("mm", [128, 2048], BF16),
    ("ones_col_f", [128, 1], F32), ("ones_row_f", [1, 128], F32),
    ("strictL_f", [128, 128], F32), ("tri_incl_f", [128, 128], F32),
    ("tri_revx_f", [128, 128], F32),
]


class Env:

    def __init__(self, nc, E, k: K):
        self.nc, self.E, self.k = nc, E, k
        self.ld = k.sem("ld_const")
        self.cn = {}
        for name, shape, dt in CONST_SPECS:
            d = nc.dram_tensor(name, shape, dt, kind="ExternalInput").ap()
            s = E(nc.sbuf_tensor(name + "_s", shape, dt))
            self.cn[name] = s
            k.op("sync", lambda e, d=d, s=s: e.dma_start(out=s[:], in_=d),
                 inc=self.ld, inck=16)
        self.S = [E(nc.psum_tensor(f"psb{i}", [128, 512], F32)) for i in range(7)]
        self.PB = E(nc.psum_tensor("psbf", [128, 512], BF16))



def emit_l0_attn(env: Env, T, xT, x1T, wqkvT, woutT, gammacol):
    nc, E, k, cn = env.nc, env.E, env.k, env.cn
    S, PB = env.S, env.PB
    G = T // 512
    QT = T // 128
    NH = 8

    xh = E(nc.sbuf_tensor("xh", [128, 8 * T], BF16))
    xb = E(nc.sbuf_tensor("xb", [128, 8 * 512], BF16))
    sq = [E(nc.sbuf_tensor(f"sq{i}", [128, 512], BF16)) for i in range(2)]
    gcol_f = E(nc.sbuf_tensor("gcol_f", [128, 8], F32))
    gcol_b = E(nc.sbuf_tensor("gcol_b", [128, 8], BF16))
    m_row = E(nc.sbuf_tensor("m_row", [1, 512], F32))
    q_row = E(nc.sbuf_tensor("q_row", [1, 512], F32))
    var_row = E(nc.sbuf_tensor("var_row", [1, 512], F32))
    rstd_row = E(nc.sbuf_tensor("rstd_row", [1, 512], F32))
    rstd_b = E(nc.sbuf_tensor("rstd_b", [1, 512], BF16))
    negmr = E(nc.sbuf_tensor("negmr", [1, T], BF16))
    wq = E(nc.sbuf_tensor("wq", [128, 8 * 128], BF16))
    wk = E(nc.sbuf_tensor("wk", [128, 8 * 128], BF16))
    wv = E(nc.sbuf_tensor("wv", [128, 8 * 128], BF16))
    wout_sb = E(nc.sbuf_tensor("wout_sb", [128, 8 * 1024], BF16))
    csq = E(nc.sbuf_tensor("csq", [1, 128], BF16))
    csk = E(nc.sbuf_tensor("csk", [1, 128], BF16))
    csv = E(nc.sbuf_tensor("csv", [1, 128], BF16))
    Qf = E(nc.sbuf_tensor("Qf", [128, T], BF16))
    Kf = E(nc.sbuf_tensor("Kf", [128, T], BF16))
    Vt = E(nc.sbuf_tensor("Vt", [128, T], BF16))
    O_dram = nc.dram_tensor("O_dram", [1024, T], BF16, kind="Internal").ap()
    ost = [E(nc.sbuf_tensor(f"ost{i}", [128, 512], BF16)) for i in range(2)]
    otile = [[E(nc.sbuf_tensor(f"otile{b}_{hh}", [128, 512], BF16)) for hh in range(8)]
             for b in range(2)]
    esc = [E(nc.sbuf_tensor(f"esc{i}", [128, 512], BF16)) for i in range(2)]
    ET = [E(nc.sbuf_tensor(f"ET{i}", [128, 512], BF16)) for i in range(2)]
    nmcols = E(nc.sbuf_tensor("nmcols", [128, 8], F32))
    lcols = E(nc.sbuf_tensor("lcols", [128, 8], F32))
    nm_p = E(nc.sbuf_tensor("nm_p", [128, 1], F32))
    nm_f = E(nc.sbuf_tensor("nm_f", [128, 1], F32))
    nmneg = E(nc.sbuf_tensor("nmneg", [128, 1], F32))
    resc = E(nc.sbuf_tensor("resc", [128, 1], F32))
    lsum = E(nc.sbuf_tensor("lsum", [128, 1], F32))
    lnl = E(nc.sbuf_tensor("lnl", [128, 1], F32))
    negml = E(nc.sbuf_tensor("negml", [128, 1], F32))
    rcol = E(nc.sbuf_tensor("rcol", [128, 1], F32))
    nbcols = E(nc.sbuf_tensor("nbcols", [128, 32], BF16))
    rbcols = E(nc.sbuf_tensor("rbcols", [128, 32], BF16))
    nrow = E(nc.sbuf_tensor("nrow", [1, 512], BF16))
    rrow = E(nc.sbuf_tensor("rrow", [1, 512], BF16))
    xre = [E(nc.sbuf_tensor(f"xre{i}", [128, 512], F32)) for i in range(4)]
    yout = [E(nc.sbuf_tensor(f"yout{i}", [128, 512], F32)) for i in range(4)]

    s_ldg = k.sem()
    s_ldw = k.sem()
    s_ldx = k.sem()
    s_pe = k.sem()
    s_act = k.sem()
    s_dve = k.sem()
    s_pool = k.sem()
    s_st = k.sem()
    s_xre = k.sem()

    W = lambda *ss: [s.now() for s in ss]
    cw = (env.ld, env.ld.n)

    for c in range(8):
        k.op("sync", lambda e, c=c: e.dma_start(
            out=gcol_f[:, c:c + 1], in_=gammacol[c * 128:(c + 1) * 128, 0:1]),
            inc=s_ldg, inck=16)
    k.op("vector", lambda e: e.tensor_copy(gcol_b[:], gcol_f[:]),
         waits=W(s_ldg) + [cw], inc=s_dve)
    for c in range(8):
        k.op("gpsimd", lambda e, c=c: e.dma_start(
            out=wout_sb[:, c * 1024:(c + 1) * 1024],
            in_=woutT[c * 128:(c + 1) * 128, :]), inc=s_ldw, inck=16)

    ln_done = k.sem()
    for g in range(G):
        ldg = s_ldx.n
        for c in range(8):
            k.op("gpsimd", lambda e, c=c, g=g: e.dma_start(
                out=xb[:, c * 512:(c + 1) * 512],
                in_=xT[c * 128:(c + 1) * 128, g * 512:(g + 1) * 512]),
                waits=[(ln_done, ln_done.n)] if g > 0 else (),
                inc=s_ldx, inck=16)
        for c in range(8):
            k.op("tensor", lambda e, c=c: e.matmul(
                S[0][0:1, :], cn["ones_col"][:], xb[:, c * 512:(c + 1) * 512],
                start=(c == 0), stop=(c == 7)),
                waits=([(s_ldx, ldg + 128)] + ([cw] if g == 0 else [])) if c == 0 else (),
                inc=s_pe if c == 7 else None)
        for c in range(8):
            k.op("scalar", lambda e, c=c: e.activation(
                sq[c % 2][:], xb[:, c * 512:(c + 1) * 512], AF.Square,
                bias=0.0, scale=1.0),
                waits=W(s_ldx) + (W(s_pe) if c >= 2 else []),
                inc=s_act)
            k.op("tensor", lambda e, c=c: e.matmul(
                S[1][0:1, :], cn["ones_col"][:], sq[c % 2][:],
                start=(c == 0), stop=(c == 7)),
                waits=W(s_act), inc=s_pe)
        k.op("vector", lambda e: e.tensor_scalar(
            m_row[:], S[0][0:1, :], 1.0 / 1024, None, op0=ALU.mult),
            waits=W(s_pe), inc=s_dve)
        k.op("vector", lambda e: e.tensor_scalar(
            q_row[:], S[1][0:1, :], 1.0 / 1024, None, op0=ALU.mult),
            inc=s_dve)
        k.op("vector", lambda e: e.tensor_mul(var_row[:], m_row[:], m_row[:]),
             waits=W(s_dve), inc=s_dve)
        k.op("vector", lambda e: e.tensor_sub(var_row[:], q_row[:], var_row[:]),
             waits=W(s_dve), inc=s_dve)
        k.op("scalar", lambda e: e.activation(rstd_row[:], var_row[:], AF.Ln,
                                              bias=0.0, scale=1.0),
             waits=W(s_dve), inc=s_act)
        k.op("vector", lambda e: e.tensor_scalar(
            rstd_row[:], rstd_row[:], -0.5, None, op0=ALU.mult),
            waits=W(s_act), inc=s_dve)
        k.op("scalar", lambda e: e.activation(rstd_row[:], rstd_row[:], AF.Exp,
                                              bias=0.0, scale=1.0),
             waits=W(s_dve), inc=s_act)
        k.op("vector", lambda e: e.tensor_copy(rstd_b[:], rstd_row[:]),
             waits=W(s_act), inc=s_dve)
        k.op("vector", lambda e, g=g: e.tensor_mul(
            negmr[0:1, g * 512:(g + 1) * 512], m_row[:], rstd_row[:]),
            waits=W(s_act, s_dve), inc=s_dve)
        k.op("vector", lambda e, g=g: e.tensor_scalar(
            negmr[0:1, g * 512:(g + 1) * 512], negmr[0:1, g * 512:(g + 1) * 512],
            -1.0, None, op0=ALU.mult),
            waits=W(s_dve), inc=s_dve)
        k.op("tensor", lambda e: e.matmul(
            S[2][:], cn["ones_row"][:], rstd_b[:], start=True, stop=True),
            waits=W(s_dve) + [(ln_done, ln_done.n)], inc=s_pe)
        for c in range(8):
            k.op("vector", lambda e, c=c, g=g: e.scalar_tensor_tensor(
                xh[:, c * T + g * 512: c * T + (g + 1) * 512],
                xb[:, c * 512:(c + 1) * 512], gcol_b[:, c:c + 1], S[2][:],
                op0=ALU.mult, op1=ALU.mult),
                waits=W(s_pe), inc=(ln_done if c == 7 else s_dve))

    wsem = k.sem()
    pj = k.sem()
    p1pe = k.sem()
    p1dve = k.sem()
    p1act = k.sem()
    p1pool = k.sem()
    st_dve = k.sem()
    p2pe = k.sem()
    p2act = k.sem()
    p2dve = k.sem()
    pvpe = k.sem()
    oact = k.sem()
    trpe = k.sem()
    trevac = k.sem()
    v_ev = k.sem()
    sodma = [k.sem() for _ in range(2)]

    for h in range(NH):
        for c in range(8):
            for wt, off in ((wq, h * 128), (wk, 1024 + h * 128), (wv, 2048 + h * 128)):
                k.op("gpsimd", lambda e, c=c, wt=wt, off=off: e.dma_start(
                    out=wt[:, c * 128:(c + 1) * 128],
                    in_=wqkvT[c * 128:(c + 1) * 128, off:off + 128]),
                    waits=W(pj, v_ev) if (h > 0 and c == 0 and wt is wq) else (),
                    inc=wsem, inck=16)
        for wi, (wt, cs) in enumerate(((wq, csq), (wk, csk), (wv, csv))):
            bank = S[3 + wi]
            for c in range(8):
                k.op("tensor", lambda e, c=c, wt=wt, bank=bank: e.matmul(
                    bank[0:1, 0:128], gcol_b[:, c:c + 1],
                    wt[:, c * 128:(c + 1) * 128],
                    start=(c == 0), stop=(c == 7)),
                    waits=W(wsem, s_dve, p1act) if c == 0 else (), inc=s_pe if c == 7 else None)
            k.op("scalar", lambda e, cs=cs, bank=bank: e.activation(
                cs[:], bank[0:1, 0:128], AF.Copy, bias=0.0, scale=1.0),
                waits=W(s_pe), inc=s_act)
        lnw = (ln_done, G)
        for g in range(G):
            for (dst, wt, cs, scale) in ((Qf, wq, csq, SCALE_ATTN), (Kf, wk, csk, 1.0)):
                bank = S[4 if dst is Qf else 5]
                for c in range(8):
                    k.op("tensor", lambda e, c=c, wt=wt, g=g, bank=bank: e.matmul(
                        bank[:], wt[:, c * 128:(c + 1) * 128],
                        xh[:, c * T + g * 512: c * T + (g + 1) * 512],
                        start=(c == 0), stop=False),
                        waits=[lnw] + W(pj, p1act, s_act) if c == 0 else ())
                k.op("tensor", lambda e, cs=cs, g=g, bank=bank: e.matmul(
                    bank[:], cs[:], negmr[0:1, g * 512:(g + 1) * 512],
                    start=False, stop=True), inc=s_pe)
                k.op("scalar", lambda e, dst=dst, g=g, scale=scale, bank=bank:
                     e.activation(dst[:, g * 512:(g + 1) * 512], bank[:],
                                  AF.Copy, bias=0.0, scale=scale),
                     waits=W(s_pe), inc=pj)
        vbase = v_ev.n
        for t in range(QT):
            vb = S[6] if t % 2 == 0 else S[3]
            for c in range(8):
                k.op("tensor", lambda e, c=c, t=t, vb=vb: e.matmul(
                    vb[:, 0:128], xh[:, c * T + t * 128: c * T + (t + 1) * 128],
                    wv[:, c * 128:(c + 1) * 128], start=(c == 0), stop=False),
                    waits=[lnw, (v_ev, vbase + max(0, t - 1))] + (W(pj, p1act, s_act) if t <= 1 else [])
                    if c == 0 else ())
            k.op("tensor", lambda e, t=t, vb=vb: e.matmul(
                vb[:, 0:128], negmr[0:1, t * 128:(t + 1) * 128], csv[:],
                start=False, stop=True), inc=s_pe)
            k.op("scalar", lambda e, t=t, vb=vb: e.activation(
                Vt[:, t * 128:(t + 1) * 128], vb[:, 0:128], AF.Copy,
                bias=0.0, scale=1.0), waits=W(s_pe), inc=v_ev)

        pjW = (pj, pj.n)
        for i in range(QT):
            nch = i // 4 + 1
            n1 = min(nch, 7)
            for g2 in range(n1):
                k.op("tensor", lambda e, i=i, g2=g2: e.matmul(
                    S[g2][:, 0:min(512, T - g2 * 512)], Qf[:, i * 128:(i + 1) * 128],
                    Kf[:, g2 * 512:min(T, (g2 + 1) * 512)], start=True, stop=True),
                    waits=[pjW] + (W(p1act, oact) if i > 0 else []),
                    inc=p1pe)
                if g2 == i // 4:
                    k.op("vector", lambda e, i=i, g2=g2: e.tensor_add(
                        S[g2][:], S[g2][:], cn["am"][:, (i % 4) * 512:(i % 4 + 1) * 512]),
                        waits=W(p1pe), inc=p1dve)
                k.op("vector", lambda e, g2=g2: e.reduce_max(
                    nmcols[:, g2:g2 + 1], S[g2][:], axis=AX.X, negate=True),
                    waits=W(p1pe, p1dve), inc=p1dve)
            k.op("vector", lambda e, n1=n1: e.tensor_reduce(
                nm_p[:], nmcols[:, 0:n1], axis=AX.X, op=ALU.min),
                waits=W(p1dve), inc=p1pool)
            for g2 in range(n1):
                k.op("scalar", lambda e, g2=g2: e.activation(
                    esc[g2 % 2][:], S[g2][:], AF.Exp, bias=nm_p[:, 0:1], scale=1.0,
                    accum_out=lcols[:, g2:g2 + 1]),
                    waits=W(p1pool), inc=p1act)
            if nch > n1:
                g2 = 7
                k.op("tensor", lambda e, i=i, g2=g2: e.matmul(
                    S[0][:], Qf[:, i * 128:(i + 1) * 128],
                    Kf[:, g2 * 512:(g2 + 1) * 512], start=True, stop=True),
                    waits=W(p1act), inc=p1pe)
                k.op("vector", lambda e, i=i: e.tensor_add(
                    S[0][:], S[0][:], cn["am"][:, (i % 4) * 512:(i % 4 + 1) * 512]),
                    waits=W(p1pe), inc=p1dve)
                k.op("vector", lambda e: e.reduce_max(
                    nmcols[:, 7:8], S[0][:], axis=AX.X, negate=True), inc=p1dve)
                k.op("vector", lambda e: e.tensor_tensor(
                    nm_f[:], nm_p[:], nmcols[:, 7:8], op=ALU.min),
                    waits=W(p1dve, p1pool), inc=p1pool)
                k.op("scalar", lambda e: e.activation(
                    esc[1][:], S[0][:], AF.Exp, bias=nm_f[:, 0:1], scale=1.0,
                    accum_out=lcols[:, 7:8]),
                    waits=W(p1pool), inc=p1act)
                k.op("vector", lambda e: e.tensor_scalar(
                    nmneg[:], nm_p[:], -1.0, None, op0=ALU.mult),
                    waits=W(p1pool), inc=p1dve)
                k.op("scalar", lambda e: e.activation(
                    resc[:], nm_f[:], AF.Exp, bias=nmneg[:, 0:1], scale=1.0),
                    waits=W(p1dve), inc=p1act)
                k.op("vector", lambda e: e.reduce_sum(
                    lsum[:], lcols[:, 0:7], axis=AX.X), waits=W(p1act), inc=p1dve)
                k.op("vector", lambda e: e.scalar_tensor_tensor(
                    lsum[:], lsum[:], resc[:, 0:1], lcols[:, 7:8],
                    op0=ALU.mult, op1=ALU.add),
                    waits=W(p1dve, p1act), inc=p1pool)
                nm_use = nm_f
                lsum_src, lsum_wait = lsum, (p1pool, p1pool.n)
            else:
                k.op("vector", lambda e, n1=n1: e.reduce_sum(
                    lsum[:], lcols[:, 0:n1], axis=AX.X), waits=W(p1act), inc=p1dve)
                nm_use = nm_p
                lsum_src, lsum_wait = lsum, (p1dve, p1dve.n)
            k.op("scalar", lambda e: e.activation(
                lnl[:], lsum[:], AF.Ln, bias=0.0, scale=1.0),
                waits=[lsum_wait], inc=p1act)
            k.op("vector", lambda e, nm_use=nm_use: e.tensor_sub(
                negml[:], nm_use[:], lnl[:]), waits=W(p1act), inc=p1dve)
            k.op("scalar", lambda e, i=i: e.activation(
                nbcols[:, i:i + 1], negml[:], AF.Copy, bias=0.0, scale=1.0),
                waits=W(p1dve), inc=p1act)
            k.op("vector", lambda e, i=i: e.tensor_sub(
                rcol[:], negml[:], nbcols[:, i:i + 1]),
                waits=W(p1act, p1dve), inc=p1pool)
            k.op("vector", lambda e, i=i: e.tensor_copy(
                rbcols[:, i:i + 1], rcol[:]), waits=W(p1pool), inc=p1dve)

        p1W = [(p1dve, p1dve.n), (p1act, p1act.n), (p1pool, p1pool.n)]
        for g in range(G):
            for r in range(4):
                qt = 4 * g + r
                k.op("tensor", lambda e, qt=qt, r=r: e.transpose(
                    PB[0:1, r * 128:(r + 1) * 128], nbcols[:, qt:qt + 1],
                    cn["ident_bf"][:]),
                    waits=p1W + W(trevac) if r == 0 else (), inc=trpe)
            k.op("vector", lambda e: e.tensor_copy(nrow[:], PB[0:1, 0:512]),
                 waits=W(trpe), inc=trevac)
            for r in range(4):
                qt = 4 * g + r
                k.op("tensor", lambda e, qt=qt, r=r: e.transpose(
                    PB[0:1, r * 128:(r + 1) * 128], rbcols[:, qt:qt + 1],
                    cn["ident_bf"][:]),
                    waits=W(trevac) if r == 0 else (), inc=trpe)
            k.op("vector", lambda e: e.tensor_copy(rrow[:], PB[0:1, 0:512]),
                 waits=W(trpe), inc=trevac)
            trW = (trevac, trevac.n)
            nkt = 4 * g + 4
            for j in range(nkt):
                bank = S[j % 2]
                k.op("tensor", lambda e, j=j, g=g, bank=bank: e.matmul(
                    bank[:], Kf[:, j * 128:(j + 1) * 128], Qf[:, g * 512:(g + 1) * 512],
                    start=True, stop=False),
                    waits=[trW] + W(p2act))
                k.op("tensor", lambda e, bank=bank: e.matmul(
                    bank[:], cn["ones_row"][:], nrow[:], start=False, stop=False))
                k.op("tensor", lambda e, bank=bank: e.matmul(
                    bank[:], cn["ones_row"][:], rrow[:], start=False, stop=True),
                    inc=p2pe)
                r = j - 4 * g
                k.op("scalar", lambda e, j=j, bank=bank: e.activation(
                    ET[j % 2][:], bank[:], AF.Exp, bias=0.0, scale=1.0),
                    waits=W(p2pe) + W(pvpe), inc=p2act)
                if r >= 0:
                    k.op("vector", lambda e, j=j, r=r: e.tensor_mul(
                        ET[j % 2][:], ET[j % 2][:],
                        cn["mm"][:, r * 512:(r + 1) * 512]),
                        waits=W(p2act), inc=p2dve)
                    evW = W(p2dve)
                else:
                    evW = W(p2act)
                k.op("tensor", lambda e, j=j, g=g: e.matmul(
                    S[2][:], Vt[:, j * 128:(j + 1) * 128], ET[j % 2][:],
                    start=(j == 0), stop=(j == nkt - 1)),
                    waits=evW, inc=pvpe)
            m = h * G + g
            k.op("scalar", lambda e, m=m: e.activation(
                ost[m % 2][:], S[2][:], AF.Copy, bias=0.0, scale=1.0),
                waits=W(pvpe) + [(sodma[m % 2], sodma[m % 2].n)], inc=oact)
            k.op("gpsimd", lambda e, h=h, g=g, m=m: e.dma_start(
                out=O_dram[h * 128:(h + 1) * 128, g * 512:(g + 1) * 512],
                in_=ost[m % 2][:]),
                waits=W(oact), inc=sodma[m % 2], inck=16)

    odW = [(x, x.n) for x in sodma]
    sxre = [k.sem() for _ in range(4)]
    sst = [k.sem() for _ in range(4)]
    solod = k.sem()
    prj = k.sem()
    for g in range(G):
        ob = g % 2
        for hh in range(8):
            k.op("sync", lambda e, hh=hh, g=g, ob=ob: e.dma_start(
                out=otile[ob][hh][:],
                in_=O_dram[hh * 128:(hh + 1) * 128, g * 512:(g + 1) * 512]),
                waits=(odW + [(prj, max(0, (g - 1) * 8))]) if hh == 0 else (),
                inc=solod, inck=16)
        for c in range(8):
            n = g * 8 + c
            b = n % 4
            k.op("sync", lambda e, c=c, g=g, b=b: e.dma_start(
                out=xre[b][:], in_=xT[c * 128:(c + 1) * 128, g * 512:(g + 1) * 512]),
                waits=[(st_dve, max(0, n - 3))], inc=sxre[b], inck=16)
            bank = S[3 + (c % 2)]
            for hh in range(8):
                k.op("tensor", lambda e, c=c, g=g, hh=hh, bank=bank, ob=ob: e.matmul(
                    bank[:], wout_sb[:, hh * 1024 + c * 128: hh * 1024 + (c + 1) * 128],
                    otile[ob][hh][:],
                    start=(hh == 0), stop=(hh == 7)),
                    waits=[(solod, 16 * 8 * (g + 1)), (s_ldw, s_ldw.n),
                           (st_dve, max(0, n - 1))] if hh == 0 else (),
                    inc=prj if hh == 7 else None)
            k.op("vector", lambda e, b=b, c=c, bank=bank: e.tensor_add(
                yout[b][:], bank[:], xre[b][:]),
                waits=[(prj, prj.n), (sxre[b], sxre[b].n), (sst[b], sst[b].n)],
                inc=st_dve)
            k.op("scalar", lambda e, c=c, g=g, b=b: e.dma_start(
                out=x1T[c * 128:(c + 1) * 128, g * 512:(g + 1) * 512], in_=yout[b][:]),
                waits=[(st_dve, st_dve.n)], inc=sst[b], inck=16)
    return



def build_l0_attn_program(T):
    nc = bass.Bass("TRN2", target_bir_lowering=False)
    xT = nc.dram_tensor("xT", [1024, T], F32, kind="ExternalInput").ap()
    wqkvT = nc.dram_tensor("wqkvT", [1024, 3072], F32, kind="ExternalInput").ap()
    woutT = nc.dram_tensor("woutT", [1024, 1024], F32, kind="ExternalInput").ap()
    gammacol = nc.dram_tensor("gammacol", [1024, 1], F32, kind="ExternalInput").ap()
    x1T = nc.dram_tensor("x1T", [1024, T], F32, kind="ExternalOutput").ap()
    ctx = contextlib.ExitStack()
    with ctx:
        E = ctx.enter_context
        k = K(nc, E)
        env = Env(nc, E, k)
        emit_l0_attn(env, T, xT, x1T, wqkvT, woutT, gammacol)
        k.final_barrier()
        block = E(nc.Block())
        k.finish(block)
    return nc


def run_l0_attn(x, w_qkv, w_out, gamma, cores=1):
    B, T, D = x.shape
    nc = build_l0_attn_program(T)
    consts = make_const_arrays()
    in_maps = []
    for b in range(B):
        m = dict(consts)
        m["xT"] = np.ascontiguousarray(x[b].T)
        m["wqkvT"] = np.ascontiguousarray(w_qkv.T)
        m["woutT"] = np.ascontiguousarray(w_out.T)
        m["gammacol"] = np.ascontiguousarray(gamma[:, None])
        in_maps.append(m)
    res = run_bass_kernel_spmd(nc, in_maps, core_ids=list(range(B)))
    return np.stack([r["x1T"].T for r in res.results], axis=0)


def sim_l0_attn(x, w_qkv, w_out, gamma):
    from concourse.bass_interp import CoreSim
    B, T, D = x.shape
    assert B == 1
    nc = build_l0_attn_program(T)
    nc.finalize()
    sim = CoreSim(nc)
    consts = make_const_arrays()
    feed = dict(consts)
    feed["xT"] = np.ascontiguousarray(x[0].T)
    feed["wqkvT"] = np.ascontiguousarray(w_qkv.T)
    feed["woutT"] = np.ascontiguousarray(w_out.T)
    feed["gammacol"] = np.ascontiguousarray(gamma[:, None])
    for name, arr in feed.items():
        v = sim.tensor(name)
        v[:] = arr.view(v.dtype) if arr.dtype == NP_BF16 else arr
    sim.simulate()
    return sim.tensor("x1T").T[None].copy()



CAPL = 640


def emit_l0_router(env: Env, T, xT, WrAp, xe_dram, idx_out, ptok_out, lrloc_out,
                   oneh_out, counts_out, capl=CAPL, lite=False):
    nc, E, k, cn = env.nc, env.E, env.k, env.cn
    S, PB = env.S, env.PB
    G = T // 512
    NT = T // 128

    wr_sb = E(nc.sbuf_tensor("f_wr", [128, 8 * 8], F32))
    cswr = E(nc.sbuf_tensor("f_cswr", [1, 8], F32))
    slotb = E(nc.sbuf_tensor("f_slotb", [1, 8], F32))
    offs = [E(nc.sbuf_tensor(f"f_offs{i}", [1, 8], F32)) for i in range(2)]
    pk_all = E(nc.sbuf_tensor("f_pk", [128, NT], F32))
    idx_all = E(nc.sbuf_tensor("f_idx", [128, NT], I32))
    lr_all = E(nc.sbuf_tensor("f_lr", [128, NT], F32))
    oh_all = E(nc.sbuf_tensor("f_oh", [128, NT * 8], F32))

    s_ldw = k.sem()
    rt_pe = k.sem()
    rt_dve = k.sem()
    rt_act = k.sem()
    rt_sc = k.sem()
    sze = k.sem()
    rt_st = k.sem()
    W = lambda *ss: [s.now() for s in ss]
    cw = (env.ld, env.ld.n)

    for c in range(8):
        k.op("sync", lambda e, c=c: e.dma_start(
            out=wr_sb[:, c * 8:(c + 1) * 8], in_=WrAp[c * 128:(c + 1) * 128, :]),
            inc=s_ldw, inck=16)
    for c in range(8):
        k.op("tensor", lambda e, c=c: e.matmul(
            S[5][0:1, 0:8], cn["ones_col_f"][:], wr_sb[:, c * 8:(c + 1) * 8],
            start=(c == 0), stop=(c == 7)),
            waits=[cw, (s_ldw, s_ldw.n)] if c == 0 else (),
            inc=rt_pe if c == 7 else None)
    k.op("vector", lambda e: e.tensor_copy(cswr[:], S[5][0:1, 0:8]),
         waits=W(rt_pe), inc=rt_dve)
    for ei in range(8):
        k.op("vector", lambda e, ei=ei: e.memset(slotb[0:1, ei:ei + 1], float(ei * capl)),
             inc=rt_dve)
    k.op("vector", lambda e: e.memset(offs[0][:], 0.0), inc=rt_dve)

    with contextlib.ExitStack() as rctx:
        RE = rctx.enter_context
        xgf = RE(nc.sbuf_tensor("f_xgf", [128, 8 * 512], F32))
        xbb = RE(nc.sbuf_tensor("f_xbb", [128, 8 * 512], BF16))
        negm = RE(nc.sbuf_tensor("f_negm", [1, 512], F32))
        lg = RE(nc.sbuf_tensor("f_lg", [128, 8], F32))
        mxc = RE(nc.sbuf_tensor("f_mxc", [128, 1], F32))
        nmxc = RE(nc.sbuf_tensor("f_nmxc", [128, 1], F32))
        sume = RE(nc.sbuf_tensor("f_sume", [128, 1], F32))
        escr = RE(nc.sbuf_tensor("f_escr", [128, 8], F32))
        scr8 = RE(nc.sbuf_tensor("f_scr8", [128, 8], F32))
        ptok = RE(nc.sbuf_tensor("f_ptok", [128, 1], F32))
        offA = RE(nc.sbuf_tensor("f_offA", [128, 1], F32))
        lrloc = RE(nc.sbuf_tensor("f_lrloc", [128, 1], F32))
        keepl = RE(nc.sbuf_tensor("f_keepl", [128, 1], F32))
        idxf = RE(nc.sbuf_tensor("f_idxf", [128, 1], F32))
        if not lite:
            xtok = [RE(nc.sbuf_tensor(f"f_xtok{i}", [128, 1024], BF16)) for i in range(2)]
            sqt = RE(nc.sbuf_tensor("f_sqt", [128, 1024], BF16))
            mcol = RE(nc.sbuf_tensor("f_mcol", [128, 1], F32))
            qcol = RE(nc.sbuf_tensor("f_qcol", [128, 1], F32))
            vcol = RE(nc.sbuf_tensor("f_vcol", [128, 1], F32))
            rstdc = RE(nc.sbuf_tensor("f_rstdc", [128, 1], F32))
            xn_tok = [RE(nc.sbuf_tensor(f"f_xn{i}", [128, 1024], BF16)) for i in range(2)]
            zt = RE(nc.sbuf_tensor("f_zt", [128, 1024], BF16))
            k.op("vector", lambda e: e.memset(zt[:], 0.0), inc=rt_dve)
            zW = (rt_dve, rt_dve.n)
            for i in range(8 * capl // 128):
                k.op("gpsimd", lambda e, i=i: e.dma_start(
                    out=xe_dram[i * 128:(i + 1) * 128, :], in_=zt[:]),
                    waits=[zW] if i == 0 else (), inc=sze, inck=16)

        s_ldx2 = k.sem()
        s_ldx3 = k.sem()
        for g in range(G):
            ldb2 = s_ldx2.n
            ldb3 = s_ldx3.n
            for c in range(8):
                k.op("gpsimd", lambda e, c=c, g=g: e.dma_start(
                    out=xbb[:, c * 512:(c + 1) * 512],
                    in_=xT[c * 128:(c + 1) * 128, g * 512:(g + 1) * 512]),
                    waits=[(rt_pe, rt_pe.n)] if (g > 0 and c == 0) else (),
                    inc=s_ldx2, inck=16)
                k.op("sync", lambda e, c=c, g=g: e.dma_start(
                    out=xgf[:, c * 512:(c + 1) * 512],
                    in_=xT[c * 128:(c + 1) * 128, g * 512:(g + 1) * 512]),
                    waits=[(rt_pe, rt_pe.n)] if (g > 0 and c == 0) else (),
                    inc=s_ldx3, inck=16)
            for c in range(8):
                k.op("tensor", lambda e, c=c: e.matmul(
                    S[5][0:1, 0:512], cn["ones_col"][:], xbb[:, c * 512:(c + 1) * 512],
                    start=(c == 0), stop=(c == 7)),
                    waits=[(s_ldx2, ldb2 + 128), (rt_dve, rt_dve.n)] if c == 0 else (),
                    inc=rt_pe if c == 7 else None)
            k.op("vector", lambda e: e.tensor_scalar(
                negm[:], S[5][0:1, 0:512], -1.0 / 1024, None, op0=ALU.mult),
                waits=W(rt_pe), inc=rt_dve)

            for ti in range(4):
                t = g * 4 + ti
                tb = t % 2
                if not lite:
                    for rnd in range(2):
                        for cc in range(4):
                            c = rnd * 4 + cc
                            k.op("tensor", lambda e, c=c, cc=cc, ti=ti: e.transpose(
                                PB[:, cc * 128:(cc + 1) * 128],
                                xbb[:, c * 512 + ti * 128: c * 512 + (ti + 1) * 128],
                                cn["ident_bf"][:]),
                                waits=W(rt_dve) if cc == 0 else (), inc=rt_pe)
                        k.op("vector", lambda e, rnd=rnd, tb=tb: e.tensor_copy(
                            xtok[tb][:, rnd * 512:(rnd + 1) * 512], PB[:, 0:512]),
                            waits=W(rt_pe), inc=rt_dve)
                    k.op("scalar", lambda e, tb=tb: e.activation(
                        sqt[:], xtok[tb][:], AF.Square, bias=0.0, scale=1.0),
                        waits=W(rt_dve), inc=rt_act)
                    k.op("vector", lambda e, tb=tb: e.reduce_sum(
                        mcol[:], xtok[tb][:], axis=AX.X), waits=W(rt_dve), inc=rt_dve)
                    k.op("vector", lambda e: e.reduce_sum(
                        qcol[:], sqt[:], axis=AX.X), waits=W(rt_act), inc=rt_dve)
                    k.op("vector", lambda e: e.tensor_scalar(
                        mcol[:], mcol[:], 1.0 / 1024, None, op0=ALU.mult),
                        waits=W(rt_dve), inc=rt_dve)
                    k.op("vector", lambda e: e.tensor_scalar(
                        vcol[:], mcol[:], mcol[:, 0:1], None, op0=ALU.mult),
                        waits=W(rt_dve), inc=rt_dve)
                    k.op("vector", lambda e: e.scalar_tensor_tensor(
                        vcol[:], qcol[:], 1.0 / 1024, vcol[:],
                        op0=ALU.mult, op1=ALU.subtract),
                        waits=W(rt_dve), inc=rt_dve)
                    k.op("vector", lambda e: e.tensor_scalar(
                        vcol[:], vcol[:], 1e-5, None, op0=ALU.add),
                        waits=W(rt_dve), inc=rt_dve)
                    k.op("scalar", lambda e: e.activation(
                        rstdc[:], vcol[:], AF.Ln, bias=0.0, scale=1.0),
                        waits=W(rt_dve), inc=rt_act)
                    k.op("scalar", lambda e: e.activation(
                        rstdc[:], rstdc[:], AF.Exp, bias=0.0, scale=-0.5),
                        waits=W(rt_act), inc=rt_act)
                    k.op("vector", lambda e, tb=tb: e.tensor_scalar(
                        xn_tok[tb][:], xtok[tb][:], mcol[:, 0:1], rstdc[:, 0:1],
                        op0=ALU.subtract, op1=ALU.mult),
                        waits=W(rt_act, rt_sc), inc=rt_dve)
                for c in range(8):
                    k.op("tensor", lambda e, c=c, ti=ti: e.matmul(
                        S[6][:, 0:8],
                        xgf[:, c * 512 + ti * 128: c * 512 + (ti + 1) * 128],
                        wr_sb[:, c * 8:(c + 1) * 8], start=(c == 0), stop=False),
                        waits=([(s_ldx3, ldb3 + 128)] + W(rt_dve)) if c == 0 else ())
                k.op("tensor", lambda e, ti=ti: e.matmul(
                    S[6][:, 0:8], negm[0:1, ti * 128:(ti + 1) * 128], cswr[:],
                    start=False, stop=True), inc=rt_pe)
                k.op("vector", lambda e: e.reduce_max(
                    mxc[:], S[6][:, 0:8], axis=AX.X), waits=W(rt_pe), inc=rt_dve)
                k.op("vector", lambda e, t=t: e.tensor_scalar(
                    oh_all[:, t * 8:(t + 1) * 8], S[6][:, 0:8], mxc[:, 0:1],
                    None, op0=ALU.is_ge),
                    waits=W(rt_dve), inc=rt_dve)
                if not lite:
                    k.op("vector", lambda e: e.tensor_scalar(
                        lg[:], S[6][:, 0:8], rstdc[:, 0:1], None, op0=ALU.mult),
                        waits=W(rt_dve, rt_act), inc=rt_dve)
                    k.op("vector", lambda e: e.reduce_max(
                        nmxc[:], lg[:], axis=AX.X, negate=True),
                        waits=W(rt_dve), inc=rt_dve)
                    k.op("scalar", lambda e: e.activation(
                        escr[:], lg[:], AF.Exp, bias=nmxc[:, 0:1], scale=1.0,
                        accum_out=sume[:, 0:1]), waits=W(rt_dve), inc=rt_act)
                    k.op("vector", lambda e: e.reciprocal(
                        ptok[:], sume[:]), waits=W(rt_act), inc=rt_dve)
                    k.op("vector", lambda e, t=t: e.tensor_copy(
                        pk_all[:, t:t + 1], ptok[:]), waits=W(rt_dve), inc=rt_dve)
                oz = (rt_dve, rt_dve.n)
                pairs = ((0, slotb), (1, None)) if not lite else ((0, slotb),)
                for bi, extra in pairs:
                    reg = bi * 16
                    k.op("tensor", lambda e, reg=reg, t=t: e.matmul(
                        S[4][:, reg:reg + 8], cn["strictL_f"][:],
                        oh_all[:, t * 8:(t + 1) * 8],
                        start=True, stop=False), waits=[oz])
                    k.op("tensor", lambda e, reg=reg, t=t, extra=extra: e.matmul(
                        S[4][:, reg:reg + 8], cn["ones_row_f"][:], offs[t % 2][:],
                        start=False, stop=(extra is None)),
                        inc=rt_pe if extra is None else None)
                    if extra is not None:
                        k.op("tensor", lambda e, reg=reg, extra=extra: e.matmul(
                            S[4][:, reg:reg + 8], cn["ones_row_f"][:], extra[:],
                            start=False, stop=True), inc=rt_pe)
                k.op("tensor", lambda e, t=t: e.matmul(
                    S[5][0:1, 504:512], cn["ones_col_f"][:],
                    oh_all[:, t * 8:(t + 1) * 8],
                    start=True, stop=True), waits=[oz], inc=rt_pe)
                k.op("vector", lambda e, t=t: e.tensor_add(
                    offs[(t + 1) % 2][:], offs[t % 2][:], S[5][0:1, 504:512]),
                    waits=W(rt_pe), inc=rt_dve)
                if not lite:
                    k.op("vector", lambda e, t=t: e.tensor_mul(
                        scr8[:], oh_all[:, t * 8:(t + 1) * 8], S[4][:, 0:8]),
                        waits=W(rt_pe), inc=rt_dve)
                    k.op("vector", lambda e: e.reduce_sum(offA[:], scr8[:], axis=AX.X),
                         waits=W(rt_dve), inc=rt_dve)
                    k.op("vector", lambda e, t=t: e.tensor_mul(
                        scr8[:], oh_all[:, t * 8:(t + 1) * 8], S[4][:, 16:24]),
                        waits=W(rt_dve), inc=rt_dve)
                    k.op("vector", lambda e, t=t: e.reduce_sum(
                        lr_all[:, t:t + 1], scr8[:], axis=AX.X),
                        waits=W(rt_dve), inc=rt_dve)
                    k.op("vector", lambda e, t=t: e.tensor_scalar(
                        keepl[:], lr_all[:, t:t + 1], float(capl), None, op0=ALU.is_lt),
                        waits=W(rt_dve), inc=rt_dve)
                    k.op("vector", lambda e: e.tensor_scalar(
                        idxf[:], keepl[:], -1e9, 1e9, op0=ALU.mult, op1=ALU.add),
                        waits=W(rt_dve), inc=rt_dve)
                    k.op("vector", lambda e: e.tensor_add(idxf[:], idxf[:], offA[:]),
                         waits=W(rt_dve), inc=rt_dve)
                    k.op("vector", lambda e, t=t: e.tensor_copy(
                        idx_all[:, t:t + 1], idxf[:]), waits=W(rt_dve), inc=rt_dve)
                    k.op("gpsimd", lambda e, t=t, tb=tb: e.indirect_dma_start(
                        xe_dram, IndirectOffsetOnAxis(idx_all[:, t:t + 1], 0),
                        xn_tok[tb][:], None,
                        bounds_check=8 * capl - 1, oob_is_err=False),
                        waits=[(rt_dve, rt_dve.n), (sze, sze.n),
                               (rt_sc, max(0, rt_sc.n - 64))], inc=rt_sc, inck=16)
        if counts_out is not None:
            k.op("scalar", lambda e: e.dma_start(
                out=counts_out, in_=offs[NT % 2][:]),
                waits=W(rt_dve), inc=rt_st, inck=16)
        if not lite:
            k.op("scalar", lambda e: e.dma_start(out=idx_out, in_=idx_all[:]),
                 waits=W(rt_dve), inc=rt_st, inck=16)
            k.op("scalar", lambda e: e.dma_start(out=ptok_out, in_=pk_all[:]),
                 waits=W(rt_dve), inc=rt_st, inck=16)
            k.op("scalar", lambda e: e.dma_start(out=lrloc_out, in_=lr_all[:]),
                 waits=W(rt_dve), inc=rt_st, inck=16)
            k.op("scalar", lambda e: e.dma_start(out=oneh_out, in_=oh_all[:]),
                 waits=W(rt_dve), inc=rt_st, inck=16)


def emit_l0_ffn_rest(env: Env, T, xT, x2T, W1Ap, W2Ap, xeAp, idxAp, ptokAp,
                     lrlocAp, onehAp, countsAp, coreltAp, cap_global, capl=CAPL):
    nc, E, k, cn = env.nc, env.E, env.k, env.cn
    S, PB = env.S, env.PB
    G = T // 512
    NT = T // 128
    ye_dram = nc.dram_tensor("ye_dram", [8 * capl, 1024], BF16, kind="Internal").ap()

    pk_all = E(nc.sbuf_tensor("f_pk", [128, NT], F32))
    idx_all = E(nc.sbuf_tensor("f_idx", [128, NT], I32))
    gidx_all = E(nc.sbuf_tensor("f_gidx", [128, NT], I32))
    lr_all = E(nc.sbuf_tensor("f_lr", [128, NT], F32))
    oh_all = E(nc.sbuf_tensor("f_oh", [128, NT * 8], F32))
    counts_sb = E(nc.sbuf_tensor("f_counts", [8, 8], F32))
    corelt_sb = E(nc.sbuf_tensor("f_corelt", [8, 1], F32))
    gbrow = E(nc.sbuf_tensor("f_gbrow", [1, 8], F32))
    gbb = E(nc.sbuf_tensor("f_gbb", [128, 8], F32))
    scr8 = E(nc.sbuf_tensor("f_scr8", [128, 8], F32))
    gbcol = E(nc.sbuf_tensor("f_gbcol", [128, 1], F32))
    keepc = E(nc.sbuf_tensor("f_keepc", [128, 1], F32))
    gif = E(nc.sbuf_tensor("f_gif", [128, 1], F32))

    s_ldm = k.sem()
    pr_pe = k.sem()
    pr_dve = k.sem()
    W = lambda *ss: [s.now() for s in ss]
    cw = (env.ld, env.ld.n)

    for (dst, src) in ((pk_all, ptokAp), (idx_all, idxAp), (lr_all, lrlocAp),
                       (oh_all, onehAp), (counts_sb, countsAp), (corelt_sb, coreltAp)):
        k.op("sync", lambda e, dst=dst, src=src: e.dma_start(out=dst[:], in_=src),
             inc=s_ldm, inck=16)
    k.op("tensor", lambda e: e.matmul(
        S[5][0:1, 0:8], corelt_sb[:], counts_sb[:], start=True, stop=True),
        waits=[cw, (s_ldm, s_ldm.n)], inc=pr_pe)
    k.op("vector", lambda e: e.tensor_copy(gbrow[:], S[5][0:1, 0:8]),
         waits=W(pr_pe), inc=pr_dve)
    k.op("tensor", lambda e: e.matmul(
        S[5][:, 16:24], cn["ones_row_f"][:], gbrow[:], start=True, stop=True),
        waits=W(pr_dve), inc=pr_pe)
    k.op("vector", lambda e: e.tensor_copy(gbb[:], S[5][:, 16:24]),
         waits=W(pr_pe), inc=pr_dve)
    for t in range(NT):
        k.op("vector", lambda e, t=t: e.tensor_mul(
            scr8[:], oh_all[:, t * 8:(t + 1) * 8], gbb[:]),
            waits=W(pr_dve), inc=pr_dve)
        k.op("vector", lambda e: e.reduce_sum(gbcol[:], scr8[:], axis=AX.X),
             waits=W(pr_dve), inc=pr_dve)
        k.op("vector", lambda e, t=t: e.tensor_add(
            gbcol[:], gbcol[:], lr_all[:, t:t + 1]), waits=W(pr_dve), inc=pr_dve)
        k.op("vector", lambda e: e.tensor_scalar(
            keepc[:], gbcol[:], float(cap_global), None, op0=ALU.is_lt),
            waits=W(pr_dve), inc=pr_dve)
        k.op("vector", lambda e, t=t: e.tensor_scalar(
            pk_all[:, t:t + 1], pk_all[:, t:t + 1], keepc[:, 0:1], None,
            op0=ALU.mult), waits=W(pr_dve), inc=pr_dve)
        k.op("vector", lambda e, t=t: e.tensor_scalar(
            gif[:], keepc[:], -1e9, 1e9, op0=ALU.mult, op1=ALU.add),
            waits=W(pr_dve), inc=pr_dve)
        k.op("vector", lambda e, t=t: e.scalar_tensor_tensor(
            gif[:], idx_all[:, t:t + 1], 1.0, gif[:], op0=ALU.mult, op1=ALU.add),
            waits=W(pr_dve), inc=pr_dve)
        k.op("vector", lambda e, t=t: e.tensor_copy(
            gidx_all[:, t:t + 1], gif[:]), waits=W(pr_dve), inc=pr_dve)
    prW = (pr_dve, pr_dve.n)

    NS = 32
    nrt = capl // 128
    with contextlib.ExitStack() as ectx:
        EE = ectx.enter_context
        w1b = [EE(nc.sbuf_tensor(f"f_w1b{i}", [128, 8 * 512], BF16)) for i in range(2)]
        w2h = EE(nc.sbuf_tensor("f_w2h", [128, NS * 512], BF16))
        xef = EE(nc.sbuf_tensor("f_xef", [128, 8 * capl], BF16))
        h_sb = EE(nc.sbuf_tensor("f_hsb", [128, NS * capl], BF16))
        yesb = [EE(nc.sbuf_tensor(f"f_yesb{i}", [128, 512], BF16)) for i in range(2)]
        s_w1 = k.sem()
        s_w2 = k.sem()
        s_xe = k.sem()
        fe_pe = k.sem()
        fe_act = k.sem()
        fe_st = k.sem()
        for ei in range(8):
            xeW = s_xe.n
            for c in range(8):
                k.op("sync", lambda e, c=c, ei=ei: e.dma_start_transpose(
                    xef[:, c * capl:(c + 1) * capl],
                    xeAp[ei * capl:(ei + 1) * capl, c * 128:(c + 1) * 128]),
                    waits=[(fe_pe, fe_pe.n)] if c == 0 else (), inc=s_xe, inck=16)
            for fb in range(8):
                dbl = fb % 2
                w1W = s_w1.n
                for c in range(8):
                    k.op("gpsimd", lambda e, c=c, ei=ei, fb=fb, dbl=dbl: e.dma_start(
                        out=w1b[dbl][:, c * 512:(c + 1) * 512],
                        in_=W1Ap[ei * 1024 + c * 128: ei * 1024 + (c + 1) * 128,
                                 fb * 512:(fb + 1) * 512]),
                        waits=[(fe_pe, fe_pe.n)] if c == 0 else (), inc=s_w1, inck=16)
                for sl in range(4):
                    sidx = fb * 4 + sl
                    pieces = [(S[6], 0, min(512, capl))]
                    if capl > 512:
                        pieces.append((S[5], 512, capl - 512))
                    for (bank, w0, w1w) in pieces:
                        for c in range(8):
                            k.op("tensor", lambda e, c=c, sl=sl, dbl=dbl, w0=w0,
                                 w1w=w1w, bank=bank:
                                 e.matmul(bank[:, 0:w1w],
                                          w1b[dbl][:, c * 512 + sl * 128:
                                                   c * 512 + (sl + 1) * 128],
                                          xef[:, c * capl + w0: c * capl + w0 + w1w],
                                          start=(c == 0), stop=(c == 7)),
                                 waits=[(s_w1, w1W + 128), (s_xe, xeW + 128),
                                        (fe_act, fe_act.n)] if c == 0 else (),
                                 inc=fe_pe if c == 7 else None)
                    k.op("scalar", lambda e, sidx=sidx: e.activation(
                        h_sb[:, sidx * capl: sidx * capl + min(512, capl)],
                        S[6][:, 0:min(512, capl)],
                        AF.Relu, bias=0.0, scale=1.0), waits=W(fe_pe), inc=fe_act)
                    if capl > 512:
                        k.op("scalar", lambda e, sidx=sidx: e.activation(
                            h_sb[:, sidx * capl + 512: (sidx + 1) * capl],
                            S[5][:, 0:capl - 512],
                            AF.Relu, bias=0.0, scale=1.0), inc=fe_act)
            for dh in range(2):
                w2W = s_w2.n
                for sidx in range(NS):
                    k.op("gpsimd", lambda e, sidx=sidx, ei=ei, dh=dh: e.dma_start(
                        out=w2h[:, sidx * 512:(sidx + 1) * 512],
                        in_=W2Ap[ei * 4096 + sidx * 128: ei * 4096 + (sidx + 1) * 128,
                                 dh * 512:(dh + 1) * 512]),
                        waits=[(fe_pe, fe_pe.n)] if sidx == 0 else (),
                        inc=s_w2, inck=16)
                for rt in range(nrt):
                    bank = S[rt % 2]
                    for sidx in range(NS):
                        k.op("tensor", lambda e, sidx=sidx, rt=rt, bank=bank: e.matmul(
                            bank[:], h_sb[:, sidx * capl + rt * 128:
                                          sidx * capl + (rt + 1) * 128],
                            w2h[:, sidx * 512:(sidx + 1) * 512],
                            start=(sidx == 0), stop=(sidx == NS - 1)),
                            waits=[(s_w2, w2W + 16 * NS), (fe_act, fe_act.n),
                                   (fe_st, fe_st.n)] if sidx == 0 else (),
                            inc=fe_pe if sidx == NS - 1 else None)
                    k.op("scalar", lambda e, rt=rt, bank=bank: e.activation(
                        yesb[rt % 2][:], bank[:], AF.Copy, bias=0.0, scale=1.0),
                        waits=W(fe_pe) + [(fe_st, fe_st.n)], inc=fe_act)
                    k.op("scalar", lambda e, rt=rt, ei=ei, dh=dh: e.dma_start(
                        out=ye_dram[ei * capl + rt * 128: ei * capl + (rt + 1) * 128,
                                    dh * 512:(dh + 1) * 512], in_=yesb[rt % 2][:]),
                        waits=W(fe_act), inc=fe_st, inck=16)

    with contextlib.ExitStack() as gctx:
        GE = gctx.enter_context
        ytile = [GE(nc.sbuf_tensor(f"f_yt{i}", [128, 1024], BF16)) for i in range(4)]
        xrg = [GE(nc.sbuf_tensor(f"f_xrg{i}", [128, 512], F32)) for i in range(4)]
        x2o = [GE(nc.sbuf_tensor(f"f_x2o{i}", [128, 512], F32)) for i in range(4)]
        ga = k.sem()
        gd = k.sem()
        gpe = k.sem()
        gst = [k.sem() for _ in range(4)]
        gxr = [k.sem() for _ in range(4)]
        yeW = (fe_st, fe_st.n)
        feW = [(fe_pe, fe_pe.n), (fe_act, fe_act.n)]
        k.op("gpsimd", lambda e: None, waits=[(s_w1, s_w1.n), (s_w2, s_w2.n)])
        prev_gpe = 0
        for g in range(G):
            for ti in range(4):
                t = g * 4 + ti
                tb = ti
                k.op("vector", lambda e, tb=tb: e.memset(ytile[tb][:], 0.0),
                     waits=[(gpe, prev_gpe), prW] + (feW if (g == 0) else []), inc=gd)
                k.op("gpsimd", lambda e, t=t, tb=tb: e.indirect_dma_start(
                    ytile[tb][:], None, ye_dram,
                    IndirectOffsetOnAxis(gidx_all[:, t:t + 1], 0),
                    bounds_check=8 * capl - 1, oob_is_err=False),
                    waits=[yeW, (gd, gd.n), (ga, ga.n)], inc=ga, inck=16)
                k.op("vector", lambda e, t=t, tb=tb: e.tensor_scalar(
                    ytile[tb][:], ytile[tb][:], pk_all[:, t:t + 1], None,
                    op0=ALU.mult), waits=[(ga, ga.n)], inc=gd)
            for c in range(8):
                n = g * 8 + c
                b = n % 4
                k.op("sync", lambda e, c=c, g=g, b=b: e.dma_start(
                    out=xrg[b][:], in_=xT[c * 128:(c + 1) * 128,
                                          g * 512:(g + 1) * 512]),
                    waits=[(gd, gd.n)] if n < 4 else [(gd, gd.n), (gst[b], gst[b].n)],
                    inc=gxr[b], inck=16)
                for ti in range(4):
                    k.op("tensor", lambda e, c=c, ti=ti: e.transpose(
                        PB[:, ti * 128:(ti + 1) * 128],
                        ytile[ti][:, c * 128:(c + 1) * 128], cn["ident_bf"][:]),
                        waits=[(gd, gd.n), (gpe, gpe.n)] if ti == 0 else (), inc=gpe)
                k.op("vector", lambda e, b=b: e.tensor_add(
                    x2o[b][:], PB[:, 0:512], xrg[b][:]),
                    waits=[(gpe, gpe.n), (gxr[b], gxr[b].n)], inc=gd)
                k.op("scalar", lambda e, c=c, g=g, b=b: e.dma_start(
                    out=x2T[c * 128:(c + 1) * 128, g * 512:(g + 1) * 512],
                    in_=x2o[b][:]), waits=[(gd, gd.n)], inc=gst[b], inck=16)
            prev_gpe = gpe.n

def build_l0_router_program(T, capl=CAPL, lite=False):
    nc = bass.Bass("TRN2", target_bir_lowering=False)
    xT = nc.dram_tensor("xT", [1024, T], F32, kind="ExternalInput").ap()
    Wr = nc.dram_tensor("Wr", [1024, 8], F32, kind="ExternalInput").ap()
    counts_out = nc.dram_tensor("counts", [1, 8], F32, kind="ExternalOutput").ap()
    NT = T // 128
    if lite:
        xe = idx = ptok = lrloc = oneh = None
    else:
        xe = nc.dram_tensor("xe", [8 * capl, 1024], BF16, kind="ExternalOutput").ap()
        idx = nc.dram_tensor("idx", [128, NT], I32, kind="ExternalOutput").ap()
        ptok = nc.dram_tensor("ptok", [128, NT], F32, kind="ExternalOutput").ap()
        lrloc = nc.dram_tensor("lrloc", [128, NT], F32, kind="ExternalOutput").ap()
        oneh = nc.dram_tensor("oneh", [128, NT * 8], F32, kind="ExternalOutput").ap()
    ctx = contextlib.ExitStack()
    with ctx:
        E = ctx.enter_context
        k = K(nc, E)
        env = Env(nc, E, k)
        emit_l0_router(env, T, xT, Wr, xe, idx, ptok, lrloc, oneh, counts_out,
                       capl=capl, lite=lite)
        k.final_barrier()
        block = E(nc.Block())
        k.finish(block)
    return nc


def build_l0_rest_program(T, cap_global, capl=CAPL):
    nc = bass.Bass("TRN2", target_bir_lowering=False)
    NT = T // 128
    xT = nc.dram_tensor("xT", [1024, T], F32, kind="ExternalInput").ap()
    W1 = nc.dram_tensor("W1", [8 * 1024, 4096], F32, kind="ExternalInput").ap()
    W2 = nc.dram_tensor("W2", [8 * 4096, 1024], F32, kind="ExternalInput").ap()
    xe = nc.dram_tensor("xe", [8 * capl, 1024], BF16, kind="ExternalInput").ap()
    idx = nc.dram_tensor("idx", [128, NT], I32, kind="ExternalInput").ap()
    ptok = nc.dram_tensor("ptok", [128, NT], F32, kind="ExternalInput").ap()
    lrloc = nc.dram_tensor("lrloc", [128, NT], F32, kind="ExternalInput").ap()
    oneh = nc.dram_tensor("oneh", [128, NT * 8], F32, kind="ExternalInput").ap()
    countsA = nc.dram_tensor("countsA", [8, 8], F32, kind="ExternalInput").ap()
    corelt = nc.dram_tensor("corelt", [8, 1], F32, kind="ExternalInput").ap()
    x2T = nc.dram_tensor("x2T", [1024, T], F32, kind="ExternalOutput").ap()
    ctx = contextlib.ExitStack()
    with ctx:
        E = ctx.enter_context
        k = K(nc, E)
        env = Env(nc, E, k)
        emit_l0_ffn_rest(env, T, xT, x2T, W1, W2, xe, idx, ptok, lrloc, oneh,
                         countsA, corelt, cap_global, capl=capl)
        k.final_barrier()
        block = E(nc.Block())
        k.finish(block)
    return nc


def run_l0_ffn(x, Wr, W1, W2, capl=CAPL):
    B, T, D = x.shape
    cap = -(-B * T // 8)
    consts = make_const_arrays()
    nc1 = build_l0_router_program(T, capl=capl)
    in_maps = []
    for b in range(B):
        m = dict(consts)
        m["xT"] = np.ascontiguousarray(x[b].T)
        m["Wr"] = np.ascontiguousarray(Wr)
        in_maps.append(m)
    res1 = run_bass_kernel_spmd(nc1, in_maps, core_ids=list(range(B)))
    counts_all = np.zeros((8, 8), np.float32)
    for b in range(B):
        counts_all[b] = res1.results[b]["counts"][0]
    nc2 = build_l0_rest_program(T, cap, capl=capl)
    in_maps = []
    for b in range(B):
        r = res1.results[b]
        m = dict(consts)
        m["xT"] = np.ascontiguousarray(x[b].T)
        m["W1"] = np.ascontiguousarray(W1.reshape(8 * 1024, 4096))
        m["W2"] = np.ascontiguousarray(W2.reshape(8 * 4096, 1024))
        m["xe"] = r["xe"]
        m["idx"] = r["idx"]
        m["ptok"] = r["ptok"]
        m["lrloc"] = r["lrloc"]
        m["oneh"] = r["oneh"]
        m["countsA"] = counts_all
        m["corelt"] = (np.arange(8) < b).astype(np.float32)[:, None]
        in_maps.append(m)
    res2 = run_bass_kernel_spmd(nc2, in_maps, core_ids=list(range(B)))
    return np.stack([r["x2T"].T for r in res2.results], axis=0)


def sim_l0_ffn(x, Wr, W1, W2, capl=CAPL):
    from concourse.bass_interp import CoreSim
    B, T, D = x.shape
    assert B == 1
    cap = -(-T // 8)
    consts = make_const_arrays()
    nc1 = build_l0_router_program(T, capl=capl)
    nc1.finalize()
    sim = CoreSim(nc1)
    feed = dict(consts)
    feed["xT"] = np.ascontiguousarray(x[0].T)
    feed["Wr"] = np.ascontiguousarray(Wr)
    for name, arr in feed.items():
        v = sim.tensor(name)
        v[:] = arr.view(v.dtype) if arr.dtype == NP_BF16 else arr
    sim.simulate()
    outs1 = {n: sim.tensor(n).copy() for n in ("xe", "idx", "ptok", "lrloc", "oneh", "counts")}
    counts_all = np.zeros((8, 8), np.float32)
    counts_all[0] = outs1["counts"][0]
    nc2 = build_l0_rest_program(T, cap, capl=capl)
    nc2.finalize()
    sim2 = CoreSim(nc2)
    feed = dict(consts)
    feed["xT"] = np.ascontiguousarray(x[0].T)
    feed["W1"] = np.ascontiguousarray(W1.reshape(8 * 1024, 4096))
    feed["W2"] = np.ascontiguousarray(W2.reshape(8 * 4096, 1024))
    for n in ("xe", "idx", "ptok", "lrloc", "oneh"):
        feed[n] = outs1[n]
    feed["countsA"] = counts_all
    feed["corelt"] = np.zeros((8, 1), np.float32)
    for name, arr in feed.items():
        v = sim2.tensor(name)
        v[:] = arr.view(v.dtype) if arr.dtype in (NP_BF16,) else arr
    sim2.simulate()
    return sim2.tensor("x2T").T[None].copy(), outs1["counts"]



def emit_l1_attn(env: Env, T, xT, x3T, nwcol, wqT, wkT, wvT, wfT, gncol, woT):
    nc, E, k, cn = env.nc, env.E, env.k, env.cn
    S, PB = env.S, env.PB
    G = T // 512
    NCH = T // 128
    H, DKH, M = 4, 256, 256
    o_dram = nc.dram_tensor("o1_dram", [1024, T], BF16, kind="Internal").ap()

    W = lambda *ss: [s.now() for s in ss]
    cw = (env.ld, env.ld.n)

    nw_f = E(nc.sbuf_tensor("a1_nwf", [128, 8], F32))
    nw_b = E(nc.sbuf_tensor("a1_nwb", [128, 8], BF16))
    gn_f = E(nc.sbuf_tensor("a1_gnf", [128, 8], F32))
    xh = E(nc.sbuf_tensor("a1_xh", [128, 8 * T], BF16))
    negmr = E(nc.sbuf_tensor("a1_negmr", [1, T], BF16))
    wo_sb = E(nc.sbuf_tensor("a1_wo", [128, 8 * 1024], BF16))

    s_ldg = k.sem()
    s_ldx = k.sem()
    s_ldw = k.sem()
    s_ldwo = k.sem()
    ln_pe = k.sem()
    ln_dve = k.sem()
    ln_act = k.sem()
    ln_done = k.sem()

    for c in range(8):
        k.op("sync", lambda e, c=c: e.dma_start(
            out=nw_f[:, c:c + 1], in_=nwcol[c * 128:(c + 1) * 128, 0:1]),
            inc=s_ldg, inck=16)
        k.op("sync", lambda e, c=c: e.dma_start(
            out=gn_f[:, c:c + 1], in_=gncol[c * 128:(c + 1) * 128, 0:1]),
            inc=s_ldg, inck=16)
    k.op("vector", lambda e: e.tensor_copy(nw_b[:], nw_f[:]),
         waits=W(s_ldg) + [cw], inc=ln_dve)
    for c in range(8):
        k.op("gpsimd", lambda e, c=c: e.dma_start(
            out=wo_sb[:, c * 1024:(c + 1) * 1024], in_=woT[c * 128:(c + 1) * 128, :]),
            inc=s_ldwo, inck=16)

    with contextlib.ExitStack() as lctx:
        LE = lctx.enter_context
        xbb = LE(nc.sbuf_tensor("a1_xbb", [128, 8 * 512], BF16))
        sq = [LE(nc.sbuf_tensor(f"a1_sq{i}", [128, 512], BF16)) for i in range(2)]
        m_row = LE(nc.sbuf_tensor("a1_mrow", [1, 512], F32))
        q_row = LE(nc.sbuf_tensor("a1_qrow", [1, 512], F32))
        v_row = LE(nc.sbuf_tensor("a1_vrow", [1, 512], F32))
        a_row = LE(nc.sbuf_tensor("a1_arow", [1, 512], F32))
        r1_row = LE(nc.sbuf_tensor("a1_r1", [1, 512], F32))
        m2_row = LE(nc.sbuf_tensor("a1_m2", [1, 512], F32))
        rs_row = LE(nc.sbuf_tensor("a1_rs", [1, 512], F32))
        rs_b = LE(nc.sbuf_tensor("a1_rsb", [1, 512], BF16))
        for g in range(G):
            ldb = s_ldx.n
            for c in range(8):
                k.op("gpsimd", lambda e, c=c, g=g: e.dma_start(
                    out=xbb[:, c * 512:(c + 1) * 512],
                    in_=xT[c * 128:(c + 1) * 128, g * 512:(g + 1) * 512]),
                    waits=[(ln_done, ln_done.n)] if (g > 0 and c == 0) else (),
                    inc=s_ldx, inck=16)
            for c in range(8):
                k.op("tensor", lambda e, c=c: e.matmul(
                    S[0][0:1, 0:512], cn["ones_col"][:], xbb[:, c * 512:(c + 1) * 512],
                    start=(c == 0), stop=(c == 7)),
                    waits=([(s_ldx, ldb + 128)] + ([cw] if g == 0 else [])) if c == 0 else (),
                    inc=ln_pe if c == 7 else None)
            for c in range(8):
                k.op("scalar", lambda e, c=c: e.activation(
                    sq[c % 2][:], xbb[:, c * 512:(c + 1) * 512], AF.Square,
                    bias=0.0, scale=1.0),
                    waits=[(s_ldx, ldb + 128)] + (W(ln_pe) if c >= 2 else []),
                    inc=ln_act)
                k.op("tensor", lambda e, c=c: e.matmul(
                    S[1][0:1, 0:512], cn["ones_col"][:], sq[c % 2][:],
                    start=(c == 0), stop=(c == 7)),
                    waits=W(ln_act), inc=ln_pe)
            k.op("vector", lambda e: e.tensor_scalar(
                m_row[:], S[0][0:1, 0:512], 1.0 / 1024, None, op0=ALU.mult),
                waits=W(ln_pe), inc=ln_dve)
            k.op("vector", lambda e: e.tensor_scalar(
                q_row[:], S[1][0:1, 0:512], 1.0 / 1024, None, op0=ALU.mult),
                inc=ln_dve)
            k.op("vector", lambda e: e.tensor_mul(v_row[:], m_row[:], m_row[:]),
                 waits=W(ln_dve), inc=ln_dve)
            k.op("vector", lambda e: e.tensor_sub(v_row[:], q_row[:], v_row[:]),
                 waits=W(ln_dve), inc=ln_dve)
            k.op("vector", lambda e: e.tensor_scalar(
                a_row[:], v_row[:], 1e-5, None, op0=ALU.add),
                waits=W(ln_dve), inc=ln_dve)
            k.op("scalar", lambda e: e.activation(r1_row[:], a_row[:], AF.Ln,
                                                  bias=0.0, scale=1.0),
                 waits=W(ln_dve), inc=ln_act)
            k.op("scalar", lambda e: e.activation(rs_row[:], r1_row[:], AF.Exp,
                                                  bias=0.0, scale=-0.5),
                 waits=W(ln_act), inc=ln_act)
            k.op("scalar", lambda e: e.activation(m2_row[:], r1_row[:], AF.Exp,
                                                  bias=0.0, scale=-1.0),
                 waits=W(ln_act), inc=ln_act)
            k.op("vector", lambda e: e.tensor_mul(m2_row[:], m2_row[:], v_row[:]),
                 waits=W(ln_act), inc=ln_dve)
            k.op("vector", lambda e: e.tensor_scalar(
                m2_row[:], m2_row[:], 1e-5, None, op0=ALU.add),
                waits=W(ln_dve), inc=ln_dve)
            k.op("scalar", lambda e: e.activation(m2_row[:], m2_row[:], AF.Ln,
                                                  bias=0.0, scale=1.0),
                 waits=W(ln_dve), inc=ln_act)
            k.op("scalar", lambda e: e.activation(m2_row[:], m2_row[:], AF.Exp,
                                                  bias=0.0, scale=-0.5),
                 waits=W(ln_act), inc=ln_act)
            k.op("vector", lambda e: e.tensor_mul(rs_row[:], rs_row[:], m2_row[:]),
                 waits=W(ln_act), inc=ln_dve)
            k.op("vector", lambda e: e.tensor_copy(rs_b[:], rs_row[:]),
                 waits=W(ln_dve), inc=ln_dve)
            k.op("vector", lambda e, g=g: e.tensor_mul(
                negmr[0:1, g * 512:(g + 1) * 512], m_row[:], rs_row[:]),
                waits=W(ln_dve), inc=ln_dve)
            k.op("vector", lambda e, g=g: e.tensor_scalar(
                negmr[0:1, g * 512:(g + 1) * 512],
                negmr[0:1, g * 512:(g + 1) * 512], -1.0, None, op0=ALU.mult),
                waits=W(ln_dve), inc=ln_dve)
            k.op("tensor", lambda e: e.matmul(
                S[2][:], cn["ones_row"][:], rs_b[:], start=True, stop=True),
                waits=W(ln_dve) + [(ln_done, ln_done.n)], inc=ln_pe)
            for c in range(8):
                k.op("vector", lambda e, c=c, g=g: e.scalar_tensor_tensor(
                    xh[:, c * T + g * 512: c * T + (g + 1) * 512],
                    xbb[:, c * 512:(c + 1) * 512], nw_f[:, c:c + 1], S[2][:],
                    op0=ALU.mult, op1=ALU.mult),
                    waits=W(ln_pe), inc=(ln_done if c == 7 else ln_dve))
    lnw = (ln_done, G)

    hctx = contextlib.ExitStack()
    E = hctx.enter_context
    wq_h = E(nc.sbuf_tensor("a1_wq", [128, 8 * 256], BF16))
    wk_h = E(nc.sbuf_tensor("a1_wk", [128, 8 * 256], BF16))
    wv_h = E(nc.sbuf_tensor("a1_wv", [128, 8 * 256], BF16))
    wf_h = E(nc.sbuf_tensor("a1_wf", [128, 8 * 256], BF16))
    csq = E(nc.sbuf_tensor("a1_csq", [1, 256], BF16))
    csk = E(nc.sbuf_tensor("a1_csk", [1, 256], BF16))
    csv = E(nc.sbuf_tensor("a1_csv", [1, 256], BF16))
    csf = E(nc.sbuf_tensor("a1_csf", [1, 256], BF16))
    Qf = E(nc.sbuf_tensor("a1_Qf", [128, 2 * T], BF16))
    Kf = E(nc.sbuf_tensor("a1_Kf", [128, 2 * T], BF16))
    Vt = E(nc.sbuf_tensor("a1_Vt", [128, 2 * T], BF16))
    S1b = [E(nc.sbuf_tensor(f"a1_S1b{d}", [128, 256], BF16)) for d in range(2)]
    S2b = [E(nc.sbuf_tensor(f"a1_S2b{m}", [128, 256], BF16)) for m in range(2)]
    t1 = E(nc.sbuf_tensor("a1_t1", [128, 256], F32))
    za = E(nc.sbuf_tensor("a1_za", [128, 256], F32))
    eb = E(nc.sbuf_tensor("a1_eb", [128, 256], F32))
    ea = E(nc.sbuf_tensor("a1_ea", [128, 256], F32))
    en = E(nc.sbuf_tensor("a1_en", [128, 256], F32))
    eR = E(nc.sbuf_tensor("a1_eR", [128, 256], F32))
    sg = E(nc.sbuf_tensor("a1_sg", [128, 256], F32))
    Dt = E(nc.sbuf_tensor("a1_D", [128, 256], BF16))
    D2t = E(nc.sbuf_tensor("a1_D2", [128, 256], BF16))
    UT = E(nc.sbuf_tensor("a1_UT", [128, 128], BF16))
    G2m = E(nc.sbuf_tensor("a1_G2m", [128, 128], BF16))
    zf = E(nc.sbuf_tensor("a1_zf", [128, 256], F32))
    nmx = E(nc.sbuf_tensor("a1_nmx", [128, 1], F32))
    esf = E(nc.sbuf_tensor("a1_esf", [128, 256], F32))
    lsm = E(nc.sbuf_tensor("a1_lsm", [128, 1], F32))
    invl = E(nc.sbuf_tensor("a1_invl", [128, 1], F32))
    ptl = E(nc.sbuf_tensor("a1_ptl", [128, 256], BF16))
    ptT = E(nc.sbuf_tensor("a1_ptT", [128, 256], BF16))
    dT = E(nc.sbuf_tensor("a1_dT", [128, 256], BF16))
    kT = E(nc.sbuf_tensor("a1_kT", [128, 256], BF16))
    eArow = E(nc.sbuf_tensor("a1_eArow", [1, 256], BF16))
    eAT = E(nc.sbuf_tensor("a1_eAT", [128, 2], F32))
    s1scr = E(nc.sbuf_tensor("a1_s1scr", [128, 256], F32))
    ost = [E(nc.sbuf_tensor(f"a1_ost{i}", [128, 256], BF16)) for i in range(2)]

    hw_sem = k.sem()
    pj = k.sem()
    sc_pe = k.sem()
    sc_dve = k.sem()
    sc_act = k.sem()
    so_dma = [k.sem(), k.sem()]
    v_ev = k.sem()

    for h in range(H):
        for (wt, src) in ((wq_h, wqT), (wk_h, wkT), (wv_h, wvT), (wf_h, wfT)):
            for c in range(8):
                k.op("gpsimd", lambda e, c=c, wt=wt, src=src, h=h: e.dma_start(
                    out=wt[:, c * 256:(c + 1) * 256],
                    in_=src[c * 128:(c + 1) * 128, h * 256:(h + 1) * 256]),
                    waits=((W(pj, v_ev, sc_pe) if h > 0 else [(s_ldx, s_ldx.n), lnw])
                           if (c == 0 and wt is wq_h) else ()),
                    inc=s_ldw, inck=16)
        wW = (s_ldw, s_ldw.n)
        for wi, (wt, cs) in enumerate(((wq_h, csq), (wk_h, csk), (wv_h, csv), (wf_h, csf))):
            bank = S[3 + (wi % 2)]
            for c in range(8):
                k.op("tensor", lambda e, c=c, wt=wt, bank=bank: e.matmul(
                    bank[0:1, 0:256], nw_b[:, c:c + 1], wt[:, c * 256:(c + 1) * 256],
                    start=(c == 0), stop=(c == 7)),
                    waits=[wW, lnw] + W(sc_act, sc_dve) if c == 0 else (),
                    inc=sc_pe if c == 7 else None)
            k.op("scalar", lambda e, cs=cs, bank=bank: e.activation(
                cs[:], bank[0:1, 0:256], AF.Copy, bias=0.0, scale=1.0),
                waits=W(sc_pe), inc=sc_act)
        for g in range(G):
            for (dst, wt, cs) in ((Qf, wq_h, csq), (Kf, wk_h, csk)):
                for d in range(2):
                    bank = S[4]
                    for c in range(8):
                        k.op("tensor", lambda e, c=c, wt=wt, g=g, d=d, bank=bank: e.matmul(
                            bank[:], wt[:, c * 256 + d * 128: c * 256 + d * 128 + 128],
                            xh[:, c * T + g * 512: c * T + (g + 1) * 512],
                            start=(c == 0), stop=False),
                            waits=W(pj, sc_act) if c == 0 else ())
                    k.op("tensor", lambda e, cs=cs, g=g, d=d, bank=bank: e.matmul(
                        bank[:], cs[0:1, d * 128:(d + 1) * 128],
                        negmr[0:1, g * 512:(g + 1) * 512],
                        start=False, stop=True), inc=sc_pe)
                    k.op("scalar", lambda e, dst=dst, g=g, d=d, bank=bank: e.activation(
                        dst[:, d * T + g * 512: d * T + (g + 1) * 512], bank[:],
                        AF.Silu, bias=0.0, scale=1.0),
                        waits=W(sc_pe), inc=pj)
        for t in range(NCH):
            vb = S[4] if t % 2 == 0 else S[3]
            for c in range(8):
                k.op("tensor", lambda e, c=c, t=t, vb=vb: e.matmul(
                    vb[:, 0:256], xh[:, c * T + t * 128: c * T + (t + 1) * 128],
                    wv_h[:, c * 256:(c + 1) * 256], start=(c == 0), stop=False),
                    waits=[(v_ev, v_ev.n - min(v_ev.n, 1))] + (W(pj, sc_act) if t <= 1 else [])
                    if c == 0 else ())
            k.op("tensor", lambda e, t=t, vb=vb: e.matmul(
                vb[:, 0:256], negmr[0:1, t * 128:(t + 1) * 128], csv[:],
                start=False, stop=True), inc=sc_pe)
            k.op("scalar", lambda e, t=t, vb=vb: e.activation(
                Vt[:, t * 256:(t + 1) * 256], vb[:, 0:256], AF.Silu,
                bias=0.0, scale=1.0), waits=W(sc_pe), inc=v_ev)
        for d in range(2):
            k.op("vector", lambda e, d=d: e.memset(S1b[d][:], 0.0),
                 waits=W(sc_pe), inc=sc_dve)
            k.op("vector", lambda e, d=d: e.memset(S2b[d][:], 0.0), inc=sc_dve)
        for ci in range(NCH):
            fb = S[3]
            for c in range(8):
                k.op("tensor", lambda e, c=c, ci=ci, fb=fb: e.matmul(
                    fb[:, 0:256], xh[:, c * T + ci * 128: c * T + (ci + 1) * 128],
                    wf_h[:, c * 256:(c + 1) * 256], start=(c == 0), stop=False),
                    waits=W(sc_act, sc_dve) + [(v_ev, v_ev.n)] if c == 0 else ())
            k.op("tensor", lambda e, ci=ci, fb=fb: e.matmul(
                fb[:, 0:256], negmr[0:1, ci * 128:(ci + 1) * 128], csf[:],
                start=False, stop=True), inc=sc_pe)
            k.op("scalar", lambda e, fb=fb: e.activation(
                za[:], fb[:, 0:256], AF.Abs, bias=0.0, scale=1.0),
                waits=W(sc_pe), inc=sc_act)
            k.op("scalar", lambda e: e.activation(
                eb[:], za[:], AF.Exp, bias=0.0, scale=-1.0),
                waits=W(sc_act), inc=sc_act)
            k.op("vector", lambda e: e.tensor_scalar(
                eb[:], eb[:], 1.0, None, op0=ALU.add), waits=W(sc_act), inc=sc_dve)
            k.op("scalar", lambda e: e.activation(
                eb[:], eb[:], AF.Ln, bias=0.0, scale=1.0),
                waits=W(sc_dve), inc=sc_act)
            k.op("vector", lambda e, fb=fb: e.tensor_scalar(
                t1[:], fb[:, 0:256], 0.0, None, op0=ALU.min),
                waits=W(sc_act), inc=sc_dve)
            k.op("vector", lambda e: e.tensor_sub(t1[:], t1[:], eb[:]),
                 waits=W(sc_dve, sc_act), inc=sc_dve)
            k.op("tensor", lambda e: e.matmul(
                S[5][:, 0:256], cn["tri_incl_f"][:], t1[:], start=True, stop=True),
                waits=W(sc_dve, sc_act), inc=sc_pe)
            k.op("tensor", lambda e: e.matmul(
                S[6][:, 0:256], cn["tri_revx_f"][:], t1[:], start=True, stop=True),
                inc=sc_pe)
            k.op("tensor", lambda e: e.matmul(
                S[4][0:1, 0:256], cn["ones_col_f"][:], t1[:], start=True, stop=True),
                waits=W(sc_dve) + [(v_ev, v_ev.n)], inc=sc_pe)
            for d in range(2):
                k.op("tensor", lambda e, d=d: e.matmul(
                    S[4][:, 256 + d:257 + d], t1[:, d * 128:(d + 1) * 128],
                    cn["ones_col_f"][:], start=True, stop=True), inc=sc_pe)
            k.op("scalar", lambda e: e.activation(
                ea[:], S[5][:, 0:256], AF.Exp, bias=0.0, scale=1.0 / 16),
                waits=W(sc_pe), inc=sc_act)
            k.op("scalar", lambda e: e.activation(
                en[:], S[5][:, 0:256], AF.Exp, bias=0.0, scale=-1.0 / 16),
                inc=sc_act)
            k.op("scalar", lambda e: e.activation(
                eR[:], S[6][:, 0:256], AF.Exp, bias=0.0, scale=1.0 / 16),
                inc=sc_act)
            k.op("scalar", lambda e: e.activation(
                eArow[:], S[4][0:1, 0:256], AF.Exp, bias=0.0, scale=1.0 / 16),
                inc=sc_act)
            for d in range(2):
                k.op("scalar", lambda e, d=d: e.activation(
                    eAT[:, d:d + 1], S[4][:, 256 + d:257 + d], AF.Exp,
                    bias=0.0, scale=1.0 / 16), inc=sc_act)
            k.op("scalar", lambda e: e.activation(
                sg[:], t1[:], AF.Exp, bias=0.0, scale=1.0 / 16),
                waits=W(sc_dve), inc=sc_act)
            k.op("vector", lambda e: e.tensor_scalar(
                sg[:], sg[:], -1.0, 1.0, op0=ALU.mult, op1=ALU.add),
                waits=W(sc_act), inc=sc_dve)
            k.op("vector", lambda e: e.tensor_mul(Dt[:], sg[:], en[:]),
                 waits=W(sc_dve, sc_act), inc=sc_dve)
            k.op("vector", lambda e: e.tensor_mul(D2t[:], sg[:], eR[:]),
                 waits=W(sc_dve), inc=sc_dve)
            k.op("tensor", lambda e: e.matmul(
                S[4][:, 0:256], cn["ones_row"][:], eArow[:], start=True, stop=True),
                waits=W(sc_act, sc_dve), inc=sc_pe)
            for d in range(2):
                k.op("tensor", lambda e, d=d, ci=ci: e.matmul(
                    S[6][:, 256:384], Kf[:, d * T + ci * 128: d * T + (ci + 1) * 128],
                    Qf[:, d * T + ci * 128: d * T + (ci + 1) * 128],
                    start=(d == 0), stop=(d == 1)),
                    waits=W(sc_act) if d == 0 else (), inc=sc_pe if d == 1 else None)
            k.op("vector", lambda e: e.tensor_mul(
                UT[:], S[6][:, 256:384], cn["mm"][:, 0:128]),
                waits=W(sc_pe), inc=sc_dve)
            for d in range(2):
                k.op("tensor", lambda e, d=d, ci=ci: e.matmul(
                    S[5][:, 256:512], Qf[:, d * T + ci * 128: d * T + (ci + 1) * 128],
                    S1b[d][:], start=(d == 0), stop=False),
                    waits=W(sc_act, sc_dve) if d == 0 else ())
            k.op("tensor", lambda e: e.matmul(
                S[5][:, 256:512], UT[:], Dt[:], start=False, stop=True),
                inc=sc_pe)
            k.op("vector", lambda e: e.tensor_mul(zf[:], ea[:], S[5][:, 256:512]),
                 waits=W(sc_pe), inc=sc_dve)
            k.op("vector", lambda e: e.reduce_max(
                nmx[:], zf[:], axis=AX.X, negate=True), waits=W(sc_dve), inc=sc_dve)
            k.op("scalar", lambda e: e.activation(
                esf[:], zf[:], AF.Exp, bias=nmx[:, 0:1], scale=1.0,
                accum_out=lsm[:, 0:1]), waits=W(sc_dve), inc=sc_act)
            k.op("vector", lambda e: e.reciprocal(invl[:], lsm[:]),
                 waits=W(sc_act), inc=sc_dve)
            k.op("vector", lambda e: e.tensor_mul(ptl[:], esf[:], ea[:]),
                 waits=W(sc_dve), inc=sc_dve)
            k.op("vector", lambda e: e.tensor_scalar(
                ptl[:], ptl[:], invl[:, 0:1], None, op0=ALU.mult),
                waits=W(sc_dve), inc=sc_dve)
            for (dst, srcf, isk) in ((ptT, ptl, False), (dT, Dt, False), (kT, None, True)):
                for d in range(2):
                    if isk:
                        k.op("tensor", lambda e, d=d, ci=ci: e.transpose(
                            PB[:, d * 128:(d + 1) * 128],
                            Kf[:, d * T + ci * 128: d * T + (ci + 1) * 128],
                            cn["ident_bf"][:]),
                            waits=W(sc_dve) if d == 0 else (), inc=sc_pe)
                    else:
                        k.op("tensor", lambda e, d=d, srcf=srcf: e.transpose(
                            PB[:, d * 128:(d + 1) * 128],
                            srcf[:, d * 128:(d + 1) * 128], cn["ident_bf"][:]),
                            waits=W(sc_dve) if d == 0 else (), inc=sc_pe)
                k.op("vector", lambda e, dst=dst: e.tensor_copy(dst[:], PB[:, 0:256]),
                     waits=W(sc_pe), inc=sc_dve)
            for d in range(2):
                k.op("tensor", lambda e, d=d: e.matmul(
                    S[6][:, 256:384], dT[:, d * 128:(d + 1) * 128],
                    ptT[:, d * 128:(d + 1) * 128], start=(d == 0), stop=(d == 1)),
                    waits=W(sc_dve) if d == 0 else (), inc=sc_pe if d == 1 else None)
            k.op("vector", lambda e: e.tensor_mul(
                G2m[:], S[6][:, 256:384], cn["mm"][:, 0:128]),
                waits=W(sc_pe), inc=sc_dve)
            for vs in range(2):
                bank = S[vs]
                for d in range(2):
                    k.op("tensor", lambda e, d=d, vs=vs, bank=bank: e.matmul(
                        bank[:, 0:128], S2b[d][:, vs * 128:(vs + 1) * 128],
                        ptT[:, d * 128:(d + 1) * 128], start=(d == 0), stop=False),
                        waits=W(sc_dve, sc_act) if d == 0 else ())
                k.op("tensor", lambda e, vs=vs, ci=ci, bank=bank: e.matmul(
                    bank[:, 0:128], Vt[:, ci * 256 + vs * 128: ci * 256 + (vs + 1) * 128],
                    G2m[:], start=False, stop=True),
                    waits=W(sc_dve), inc=sc_pe)
            for d in range(2):
                k.op("tensor", lambda e, d=d: e.matmul(
                    S[3][:, 256:512], kT[:, d * 128:(d + 1) * 128], D2t[:],
                    start=True, stop=True), waits=W(sc_dve),
                    inc=sc_pe)
                k.op("vector", lambda e, d=d: e.tensor_mul(
                    s1scr[:], S1b[d][:], S[4][:, 0:256]),
                    waits=W(sc_pe), inc=sc_dve)
                k.op("vector", lambda e, d=d: e.tensor_add(
                    S1b[d][:], s1scr[:], S[3][:, 256:512]),
                    waits=W(sc_dve), inc=sc_dve)
            for mcb in range(2):
                k.op("tensor", lambda e, mcb=mcb, ci=ci: e.matmul(
                    S[3][:, 256:512], D2t[:, mcb * 128:(mcb + 1) * 128],
                    Vt[:, ci * 256:(ci + 1) * 256],
                    start=True, stop=True), waits=W(sc_dve), inc=sc_pe)
                k.op("vector", lambda e, mcb=mcb: e.tensor_scalar(
                    s1scr[:], S2b[mcb][:], eAT[:, mcb:mcb + 1], None, op0=ALU.mult),
                    waits=W(sc_pe), inc=sc_dve)
                k.op("vector", lambda e, mcb=mcb: e.tensor_add(
                    S2b[mcb][:], s1scr[:], S[3][:, 256:512]),
                    waits=W(sc_dve), inc=sc_dve)
            midx = (h * NCH + ci)
            k.op("scalar", lambda e, m2=midx % 2: e.activation(
                ost[m2][:, 0:128], S[0][:, 0:128], AF.Silu, bias=0.0, scale=1.0),
                waits=W(sc_pe) + [(so_dma[midx % 2], so_dma[midx % 2].n)], inc=sc_act)
            k.op("scalar", lambda e, m2=midx % 2: e.activation(
                ost[m2][:, 128:256], S[1][:, 0:128], AF.Silu, bias=0.0, scale=1.0),
                inc=sc_act)
            for vs in range(2):
                k.op("gpsimd", lambda e, h=h, ci=ci, vs=vs, m2=midx % 2: e.dma_start(
                    out=o_dram[h * 256 + vs * 128: h * 256 + (vs + 1) * 128,
                               ci * 128:(ci + 1) * 128],
                    in_=ost[m2][:, vs * 128:(vs + 1) * 128]),
                    waits=W(sc_act) if vs == 0 else (), inc=so_dma[midx % 2], inck=16)

    hctx.close()
    scanW = [(sc_pe, sc_pe.n), (sc_act, sc_act.n), (sc_dve, sc_dve.n),
             (s_ldw, s_ldw.n), (v_ev, v_ev.n), (pj, pj.n)]
    with contextlib.ExitStack() as octx:
        OE = octx.enter_context
        ob = OE(nc.sbuf_tensor("a1_ob", [128, 8 * 512], BF16))
        osq = [OE(nc.sbuf_tensor(f"a1_osq{i}", [128, 512], BF16)) for i in range(2)]
        oh_ = OE(nc.sbuf_tensor("a1_oh", [128, 8 * 512], BF16))
        q2_row = OE(nc.sbuf_tensor("a1_q2", [1, 512], F32))
        rso_row = OE(nc.sbuf_tensor("a1_rso", [1, 512], F32))
        rso_b = OE(nc.sbuf_tensor("a1_rsob", [1, 512], BF16))
        xrg = [OE(nc.sbuf_tensor(f"a1_xrg{i}", [128, 512], F32)) for i in range(4)]
        x3o = [OE(nc.sbuf_tensor(f"a1_x3o{i}", [128, 512], F32)) for i in range(4)]
        gn_b = OE(nc.sbuf_tensor("a1_gnb", [128, 8], BF16))
        go_ld = k.sem()
        go_pe = k.sem()
        go_dve = k.sem()
        go_act = k.sem()
        gst = [k.sem() for _ in range(4)]
        gxr = [k.sem() for _ in range(4)]
        st_dve = k.sem()
        k.op("vector", lambda e: e.tensor_copy(gn_b[:], gn_f[:]),
             waits=W(s_ldg) + [cw], inc=go_dve)
        for c in range(8):
            k.op("vector", lambda e, c=c: e.tensor_scalar(
                wo_sb[:, c * 1024:(c + 1) * 1024], wo_sb[:, c * 1024:(c + 1) * 1024],
                gn_f[:, c:c + 1], None, op0=ALU.mult),
                waits=[(s_ldwo, s_ldwo.n)] + W(go_dve), inc=go_dve)
        woW = (go_dve, go_dve.n)
        soW = (so_dma[0], so_dma[0].n)
        soW2 = (so_dma[1], so_dma[1].n)
        for g in range(G):
            ldb = go_ld.n
            for c in range(8):
                k.op("sync", lambda e, c=c, g=g: e.dma_start(
                    out=ob[:, c * 512:(c + 1) * 512],
                    in_=o_dram[c * 128:(c + 1) * 128, g * 512:(g + 1) * 512]),
                    waits=([soW, soW2, (go_pe, go_pe.n)] + (scanW if g == 0 else []))
                    if c == 0 else (), inc=go_ld, inck=16)
            for c in range(8):
                k.op("scalar", lambda e, c=c: e.activation(
                    osq[c % 2][:], ob[:, c * 512:(c + 1) * 512], AF.Square,
                    bias=0.0, scale=1.0),
                    waits=[(go_ld, ldb + 128)] + (W(go_pe) if c >= 2 else []),
                    inc=go_act)
                k.op("tensor", lambda e, c=c: e.matmul(
                    S[1][0:1, 0:512], cn["ones_col"][:], osq[c % 2][:],
                    start=(c == 0), stop=(c == 7)),
                    waits=W(go_act), inc=go_pe)
            k.op("vector", lambda e: e.tensor_scalar(
                q2_row[:], S[1][0:1, 0:512], 1.0 / 1024, None, op0=ALU.mult),
                waits=W(go_pe), inc=go_dve)
            k.op("vector", lambda e: e.tensor_scalar(
                q2_row[:], q2_row[:], 1e-5, None, op0=ALU.add),
                waits=W(go_dve), inc=go_dve)
            k.op("scalar", lambda e: e.activation(rso_row[:], q2_row[:], AF.Ln,
                                                  bias=0.0, scale=1.0),
                 waits=W(go_dve), inc=go_act)
            k.op("scalar", lambda e: e.activation(rso_row[:], rso_row[:], AF.Exp,
                                                  bias=0.0, scale=-0.5),
                 waits=W(go_act), inc=go_act)
            k.op("vector", lambda e: e.tensor_copy(rso_b[:], rso_row[:]),
                 waits=W(go_act), inc=go_dve)
            k.op("tensor", lambda e: e.matmul(
                S[2][:], cn["ones_row"][:], rso_b[:], start=True, stop=True),
                waits=W(go_dve), inc=go_pe)
            for c in range(8):
                k.op("vector", lambda e, c=c: e.tensor_mul(
                    oh_[:, c * 512:(c + 1) * 512], ob[:, c * 512:(c + 1) * 512],
                    S[2][:]), waits=W(go_pe) if c == 0 else W(go_dve), inc=go_dve)
            ohW = (go_dve, go_dve.n)
            for c in range(8):
                n = g * 8 + c
                b = n % 4
                k.op("sync", lambda e, c=c, g=g, b=b: e.dma_start(
                    out=xrg[b][:], in_=xT[c * 128:(c + 1) * 128, g * 512:(g + 1) * 512]),
                    waits=[(st_dve, max(0, st_dve.n - 3))], inc=gxr[b], inck=16)
                bank = S[3 + (c % 2)]
                for f in range(8):
                    k.op("tensor", lambda e, c=c, g=g, f=f, bank=bank: e.matmul(
                        bank[:], wo_sb[:, f * 1024 + c * 128: f * 1024 + (c + 1) * 128],
                        oh_[:, f * 512:(f + 1) * 512],
                        start=(f == 0), stop=(f == 7)),
                        waits=[ohW, woW, (st_dve, max(0, st_dve.n - 1))] if f == 0 else (),
                        inc=go_pe if f == 7 else None)
                k.op("vector", lambda e, b=b, bank=bank: e.tensor_add(
                    x3o[b][:], bank[:], xrg[b][:]),
                    waits=[(go_pe, go_pe.n), (gxr[b], gxr[b].n), (gst[b], gst[b].n)],
                    inc=st_dve)
                k.op("scalar", lambda e, c=c, g=g, b=b: e.dma_start(
                    out=x3T[c * 128:(c + 1) * 128, g * 512:(g + 1) * 512],
                    in_=x3o[b][:]), waits=[(st_dve, st_dve.n)], inc=gst[b], inck=16)


def build_l1_attn_program(T):
    nc = bass.Bass("TRN2", target_bir_lowering=False)
    xT = nc.dram_tensor("xT", [1024, T], F32, kind="ExternalInput").ap()
    nw = nc.dram_tensor("nw", [1024, 1], F32, kind="ExternalInput").ap()
    gn = nc.dram_tensor("gn", [1024, 1], F32, kind="ExternalInput").ap()
    wq = nc.dram_tensor("wq", [1024, 1024], F32, kind="ExternalInput").ap()
    wk = nc.dram_tensor("wk", [1024, 1024], F32, kind="ExternalInput").ap()
    wv = nc.dram_tensor("wv", [1024, 1024], F32, kind="ExternalInput").ap()
    wf = nc.dram_tensor("wf", [1024, 1024], F32, kind="ExternalInput").ap()
    wo = nc.dram_tensor("wo", [1024, 1024], F32, kind="ExternalInput").ap()
    x3T = nc.dram_tensor("x3T", [1024, T], F32, kind="ExternalOutput").ap()
    ctx = contextlib.ExitStack()
    with ctx:
        E = ctx.enter_context
        k = K(nc, E)
        env = Env(nc, E, k)
        emit_l1_attn(env, T, xT, x3T, nw, wq, wk, wv, wf, gn, wo)
        k.final_barrier()
        block = E(nc.Block())
        k.finish(block)
    return nc


def _l1_feed(x, b, nw, gn, wq, wk, wv, wf, wo):
    m = dict(make_const_arrays())
    m["xT"] = np.ascontiguousarray(x[b].T)
    m["nw"] = np.ascontiguousarray(nw[:, None])
    m["gn"] = np.ascontiguousarray(gn[:, None])
    m["wq"] = np.ascontiguousarray(wq.T)
    m["wk"] = np.ascontiguousarray(wk.T)
    m["wv"] = np.ascontiguousarray(wv.T)
    m["wf"] = np.ascontiguousarray(wf.T)
    m["wo"] = np.ascontiguousarray(wo.T)
    return m


def run_l1_attn(x, nw, gn, wq, wk, wv, wf, wo):
    B, T, D = x.shape
    nc = build_l1_attn_program(T)
    in_maps = [_l1_feed(x, b, nw, gn, wq, wk, wv, wf, wo) for b in range(B)]
    res = run_bass_kernel_spmd(nc, in_maps, core_ids=list(range(B)))
    return np.stack([r["x3T"].T for r in res.results], axis=0)


def sim_l1_attn(x, nw, gn, wq, wk, wv, wf, wo):
    from concourse.bass_interp import CoreSim
    B, T, D = x.shape
    assert B == 1
    nc = build_l1_attn_program(T)
    nc.finalize()
    sim = CoreSim(nc)
    feed = _l1_feed(x, 0, nw, gn, wq, wk, wv, wf, wo)
    for name, arr in feed.items():
        v = sim.tensor(name)
        v[:] = arr.view(v.dtype) if arr.dtype == NP_BF16 else arr
    sim.simulate()
    return sim.tensor("x3T").T[None].copy()



L1FFN_TRUNC = 0

def emit_l1_ffn(env: Env, T, xT, x4T, gwT, gbRow, WE1, BE1col, WE2, BE2, WE3, BE3col,
                ws1A, bs1C, ws2A, bs2R, ws3A, bs3C):
    trunc = L1FFN_TRUNC
    """x4 = x3 + moe_top2of16(LN(x3)) + shared_swiglu(LN(x3)). Dense v1.
    WE1/WE3: [16*1024, 512] f32; WE2: [16*512, 1024] f32; BE1col/BE3col: [16*512, 1] f32;
    BE2: [16, 1024] f32; ws*: [1024, 1024]; bs1C/bs3C: [1024, 1]; bs2R: [1, 1024]."""
    nc, E, k, cn = env.nc, env.E, env.k, env.cn
    S, PB = env.S, env.PB
    G = T // 512
    NT = T // 128
    NE = 16
    wbf1 = nc.dram_tensor("wbf1", [16 * 1024, 512], BF16, kind="Internal").ap()
    wbf3 = nc.dram_tensor("wbf3", [16 * 1024, 512], BF16, kind="Internal").ap()
    wbf2 = nc.dram_tensor("wbf2", [16 * 512, 1024], BF16, kind="Internal").ap()

    W = lambda *ss: [s.now() for s in ss]
    cw = (env.ld, env.ld.n)

    xh = E(nc.sbuf_tensor("m_xh", [128, 8 * 512], BF16))
    negmr = E(nc.sbuf_tensor("m_negmr", [1, 512], BF16))
    comb = E(nc.sbuf_tensor("m_comb", [128, 4 * 16], F32))
    gw_sb = E(nc.sbuf_tensor("m_gw", [128, 8 * 16], F32))
    csgw = E(nc.sbuf_tensor("m_csgw", [1, 16], F32))
    gbb = E(nc.sbuf_tensor("m_gbb", [128, 16], F32))
    gb_sb = E(nc.sbuf_tensor("m_gb", [1, 16], F32))
    cs1d = nc.dram_tensor("m_cs1d", [16, 512], BF16, kind="Internal").ap()
    cs3d = nc.dram_tensor("m_cs3d", [16, 512], BF16, kind="Internal").ap()
    cs1t = [E(nc.sbuf_tensor(f"m_cs1t{i}", [1, 512], BF16)) for i in range(2)]
    cs3t = [E(nc.sbuf_tensor(f"m_cs3t{i}", [1, 512], BF16)) for i in range(2)]
    be1t = E(nc.sbuf_tensor("m_be1t", [128, 16 * 4], F32))
    be3t = E(nc.sbuf_tensor("m_be3t", [128, 16 * 4], F32))
    be2t = [E(nc.sbuf_tensor(f"m_be2t{i}", [1, 1024], BF16)) for i in range(2)]
    ws1_sb = E(nc.sbuf_tensor("m_ws1", [128, 8 * 1024], BF16))
    ws3_sb = E(nc.sbuf_tensor("m_ws3", [128, 8 * 1024], BF16))
    ws2_sb = E(nc.sbuf_tensor("m_ws2", [128, 8 * 1024], BF16))
    css1 = E(nc.sbuf_tensor("m_css1", [1, 1024], BF16))
    css3 = E(nc.sbuf_tensor("m_css3", [1, 1024], BF16))
    bs1c = E(nc.sbuf_tensor("m_bs1c", [128, 8], F32))
    bs3c = E(nc.sbuf_tensor("m_bs3c", [128, 8], F32))
    bs2r = E(nc.sbuf_tensor("m_bs2r", [1, 1024], BF16))

    s_ldg = k.sem()
    s_ldx = k.sem()
    s_ldx3 = k.sem()
    s_pc = k.sem()
    s_pc2 = k.sem()
    s_pcs = k.sem()
    s_b = k.sem()
    ln_pe = k.sem()
    ln_dve = k.sem()
    ln_act = k.sem()
    ln_done = k.sem()
    gt_pe = k.sem()
    gt_dve = k.sem()
    gt_act = k.sem()
    fe_pe = k.sem()
    fe_act = k.sem()
    fe_dve = k.sem()
    s_w = k.sem()
    st_dve = k.sem()
    gst = [k.sem() for _ in range(4)]
    gxr = [k.sem() for _ in range(4)]

    for c in range(8):
        k.op("sync", lambda e, c=c: e.dma_start(
            out=gw_sb[:, c * 16:(c + 1) * 16], in_=gwT[c * 128:(c + 1) * 128, :]),
            inc=s_ldg, inck=16)
    k.op("sync", lambda e: e.dma_start(out=gb_sb[:], in_=gbRow), inc=s_ldg, inck=16)
    for ei in range(NE):
        for sl in range(4):
            k.op("scalar", lambda e, ei=ei, sl=sl: e.dma_start(
                out=be1t[:, ei * 4 + sl: ei * 4 + sl + 1],
                in_=BE1col[ei * 512 + sl * 128: ei * 512 + (sl + 1) * 128, 0:1]),
                inc=s_b, inck=16)
            k.op("scalar", lambda e, ei=ei, sl=sl: e.dma_start(
                out=be3t[:, ei * 4 + sl: ei * 4 + sl + 1],
                in_=BE3col[ei * 512 + sl * 128: ei * 512 + (sl + 1) * 128, 0:1]),
                inc=s_b, inck=16)
    for c in range(8):
        k.op("scalar", lambda e, c=c: e.dma_start(
            out=bs1c[:, c:c + 1], in_=bs1C[c * 128:(c + 1) * 128, 0:1]),
            inc=s_b, inck=16)
        k.op("scalar", lambda e, c=c: e.dma_start(
            out=bs3c[:, c:c + 1], in_=bs3C[c * 128:(c + 1) * 128, 0:1]),
            inc=s_b, inck=16)
    k.op("gpsimd", lambda e: e.dma_start(out=bs2r[:], in_=bs2R), inc=s_pc, inck=16)
    for (dst, src) in ((ws1_sb, ws1A), (ws3_sb, ws3A), (ws2_sb, ws2A)):
        for c in range(8):
            k.op("gpsimd", lambda e, c=c, dst=dst, src=src: e.dma_start(
                out=dst[:, c * 1024:(c + 1) * 1024],
                in_=src[c * 128:(c + 1) * 128, :]), inc=s_pc, inck=16)
    for c in range(8):
        k.op("tensor", lambda e, c=c: e.matmul(
            S[5][0:1, 0:16], cn["ones_col_f"][:], gw_sb[:, c * 16:(c + 1) * 16],
            start=(c == 0), stop=(c == 7)),
            waits=[cw, (s_ldg, s_ldg.n)] if c == 0 else (),
            inc=gt_pe if c == 7 else None)
    k.op("vector", lambda e: e.tensor_copy(csgw[:], S[5][0:1, 0:16]),
         waits=W(gt_pe), inc=gt_dve)
    k.op("tensor", lambda e: e.matmul(
        S[5][:, 16:32], cn["ones_row_f"][:], gb_sb[:], start=True, stop=True),
        waits=W(gt_dve) + [(s_ldg, s_ldg.n)], inc=gt_pe)
    k.op("vector", lambda e: e.tensor_copy(gbb[:], S[5][:, 16:32]),
         waits=W(gt_pe), inc=gt_dve)
    for (wt, cs) in ((ws1_sb, css1), (ws3_sb, css3)):
        for half in range(2):
            for c in range(8):
                k.op("tensor", lambda e, c=c, wt=wt, half=half: e.matmul(
                    S[5][0:1, 0:512], cn["ones_col"][:],
                    wt[:, c * 1024 + half * 512: c * 1024 + (half + 1) * 512],
                    start=(c == 0), stop=(c == 7)),
                    waits=[(s_pc, s_pc.n), (gt_dve, gt_dve.n), (gt_act, gt_act.n)]
                    if c == 0 else (),
                    inc=gt_pe if c == 7 else None)
            k.op("scalar", lambda e, cs=cs, half=half: e.activation(
                cs[0:1, half * 512:(half + 1) * 512], S[5][0:1, 0:512],
                AF.Copy, bias=0.0, scale=1.0),
                waits=W(gt_pe), inc=gt_act)

    with contextlib.ExitStack() as pctx:
        PE_ = pctx.enter_context
        wtmp = [PE_(nc.sbuf_tensor(f"m_wtmp{i}", [128, 8 * 512], BF16)) for i in range(2)]
        for ei in range(NE):
            for wi, (srcW, dstW, cstab) in enumerate(
                    ((WE1, wbf1, "cs1"), (WE3, wbf3, "cs3"), (WE2, wbf2, None))):
                buf = wtmp[(ei * 3 + wi) % 2]
                ldb = s_pc2.n
                nch = 8 if cstab is not None else 4
                width = 512 if cstab is not None else 1024
                for c in range(nch):
                    k.op("gpsimd", lambda e, c=c, ei=ei, srcW=srcW, buf=buf,
                         width=width, nch=nch: e.dma_start(
                        out=buf[:, c * width:(c + 1) * width],
                        in_=srcW[ei * nch * 128 + c * 128: ei * nch * 128 + (c + 1) * 128, :]),
                        waits=[(s_pcs, s_pcs.n)] if c == 0 else (), inc=s_pc2, inck=16)
                for c in range(nch):
                    k.op("scalar", lambda e, c=c, ei=ei, dstW=dstW, buf=buf,
                         width=width, nch=nch: e.dma_start(
                        out=dstW[ei * nch * 128 + c * 128: ei * nch * 128 + (c + 1) * 128, :],
                        in_=buf[:, c * width:(c + 1) * width]),
                        waits=[(s_pc2, ldb + 16 * nch)] if c == 0 else (),
                        inc=s_pcs, inck=16)
                if cstab is not None:
                    csd = cs1d if cstab == "cs1" else cs3d
                    for c in range(8):
                        k.op("tensor", lambda e, c=c, buf=buf: e.matmul(
                            S[6][0:1, 0:512], cn["ones_col"][:],
                            buf[:, c * 512:(c + 1) * 512],
                            start=(c == 0), stop=(c == 7)),
                            waits=[(s_pc2, ldb + 128), (gt_act, gt_act.n),
                                   (fe_act, fe_act.n)] if c == 0 else (),
                            inc=fe_pe if c == 7 else None)
                    k.op("scalar", lambda e: e.activation(
                        cs1t[0][:], S[6][0:1, 0:512],
                        AF.Copy, bias=0.0, scale=1.0),
                        waits=W(fe_pe) + [(s_pcs, s_pcs.n)], inc=fe_act)
                    k.op("scalar", lambda e, ei=ei, csd=csd: e.dma_start(
                        out=csd[ei:ei + 1, :], in_=cs1t[0][:]),
                        waits=W(fe_act), inc=s_pcs, inck=16)
    pcW = [(s_pcs, s_pcs.n), (s_pc, s_pc.n), (s_pc2, s_pc2.n)]

    LE = E
    xbb = LE(nc.sbuf_tensor("m_xbb", [128, 8 * 512], BF16))
    xgf = LE(nc.sbuf_tensor("m_xgf", [128, 8 * 512], F32))
    sq = [LE(nc.sbuf_tensor(f"m_sq{i}", [128, 512], BF16)) for i in range(2)]
    m_row = LE(nc.sbuf_tensor("m_mrow", [1, 512], F32))
    q_row = LE(nc.sbuf_tensor("m_qrow", [1, 512], F32))
    v_row = LE(nc.sbuf_tensor("m_vrow", [1, 512], F32))
    rs_row = LE(nc.sbuf_tensor("m_rsrow", [1, 512], F32))
    rs_b = LE(nc.sbuf_tensor("m_rsb", [1, 512], BF16))
    rstdc = LE(nc.sbuf_tensor("m_rstdc", [128, 1], F32))
    lg = LE(nc.sbuf_tensor("m_lg", [128, 16], F32))
    nmx = LE(nc.sbuf_tensor("m_nmx", [128, 1], F32))
    sume = LE(nc.sbuf_tensor("m_sume", [128, 1], F32))
    esc = LE(nc.sbuf_tensor("m_esc", [128, 16], F32))
    inv = LE(nc.sbuf_tensor("m_inv", [128, 1], F32))
    orig = LE(nc.sbuf_tensor("m_orig", [128, 16], F32))
    sb16 = LE(nc.sbuf_tensor("m_sb16", [128, 16], F32))
    gsc = LE(nc.sbuf_tensor("m_gsc", [128, 4], F32))
    scr4 = LE(nc.sbuf_tensor("m_scr4", [128, 4], F32))
    pmin = LE(nc.sbuf_tensor("m_pmin", [128, 1], F32))
    thr = LE(nc.sbuf_tensor("m_thr", [128, 1], F32))
    keepg = LE(nc.sbuf_tensor("m_keepg", [128, 4], F32))
    m1 = LE(nc.sbuf_tensor("m_m1", [128, 1], F32))
    m2 = LE(nc.sbuf_tensor("m_m2", [128, 1], F32))
    sel = LE(nc.sbuf_tensor("m_sel", [128, 16], F32))
    negmf = LE(nc.sbuf_tensor("m_negmf", [1, 512], F32))
    ysb = E(nc.sbuf_tensor("m_ysb", [128, 4 * 1024], F32))
    w1e = [E(nc.sbuf_tensor(f"m_w1e{i}", [128, 8 * 512], BF16)) for i in range(2)]
    w3e = [E(nc.sbuf_tensor(f"m_w3e{i}", [128, 8 * 512], BF16)) for i in range(2)]
    w2e_s = E(nc.sbuf_tensor("m_w2e", [128, 4 * 1024], BF16))
    w2e = [w2e_s, w2e_s]
    h1sb = E(nc.sbuf_tensor("m_h1sb", [128, 512], F32))
    h3scr = E(nc.sbuf_tensor("m_h3", [128, 512], F32))
    h_all = E(nc.sbuf_tensor("m_hall", [128, 8 * 512], BF16))

    xrg = [E(nc.sbuf_tensor(f"m_xrg{i}", [128, 512], F32)) for i in range(2)]
    x4o = [E(nc.sbuf_tensor(f"m_x4o{i}", [128, 512], F32)) for i in range(2)]
    s_cs = k.sem()
    res_pe = k.sem()
    prev_res = 0

    for g in range(G):
        ldb = s_ldx.n
        ldb3 = s_ldx3.n
        for c in range(8):
            k.op("gpsimd", lambda e, c=c, g=g: e.dma_start(
                out=xbb[:, c * 512:(c + 1) * 512],
                in_=xT[c * 128:(c + 1) * 128, g * 512:(g + 1) * 512]),
                waits=([(ln_done, ln_done.n)] if g > 0 else pcW) if c == 0 else (),
                inc=s_ldx, inck=16)
            k.op("sync", lambda e, c=c, g=g: e.dma_start(
                out=xgf[:, c * 512:(c + 1) * 512],
                in_=xT[c * 128:(c + 1) * 128, g * 512:(g + 1) * 512]),
                waits=([(gt_dve, gt_dve.n)] if g > 0 else pcW) if c == 0 else (),
                inc=s_ldx3, inck=16)
        for c in range(8):
            k.op("tensor", lambda e, c=c: e.matmul(
                S[0][0:1, 0:512], cn["ones_col"][:], xbb[:, c * 512:(c + 1) * 512],
                start=(c == 0), stop=(c == 7)),
                waits=[(s_ldx, ldb + 128)] if c == 0 else (),
                inc=ln_pe if c == 7 else None)
        for c in range(8):
            k.op("scalar", lambda e, c=c: e.activation(
                sq[c % 2][:], xbb[:, c * 512:(c + 1) * 512], AF.Square,
                bias=0.0, scale=1.0),
                waits=[(s_ldx, ldb + 128)] + (W(ln_pe) if c >= 2 else []),
                inc=ln_act)
            k.op("tensor", lambda e, c=c: e.matmul(
                S[1][0:1, 0:512], cn["ones_col"][:], sq[c % 2][:],
                start=(c == 0), stop=(c == 7)),
                waits=W(ln_act), inc=ln_pe)
        k.op("vector", lambda e: e.tensor_scalar(
            m_row[:], S[0][0:1, 0:512], 1.0 / 1024, None, op0=ALU.mult),
            waits=W(ln_pe), inc=ln_dve)
        k.op("vector", lambda e: e.tensor_scalar(
            q_row[:], S[1][0:1, 0:512], 1.0 / 1024, None, op0=ALU.mult),
            inc=ln_dve)
        k.op("vector", lambda e: e.tensor_mul(v_row[:], m_row[:], m_row[:]),
             waits=W(ln_dve), inc=ln_dve)
        k.op("vector", lambda e: e.tensor_sub(v_row[:], q_row[:], v_row[:]),
             waits=W(ln_dve), inc=ln_dve)
        k.op("vector", lambda e: e.tensor_scalar(
            v_row[:], v_row[:], 1e-5, None, op0=ALU.add),
            waits=W(ln_dve), inc=ln_dve)
        k.op("scalar", lambda e: e.activation(rs_row[:], v_row[:], AF.Ln,
                                              bias=0.0, scale=1.0),
             waits=W(ln_dve), inc=ln_act)
        k.op("scalar", lambda e: e.activation(rs_row[:], rs_row[:], AF.Exp,
                                              bias=0.0, scale=-0.5),
             waits=W(ln_act), inc=ln_act)
        k.op("vector", lambda e: e.tensor_copy(rs_b[:], rs_row[:]),
             waits=W(ln_act), inc=ln_dve)
        k.op("vector", lambda e: e.tensor_mul(negmf[:], m_row[:], rs_row[:]),
             waits=W(ln_dve), inc=ln_dve)
        k.op("vector", lambda e: e.tensor_scalar(
            negmf[:], negmf[:], -1.0, None, op0=ALU.mult),
            waits=W(ln_dve), inc=ln_dve)
        k.op("vector", lambda e: e.tensor_copy(negmr[:], negmf[:]),
             waits=W(ln_dve), inc=ln_dve)
        k.op("tensor", lambda e: e.matmul(
            S[2][:], cn["ones_row"][:], rs_b[:], start=True, stop=True),
            waits=W(ln_dve) + [(ln_done, ln_done.n), (fe_pe, fe_pe.n)], inc=ln_pe)
        for c in range(8):
            k.op("vector", lambda e, c=c: e.tensor_mul(
                xh[:, c * 512:(c + 1) * 512],
                xbb[:, c * 512:(c + 1) * 512], S[2][:]),
                waits=W(ln_pe), inc=(ln_done if c == 7 else ln_dve))
        for ti in range(4) if trunc < 2 else []:
            t = g * 4 + ti
            k.op("tensor", lambda e, ti=ti: e.transpose(
                PB[0:128, 0:1], rs_b[0:1, ti * 128:(ti + 1) * 128],
                cn["ident_bf"][0:1, 0:1]),
                waits=W(gt_dve, ln_dve), inc=gt_pe)
            k.op("vector", lambda e: e.tensor_copy(rstdc[:], PB[0:128, 0:1]),
                 waits=W(gt_pe), inc=gt_dve)
            for c in range(8):
                k.op("tensor", lambda e, c=c, ti=ti: e.matmul(
                    S[6][:, 0:16],
                    xgf[:, c * 512 + ti * 128: c * 512 + (ti + 1) * 128],
                    gw_sb[:, c * 16:(c + 1) * 16], start=(c == 0), stop=False),
                    waits=[(s_ldx3, ldb3 + 128), (gt_dve, gt_dve.n)]
                    if c == 0 else ())
            k.op("tensor", lambda e, ti=ti: e.matmul(
                S[6][:, 0:16], negmf[0:1, ti * 128:(ti + 1) * 128],
                csgw[:], start=False, stop=True), inc=gt_pe)
            k.op("vector", lambda e: e.tensor_scalar(
                lg[:], S[6][:, 0:16], rstdc[:, 0:1], None, op0=ALU.mult),
                waits=W(gt_pe, gt_dve), inc=gt_dve)
            k.op("vector", lambda e: e.reduce_max(
                nmx[:], lg[:], axis=AX.X, negate=True), waits=W(gt_dve), inc=gt_dve)
            k.op("scalar", lambda e: e.activation(
                esc[:], lg[:], AF.Exp, bias=nmx[:, 0:1], scale=1.0,
                accum_out=sume[:, 0:1]), waits=W(gt_dve), inc=gt_act)
            k.op("vector", lambda e: e.reciprocal(inv[:], sume[:]),
                 waits=W(gt_act), inc=gt_dve)
            k.op("vector", lambda e: e.tensor_scalar(
                orig[:], esc[:], inv[:, 0:1], None, op0=ALU.mult),
                waits=W(gt_dve), inc=gt_dve)
            k.op("vector", lambda e: e.tensor_add(sb16[:], orig[:], gbb[:]),
                 waits=W(gt_dve), inc=gt_dve)
            first = True
            for i in range(4):
                for j in range(i + 1, 4):
                    if first:
                        k.op("vector", lambda e, i=i, j=j: e.tensor_add(
                            gsc[:], sb16[:, i:13 + i:4], sb16[:, j:13 + j:4]),
                            waits=W(gt_dve), inc=gt_dve)
                    else:
                        k.op("vector", lambda e, i=i, j=j: e.tensor_add(
                            scr4[:], sb16[:, i:13 + i:4], sb16[:, j:13 + j:4]),
                            waits=W(gt_dve), inc=gt_dve)
                        k.op("vector", lambda e: e.tensor_max(
                            gsc[:], gsc[:], scr4[:]), waits=W(gt_dve), inc=gt_dve)
                    first = False
            first = True
            for i in range(4):
                for j in range(i + 1, 4):
                    if first:
                        k.op("vector", lambda e, i=i, j=j: e.tensor_tensor(
                            thr[:], gsc[:, i:i + 1], gsc[:, j:j + 1], op=ALU.min),
                            waits=W(gt_dve), inc=gt_dve)
                    else:
                        k.op("vector", lambda e, i=i, j=j: e.tensor_tensor(
                            pmin[:], gsc[:, i:i + 1], gsc[:, j:j + 1], op=ALU.min),
                            waits=W(gt_dve), inc=gt_dve)
                        k.op("vector", lambda e: e.tensor_max(
                            thr[:], thr[:], pmin[:]), waits=W(gt_dve), inc=gt_dve)
                    first = False
            k.op("vector", lambda e: e.tensor_scalar(
                keepg[:], gsc[:], thr[:, 0:1], None, op0=ALU.is_ge),
                waits=W(gt_dve), inc=gt_dve)
            k.op("vector", lambda e: e.tensor_scalar(
                keepg[:], keepg[:], 1e30, -1e30, op0=ALU.mult, op1=ALU.add),
                waits=W(gt_dve), inc=gt_dve)
            for gg in range(4):
                k.op("vector", lambda e, gg=gg: e.tensor_scalar(
                    sb16[:, gg * 4:(gg + 1) * 4], sb16[:, gg * 4:(gg + 1) * 4],
                    keepg[:, gg:gg + 1], None, op0=ALU.add),
                    waits=W(gt_dve), inc=gt_dve)
            k.op("vector", lambda e: e.reduce_max(m1[:], sb16[:], axis=AX.X),
                 waits=W(gt_dve), inc=gt_dve)
            k.op("vector", lambda e: e.tensor_scalar(
                sel[:], sb16[:], m1[:, 0:1], None, op0=ALU.is_ge),
                waits=W(gt_dve), inc=gt_dve)
            k.op("vector", lambda e: e.tensor_scalar(
                sel[:], sel[:], -2e30, None, op0=ALU.mult),
                waits=W(gt_dve), inc=gt_dve)
            k.op("vector", lambda e: e.tensor_add(sel[:], sb16[:], sel[:]),
                 waits=W(gt_dve), inc=gt_dve)
            k.op("vector", lambda e: e.reduce_max(m2[:], sel[:], axis=AX.X),
                 waits=W(gt_dve), inc=gt_dve)
            k.op("vector", lambda e: e.tensor_scalar(
                sel[:], sb16[:], m2[:, 0:1], None, op0=ALU.is_ge),
                waits=W(gt_dve), inc=gt_dve)
            k.op("vector", lambda e, ti=ti: e.tensor_mul(
                comb[:, ti * 16:(ti + 1) * 16], orig[:], sel[:]),
                waits=W(gt_dve), inc=gt_dve)
        gtW = (gt_dve, gt_dve.n)
        lnwg = (ln_done, ln_done.n)
        k.op("vector", lambda e: e.memset(ysb[:], 0.0),
             waits=[(st_dve, st_dve.n), gtW], inc=fe_dve)
        for ei in (range(NE + 1) if trunc == 0 else
                   ([] if trunc >= 3 else [NE])):
            shared = (ei == NE)
            dbl = ei % 2
            wW = s_w.n
            csW = s_cs.n
            if not shared:
                for c in range(8):
                    k.op("gpsimd", lambda e, c=c, ei=ei, dbl=dbl: e.dma_start(
                        out=w1e[dbl][:, c * 512:(c + 1) * 512],
                        in_=wbf1[ei * 1024 + c * 128: ei * 1024 + (c + 1) * 128, :]),
                        waits=([(fe_pe, fe_pe.n)] + (pcW if (g == 0 and ei == 0) else []))
                        if c == 0 else (), inc=s_w, inck=16)
                    k.op("gpsimd", lambda e, c=c, ei=ei, dbl=dbl: e.dma_start(
                        out=w3e[dbl][:, c * 512:(c + 1) * 512],
                        in_=wbf3[ei * 1024 + c * 128: ei * 1024 + (c + 1) * 128, :]),
                        inc=s_w, inck=16)
                for sl in range(4):
                    k.op("gpsimd", lambda e, sl=sl, ei=ei, dbl=dbl: e.dma_start(
                        out=w2e[dbl][:, sl * 1024:(sl + 1) * 1024],
                        in_=wbf2[ei * 512 + sl * 128: ei * 512 + (sl + 1) * 128, :]),
                        waits=[(fe_pe, fe_pe.n)] if sl == 0 else (), inc=s_w, inck=16)
                k.op("scalar", lambda e, ei=ei, dbl=dbl: e.dma_start(
                    out=cs1t[dbl][:], in_=cs1d[ei:ei + 1, :]),
                    waits=[(fe_pe, fe_pe.n)] + (pcW if (g == 0 and ei == 0) else []),
                    inc=s_cs, inck=16)
                k.op("scalar", lambda e, ei=ei, dbl=dbl: e.dma_start(
                    out=cs3t[dbl][:], in_=cs3d[ei:ei + 1, :]), inc=s_cs, inck=16)
                k.op("gpsimd", lambda e, ei=ei, dbl=dbl: e.dma_start(
                    out=be2t[dbl][:], in_=BE2[ei:ei + 1, :]),
                    waits=[(fe_pe, fe_pe.n)], inc=s_w, inck=16)
            nsl = 8 if shared else 4
            wid = 1024 if shared else 512
            wt1 = ws1_sb if shared else w1e[dbl]
            wt3 = ws3_sb if shared else w3e[dbl]
            wt2 = ws2_sb if shared else w2e[dbl]
            for sl in range(nsl):
                for c in range(8):
                    k.op("tensor", lambda e, c=c, sl=sl, wt1=wt1, wid=wid:
                         e.matmul(S[6][:, 0:512],
                                  wt1[:, c * wid + sl * 128: c * wid + (sl + 1) * 128],
                                  xh[:, c * 512:(c + 1) * 512],
                                  start=(c == 0), stop=False),
                         waits=(W(fe_act, fe_dve) + [lnwg]
                                + ([(s_w, wW + 336), (s_cs, csW + 32)]
                                   if not shared else []))
                         if c == 0 else ())
                csrow = css1 if shared else cs1t[dbl]
                k.op("tensor", lambda e, sl=sl, csrow=csrow: e.matmul(
                    S[6][:, 0:512], csrow[0:1, sl * 128:(sl + 1) * 128],
                    negmr[:], start=False, stop=True),
                    inc=fe_pe)
                for c in range(8):
                    k.op("tensor", lambda e, c=c, sl=sl, wt3=wt3, wid=wid:
                         e.matmul(S[4][:, 0:512],
                                  wt3[:, c * wid + sl * 128: c * wid + (sl + 1) * 128],
                                  xh[:, c * 512:(c + 1) * 512],
                                  start=(c == 0), stop=False),
                         waits=W(fe_act, fe_dve) if c == 0 else ())
                csrow3 = css3 if shared else cs3t[dbl]
                k.op("tensor", lambda e, sl=sl, csrow3=csrow3: e.matmul(
                    S[4][:, 0:512], csrow3[0:1, sl * 128:(sl + 1) * 128],
                    negmr[:], start=False, stop=True),
                    inc=fe_pe)
                bcol1 = bs1c[:, sl:sl + 1] if shared else be1t[:, ei * 4 + sl: ei * 4 + sl + 1]
                bcol3 = bs3c[:, sl:sl + 1] if shared else be3t[:, ei * 4 + sl: ei * 4 + sl + 1]
                k.op("scalar", lambda e, bcol1=bcol1: e.activation(
                    h1sb[:], S[6][:, 0:512], AF.Silu, bias=bcol1, scale=1.0),
                    waits=W(fe_pe) + [(s_b, s_b.n)], inc=fe_act)
                k.op("vector", lambda e, bcol3=bcol3: e.tensor_scalar(
                    h3scr[:], S[4][:, 0:512], bcol3, None, op0=ALU.add),
                    waits=W(fe_pe) + [(s_b, s_b.n)], inc=fe_dve)
                k.op("vector", lambda e, sl=sl: e.tensor_mul(
                    h_all[:, sl * 512:(sl + 1) * 512], h1sb[:], h3scr[:]),
                    waits=W(fe_act, fe_dve), inc=fe_dve)
            hW = (fe_dve, fe_dve.n)
            for tt in range(4):
                for dh in range(2):
                    bank = S[dh]
                    for sl in range(nsl):
                        k.op("tensor", lambda e, sl=sl, tt=tt, dh=dh, bank=bank,
                             wt2=wt2:
                             e.matmul(bank[:],
                                      h_all[:, sl * 512 + tt * 128: sl * 512 + tt * 128 + 128],
                                      wt2[:, sl * 1024 + dh * 512: sl * 1024 + (dh + 1) * 512],
                                      start=(sl == 0), stop=False),
                             waits=[hW, (fe_dve, fe_dve.n)] if sl == 0 else ())
                    berow = bs2r if shared else be2t[dbl]
                    k.op("tensor", lambda e, dh=dh, bank=bank, berow=berow: e.matmul(
                        bank[:], cn["ones_row"][0:1, 0:128],
                        berow[0:1, dh * 512:(dh + 1) * 512],
                        start=False, stop=True), inc=fe_pe)
                    if shared:
                        k.op("vector", lambda e, tt=tt, dh=dh, bank=bank: e.tensor_add(
                            ysb[:, tt * 1024 + dh * 512: tt * 1024 + (dh + 1) * 512],
                            ysb[:, tt * 1024 + dh * 512: tt * 1024 + (dh + 1) * 512],
                            bank[:]), waits=W(fe_pe), inc=fe_dve)
                    else:
                        k.op("vector", lambda e, tt=tt, dh=dh, ei=ei, bank=bank:
                             e.scalar_tensor_tensor(
                                ysb[:, tt * 1024 + dh * 512: tt * 1024 + (dh + 1) * 512],
                                bank[:],
                                comb[:, tt * 16 + ei: tt * 16 + ei + 1],
                                ysb[:, tt * 1024 + dh * 512: tt * 1024 + (dh + 1) * 512],
                                op0=ALU.mult, op1=ALU.add),
                             waits=W(fe_pe), inc=fe_dve)
        ydW = (fe_dve, fe_dve.n)
        for ti in range(4):
            k.op("vector", lambda e, ti=ti: e.tensor_copy(
                h_all[:, ti * 1024:(ti + 1) * 1024], ysb[:, ti * 1024:(ti + 1) * 1024]),
                waits=[ydW] + ([(res_pe, prev_res)] if ti == 0 else []), inc=fe_dve)
        for c in range(8):
            n = g * 8 + c
            b = n % 2
            k.op("sync", lambda e, c=c, g=g, b=b: e.dma_start(
                out=xrg[b][:], in_=xT[c * 128:(c + 1) * 128,
                                      g * 512:(g + 1) * 512]),
                waits=[(st_dve, max(0, st_dve.n - 1))], inc=gxr[b], inck=16)
            for ti in range(4):
                k.op("tensor", lambda e, c=c, ti=ti: e.transpose(
                    PB[:, ti * 128:(ti + 1) * 128],
                    h_all[:, ti * 1024 + c * 128: ti * 1024 + (c + 1) * 128],
                    cn["ident_bf"][:]),
                    waits=[(fe_dve, fe_dve.n), (res_pe, res_pe.n),
                           (st_dve, st_dve.n)] if ti == 0 else (),
                    inc=res_pe)
            k.op("vector", lambda e, b=b: e.tensor_add(
                x4o[b][:], PB[:, 0:512], xrg[b][:]),
                waits=[(res_pe, res_pe.n), (gxr[b], gxr[b].n), (gst[b], gst[b].n)],
                inc=st_dve)
            k.op("scalar", lambda e, c=c, g=g, b=b: e.dma_start(
                out=x4T[c * 128:(c + 1) * 128, g * 512:(g + 1) * 512],
                in_=x4o[b][:]), waits=[(st_dve, st_dve.n)], inc=gst[b], inck=16)
        prev_res = res_pe.n


def build_l1_ffn_program(T):
    nc = bass.Bass("TRN2", target_bir_lowering=False)
    xT = nc.dram_tensor("xT", [1024, T], F32, kind="ExternalInput").ap()
    gwT = nc.dram_tensor("gwT", [1024, 16], F32, kind="ExternalInput").ap()
    gbR = nc.dram_tensor("gbR", [1, 16], F32, kind="ExternalInput").ap()
    WE1 = nc.dram_tensor("WE1", [16 * 1024, 512], F32, kind="ExternalInput").ap()
    BE1 = nc.dram_tensor("BE1", [16 * 512, 1], F32, kind="ExternalInput").ap()
    WE2 = nc.dram_tensor("WE2", [16 * 512, 1024], F32, kind="ExternalInput").ap()
    BE2 = nc.dram_tensor("BE2", [16, 1024], F32, kind="ExternalInput").ap()
    WE3 = nc.dram_tensor("WE3", [16 * 1024, 512], F32, kind="ExternalInput").ap()
    BE3 = nc.dram_tensor("BE3", [16 * 512, 1], F32, kind="ExternalInput").ap()
    ws1 = nc.dram_tensor("ws1", [1024, 1024], F32, kind="ExternalInput").ap()
    bs1 = nc.dram_tensor("bs1", [1024, 1], F32, kind="ExternalInput").ap()
    ws2 = nc.dram_tensor("ws2", [1024, 1024], F32, kind="ExternalInput").ap()
    bs2 = nc.dram_tensor("bs2", [1, 1024], F32, kind="ExternalInput").ap()
    ws3 = nc.dram_tensor("ws3", [1024, 1024], F32, kind="ExternalInput").ap()
    bs3 = nc.dram_tensor("bs3", [1024, 1], F32, kind="ExternalInput").ap()
    x4T = nc.dram_tensor("x4T", [1024, T], F32, kind="ExternalOutput").ap()
    ctx = contextlib.ExitStack()
    with ctx:
        E = ctx.enter_context
        k = K(nc, E)
        env = Env(nc, E, k)
        emit_l1_ffn(env, T, xT, x4T, gwT, gbR, WE1, BE1, WE2, BE2, WE3, BE3,
                    ws1, bs1, ws2, bs2, ws3, bs3)
        k.final_barrier()
        block = E(nc.Block())
        k.finish(block)
    return nc


def _l1f_feed(x, b, p):
    m = dict(make_const_arrays())
    m["xT"] = np.ascontiguousarray(x[b].T)
    m["gwT"] = np.ascontiguousarray(p["gate_w"].T)
    m["gbR"] = np.ascontiguousarray(p["gate_b"][None, :])
    m["WE1"] = np.ascontiguousarray(p["We1"].reshape(16 * 1024, 512))
    m["BE1"] = np.ascontiguousarray(p["be1"].reshape(16 * 512, 1))
    m["WE2"] = np.ascontiguousarray(p["We2"].reshape(16 * 512, 1024))
    m["BE2"] = np.ascontiguousarray(p["be2"])
    m["WE3"] = np.ascontiguousarray(p["We3"].reshape(16 * 1024, 512))
    m["BE3"] = np.ascontiguousarray(p["be3"].reshape(16 * 512, 1))
    m["ws1"] = np.ascontiguousarray(p["ws1"])
    m["bs1"] = np.ascontiguousarray(p["bs1"][:, None])
    m["ws2"] = np.ascontiguousarray(p["ws2"])
    m["bs2"] = np.ascontiguousarray(p["bs2"][None, :])
    m["ws3"] = np.ascontiguousarray(p["ws3"])
    m["bs3"] = np.ascontiguousarray(p["bs3"][:, None])
    return m


def run_l1_ffn(x, p):
    B, T, D = x.shape
    nc = build_l1_ffn_program(T)
    in_maps = [_l1f_feed(x, b, p) for b in range(B)]
    res = run_bass_kernel_spmd(nc, in_maps, core_ids=list(range(B)))
    return np.stack([r["x4T"].T for r in res.results], axis=0)


def sim_l1_ffn(x, p):
    from concourse.bass_interp import CoreSim
    B, T, D = x.shape
    assert B == 1
    nc = build_l1_ffn_program(T)
    nc.finalize()
    sim = CoreSim(nc)
    feed = _l1f_feed(x, 0, p)
    for name, arr in feed.items():
        v = sim.tensor(name)
        v[:] = arr.view(v.dtype) if arr.dtype == NP_BF16 else arr
    sim.simulate()
    return sim.tensor("x4T").T[None].copy()



def kernel(**inp):
    T = 4096
    B = 8
    cap = 4096
    consts = make_const_arrays()
    x = np.asarray(inp["L0_attn_x"])

    def maps(extra):
        out = []
        for b in range(B):
            m = dict(consts)
            m.update(extra(b))
            out.append(m)
        return out

    nc1 = build_l0_attn_program(T)
    wqkvT = np.ascontiguousarray(np.asarray(inp["L0_attn_w_qkv"]).T)
    woutT = np.ascontiguousarray(np.asarray(inp["L0_attn_w_out"]).T)
    gcol = np.ascontiguousarray(np.asarray(inp["L0_attn_gamma"])[:, None])
    r1 = run_bass_kernel_spmd(nc1, maps(lambda b: {
        "xT": np.ascontiguousarray(x[b].T), "wqkvT": wqkvT, "woutT": woutT,
        "gammacol": gcol}), core_ids=list(range(B)))

    nc2 = build_l0_router_program(T, capl=CAPL)
    Wr = np.ascontiguousarray(np.asarray(inp["L0_ffn_Wr"]))
    r2 = run_bass_kernel_spmd(nc2, maps(lambda b: {
        "xT": r1.results[b]["x1T"], "Wr": Wr}), core_ids=list(range(B)))
    counts_all = np.stack([r2.results[b]["counts"][0] for b in range(B)], 0)

    nc3 = build_l0_rest_program(T, cap, capl=CAPL)
    W1 = np.ascontiguousarray(np.asarray(inp["L0_ffn_W1"]).reshape(8 * 1024, 4096))
    W2 = np.ascontiguousarray(np.asarray(inp["L0_ffn_W2"]).reshape(8 * 4096, 1024))
    r3 = run_bass_kernel_spmd(nc3, maps(lambda b: {
        "xT": r1.results[b]["x1T"], "W1": W1, "W2": W2,
        "xe": r2.results[b]["xe"], "idx": r2.results[b]["idx"],
        "ptok": r2.results[b]["ptok"], "lrloc": r2.results[b]["lrloc"],
        "oneh": r2.results[b]["oneh"], "countsA": counts_all,
        "corelt": (np.arange(8) < b).astype(np.float32)[:, None]}),
        core_ids=list(range(B)))

    nc4 = build_l1_attn_program(T)
    l1w = {n: np.ascontiguousarray(np.asarray(inp["L1_attn_w_" + n]).T)
           for n in ("q", "k", "v", "f", "o")}
    nwc = np.ascontiguousarray(np.asarray(inp["L1_attn_norm_weight"])[:, None])
    gnc = np.ascontiguousarray(np.asarray(inp["L1_attn_gnorm_weight"])[:, None])
    r4 = run_bass_kernel_spmd(nc4, maps(lambda b: {
        "xT": r3.results[b]["x2T"], "nw": nwc, "gn": gnc,
        "wq": l1w["q"], "wk": l1w["k"], "wv": l1w["v"], "wf": l1w["f"],
        "wo": l1w["o"]}), core_ids=list(range(B)))

    nc5 = build_l1_ffn_program(T)
    p5 = {
        "gwT": np.ascontiguousarray(np.asarray(inp["L1_ffn_gate_w"]).T),
        "gbR": np.ascontiguousarray(np.asarray(inp["L1_ffn_gate_b"])[None, :]),
        "WE1": np.ascontiguousarray(np.asarray(inp["L1_ffn_We1"]).reshape(16 * 1024, 512)),
        "BE1": np.ascontiguousarray(np.asarray(inp["L1_ffn_be1"]).reshape(16 * 512, 1)),
        "WE2": np.ascontiguousarray(np.asarray(inp["L1_ffn_We2"]).reshape(16 * 512, 1024)),
        "BE2": np.ascontiguousarray(np.asarray(inp["L1_ffn_be2"])),
        "WE3": np.ascontiguousarray(np.asarray(inp["L1_ffn_We3"]).reshape(16 * 1024, 512)),
        "BE3": np.ascontiguousarray(np.asarray(inp["L1_ffn_be3"]).reshape(16 * 512, 1)),
        "ws1": np.ascontiguousarray(np.asarray(inp["L1_ffn_ws1"])),
        "bs1": np.ascontiguousarray(np.asarray(inp["L1_ffn_bs1"])[:, None]),
        "ws2": np.ascontiguousarray(np.asarray(inp["L1_ffn_ws2"])),
        "bs2": np.ascontiguousarray(np.asarray(inp["L1_ffn_bs2"])[None, :]),
        "ws3": np.ascontiguousarray(np.asarray(inp["L1_ffn_ws3"])),
        "bs3": np.ascontiguousarray(np.asarray(inp["L1_ffn_bs3"])[:, None]),
    }
    r5 = run_bass_kernel_spmd(nc5, maps(lambda b: dict(p5, xT=r4.results[b]["x3T"])),
                              core_ids=list(range(B)))
    return np.stack([r5.results[b]["x4T"].T for b in range(B)], axis=0).astype(np.float32)


# version 02
# speedup vs baseline: 1.1605x; 1.1605x over previous
import contextlib
import numpy as np
import ml_dtypes
import concourse.bass as bass
import concourse.mybir as mybir
from concourse.bass_utils import run_bass_kernel_spmd
from concourse.bass_core_rust import IndirectOffsetOnAxis

F32 = mybir.dt.float32
BF16 = mybir.dt.bfloat16
I32 = mybir.dt.int32
AF = mybir.ActivationFunctionType
ALU = mybir.AluOpType
AX = mybir.AxisListType
NP_BF16 = ml_dtypes.bfloat16

ENGINES = ("sync", "vector", "scalar", "tensor", "gpsimd")
SCALE_ATTN = 128.0 ** -0.5


class Sem:
    def __init__(self, handle, name):
        self.h = handle
        self.n = 0
        self.name = name
        self.eng = None

    def now(self):
        return (self, self.n)


class K:

    def __init__(self, nc, E):
        self.nc = nc
        self.E = E
        self.q = {e: [] for e in ENGINES}
        self._semi = 0
        self.dma_sems = set()

    def sem(self, name=None):
        name = name or f"s{self._semi}"
        self._semi += 1
        h = self.E(self.nc.semaphore(name))
        return Sem(h, name)

    def op(self, eng, emit, waits=(), inc=None, inck=1):
        wl = []
        for w in waits:
            if isinstance(w, Sem):
                wl.append((w, w.n))
            else:
                wl.append(w)
        if inc is not None:
            if inc.eng is None:
                inc.eng = eng
            assert inc.eng == eng, f"sem {inc.name} inc from {eng} and {inc.eng}"
            inc.n += inck
            if inck >= 16:
                self.dma_sems.add(inc)

        def run(e, emit=emit, wl=wl, inc=inc, inck=inck):
            last_wait = None
            for (s, v) in wl:
                if v > 0:
                    e.wait_ge(s.h, v)
            inst = emit(e)
            if inc is not None:
                assert inst is not None, "inc on an op that emitted nothing"
                inst.then_inc(inc.h, inck)
        self.q[eng].append(run)

    def final_barrier(self):
        self.op("sync", lambda e: None,
                waits=[(x, x.n) for x in sorted(self.dma_sems, key=lambda s: s.name)])

    def finish(self, block):
        for name in ENGINES:
            fns = self.q[name]
            if not fns:
                continue

            def body(e, fns=fns):
                for f in fns:
                    f(e)
            getattr(block, name)(body)



def make_const_arrays():
    c = {}
    c["ones_col"] = np.ones((128, 1), NP_BF16)
    c["ones_row"] = np.ones((1, 128), NP_BF16)
    c["ident_bf"] = np.eye(128, dtype=NP_BF16)
    c["ident_f"] = np.eye(128, dtype=np.float32)
    am = np.zeros((128, 4 * 512), np.float32)
    for r in range(4):
        t = np.arange(128)[:, None]
        kk = np.arange(512)[None, :]
        am[:, r * 512:(r + 1) * 512] = np.where(kk <= 128 * r + t, 0.0, -1e38)
    c["am"] = am
    mmk = np.zeros((128, 4 * 512), np.float32)
    for r in range(4):
        kk = np.arange(128)[:, None]
        t = np.arange(512)[None, :]
        mmk[:, r * 512:(r + 1) * 512] = (t >= 128 * r + kk).astype(np.float32)
    c["mm"] = mmk.astype(NP_BF16)
    c["ones_col_f"] = np.ones((128, 1), np.float32)
    c["ones_row_f"] = np.ones((1, 128), np.float32)
    j = np.arange(128)[:, None]
    t = np.arange(128)[None, :]
    c["strictL_f"] = (j < t).astype(np.float32)
    c["tri_incl_f"] = (j <= t).astype(np.float32)
    c["tri_revx_f"] = (j > t).astype(np.float32)
    return c

CONST_SPECS = [
    ("ones_col", [128, 1], BF16), ("ones_row", [1, 128], BF16),
    ("ident_bf", [128, 128], BF16), ("ident_f", [128, 128], F32),
    ("am", [128, 2048], F32), ("mm", [128, 2048], BF16),
    ("ones_col_f", [128, 1], F32), ("ones_row_f", [1, 128], F32),
    ("strictL_f", [128, 128], F32), ("tri_incl_f", [128, 128], F32),
    ("tri_revx_f", [128, 128], F32),
]


class Env:

    def __init__(self, nc, E, k: K):
        self.nc, self.E, self.k = nc, E, k
        self.ld = k.sem("ld_const")
        self.cn = {}
        for name, shape, dt in CONST_SPECS:
            d = nc.dram_tensor(name, shape, dt, kind="ExternalInput").ap()
            s = E(nc.sbuf_tensor(name + "_s", shape, dt))
            self.cn[name] = s
            k.op("sync", lambda e, d=d, s=s: e.dma_start(out=s[:], in_=d),
                 inc=self.ld, inck=16)
        self.S = [E(nc.psum_tensor(f"psb{i}", [128, 512], F32)) for i in range(7)]
        self.PB = E(nc.psum_tensor("psbf", [128, 512], BF16))



def emit_l0_attn(env: Env, T, xT, x1T, wqkvT, woutT, gammacol):
    nc, E, k, cn = env.nc, env.E, env.k, env.cn
    S, PB = env.S, env.PB
    G = T // 512
    QT = T // 128
    NH = 8

    xh = E(nc.sbuf_tensor("xh", [128, 8 * T], BF16))
    xb = E(nc.sbuf_tensor("xb", [128, 8 * 512], BF16))
    sq = [E(nc.sbuf_tensor(f"sq{i}", [128, 512], BF16)) for i in range(2)]
    gcol_f = E(nc.sbuf_tensor("gcol_f", [128, 8], F32))
    gcol_b = E(nc.sbuf_tensor("gcol_b", [128, 8], BF16))
    m_row = E(nc.sbuf_tensor("m_row", [1, 512], F32))
    q_row = E(nc.sbuf_tensor("q_row", [1, 512], F32))
    var_row = E(nc.sbuf_tensor("var_row", [1, 512], F32))
    rstd_row = E(nc.sbuf_tensor("rstd_row", [1, 512], F32))
    rstd_b = E(nc.sbuf_tensor("rstd_b", [1, 512], BF16))
    negmr = E(nc.sbuf_tensor("negmr", [1, T], BF16))
    wq = E(nc.sbuf_tensor("wq", [128, 8 * 128], BF16))
    wk = E(nc.sbuf_tensor("wk", [128, 8 * 128], BF16))
    wv = E(nc.sbuf_tensor("wv", [128, 8 * 128], BF16))
    wout_sb = E(nc.sbuf_tensor("wout_sb", [128, 8 * 1024], BF16))
    csq = E(nc.sbuf_tensor("csq", [1, 128], BF16))
    csk = E(nc.sbuf_tensor("csk", [1, 128], BF16))
    csv = E(nc.sbuf_tensor("csv", [1, 128], BF16))
    Qf = E(nc.sbuf_tensor("Qf", [128, T], BF16))
    Kf = E(nc.sbuf_tensor("Kf", [128, T], BF16))
    Vt = E(nc.sbuf_tensor("Vt", [128, T], BF16))
    O_dram = nc.dram_tensor("O_dram", [1024, T], BF16, kind="Internal").ap()
    ost = [E(nc.sbuf_tensor(f"ost{i}", [128, 512], BF16)) for i in range(2)]
    otile = [[E(nc.sbuf_tensor(f"otile{b}_{hh}", [128, 512], BF16)) for hh in range(8)]
             for b in range(2)]
    esc = [E(nc.sbuf_tensor(f"esc{i}", [128, 512], BF16)) for i in range(2)]
    ET = [E(nc.sbuf_tensor(f"ET{i}", [128, 512], BF16)) for i in range(2)]
    nmcols = E(nc.sbuf_tensor("nmcols", [128, 8], F32))
    lcols = E(nc.sbuf_tensor("lcols", [128, 8], F32))
    nm_p = E(nc.sbuf_tensor("nm_p", [128, 1], F32))
    nm_f = E(nc.sbuf_tensor("nm_f", [128, 1], F32))
    nmneg = E(nc.sbuf_tensor("nmneg", [128, 1], F32))
    resc = E(nc.sbuf_tensor("resc", [128, 1], F32))
    lsum = E(nc.sbuf_tensor("lsum", [128, 1], F32))
    lnl = E(nc.sbuf_tensor("lnl", [128, 1], F32))
    negml = E(nc.sbuf_tensor("negml", [128, 1], F32))
    rcol = E(nc.sbuf_tensor("rcol", [128, 1], F32))
    nbcols = E(nc.sbuf_tensor("nbcols", [128, 32], BF16))
    rbcols = E(nc.sbuf_tensor("rbcols", [128, 32], BF16))
    nrow = E(nc.sbuf_tensor("nrow", [1, 512], BF16))
    rrow = E(nc.sbuf_tensor("rrow", [1, 512], BF16))
    xre = [E(nc.sbuf_tensor(f"xre{i}", [128, 512], F32)) for i in range(4)]
    yout = [E(nc.sbuf_tensor(f"yout{i}", [128, 512], F32)) for i in range(4)]

    s_ldg = k.sem()
    s_ldw = k.sem()
    s_ldx = k.sem()
    s_pe = k.sem()
    s_act = k.sem()
    s_dve = k.sem()
    s_pool = k.sem()
    s_st = k.sem()
    s_xre = k.sem()

    W = lambda *ss: [s.now() for s in ss]
    cw = (env.ld, env.ld.n)

    for c in range(8):
        k.op("sync", lambda e, c=c: e.dma_start(
            out=gcol_f[:, c:c + 1], in_=gammacol[c * 128:(c + 1) * 128, 0:1]),
            inc=s_ldg, inck=16)
    k.op("vector", lambda e: e.tensor_copy(gcol_b[:], gcol_f[:]),
         waits=W(s_ldg) + [cw], inc=s_dve)
    for c in range(8):
        k.op("gpsimd", lambda e, c=c: e.dma_start(
            out=wout_sb[:, c * 1024:(c + 1) * 1024],
            in_=woutT[c * 128:(c + 1) * 128, :]), inc=s_ldw, inck=16)

    ln_done = k.sem()
    for g in range(G):
        ldg = s_ldx.n
        for c in range(8):
            k.op("gpsimd", lambda e, c=c, g=g: e.dma_start(
                out=xb[:, c * 512:(c + 1) * 512],
                in_=xT[c * 128:(c + 1) * 128, g * 512:(g + 1) * 512]),
                waits=[(ln_done, ln_done.n)] if g > 0 else (),
                inc=s_ldx, inck=16)
        for c in range(8):
            k.op("tensor", lambda e, c=c: e.matmul(
                S[0][0:1, :], cn["ones_col"][:], xb[:, c * 512:(c + 1) * 512],
                start=(c == 0), stop=(c == 7)),
                waits=([(s_ldx, ldg + 128)] + ([cw] if g == 0 else [])) if c == 0 else (),
                inc=s_pe if c == 7 else None)
        for c in range(8):
            k.op("scalar", lambda e, c=c: e.activation(
                sq[c % 2][:], xb[:, c * 512:(c + 1) * 512], AF.Square,
                bias=0.0, scale=1.0),
                waits=W(s_ldx) + (W(s_pe) if c >= 2 else []),
                inc=s_act)
            k.op("tensor", lambda e, c=c: e.matmul(
                S[1][0:1, :], cn["ones_col"][:], sq[c % 2][:],
                start=(c == 0), stop=(c == 7)),
                waits=W(s_act), inc=s_pe)
        k.op("vector", lambda e: e.tensor_scalar(
            m_row[:], S[0][0:1, :], 1.0 / 1024, None, op0=ALU.mult),
            waits=W(s_pe), inc=s_dve)
        k.op("vector", lambda e: e.tensor_scalar(
            q_row[:], S[1][0:1, :], 1.0 / 1024, None, op0=ALU.mult),
            inc=s_dve)
        k.op("vector", lambda e: e.tensor_mul(var_row[:], m_row[:], m_row[:]),
             waits=W(s_dve), inc=s_dve)
        k.op("vector", lambda e: e.tensor_sub(var_row[:], q_row[:], var_row[:]),
             waits=W(s_dve), inc=s_dve)
        k.op("scalar", lambda e: e.activation(rstd_row[:], var_row[:], AF.Ln,
                                              bias=0.0, scale=1.0),
             waits=W(s_dve), inc=s_act)
        k.op("vector", lambda e: e.tensor_scalar(
            rstd_row[:], rstd_row[:], -0.5, None, op0=ALU.mult),
            waits=W(s_act), inc=s_dve)
        k.op("scalar", lambda e: e.activation(rstd_row[:], rstd_row[:], AF.Exp,
                                              bias=0.0, scale=1.0),
             waits=W(s_dve), inc=s_act)
        k.op("vector", lambda e: e.tensor_copy(rstd_b[:], rstd_row[:]),
             waits=W(s_act), inc=s_dve)
        k.op("vector", lambda e, g=g: e.tensor_mul(
            negmr[0:1, g * 512:(g + 1) * 512], m_row[:], rstd_row[:]),
            waits=W(s_act, s_dve), inc=s_dve)
        k.op("vector", lambda e, g=g: e.tensor_scalar(
            negmr[0:1, g * 512:(g + 1) * 512], negmr[0:1, g * 512:(g + 1) * 512],
            -1.0, None, op0=ALU.mult),
            waits=W(s_dve), inc=s_dve)
        k.op("tensor", lambda e: e.matmul(
            S[2][:], cn["ones_row"][:], rstd_b[:], start=True, stop=True),
            waits=W(s_dve) + [(ln_done, ln_done.n)], inc=s_pe)
        for c in range(8):
            k.op("vector", lambda e, c=c, g=g: e.scalar_tensor_tensor(
                xh[:, c * T + g * 512: c * T + (g + 1) * 512],
                xb[:, c * 512:(c + 1) * 512], gcol_b[:, c:c + 1], S[2][:],
                op0=ALU.mult, op1=ALU.mult),
                waits=W(s_pe), inc=(ln_done if c == 7 else s_dve))

    wsem = k.sem()
    pj = k.sem()
    p1pe = k.sem()
    p1dve = k.sem()
    p1act = k.sem()
    p1pool = k.sem()
    st_dve = k.sem()
    p2pe = k.sem()
    p2act = k.sem()
    p2dve = k.sem()
    pvpe = k.sem()
    oact = k.sem()
    trpe = k.sem()
    trevac = k.sem()
    v_ev = k.sem()
    sodma = [k.sem() for _ in range(2)]

    for h in range(NH):
        for c in range(8):
            for wt, off in ((wq, h * 128), (wk, 1024 + h * 128), (wv, 2048 + h * 128)):
                k.op("gpsimd", lambda e, c=c, wt=wt, off=off: e.dma_start(
                    out=wt[:, c * 128:(c + 1) * 128],
                    in_=wqkvT[c * 128:(c + 1) * 128, off:off + 128]),
                    waits=W(pj, v_ev) if (h > 0 and c == 0 and wt is wq) else (),
                    inc=wsem, inck=16)
        for wi, (wt, cs) in enumerate(((wq, csq), (wk, csk), (wv, csv))):
            bank = S[3 + wi]
            for c in range(8):
                k.op("tensor", lambda e, c=c, wt=wt, bank=bank: e.matmul(
                    bank[0:1, 0:128], gcol_b[:, c:c + 1],
                    wt[:, c * 128:(c + 1) * 128],
                    start=(c == 0), stop=(c == 7)),
                    waits=W(wsem, s_dve, p1act) if c == 0 else (), inc=s_pe if c == 7 else None)
            k.op("scalar", lambda e, cs=cs, bank=bank: e.activation(
                cs[:], bank[0:1, 0:128], AF.Copy, bias=0.0, scale=1.0),
                waits=W(s_pe), inc=s_act)
        lnw = (ln_done, G)
        for g in range(G):
            for (dst, wt, cs, scale) in ((Qf, wq, csq, SCALE_ATTN), (Kf, wk, csk, 1.0)):
                bank = S[4 if dst is Qf else 5]
                for c in range(8):
                    k.op("tensor", lambda e, c=c, wt=wt, g=g, bank=bank: e.matmul(
                        bank[:], wt[:, c * 128:(c + 1) * 128],
                        xh[:, c * T + g * 512: c * T + (g + 1) * 512],
                        start=(c == 0), stop=False),
                        waits=[lnw] + W(pj, p1act, s_act) if c == 0 else ())
                k.op("tensor", lambda e, cs=cs, g=g, bank=bank: e.matmul(
                    bank[:], cs[:], negmr[0:1, g * 512:(g + 1) * 512],
                    start=False, stop=True), inc=s_pe)
                k.op("scalar", lambda e, dst=dst, g=g, scale=scale, bank=bank:
                     e.activation(dst[:, g * 512:(g + 1) * 512], bank[:],
                                  AF.Copy, bias=0.0, scale=scale),
                     waits=W(s_pe), inc=pj)
        vbase = v_ev.n
        for t in range(QT):
            vb = S[6] if t % 2 == 0 else S[3]
            for c in range(8):
                k.op("tensor", lambda e, c=c, t=t, vb=vb: e.matmul(
                    vb[:, 0:128], xh[:, c * T + t * 128: c * T + (t + 1) * 128],
                    wv[:, c * 128:(c + 1) * 128], start=(c == 0), stop=False),
                    waits=[lnw, (v_ev, vbase + max(0, t - 1))] + (W(pj, p1act, s_act) if t <= 1 else [])
                    if c == 0 else ())
            k.op("tensor", lambda e, t=t, vb=vb: e.matmul(
                vb[:, 0:128], negmr[0:1, t * 128:(t + 1) * 128], csv[:],
                start=False, stop=True), inc=s_pe)
            k.op("scalar", lambda e, t=t, vb=vb: e.activation(
                Vt[:, t * 128:(t + 1) * 128], vb[:, 0:128], AF.Copy,
                bias=0.0, scale=1.0), waits=W(s_pe), inc=v_ev)

        pjW = (pj, pj.n)
        for i in range(QT):
            nch = i // 4 + 1
            n1 = min(nch, 7)
            for g2 in range(n1):
                k.op("tensor", lambda e, i=i, g2=g2: e.matmul(
                    S[g2][:, 0:min(512, T - g2 * 512)], Qf[:, i * 128:(i + 1) * 128],
                    Kf[:, g2 * 512:min(T, (g2 + 1) * 512)], start=True, stop=True),
                    waits=[pjW] + (W(p1act, oact) if i > 0 else []),
                    inc=p1pe)
                if g2 == i // 4:
                    k.op("vector", lambda e, i=i, g2=g2: e.tensor_add(
                        S[g2][:], S[g2][:], cn["am"][:, (i % 4) * 512:(i % 4 + 1) * 512]),
                        waits=W(p1pe), inc=p1dve)
                k.op("vector", lambda e, g2=g2: e.reduce_max(
                    nmcols[:, g2:g2 + 1], S[g2][:], axis=AX.X, negate=True),
                    waits=W(p1pe, p1dve), inc=p1dve)
            k.op("vector", lambda e, n1=n1: e.tensor_reduce(
                nm_p[:], nmcols[:, 0:n1], axis=AX.X, op=ALU.min),
                waits=W(p1dve), inc=p1pool)
            for g2 in range(n1):
                k.op("scalar", lambda e, g2=g2: e.activation(
                    esc[g2 % 2][:], S[g2][:], AF.Exp, bias=nm_p[:, 0:1], scale=1.0,
                    accum_out=lcols[:, g2:g2 + 1]),
                    waits=W(p1pool), inc=p1act)
            if nch > n1:
                g2 = 7
                k.op("tensor", lambda e, i=i, g2=g2: e.matmul(
                    S[0][:], Qf[:, i * 128:(i + 1) * 128],
                    Kf[:, g2 * 512:(g2 + 1) * 512], start=True, stop=True),
                    waits=W(p1act), inc=p1pe)
                k.op("vector", lambda e, i=i: e.tensor_add(
                    S[0][:], S[0][:], cn["am"][:, (i % 4) * 512:(i % 4 + 1) * 512]),
                    waits=W(p1pe), inc=p1dve)
                k.op("vector", lambda e: e.reduce_max(
                    nmcols[:, 7:8], S[0][:], axis=AX.X, negate=True), inc=p1dve)
                k.op("vector", lambda e: e.tensor_tensor(
                    nm_f[:], nm_p[:], nmcols[:, 7:8], op=ALU.min),
                    waits=W(p1dve, p1pool), inc=p1pool)
                k.op("scalar", lambda e: e.activation(
                    esc[1][:], S[0][:], AF.Exp, bias=nm_f[:, 0:1], scale=1.0,
                    accum_out=lcols[:, 7:8]),
                    waits=W(p1pool), inc=p1act)
                k.op("vector", lambda e: e.tensor_scalar(
                    nmneg[:], nm_p[:], -1.0, None, op0=ALU.mult),
                    waits=W(p1pool), inc=p1dve)
                k.op("scalar", lambda e: e.activation(
                    resc[:], nm_f[:], AF.Exp, bias=nmneg[:, 0:1], scale=1.0),
                    waits=W(p1dve), inc=p1act)
                k.op("vector", lambda e: e.reduce_sum(
                    lsum[:], lcols[:, 0:7], axis=AX.X), waits=W(p1act), inc=p1dve)
                k.op("vector", lambda e: e.scalar_tensor_tensor(
                    lsum[:], lsum[:], resc[:, 0:1], lcols[:, 7:8],
                    op0=ALU.mult, op1=ALU.add),
                    waits=W(p1dve, p1act), inc=p1pool)
                nm_use = nm_f
                lsum_src, lsum_wait = lsum, (p1pool, p1pool.n)
            else:
                k.op("vector", lambda e, n1=n1: e.reduce_sum(
                    lsum[:], lcols[:, 0:n1], axis=AX.X), waits=W(p1act), inc=p1dve)
                nm_use = nm_p
                lsum_src, lsum_wait = lsum, (p1dve, p1dve.n)
            k.op("scalar", lambda e: e.activation(
                lnl[:], lsum[:], AF.Ln, bias=0.0, scale=1.0),
                waits=[lsum_wait], inc=p1act)
            k.op("vector", lambda e, nm_use=nm_use: e.tensor_sub(
                negml[:], nm_use[:], lnl[:]), waits=W(p1act), inc=p1dve)
            k.op("scalar", lambda e, i=i: e.activation(
                nbcols[:, i:i + 1], negml[:], AF.Copy, bias=0.0, scale=1.0),
                waits=W(p1dve), inc=p1act)
            k.op("vector", lambda e, i=i: e.tensor_sub(
                rcol[:], negml[:], nbcols[:, i:i + 1]),
                waits=W(p1act, p1dve), inc=p1pool)
            k.op("vector", lambda e, i=i: e.tensor_copy(
                rbcols[:, i:i + 1], rcol[:]), waits=W(p1pool), inc=p1dve)

        p1W = [(p1dve, p1dve.n), (p1act, p1act.n), (p1pool, p1pool.n)]
        for g in range(G):
            for r in range(4):
                qt = 4 * g + r
                k.op("tensor", lambda e, qt=qt, r=r: e.transpose(
                    PB[0:1, r * 128:(r + 1) * 128], nbcols[:, qt:qt + 1],
                    cn["ident_bf"][:]),
                    waits=p1W + W(trevac) if r == 0 else (), inc=trpe)
            k.op("vector", lambda e: e.tensor_copy(nrow[:], PB[0:1, 0:512]),
                 waits=W(trpe), inc=trevac)
            for r in range(4):
                qt = 4 * g + r
                k.op("tensor", lambda e, qt=qt, r=r: e.transpose(
                    PB[0:1, r * 128:(r + 1) * 128], rbcols[:, qt:qt + 1],
                    cn["ident_bf"][:]),
                    waits=W(trevac) if r == 0 else (), inc=trpe)
            k.op("vector", lambda e: e.tensor_copy(rrow[:], PB[0:1, 0:512]),
                 waits=W(trpe), inc=trevac)
            trW = (trevac, trevac.n)
            nkt = 4 * g + 4
            expc = []
            pvc = []
            for j in range(nkt):
                bank = S[j % 2]
                k.op("tensor", lambda e, j=j, g=g, bank=bank: e.matmul(
                    bank[:], Kf[:, j * 128:(j + 1) * 128], Qf[:, g * 512:(g + 1) * 512],
                    start=True, stop=False),
                    waits=[trW] + ([(p2act, expc[j - 2])] if j >= 2 else W(p2act)))
                k.op("tensor", lambda e, bank=bank: e.matmul(
                    bank[:], cn["ones_row"][:], nrow[:], start=False, stop=False))
                k.op("tensor", lambda e, bank=bank: e.matmul(
                    bank[:], cn["ones_row"][:], rrow[:], start=False, stop=True),
                    inc=p2pe)
                r = j - 4 * g
                k.op("scalar", lambda e, j=j, bank=bank: e.activation(
                    ET[j % 2][:], bank[:], AF.Exp, bias=0.0, scale=1.0),
                    waits=W(p2pe) + ([(pvpe, pvc[j - 2])] if j >= 2 else W(pvpe)),
                    inc=p2act)
                if r >= 0:
                    k.op("vector", lambda e, j=j, r=r: e.tensor_mul(
                        ET[j % 2][:], ET[j % 2][:],
                        cn["mm"][:, r * 512:(r + 1) * 512]),
                        waits=W(p2act), inc=p2dve)
                    evW = W(p2dve)
                else:
                    evW = W(p2act)
                expc.append(p2act.n)
                k.op("tensor", lambda e, j=j, g=g: e.matmul(
                    S[2][:], Vt[:, j * 128:(j + 1) * 128], ET[j % 2][:],
                    start=(j == 0), stop=(j == nkt - 1)),
                    waits=evW, inc=pvpe)
                pvc.append(pvpe.n)
            m = h * G + g
            k.op("scalar", lambda e, m=m: e.activation(
                ost[m % 2][:], S[2][:], AF.Copy, bias=0.0, scale=1.0),
                waits=W(pvpe) + [(sodma[m % 2], sodma[m % 2].n)], inc=oact)
            k.op("gpsimd", lambda e, h=h, g=g, m=m: e.dma_start(
                out=O_dram[h * 128:(h + 1) * 128, g * 512:(g + 1) * 512],
                in_=ost[m % 2][:]),
                waits=W(oact), inc=sodma[m % 2], inck=16)

    odW = [(x, x.n) for x in sodma]
    sxre = [k.sem() for _ in range(4)]
    sst = [k.sem() for _ in range(4)]
    solod = k.sem()
    prj = k.sem()
    for g in range(G):
        ob = g % 2
        for hh in range(8):
            k.op("sync", lambda e, hh=hh, g=g, ob=ob: e.dma_start(
                out=otile[ob][hh][:],
                in_=O_dram[hh * 128:(hh + 1) * 128, g * 512:(g + 1) * 512]),
                waits=(odW + [(prj, max(0, (g - 1) * 8))]) if hh == 0 else (),
                inc=solod, inck=16)
        for c in range(8):
            n = g * 8 + c
            b = n % 4
            k.op("sync", lambda e, c=c, g=g, b=b: e.dma_start(
                out=xre[b][:], in_=xT[c * 128:(c + 1) * 128, g * 512:(g + 1) * 512]),
                waits=[(st_dve, max(0, n - 3))], inc=sxre[b], inck=16)
            bank = S[3 + (c % 2)]
            for hh in range(8):
                k.op("tensor", lambda e, c=c, g=g, hh=hh, bank=bank, ob=ob: e.matmul(
                    bank[:], wout_sb[:, hh * 1024 + c * 128: hh * 1024 + (c + 1) * 128],
                    otile[ob][hh][:],
                    start=(hh == 0), stop=(hh == 7)),
                    waits=[(solod, 16 * 8 * (g + 1)), (s_ldw, s_ldw.n),
                           (st_dve, max(0, n - 1))] if hh == 0 else (),
                    inc=prj if hh == 7 else None)
            k.op("vector", lambda e, b=b, c=c, bank=bank: e.tensor_add(
                yout[b][:], bank[:], xre[b][:]),
                waits=[(prj, prj.n), (sxre[b], sxre[b].n), (sst[b], sst[b].n)],
                inc=st_dve)
            k.op("scalar", lambda e, c=c, g=g, b=b: e.dma_start(
                out=x1T[c * 128:(c + 1) * 128, g * 512:(g + 1) * 512], in_=yout[b][:]),
                waits=[(st_dve, st_dve.n)], inc=sst[b], inck=16)
    return



def build_l0_attn_program(T):
    nc = bass.Bass("TRN2", target_bir_lowering=False)
    xT = nc.dram_tensor("xT", [1024, T], F32, kind="ExternalInput").ap()
    wqkvT = nc.dram_tensor("wqkvT", [1024, 3072], F32, kind="ExternalInput").ap()
    woutT = nc.dram_tensor("woutT", [1024, 1024], F32, kind="ExternalInput").ap()
    gammacol = nc.dram_tensor("gammacol", [1024, 1], F32, kind="ExternalInput").ap()
    x1T = nc.dram_tensor("x1T", [1024, T], F32, kind="ExternalOutput").ap()
    ctx = contextlib.ExitStack()
    with ctx:
        E = ctx.enter_context
        k = K(nc, E)
        env = Env(nc, E, k)
        emit_l0_attn(env, T, xT, x1T, wqkvT, woutT, gammacol)
        k.final_barrier()
        block = E(nc.Block())
        k.finish(block)
    return nc


def run_l0_attn(x, w_qkv, w_out, gamma, cores=1):
    B, T, D = x.shape
    nc = build_l0_attn_program(T)
    consts = make_const_arrays()
    in_maps = []
    for b in range(B):
        m = dict(consts)
        m["xT"] = np.ascontiguousarray(x[b].T)
        m["wqkvT"] = np.ascontiguousarray(w_qkv.T)
        m["woutT"] = np.ascontiguousarray(w_out.T)
        m["gammacol"] = np.ascontiguousarray(gamma[:, None])
        in_maps.append(m)
    res = run_bass_kernel_spmd(nc, in_maps, core_ids=list(range(B)))
    return np.stack([r["x1T"].T for r in res.results], axis=0)


def sim_l0_attn(x, w_qkv, w_out, gamma):
    from concourse.bass_interp import CoreSim
    B, T, D = x.shape
    assert B == 1
    nc = build_l0_attn_program(T)
    nc.finalize()
    sim = CoreSim(nc)
    consts = make_const_arrays()
    feed = dict(consts)
    feed["xT"] = np.ascontiguousarray(x[0].T)
    feed["wqkvT"] = np.ascontiguousarray(w_qkv.T)
    feed["woutT"] = np.ascontiguousarray(w_out.T)
    feed["gammacol"] = np.ascontiguousarray(gamma[:, None])
    for name, arr in feed.items():
        v = sim.tensor(name)
        v[:] = arr.view(v.dtype) if arr.dtype == NP_BF16 else arr
    sim.simulate()
    return sim.tensor("x1T").T[None].copy()



CAPL = 640


def emit_l0_router(env: Env, T, xT, WrAp, xe_dram, idx_out, ptok_out, lrloc_out,
                   oneh_out, counts_out, capl=CAPL, lite=False):
    nc, E, k, cn = env.nc, env.E, env.k, env.cn
    S, PB = env.S, env.PB
    G = T // 512
    NT = T // 128

    wr_sb = E(nc.sbuf_tensor("f_wr", [128, 8 * 8], F32))
    cswr = E(nc.sbuf_tensor("f_cswr", [1, 8], F32))
    slotb = E(nc.sbuf_tensor("f_slotb", [1, 8], F32))
    offs = [E(nc.sbuf_tensor(f"f_offs{i}", [1, 8], F32)) for i in range(2)]
    pk_all = E(nc.sbuf_tensor("f_pk", [128, NT], F32))
    idx_all = E(nc.sbuf_tensor("f_idx", [128, NT], I32))
    lr_all = E(nc.sbuf_tensor("f_lr", [128, NT], F32))
    oh_all = E(nc.sbuf_tensor("f_oh", [128, NT * 8], F32))

    s_ldw = k.sem()
    rt_pe = k.sem()
    rt_dve = k.sem()
    rt_act = k.sem()
    rt_sc = k.sem()
    sze = k.sem()
    rt_st = k.sem()
    W = lambda *ss: [s.now() for s in ss]
    cw = (env.ld, env.ld.n)

    for c in range(8):
        k.op("sync", lambda e, c=c: e.dma_start(
            out=wr_sb[:, c * 8:(c + 1) * 8], in_=WrAp[c * 128:(c + 1) * 128, :]),
            inc=s_ldw, inck=16)
    for c in range(8):
        k.op("tensor", lambda e, c=c: e.matmul(
            S[5][0:1, 0:8], cn["ones_col_f"][:], wr_sb[:, c * 8:(c + 1) * 8],
            start=(c == 0), stop=(c == 7)),
            waits=[cw, (s_ldw, s_ldw.n)] if c == 0 else (),
            inc=rt_pe if c == 7 else None)
    k.op("vector", lambda e: e.tensor_copy(cswr[:], S[5][0:1, 0:8]),
         waits=W(rt_pe), inc=rt_dve)
    for ei in range(8):
        k.op("vector", lambda e, ei=ei: e.memset(slotb[0:1, ei:ei + 1], float(ei * capl)),
             inc=rt_dve)
    k.op("vector", lambda e: e.memset(offs[0][:], 0.0), inc=rt_dve)

    with contextlib.ExitStack() as rctx:
        RE = rctx.enter_context
        xgf = RE(nc.sbuf_tensor("f_xgf", [128, 8 * 512], F32))
        xbb = RE(nc.sbuf_tensor("f_xbb", [128, 8 * 512], BF16))
        negm = RE(nc.sbuf_tensor("f_negm", [1, 512], F32))
        lg = RE(nc.sbuf_tensor("f_lg", [128, 8], F32))
        mxc = RE(nc.sbuf_tensor("f_mxc", [128, 1], F32))
        nmxc = RE(nc.sbuf_tensor("f_nmxc", [128, 1], F32))
        sume = RE(nc.sbuf_tensor("f_sume", [128, 1], F32))
        escr = RE(nc.sbuf_tensor("f_escr", [128, 8], F32))
        scr8 = RE(nc.sbuf_tensor("f_scr8", [128, 8], F32))
        ptok = RE(nc.sbuf_tensor("f_ptok", [128, 1], F32))
        offA = RE(nc.sbuf_tensor("f_offA", [128, 1], F32))
        lrloc = RE(nc.sbuf_tensor("f_lrloc", [128, 1], F32))
        keepl = RE(nc.sbuf_tensor("f_keepl", [128, 1], F32))
        idxf = RE(nc.sbuf_tensor("f_idxf", [128, 1], F32))
        if not lite:
            xtok = [RE(nc.sbuf_tensor(f"f_xtok{i}", [128, 1024], BF16)) for i in range(2)]
            sqt = RE(nc.sbuf_tensor("f_sqt", [128, 1024], BF16))
            mcol = RE(nc.sbuf_tensor("f_mcol", [128, 1], F32))
            qcol = RE(nc.sbuf_tensor("f_qcol", [128, 1], F32))
            vcol = RE(nc.sbuf_tensor("f_vcol", [128, 1], F32))
            rstdc = RE(nc.sbuf_tensor("f_rstdc", [128, 1], F32))
            xn_tok = [RE(nc.sbuf_tensor(f"f_xn{i}", [128, 1024], BF16)) for i in range(2)]
            zt = RE(nc.sbuf_tensor("f_zt", [128, 1024], BF16))
            k.op("vector", lambda e: e.memset(zt[:], 0.0), inc=rt_dve)
            zW = (rt_dve, rt_dve.n)
            for i in range(8 * capl // 128):
                k.op("gpsimd", lambda e, i=i: e.dma_start(
                    out=xe_dram[i * 128:(i + 1) * 128, :], in_=zt[:]),
                    waits=[zW] if i == 0 else (), inc=sze, inck=16)

        s_ldx2 = k.sem()
        s_ldx3 = k.sem()
        for g in range(G):
            ldb2 = s_ldx2.n
            ldb3 = s_ldx3.n
            for c in range(8):
                k.op("gpsimd", lambda e, c=c, g=g: e.dma_start(
                    out=xbb[:, c * 512:(c + 1) * 512],
                    in_=xT[c * 128:(c + 1) * 128, g * 512:(g + 1) * 512]),
                    waits=[(rt_pe, rt_pe.n)] if (g > 0 and c == 0) else (),
                    inc=s_ldx2, inck=16)
                k.op("sync", lambda e, c=c, g=g: e.dma_start(
                    out=xgf[:, c * 512:(c + 1) * 512],
                    in_=xT[c * 128:(c + 1) * 128, g * 512:(g + 1) * 512]),
                    waits=[(rt_pe, rt_pe.n)] if (g > 0 and c == 0) else (),
                    inc=s_ldx3, inck=16)
            for c in range(8):
                k.op("tensor", lambda e, c=c: e.matmul(
                    S[5][0:1, 0:512], cn["ones_col"][:], xbb[:, c * 512:(c + 1) * 512],
                    start=(c == 0), stop=(c == 7)),
                    waits=[(s_ldx2, ldb2 + 128), (rt_dve, rt_dve.n)] if c == 0 else (),
                    inc=rt_pe if c == 7 else None)
            k.op("vector", lambda e: e.tensor_scalar(
                negm[:], S[5][0:1, 0:512], -1.0 / 1024, None, op0=ALU.mult),
                waits=W(rt_pe), inc=rt_dve)

            for ti in range(4):
                t = g * 4 + ti
                tb = t % 2
                if not lite:
                    for rnd in range(2):
                        for cc in range(4):
                            c = rnd * 4 + cc
                            k.op("tensor", lambda e, c=c, cc=cc, ti=ti: e.transpose(
                                PB[:, cc * 128:(cc + 1) * 128],
                                xbb[:, c * 512 + ti * 128: c * 512 + (ti + 1) * 128],
                                cn["ident_bf"][:]),
                                waits=W(rt_dve) if cc == 0 else (), inc=rt_pe)
                        k.op("vector", lambda e, rnd=rnd, tb=tb: e.tensor_copy(
                            xtok[tb][:, rnd * 512:(rnd + 1) * 512], PB[:, 0:512]),
                            waits=W(rt_pe), inc=rt_dve)
                    k.op("scalar", lambda e, tb=tb: e.activation(
                        sqt[:], xtok[tb][:], AF.Square, bias=0.0, scale=1.0),
                        waits=W(rt_dve), inc=rt_act)
                    k.op("vector", lambda e, tb=tb: e.reduce_sum(
                        mcol[:], xtok[tb][:], axis=AX.X), waits=W(rt_dve), inc=rt_dve)
                    k.op("vector", lambda e: e.reduce_sum(
                        qcol[:], sqt[:], axis=AX.X), waits=W(rt_act), inc=rt_dve)
                    k.op("vector", lambda e: e.tensor_scalar(
                        mcol[:], mcol[:], 1.0 / 1024, None, op0=ALU.mult),
                        waits=W(rt_dve), inc=rt_dve)
                    k.op("vector", lambda e: e.tensor_scalar(
                        vcol[:], mcol[:], mcol[:, 0:1], None, op0=ALU.mult),
                        waits=W(rt_dve), inc=rt_dve)
                    k.op("vector", lambda e: e.scalar_tensor_tensor(
                        vcol[:], qcol[:], 1.0 / 1024, vcol[:],
                        op0=ALU.mult, op1=ALU.subtract),
                        waits=W(rt_dve), inc=rt_dve)
                    k.op("vector", lambda e: e.tensor_scalar(
                        vcol[:], vcol[:], 1e-5, None, op0=ALU.add),
                        waits=W(rt_dve), inc=rt_dve)
                    k.op("scalar", lambda e: e.activation(
                        rstdc[:], vcol[:], AF.Ln, bias=0.0, scale=1.0),
                        waits=W(rt_dve), inc=rt_act)
                    k.op("scalar", lambda e: e.activation(
                        rstdc[:], rstdc[:], AF.Exp, bias=0.0, scale=-0.5),
                        waits=W(rt_act), inc=rt_act)
                    k.op("vector", lambda e, tb=tb: e.tensor_scalar(
                        xn_tok[tb][:], xtok[tb][:], mcol[:, 0:1], rstdc[:, 0:1],
                        op0=ALU.subtract, op1=ALU.mult),
                        waits=W(rt_act, rt_sc), inc=rt_dve)
                for c in range(8):
                    k.op("tensor", lambda e, c=c, ti=ti: e.matmul(
                        S[6][:, 0:8],
                        xgf[:, c * 512 + ti * 128: c * 512 + (ti + 1) * 128],
                        wr_sb[:, c * 8:(c + 1) * 8], start=(c == 0), stop=False),
                        waits=([(s_ldx3, ldb3 + 128)] + W(rt_dve)) if c == 0 else ())
                k.op("tensor", lambda e, ti=ti: e.matmul(
                    S[6][:, 0:8], negm[0:1, ti * 128:(ti + 1) * 128], cswr[:],
                    start=False, stop=True), inc=rt_pe)
                k.op("vector", lambda e: e.reduce_max(
                    mxc[:], S[6][:, 0:8], axis=AX.X), waits=W(rt_pe), inc=rt_dve)
                k.op("vector", lambda e, t=t: e.tensor_scalar(
                    oh_all[:, t * 8:(t + 1) * 8], S[6][:, 0:8], mxc[:, 0:1],
                    None, op0=ALU.is_ge),
                    waits=W(rt_dve), inc=rt_dve)
                if not lite:
                    k.op("vector", lambda e: e.tensor_scalar(
                        lg[:], S[6][:, 0:8], rstdc[:, 0:1], None, op0=ALU.mult),
                        waits=W(rt_dve, rt_act), inc=rt_dve)
                    k.op("vector", lambda e: e.reduce_max(
                        nmxc[:], lg[:], axis=AX.X, negate=True),
                        waits=W(rt_dve), inc=rt_dve)
                    k.op("scalar", lambda e: e.activation(
                        escr[:], lg[:], AF.Exp, bias=nmxc[:, 0:1], scale=1.0,
                        accum_out=sume[:, 0:1]), waits=W(rt_dve), inc=rt_act)
                    k.op("vector", lambda e: e.reciprocal(
                        ptok[:], sume[:]), waits=W(rt_act), inc=rt_dve)
                    k.op("vector", lambda e, t=t: e.tensor_copy(
                        pk_all[:, t:t + 1], ptok[:]), waits=W(rt_dve), inc=rt_dve)
                oz = (rt_dve, rt_dve.n)
                pairs = ((0, slotb), (1, None)) if not lite else ((0, slotb),)
                for bi, extra in pairs:
                    reg = bi * 16
                    k.op("tensor", lambda e, reg=reg, t=t: e.matmul(
                        S[4][:, reg:reg + 8], cn["strictL_f"][:],
                        oh_all[:, t * 8:(t + 1) * 8],
                        start=True, stop=False), waits=[oz])
                    k.op("tensor", lambda e, reg=reg, t=t, extra=extra: e.matmul(
                        S[4][:, reg:reg + 8], cn["ones_row_f"][:], offs[t % 2][:],
                        start=False, stop=(extra is None)),
                        inc=rt_pe if extra is None else None)
                    if extra is not None:
                        k.op("tensor", lambda e, reg=reg, extra=extra: e.matmul(
                            S[4][:, reg:reg + 8], cn["ones_row_f"][:], extra[:],
                            start=False, stop=True), inc=rt_pe)
                k.op("tensor", lambda e, t=t: e.matmul(
                    S[5][0:1, 504:512], cn["ones_col_f"][:],
                    oh_all[:, t * 8:(t + 1) * 8],
                    start=True, stop=True), waits=[oz], inc=rt_pe)
                k.op("vector", lambda e, t=t: e.tensor_add(
                    offs[(t + 1) % 2][:], offs[t % 2][:], S[5][0:1, 504:512]),
                    waits=W(rt_pe), inc=rt_dve)
                if not lite:
                    k.op("vector", lambda e, t=t: e.tensor_mul(
                        scr8[:], oh_all[:, t * 8:(t + 1) * 8], S[4][:, 0:8]),
                        waits=W(rt_pe), inc=rt_dve)
                    k.op("vector", lambda e: e.reduce_sum(offA[:], scr8[:], axis=AX.X),
                         waits=W(rt_dve), inc=rt_dve)
                    k.op("vector", lambda e, t=t: e.tensor_mul(
                        scr8[:], oh_all[:, t * 8:(t + 1) * 8], S[4][:, 16:24]),
                        waits=W(rt_dve), inc=rt_dve)
                    k.op("vector", lambda e, t=t: e.reduce_sum(
                        lr_all[:, t:t + 1], scr8[:], axis=AX.X),
                        waits=W(rt_dve), inc=rt_dve)
                    k.op("vector", lambda e, t=t: e.tensor_scalar(
                        keepl[:], lr_all[:, t:t + 1], float(capl), None, op0=ALU.is_lt),
                        waits=W(rt_dve), inc=rt_dve)
                    k.op("vector", lambda e: e.tensor_scalar(
                        idxf[:], keepl[:], -1e9, 1e9, op0=ALU.mult, op1=ALU.add),
                        waits=W(rt_dve), inc=rt_dve)
                    k.op("vector", lambda e: e.tensor_add(idxf[:], idxf[:], offA[:]),
                         waits=W(rt_dve), inc=rt_dve)
                    k.op("vector", lambda e, t=t: e.tensor_copy(
                        idx_all[:, t:t + 1], idxf[:]), waits=W(rt_dve), inc=rt_dve)
                    k.op("gpsimd", lambda e, t=t, tb=tb: e.indirect_dma_start(
                        xe_dram, IndirectOffsetOnAxis(idx_all[:, t:t + 1], 0),
                        xn_tok[tb][:], None,
                        bounds_check=8 * capl - 1, oob_is_err=False),
                        waits=[(rt_dve, rt_dve.n), (sze, sze.n),
                               (rt_sc, max(0, rt_sc.n - 64))], inc=rt_sc, inck=16)
        if counts_out is not None:
            k.op("scalar", lambda e: e.dma_start(
                out=counts_out, in_=offs[NT % 2][:]),
                waits=W(rt_dve), inc=rt_st, inck=16)
        if not lite:
            k.op("scalar", lambda e: e.dma_start(out=idx_out, in_=idx_all[:]),
                 waits=W(rt_dve), inc=rt_st, inck=16)
            k.op("scalar", lambda e: e.dma_start(out=ptok_out, in_=pk_all[:]),
                 waits=W(rt_dve), inc=rt_st, inck=16)
            k.op("scalar", lambda e: e.dma_start(out=lrloc_out, in_=lr_all[:]),
                 waits=W(rt_dve), inc=rt_st, inck=16)
            k.op("scalar", lambda e: e.dma_start(out=oneh_out, in_=oh_all[:]),
                 waits=W(rt_dve), inc=rt_st, inck=16)


def emit_l0_ffn_rest(env: Env, T, xT, x2T, W1Ap, W2Ap, xeAp, idxAp, ptokAp,
                     lrlocAp, onehAp, countsAp, coreltAp, cap_global, capl=CAPL):
    nc, E, k, cn = env.nc, env.E, env.k, env.cn
    S, PB = env.S, env.PB
    G = T // 512
    NT = T // 128
    ye_dram = nc.dram_tensor("ye_dram", [8 * capl, 1024], BF16, kind="Internal").ap()

    pk_all = E(nc.sbuf_tensor("f_pk", [128, NT], F32))
    idx_all = E(nc.sbuf_tensor("f_idx", [128, NT], I32))
    gidx_all = E(nc.sbuf_tensor("f_gidx", [128, NT], I32))
    lr_all = E(nc.sbuf_tensor("f_lr", [128, NT], F32))
    oh_all = E(nc.sbuf_tensor("f_oh", [128, NT * 8], F32))
    counts_sb = E(nc.sbuf_tensor("f_counts", [8, 8], F32))
    corelt_sb = E(nc.sbuf_tensor("f_corelt", [8, 1], F32))
    gbrow = E(nc.sbuf_tensor("f_gbrow", [1, 8], F32))
    gbb = E(nc.sbuf_tensor("f_gbb", [128, 8], F32))
    scr8 = E(nc.sbuf_tensor("f_scr8", [128, 8], F32))
    gbcol = E(nc.sbuf_tensor("f_gbcol", [128, 1], F32))
    keepc = E(nc.sbuf_tensor("f_keepc", [128, 1], F32))
    gif = E(nc.sbuf_tensor("f_gif", [128, 1], F32))

    s_ldm = k.sem()
    pr_pe = k.sem()
    pr_dve = k.sem()
    W = lambda *ss: [s.now() for s in ss]
    cw = (env.ld, env.ld.n)

    for (dst, src) in ((pk_all, ptokAp), (idx_all, idxAp), (lr_all, lrlocAp),
                       (oh_all, onehAp), (counts_sb, countsAp), (corelt_sb, coreltAp)):
        k.op("sync", lambda e, dst=dst, src=src: e.dma_start(out=dst[:], in_=src),
             inc=s_ldm, inck=16)
    k.op("tensor", lambda e: e.matmul(
        S[5][0:1, 0:8], corelt_sb[:], counts_sb[:], start=True, stop=True),
        waits=[cw, (s_ldm, s_ldm.n)], inc=pr_pe)
    k.op("vector", lambda e: e.tensor_copy(gbrow[:], S[5][0:1, 0:8]),
         waits=W(pr_pe), inc=pr_dve)
    k.op("tensor", lambda e: e.matmul(
        S[5][:, 16:24], cn["ones_row_f"][:], gbrow[:], start=True, stop=True),
        waits=W(pr_dve), inc=pr_pe)
    k.op("vector", lambda e: e.tensor_copy(gbb[:], S[5][:, 16:24]),
         waits=W(pr_pe), inc=pr_dve)
    for t in range(NT):
        k.op("vector", lambda e, t=t: e.tensor_mul(
            scr8[:], oh_all[:, t * 8:(t + 1) * 8], gbb[:]),
            waits=W(pr_dve), inc=pr_dve)
        k.op("vector", lambda e: e.reduce_sum(gbcol[:], scr8[:], axis=AX.X),
             waits=W(pr_dve), inc=pr_dve)
        k.op("vector", lambda e, t=t: e.tensor_add(
            gbcol[:], gbcol[:], lr_all[:, t:t + 1]), waits=W(pr_dve), inc=pr_dve)
        k.op("vector", lambda e: e.tensor_scalar(
            keepc[:], gbcol[:], float(cap_global), None, op0=ALU.is_lt),
            waits=W(pr_dve), inc=pr_dve)
        k.op("vector", lambda e, t=t: e.tensor_scalar(
            pk_all[:, t:t + 1], pk_all[:, t:t + 1], keepc[:, 0:1], None,
            op0=ALU.mult), waits=W(pr_dve), inc=pr_dve)
        k.op("vector", lambda e, t=t: e.tensor_scalar(
            gif[:], keepc[:], -1e9, 1e9, op0=ALU.mult, op1=ALU.add),
            waits=W(pr_dve), inc=pr_dve)
        k.op("vector", lambda e, t=t: e.scalar_tensor_tensor(
            gif[:], idx_all[:, t:t + 1], 1.0, gif[:], op0=ALU.mult, op1=ALU.add),
            waits=W(pr_dve), inc=pr_dve)
        k.op("vector", lambda e, t=t: e.tensor_copy(
            gidx_all[:, t:t + 1], gif[:]), waits=W(pr_dve), inc=pr_dve)
    prW = (pr_dve, pr_dve.n)

    NS = 32
    nrt = capl // 128
    with contextlib.ExitStack() as ectx:
        EE = ectx.enter_context
        w1b = [EE(nc.sbuf_tensor(f"f_w1b{i}", [128, 8 * 512], BF16)) for i in range(2)]
        w2h = EE(nc.sbuf_tensor("f_w2h", [128, NS * 512], BF16))
        xef = EE(nc.sbuf_tensor("f_xef", [128, 8 * capl], BF16))
        h_sb = EE(nc.sbuf_tensor("f_hsb", [128, NS * capl], BF16))
        yesb = [EE(nc.sbuf_tensor(f"f_yesb{i}", [128, 512], BF16)) for i in range(2)]
        s_w1 = k.sem()
        s_w2 = k.sem()
        s_xe = k.sem()
        fe_pe = k.sem()
        fe_act = k.sem()
        fe_st = k.sem()
        for ei in range(8):
            xeW = s_xe.n
            for c in range(8):
                k.op("sync", lambda e, c=c, ei=ei: e.dma_start_transpose(
                    xef[:, c * capl:(c + 1) * capl],
                    xeAp[ei * capl:(ei + 1) * capl, c * 128:(c + 1) * 128]),
                    waits=[(fe_pe, fe_pe.n)] if c == 0 else (), inc=s_xe, inck=16)
            for fb in range(8):
                dbl = fb % 2
                w1W = s_w1.n
                for c in range(8):
                    k.op("gpsimd", lambda e, c=c, ei=ei, fb=fb, dbl=dbl: e.dma_start(
                        out=w1b[dbl][:, c * 512:(c + 1) * 512],
                        in_=W1Ap[ei * 1024 + c * 128: ei * 1024 + (c + 1) * 128,
                                 fb * 512:(fb + 1) * 512]),
                        waits=[(fe_pe, fe_pe.n)] if c == 0 else (), inc=s_w1, inck=16)
                for sl in range(4):
                    sidx = fb * 4 + sl
                    pieces = [(S[6], 0, min(512, capl))]
                    if capl > 512:
                        pieces.append((S[5], 512, capl - 512))
                    for (bank, w0, w1w) in pieces:
                        for c in range(8):
                            k.op("tensor", lambda e, c=c, sl=sl, dbl=dbl, w0=w0,
                                 w1w=w1w, bank=bank:
                                 e.matmul(bank[:, 0:w1w],
                                          w1b[dbl][:, c * 512 + sl * 128:
                                                   c * 512 + (sl + 1) * 128],
                                          xef[:, c * capl + w0: c * capl + w0 + w1w],
                                          start=(c == 0), stop=(c == 7)),
                                 waits=[(s_w1, w1W + 128), (s_xe, xeW + 128),
                                        (fe_act, fe_act.n)] if c == 0 else (),
                                 inc=fe_pe if c == 7 else None)
                    k.op("scalar", lambda e, sidx=sidx: e.activation(
                        h_sb[:, sidx * capl: sidx * capl + min(512, capl)],
                        S[6][:, 0:min(512, capl)],
                        AF.Relu, bias=0.0, scale=1.0), waits=W(fe_pe), inc=fe_act)
                    if capl > 512:
                        k.op("scalar", lambda e, sidx=sidx: e.activation(
                            h_sb[:, sidx * capl + 512: (sidx + 1) * capl],
                            S[5][:, 0:capl - 512],
                            AF.Relu, bias=0.0, scale=1.0), inc=fe_act)
            for dh in range(2):
                w2W = s_w2.n
                for sidx in range(NS):
                    k.op("gpsimd", lambda e, sidx=sidx, ei=ei, dh=dh: e.dma_start(
                        out=w2h[:, sidx * 512:(sidx + 1) * 512],
                        in_=W2Ap[ei * 4096 + sidx * 128: ei * 4096 + (sidx + 1) * 128,
                                 dh * 512:(dh + 1) * 512]),
                        waits=[(fe_pe, fe_pe.n)] if sidx == 0 else (),
                        inc=s_w2, inck=16)
                for rt in range(nrt):
                    bank = S[rt % 2]
                    for sidx in range(NS):
                        k.op("tensor", lambda e, sidx=sidx, rt=rt, bank=bank: e.matmul(
                            bank[:], h_sb[:, sidx * capl + rt * 128:
                                          sidx * capl + (rt + 1) * 128],
                            w2h[:, sidx * 512:(sidx + 1) * 512],
                            start=(sidx == 0), stop=(sidx == NS - 1)),
                            waits=[(s_w2, w2W + 16 * NS), (fe_act, fe_act.n),
                                   (fe_st, fe_st.n)] if sidx == 0 else (),
                            inc=fe_pe if sidx == NS - 1 else None)
                    k.op("scalar", lambda e, rt=rt, bank=bank: e.activation(
                        yesb[rt % 2][:], bank[:], AF.Copy, bias=0.0, scale=1.0),
                        waits=W(fe_pe) + [(fe_st, fe_st.n)], inc=fe_act)
                    k.op("scalar", lambda e, rt=rt, ei=ei, dh=dh: e.dma_start(
                        out=ye_dram[ei * capl + rt * 128: ei * capl + (rt + 1) * 128,
                                    dh * 512:(dh + 1) * 512], in_=yesb[rt % 2][:]),
                        waits=W(fe_act), inc=fe_st, inck=16)

    with contextlib.ExitStack() as gctx:
        GE = gctx.enter_context
        ytile = [GE(nc.sbuf_tensor(f"f_yt{i}", [128, 1024], BF16)) for i in range(4)]
        xrg = [GE(nc.sbuf_tensor(f"f_xrg{i}", [128, 512], F32)) for i in range(4)]
        x2o = [GE(nc.sbuf_tensor(f"f_x2o{i}", [128, 512], F32)) for i in range(4)]
        ga = k.sem()
        gd = k.sem()
        gpe = k.sem()
        gst = [k.sem() for _ in range(4)]
        gxr = [k.sem() for _ in range(4)]
        yeW = (fe_st, fe_st.n)
        feW = [(fe_pe, fe_pe.n), (fe_act, fe_act.n)]
        k.op("gpsimd", lambda e: None, waits=[(s_w1, s_w1.n), (s_w2, s_w2.n)])
        prev_gpe = 0
        for g in range(G):
            for ti in range(4):
                t = g * 4 + ti
                tb = ti
                k.op("vector", lambda e, tb=tb: e.memset(ytile[tb][:], 0.0),
                     waits=[(gpe, prev_gpe), prW] + (feW if (g == 0) else []), inc=gd)
                k.op("gpsimd", lambda e, t=t, tb=tb: e.indirect_dma_start(
                    ytile[tb][:], None, ye_dram,
                    IndirectOffsetOnAxis(gidx_all[:, t:t + 1], 0),
                    bounds_check=8 * capl - 1, oob_is_err=False),
                    waits=[yeW, (gd, gd.n), (ga, ga.n)], inc=ga, inck=16)
                k.op("vector", lambda e, t=t, tb=tb: e.tensor_scalar(
                    ytile[tb][:], ytile[tb][:], pk_all[:, t:t + 1], None,
                    op0=ALU.mult), waits=[(ga, ga.n)], inc=gd)
            for c in range(8):
                n = g * 8 + c
                b = n % 4
                k.op("sync", lambda e, c=c, g=g, b=b: e.dma_start(
                    out=xrg[b][:], in_=xT[c * 128:(c + 1) * 128,
                                          g * 512:(g + 1) * 512]),
                    waits=[(gd, gd.n)] if n < 4 else [(gd, gd.n), (gst[b], gst[b].n)],
                    inc=gxr[b], inck=16)
                for ti in range(4):
                    k.op("tensor", lambda e, c=c, ti=ti: e.transpose(
                        PB[:, ti * 128:(ti + 1) * 128],
                        ytile[ti][:, c * 128:(c + 1) * 128], cn["ident_bf"][:]),
                        waits=[(gd, gd.n), (gpe, gpe.n)] if ti == 0 else (), inc=gpe)
                k.op("vector", lambda e, b=b: e.tensor_add(
                    x2o[b][:], PB[:, 0:512], xrg[b][:]),
                    waits=[(gpe, gpe.n), (gxr[b], gxr[b].n)], inc=gd)
                k.op("scalar", lambda e, c=c, g=g, b=b: e.dma_start(
                    out=x2T[c * 128:(c + 1) * 128, g * 512:(g + 1) * 512],
                    in_=x2o[b][:]), waits=[(gd, gd.n)], inc=gst[b], inck=16)
            prev_gpe = gpe.n

def build_l0_router_program(T, capl=CAPL, lite=False):
    nc = bass.Bass("TRN2", target_bir_lowering=False)
    xT = nc.dram_tensor("xT", [1024, T], F32, kind="ExternalInput").ap()
    Wr = nc.dram_tensor("Wr", [1024, 8], F32, kind="ExternalInput").ap()
    counts_out = nc.dram_tensor("counts", [1, 8], F32, kind="ExternalOutput").ap()
    NT = T // 128
    if lite:
        xe = idx = ptok = lrloc = oneh = None
    else:
        xe = nc.dram_tensor("xe", [8 * capl, 1024], BF16, kind="ExternalOutput").ap()
        idx = nc.dram_tensor("idx", [128, NT], I32, kind="ExternalOutput").ap()
        ptok = nc.dram_tensor("ptok", [128, NT], F32, kind="ExternalOutput").ap()
        lrloc = nc.dram_tensor("lrloc", [128, NT], F32, kind="ExternalOutput").ap()
        oneh = nc.dram_tensor("oneh", [128, NT * 8], F32, kind="ExternalOutput").ap()
    ctx = contextlib.ExitStack()
    with ctx:
        E = ctx.enter_context
        k = K(nc, E)
        env = Env(nc, E, k)
        emit_l0_router(env, T, xT, Wr, xe, idx, ptok, lrloc, oneh, counts_out,
                       capl=capl, lite=lite)
        k.final_barrier()
        block = E(nc.Block())
        k.finish(block)
    return nc


def build_l0_rest_program(T, cap_global, capl=CAPL):
    nc = bass.Bass("TRN2", target_bir_lowering=False)
    NT = T // 128
    xT = nc.dram_tensor("xT", [1024, T], F32, kind="ExternalInput").ap()
    W1 = nc.dram_tensor("W1", [8 * 1024, 4096], F32, kind="ExternalInput").ap()
    W2 = nc.dram_tensor("W2", [8 * 4096, 1024], F32, kind="ExternalInput").ap()
    xe = nc.dram_tensor("xe", [8 * capl, 1024], BF16, kind="ExternalInput").ap()
    idx = nc.dram_tensor("idx", [128, NT], I32, kind="ExternalInput").ap()
    ptok = nc.dram_tensor("ptok", [128, NT], F32, kind="ExternalInput").ap()
    lrloc = nc.dram_tensor("lrloc", [128, NT], F32, kind="ExternalInput").ap()
    oneh = nc.dram_tensor("oneh", [128, NT * 8], F32, kind="ExternalInput").ap()
    countsA = nc.dram_tensor("countsA", [8, 8], F32, kind="ExternalInput").ap()
    corelt = nc.dram_tensor("corelt", [8, 1], F32, kind="ExternalInput").ap()
    x2T = nc.dram_tensor("x2T", [1024, T], F32, kind="ExternalOutput").ap()
    ctx = contextlib.ExitStack()
    with ctx:
        E = ctx.enter_context
        k = K(nc, E)
        env = Env(nc, E, k)
        emit_l0_ffn_rest(env, T, xT, x2T, W1, W2, xe, idx, ptok, lrloc, oneh,
                         countsA, corelt, cap_global, capl=capl)
        k.final_barrier()
        block = E(nc.Block())
        k.finish(block)
    return nc


def run_l0_ffn(x, Wr, W1, W2, capl=CAPL):
    B, T, D = x.shape
    cap = -(-B * T // 8)
    consts = make_const_arrays()
    nc1 = build_l0_router_program(T, capl=capl)
    in_maps = []
    for b in range(B):
        m = dict(consts)
        m["xT"] = np.ascontiguousarray(x[b].T)
        m["Wr"] = np.ascontiguousarray(Wr)
        in_maps.append(m)
    res1 = run_bass_kernel_spmd(nc1, in_maps, core_ids=list(range(B)))
    counts_all = np.zeros((8, 8), np.float32)
    for b in range(B):
        counts_all[b] = res1.results[b]["counts"][0]
    nc2 = build_l0_rest_program(T, cap, capl=capl)
    in_maps = []
    for b in range(B):
        r = res1.results[b]
        m = dict(consts)
        m["xT"] = np.ascontiguousarray(x[b].T)
        m["W1"] = np.ascontiguousarray(W1.reshape(8 * 1024, 4096))
        m["W2"] = np.ascontiguousarray(W2.reshape(8 * 4096, 1024))
        m["xe"] = r["xe"]
        m["idx"] = r["idx"]
        m["ptok"] = r["ptok"]
        m["lrloc"] = r["lrloc"]
        m["oneh"] = r["oneh"]
        m["countsA"] = counts_all
        m["corelt"] = (np.arange(8) < b).astype(np.float32)[:, None]
        in_maps.append(m)
    res2 = run_bass_kernel_spmd(nc2, in_maps, core_ids=list(range(B)))
    return np.stack([r["x2T"].T for r in res2.results], axis=0)


def sim_l0_ffn(x, Wr, W1, W2, capl=CAPL):
    from concourse.bass_interp import CoreSim
    B, T, D = x.shape
    assert B == 1
    cap = -(-T // 8)
    consts = make_const_arrays()
    nc1 = build_l0_router_program(T, capl=capl)
    nc1.finalize()
    sim = CoreSim(nc1)
    feed = dict(consts)
    feed["xT"] = np.ascontiguousarray(x[0].T)
    feed["Wr"] = np.ascontiguousarray(Wr)
    for name, arr in feed.items():
        v = sim.tensor(name)
        v[:] = arr.view(v.dtype) if arr.dtype == NP_BF16 else arr
    sim.simulate()
    outs1 = {n: sim.tensor(n).copy() for n in ("xe", "idx", "ptok", "lrloc", "oneh", "counts")}
    counts_all = np.zeros((8, 8), np.float32)
    counts_all[0] = outs1["counts"][0]
    nc2 = build_l0_rest_program(T, cap, capl=capl)
    nc2.finalize()
    sim2 = CoreSim(nc2)
    feed = dict(consts)
    feed["xT"] = np.ascontiguousarray(x[0].T)
    feed["W1"] = np.ascontiguousarray(W1.reshape(8 * 1024, 4096))
    feed["W2"] = np.ascontiguousarray(W2.reshape(8 * 4096, 1024))
    for n in ("xe", "idx", "ptok", "lrloc", "oneh"):
        feed[n] = outs1[n]
    feed["countsA"] = counts_all
    feed["corelt"] = np.zeros((8, 1), np.float32)
    for name, arr in feed.items():
        v = sim2.tensor(name)
        v[:] = arr.view(v.dtype) if arr.dtype in (NP_BF16,) else arr
    sim2.simulate()
    return sim2.tensor("x2T").T[None].copy(), outs1["counts"]



def emit_l1_attn(env: Env, T, xT, x3T, nwcol, wqT, wkT, wvT, wfT, gncol, woT):
    nc, E, k, cn = env.nc, env.E, env.k, env.cn
    S, PB = env.S, env.PB
    G = T // 512
    NCH = T // 128
    H, DKH, M = 4, 256, 256
    o_dram = nc.dram_tensor("o1_dram", [1024, T], BF16, kind="Internal").ap()

    W = lambda *ss: [s.now() for s in ss]
    cw = (env.ld, env.ld.n)

    nw_f = E(nc.sbuf_tensor("a1_nwf", [128, 8], F32))
    nw_b = E(nc.sbuf_tensor("a1_nwb", [128, 8], BF16))
    gn_f = E(nc.sbuf_tensor("a1_gnf", [128, 8], F32))
    xh = E(nc.sbuf_tensor("a1_xh", [128, 8 * T], BF16))
    negmr = E(nc.sbuf_tensor("a1_negmr", [1, T], BF16))
    wo_sb = E(nc.sbuf_tensor("a1_wo", [128, 8 * 1024], BF16))

    s_ldg = k.sem()
    s_ldx = k.sem()
    s_ldw = k.sem()
    s_ldwo = k.sem()
    ln_pe = k.sem()
    ln_dve = k.sem()
    ln_act = k.sem()
    ln_done = k.sem()

    for c in range(8):
        k.op("sync", lambda e, c=c: e.dma_start(
            out=nw_f[:, c:c + 1], in_=nwcol[c * 128:(c + 1) * 128, 0:1]),
            inc=s_ldg, inck=16)
        k.op("sync", lambda e, c=c: e.dma_start(
            out=gn_f[:, c:c + 1], in_=gncol[c * 128:(c + 1) * 128, 0:1]),
            inc=s_ldg, inck=16)
    k.op("vector", lambda e: e.tensor_copy(nw_b[:], nw_f[:]),
         waits=W(s_ldg) + [cw], inc=ln_dve)
    for c in range(8):
        k.op("gpsimd", lambda e, c=c: e.dma_start(
            out=wo_sb[:, c * 1024:(c + 1) * 1024], in_=woT[c * 128:(c + 1) * 128, :]),
            inc=s_ldwo, inck=16)

    with contextlib.ExitStack() as lctx:
        LE = lctx.enter_context
        xbb = LE(nc.sbuf_tensor("a1_xbb", [128, 8 * 512], BF16))
        sq = [LE(nc.sbuf_tensor(f"a1_sq{i}", [128, 512], BF16)) for i in range(2)]
        m_row = LE(nc.sbuf_tensor("a1_mrow", [1, 512], F32))
        q_row = LE(nc.sbuf_tensor("a1_qrow", [1, 512], F32))
        v_row = LE(nc.sbuf_tensor("a1_vrow", [1, 512], F32))
        a_row = LE(nc.sbuf_tensor("a1_arow", [1, 512], F32))
        r1_row = LE(nc.sbuf_tensor("a1_r1", [1, 512], F32))
        m2_row = LE(nc.sbuf_tensor("a1_m2", [1, 512], F32))
        rs_row = LE(nc.sbuf_tensor("a1_rs", [1, 512], F32))
        rs_b = LE(nc.sbuf_tensor("a1_rsb", [1, 512], BF16))
        for g in range(G):
            ldb = s_ldx.n
            for c in range(8):
                k.op("gpsimd", lambda e, c=c, g=g: e.dma_start(
                    out=xbb[:, c * 512:(c + 1) * 512],
                    in_=xT[c * 128:(c + 1) * 128, g * 512:(g + 1) * 512]),
                    waits=[(ln_done, ln_done.n)] if (g > 0 and c == 0) else (),
                    inc=s_ldx, inck=16)
            for c in range(8):
                k.op("tensor", lambda e, c=c: e.matmul(
                    S[0][0:1, 0:512], cn["ones_col"][:], xbb[:, c * 512:(c + 1) * 512],
                    start=(c == 0), stop=(c == 7)),
                    waits=([(s_ldx, ldb + 128)] + ([cw] if g == 0 else [])) if c == 0 else (),
                    inc=ln_pe if c == 7 else None)
            for c in range(8):
                k.op("scalar", lambda e, c=c: e.activation(
                    sq[c % 2][:], xbb[:, c * 512:(c + 1) * 512], AF.Square,
                    bias=0.0, scale=1.0),
                    waits=[(s_ldx, ldb + 128)] + (W(ln_pe) if c >= 2 else []),
                    inc=ln_act)
                k.op("tensor", lambda e, c=c: e.matmul(
                    S[1][0:1, 0:512], cn["ones_col"][:], sq[c % 2][:],
                    start=(c == 0), stop=(c == 7)),
                    waits=W(ln_act), inc=ln_pe)
            k.op("vector", lambda e: e.tensor_scalar(
                m_row[:], S[0][0:1, 0:512], 1.0 / 1024, None, op0=ALU.mult),
                waits=W(ln_pe), inc=ln_dve)
            k.op("vector", lambda e: e.tensor_scalar(
                q_row[:], S[1][0:1, 0:512], 1.0 / 1024, None, op0=ALU.mult),
                inc=ln_dve)
            k.op("vector", lambda e: e.tensor_mul(v_row[:], m_row[:], m_row[:]),
                 waits=W(ln_dve), inc=ln_dve)
            k.op("vector", lambda e: e.tensor_sub(v_row[:], q_row[:], v_row[:]),
                 waits=W(ln_dve), inc=ln_dve)
            k.op("vector", lambda e: e.tensor_scalar(
                a_row[:], v_row[:], 1e-5, None, op0=ALU.add),
                waits=W(ln_dve), inc=ln_dve)
            k.op("scalar", lambda e: e.activation(r1_row[:], a_row[:], AF.Ln,
                                                  bias=0.0, scale=1.0),
                 waits=W(ln_dve), inc=ln_act)
            k.op("scalar", lambda e: e.activation(rs_row[:], r1_row[:], AF.Exp,
                                                  bias=0.0, scale=-0.5),
                 waits=W(ln_act), inc=ln_act)
            k.op("scalar", lambda e: e.activation(m2_row[:], r1_row[:], AF.Exp,
                                                  bias=0.0, scale=-1.0),
                 waits=W(ln_act), inc=ln_act)
            k.op("vector", lambda e: e.tensor_mul(m2_row[:], m2_row[:], v_row[:]),
                 waits=W(ln_act), inc=ln_dve)
            k.op("vector", lambda e: e.tensor_scalar(
                m2_row[:], m2_row[:], 1e-5, None, op0=ALU.add),
                waits=W(ln_dve), inc=ln_dve)
            k.op("scalar", lambda e: e.activation(m2_row[:], m2_row[:], AF.Ln,
                                                  bias=0.0, scale=1.0),
                 waits=W(ln_dve), inc=ln_act)
            k.op("scalar", lambda e: e.activation(m2_row[:], m2_row[:], AF.Exp,
                                                  bias=0.0, scale=-0.5),
                 waits=W(ln_act), inc=ln_act)
            k.op("vector", lambda e: e.tensor_mul(rs_row[:], rs_row[:], m2_row[:]),
                 waits=W(ln_act), inc=ln_dve)
            k.op("vector", lambda e: e.tensor_copy(rs_b[:], rs_row[:]),
                 waits=W(ln_dve), inc=ln_dve)
            k.op("vector", lambda e, g=g: e.tensor_mul(
                negmr[0:1, g * 512:(g + 1) * 512], m_row[:], rs_row[:]),
                waits=W(ln_dve), inc=ln_dve)
            k.op("vector", lambda e, g=g: e.tensor_scalar(
                negmr[0:1, g * 512:(g + 1) * 512],
                negmr[0:1, g * 512:(g + 1) * 512], -1.0, None, op0=ALU.mult),
                waits=W(ln_dve), inc=ln_dve)
            k.op("tensor", lambda e: e.matmul(
                S[2][:], cn["ones_row"][:], rs_b[:], start=True, stop=True),
                waits=W(ln_dve) + [(ln_done, ln_done.n)], inc=ln_pe)
            for c in range(8):
                k.op("vector", lambda e, c=c, g=g: e.scalar_tensor_tensor(
                    xh[:, c * T + g * 512: c * T + (g + 1) * 512],
                    xbb[:, c * 512:(c + 1) * 512], nw_f[:, c:c + 1], S[2][:],
                    op0=ALU.mult, op1=ALU.mult),
                    waits=W(ln_pe), inc=(ln_done if c == 7 else ln_dve))
    lnw = (ln_done, G)

    hctx = contextlib.ExitStack()
    E = hctx.enter_context
    wq_h = E(nc.sbuf_tensor("a1_wq", [128, 8 * 256], BF16))
    wk_h = E(nc.sbuf_tensor("a1_wk", [128, 8 * 256], BF16))
    wv_h = E(nc.sbuf_tensor("a1_wv", [128, 8 * 256], BF16))
    wf_h = E(nc.sbuf_tensor("a1_wf", [128, 8 * 256], BF16))
    csq = E(nc.sbuf_tensor("a1_csq", [1, 256], BF16))
    csk = E(nc.sbuf_tensor("a1_csk", [1, 256], BF16))
    csv = E(nc.sbuf_tensor("a1_csv", [1, 256], BF16))
    csf = E(nc.sbuf_tensor("a1_csf", [1, 256], BF16))
    Qf = E(nc.sbuf_tensor("a1_Qf", [128, 2 * T], BF16))
    Kf = E(nc.sbuf_tensor("a1_Kf", [128, 2 * T], BF16))
    Vt = E(nc.sbuf_tensor("a1_Vt", [128, 2 * T], BF16))
    S1b = [E(nc.sbuf_tensor(f"a1_S1b{d}", [128, 256], BF16)) for d in range(2)]
    S2b = [E(nc.sbuf_tensor(f"a1_S2b{m}", [128, 256], BF16)) for m in range(2)]
    t1 = E(nc.sbuf_tensor("a1_t1", [128, 256], F32))
    za = E(nc.sbuf_tensor("a1_za", [128, 256], F32))
    eb = E(nc.sbuf_tensor("a1_eb", [128, 256], F32))
    ea = E(nc.sbuf_tensor("a1_ea", [128, 256], F32))
    en = E(nc.sbuf_tensor("a1_en", [128, 256], F32))
    eR = E(nc.sbuf_tensor("a1_eR", [128, 256], F32))
    sg = E(nc.sbuf_tensor("a1_sg", [128, 256], F32))
    Dt = E(nc.sbuf_tensor("a1_D", [128, 256], BF16))
    D2t = E(nc.sbuf_tensor("a1_D2", [128, 256], BF16))
    UT = E(nc.sbuf_tensor("a1_UT", [128, 128], BF16))
    G2m = E(nc.sbuf_tensor("a1_G2m", [128, 128], BF16))
    zf = E(nc.sbuf_tensor("a1_zf", [128, 256], F32))
    nmx = E(nc.sbuf_tensor("a1_nmx", [128, 1], F32))
    esf = E(nc.sbuf_tensor("a1_esf", [128, 256], F32))
    lsm = E(nc.sbuf_tensor("a1_lsm", [128, 1], F32))
    invl = E(nc.sbuf_tensor("a1_invl", [128, 1], F32))
    ptl = E(nc.sbuf_tensor("a1_ptl", [128, 256], BF16))
    ptT = E(nc.sbuf_tensor("a1_ptT", [128, 256], BF16))
    dT = E(nc.sbuf_tensor("a1_dT", [128, 256], BF16))
    kT = E(nc.sbuf_tensor("a1_kT", [128, 256], BF16))
    eArow = E(nc.sbuf_tensor("a1_eArow", [1, 256], BF16))
    eAT = E(nc.sbuf_tensor("a1_eAT", [128, 2], F32))
    s1scr = E(nc.sbuf_tensor("a1_s1scr", [128, 256], F32))
    ost = [E(nc.sbuf_tensor(f"a1_ost{i}", [128, 256], BF16)) for i in range(2)]

    hw_sem = k.sem()
    pj = k.sem()
    sc_pe = k.sem()
    sc_dve = k.sem()
    sc_act = k.sem()
    so_dma = [k.sem(), k.sem()]
    v_ev = k.sem()

    for h in range(H):
        for (wt, src) in ((wq_h, wqT), (wk_h, wkT), (wv_h, wvT), (wf_h, wfT)):
            for c in range(8):
                k.op("gpsimd", lambda e, c=c, wt=wt, src=src, h=h: e.dma_start(
                    out=wt[:, c * 256:(c + 1) * 256],
                    in_=src[c * 128:(c + 1) * 128, h * 256:(h + 1) * 256]),
                    waits=((W(pj, v_ev, sc_pe) if h > 0 else [(s_ldx, s_ldx.n), lnw])
                           if (c == 0 and wt is wq_h) else ()),
                    inc=s_ldw, inck=16)
        wW = (s_ldw, s_ldw.n)
        for wi, (wt, cs) in enumerate(((wq_h, csq), (wk_h, csk), (wv_h, csv), (wf_h, csf))):
            bank = S[3 + (wi % 2)]
            for c in range(8):
                k.op("tensor", lambda e, c=c, wt=wt, bank=bank: e.matmul(
                    bank[0:1, 0:256], nw_b[:, c:c + 1], wt[:, c * 256:(c + 1) * 256],
                    start=(c == 0), stop=(c == 7)),
                    waits=[wW, lnw] + W(sc_act, sc_dve) if c == 0 else (),
                    inc=sc_pe if c == 7 else None)
            k.op("scalar", lambda e, cs=cs, bank=bank: e.activation(
                cs[:], bank[0:1, 0:256], AF.Copy, bias=0.0, scale=1.0),
                waits=W(sc_pe), inc=sc_act)
        for g in range(G):
            for (dst, wt, cs) in ((Qf, wq_h, csq), (Kf, wk_h, csk)):
                for d in range(2):
                    bank = S[4]
                    for c in range(8):
                        k.op("tensor", lambda e, c=c, wt=wt, g=g, d=d, bank=bank: e.matmul(
                            bank[:], wt[:, c * 256 + d * 128: c * 256 + d * 128 + 128],
                            xh[:, c * T + g * 512: c * T + (g + 1) * 512],
                            start=(c == 0), stop=False),
                            waits=W(pj, sc_act) if c == 0 else ())
                    k.op("tensor", lambda e, cs=cs, g=g, d=d, bank=bank: e.matmul(
                        bank[:], cs[0:1, d * 128:(d + 1) * 128],
                        negmr[0:1, g * 512:(g + 1) * 512],
                        start=False, stop=True), inc=sc_pe)
                    k.op("scalar", lambda e, dst=dst, g=g, d=d, bank=bank: e.activation(
                        dst[:, d * T + g * 512: d * T + (g + 1) * 512], bank[:],
                        AF.Silu, bias=0.0, scale=1.0),
                        waits=W(sc_pe), inc=pj)
        for t in range(NCH):
            vb = S[4] if t % 2 == 0 else S[3]
            for c in range(8):
                k.op("tensor", lambda e, c=c, t=t, vb=vb: e.matmul(
                    vb[:, 0:256], xh[:, c * T + t * 128: c * T + (t + 1) * 128],
                    wv_h[:, c * 256:(c + 1) * 256], start=(c == 0), stop=False),
                    waits=[(v_ev, v_ev.n - min(v_ev.n, 1))] + (W(pj, sc_act) if t <= 1 else [])
                    if c == 0 else ())
            k.op("tensor", lambda e, t=t, vb=vb: e.matmul(
                vb[:, 0:256], negmr[0:1, t * 128:(t + 1) * 128], csv[:],
                start=False, stop=True), inc=sc_pe)
            k.op("scalar", lambda e, t=t, vb=vb: e.activation(
                Vt[:, t * 256:(t + 1) * 256], vb[:, 0:256], AF.Silu,
                bias=0.0, scale=1.0), waits=W(sc_pe), inc=v_ev)
        for d in range(2):
            k.op("vector", lambda e, d=d: e.memset(S1b[d][:], 0.0),
                 waits=W(sc_pe), inc=sc_dve)
            k.op("vector", lambda e, d=d: e.memset(S2b[d][:], 0.0), inc=sc_dve)
        for ci in range(NCH):
            fb = S[3]
            for c in range(8):
                k.op("tensor", lambda e, c=c, ci=ci, fb=fb: e.matmul(
                    fb[:, 0:256], xh[:, c * T + ci * 128: c * T + (ci + 1) * 128],
                    wf_h[:, c * 256:(c + 1) * 256], start=(c == 0), stop=False),
                    waits=W(sc_act, sc_dve) + [(v_ev, v_ev.n)] if c == 0 else ())
            k.op("tensor", lambda e, ci=ci, fb=fb: e.matmul(
                fb[:, 0:256], negmr[0:1, ci * 128:(ci + 1) * 128], csf[:],
                start=False, stop=True), inc=sc_pe)
            k.op("scalar", lambda e, fb=fb: e.activation(
                za[:], fb[:, 0:256], AF.Abs, bias=0.0, scale=1.0),
                waits=W(sc_pe), inc=sc_act)
            k.op("scalar", lambda e: e.activation(
                eb[:], za[:], AF.Exp, bias=0.0, scale=-1.0),
                waits=W(sc_act), inc=sc_act)
            k.op("vector", lambda e: e.tensor_scalar(
                eb[:], eb[:], 1.0, None, op0=ALU.add), waits=W(sc_act), inc=sc_dve)
            k.op("scalar", lambda e: e.activation(
                eb[:], eb[:], AF.Ln, bias=0.0, scale=1.0),
                waits=W(sc_dve), inc=sc_act)
            k.op("vector", lambda e, fb=fb: e.tensor_scalar(
                t1[:], fb[:, 0:256], 0.0, None, op0=ALU.min),
                waits=W(sc_act), inc=sc_dve)
            k.op("vector", lambda e: e.tensor_sub(t1[:], t1[:], eb[:]),
                 waits=W(sc_dve, sc_act), inc=sc_dve)
            k.op("tensor", lambda e: e.matmul(
                S[5][:, 0:256], cn["tri_incl_f"][:], t1[:], start=True, stop=True),
                waits=W(sc_dve, sc_act), inc=sc_pe)
            k.op("tensor", lambda e: e.matmul(
                S[6][:, 0:256], cn["tri_revx_f"][:], t1[:], start=True, stop=True),
                inc=sc_pe)
            k.op("tensor", lambda e: e.matmul(
                S[4][0:1, 0:256], cn["ones_col_f"][:], t1[:], start=True, stop=True),
                waits=W(sc_dve) + [(v_ev, v_ev.n)], inc=sc_pe)
            for d in range(2):
                k.op("tensor", lambda e, d=d: e.matmul(
                    S[4][:, 256 + d:257 + d], t1[:, d * 128:(d + 1) * 128],
                    cn["ones_col_f"][:], start=True, stop=True), inc=sc_pe)
            k.op("scalar", lambda e: e.activation(
                ea[:], S[5][:, 0:256], AF.Exp, bias=0.0, scale=1.0 / 16),
                waits=W(sc_pe), inc=sc_act)
            k.op("scalar", lambda e: e.activation(
                en[:], S[5][:, 0:256], AF.Exp, bias=0.0, scale=-1.0 / 16),
                inc=sc_act)
            k.op("scalar", lambda e: e.activation(
                eR[:], S[6][:, 0:256], AF.Exp, bias=0.0, scale=1.0 / 16),
                inc=sc_act)
            k.op("scalar", lambda e: e.activation(
                eArow[:], S[4][0:1, 0:256], AF.Exp, bias=0.0, scale=1.0 / 16),
                inc=sc_act)
            for d in range(2):
                k.op("scalar", lambda e, d=d: e.activation(
                    eAT[:, d:d + 1], S[4][:, 256 + d:257 + d], AF.Exp,
                    bias=0.0, scale=1.0 / 16), inc=sc_act)
            k.op("scalar", lambda e: e.activation(
                sg[:], t1[:], AF.Exp, bias=0.0, scale=1.0 / 16),
                waits=W(sc_dve), inc=sc_act)
            k.op("vector", lambda e: e.tensor_scalar(
                sg[:], sg[:], -1.0, 1.0, op0=ALU.mult, op1=ALU.add),
                waits=W(sc_act), inc=sc_dve)
            k.op("vector", lambda e: e.tensor_mul(Dt[:], sg[:], en[:]),
                 waits=W(sc_dve, sc_act), inc=sc_dve)
            k.op("vector", lambda e: e.tensor_mul(D2t[:], sg[:], eR[:]),
                 waits=W(sc_dve), inc=sc_dve)
            k.op("tensor", lambda e: e.matmul(
                S[4][:, 0:256], cn["ones_row"][:], eArow[:], start=True, stop=True),
                waits=W(sc_act, sc_dve), inc=sc_pe)
            for d in range(2):
                k.op("tensor", lambda e, d=d, ci=ci: e.matmul(
                    S[6][:, 256:384], Kf[:, d * T + ci * 128: d * T + (ci + 1) * 128],
                    Qf[:, d * T + ci * 128: d * T + (ci + 1) * 128],
                    start=(d == 0), stop=(d == 1)),
                    waits=W(sc_act) if d == 0 else (), inc=sc_pe if d == 1 else None)
            k.op("vector", lambda e: e.tensor_mul(
                UT[:], S[6][:, 256:384], cn["mm"][:, 0:128]),
                waits=W(sc_pe), inc=sc_dve)
            for d in range(2):
                k.op("tensor", lambda e, d=d, ci=ci: e.matmul(
                    S[5][:, 256:512], Qf[:, d * T + ci * 128: d * T + (ci + 1) * 128],
                    S1b[d][:], start=(d == 0), stop=False),
                    waits=W(sc_act, sc_dve) if d == 0 else ())
            k.op("tensor", lambda e: e.matmul(
                S[5][:, 256:512], UT[:], Dt[:], start=False, stop=True),
                inc=sc_pe)
            k.op("vector", lambda e: e.tensor_mul(zf[:], ea[:], S[5][:, 256:512]),
                 waits=W(sc_pe), inc=sc_dve)
            k.op("vector", lambda e: e.reduce_max(
                nmx[:], zf[:], axis=AX.X, negate=True), waits=W(sc_dve), inc=sc_dve)
            k.op("scalar", lambda e: e.activation(
                esf[:], zf[:], AF.Exp, bias=nmx[:, 0:1], scale=1.0,
                accum_out=lsm[:, 0:1]), waits=W(sc_dve), inc=sc_act)
            k.op("vector", lambda e: e.reciprocal(invl[:], lsm[:]),
                 waits=W(sc_act), inc=sc_dve)
            k.op("vector", lambda e: e.tensor_mul(ptl[:], esf[:], ea[:]),
                 waits=W(sc_dve), inc=sc_dve)
            k.op("vector", lambda e: e.tensor_scalar(
                ptl[:], ptl[:], invl[:, 0:1], None, op0=ALU.mult),
                waits=W(sc_dve), inc=sc_dve)
            for (dst, srcf, isk) in ((ptT, ptl, False), (dT, Dt, False), (kT, None, True)):
                for d in range(2):
                    if isk:
                        k.op("tensor", lambda e, d=d, ci=ci: e.transpose(
                            PB[:, d * 128:(d + 1) * 128],
                            Kf[:, d * T + ci * 128: d * T + (ci + 1) * 128],
                            cn["ident_bf"][:]),
                            waits=W(sc_dve) if d == 0 else (), inc=sc_pe)
                    else:
                        k.op("tensor", lambda e, d=d, srcf=srcf: e.transpose(
                            PB[:, d * 128:(d + 1) * 128],
                            srcf[:, d * 128:(d + 1) * 128], cn["ident_bf"][:]),
                            waits=W(sc_dve) if d == 0 else (), inc=sc_pe)
                k.op("vector", lambda e, dst=dst: e.tensor_copy(dst[:], PB[:, 0:256]),
                     waits=W(sc_pe), inc=sc_dve)
            for d in range(2):
                k.op("tensor", lambda e, d=d: e.matmul(
                    S[6][:, 256:384], dT[:, d * 128:(d + 1) * 128],
                    ptT[:, d * 128:(d + 1) * 128], start=(d == 0), stop=(d == 1)),
                    waits=W(sc_dve) if d == 0 else (), inc=sc_pe if d == 1 else None)
            k.op("vector", lambda e: e.tensor_mul(
                G2m[:], S[6][:, 256:384], cn["mm"][:, 0:128]),
                waits=W(sc_pe), inc=sc_dve)
            for vs in range(2):
                bank = S[vs]
                for d in range(2):
                    k.op("tensor", lambda e, d=d, vs=vs, bank=bank: e.matmul(
                        bank[:, 0:128], S2b[d][:, vs * 128:(vs + 1) * 128],
                        ptT[:, d * 128:(d + 1) * 128], start=(d == 0), stop=False),
                        waits=W(sc_dve, sc_act) if d == 0 else ())
                k.op("tensor", lambda e, vs=vs, ci=ci, bank=bank: e.matmul(
                    bank[:, 0:128], Vt[:, ci * 256 + vs * 128: ci * 256 + (vs + 1) * 128],
                    G2m[:], start=False, stop=True),
                    waits=W(sc_dve), inc=sc_pe)
            for d in range(2):
                k.op("tensor", lambda e, d=d: e.matmul(
                    S[3][:, 256:512], kT[:, d * 128:(d + 1) * 128], D2t[:],
                    start=True, stop=True), waits=W(sc_dve),
                    inc=sc_pe)
                k.op("vector", lambda e, d=d: e.tensor_mul(
                    s1scr[:], S1b[d][:], S[4][:, 0:256]),
                    waits=W(sc_pe), inc=sc_dve)
                k.op("vector", lambda e, d=d: e.tensor_add(
                    S1b[d][:], s1scr[:], S[3][:, 256:512]),
                    waits=W(sc_dve), inc=sc_dve)
            for mcb in range(2):
                k.op("tensor", lambda e, mcb=mcb, ci=ci: e.matmul(
                    S[3][:, 256:512], D2t[:, mcb * 128:(mcb + 1) * 128],
                    Vt[:, ci * 256:(ci + 1) * 256],
                    start=True, stop=True), waits=W(sc_dve), inc=sc_pe)
                k.op("vector", lambda e, mcb=mcb: e.tensor_scalar(
                    s1scr[:], S2b[mcb][:], eAT[:, mcb:mcb + 1], None, op0=ALU.mult),
                    waits=W(sc_pe), inc=sc_dve)
                k.op("vector", lambda e, mcb=mcb: e.tensor_add(
                    S2b[mcb][:], s1scr[:], S[3][:, 256:512]),
                    waits=W(sc_dve), inc=sc_dve)
            midx = (h * NCH + ci)
            k.op("scalar", lambda e, m2=midx % 2: e.activation(
                ost[m2][:, 0:128], S[0][:, 0:128], AF.Silu, bias=0.0, scale=1.0),
                waits=W(sc_pe) + [(so_dma[midx % 2], so_dma[midx % 2].n)], inc=sc_act)
            k.op("scalar", lambda e, m2=midx % 2: e.activation(
                ost[m2][:, 128:256], S[1][:, 0:128], AF.Silu, bias=0.0, scale=1.0),
                inc=sc_act)
            for vs in range(2):
                k.op("gpsimd", lambda e, h=h, ci=ci, vs=vs, m2=midx % 2: e.dma_start(
                    out=o_dram[h * 256 + vs * 128: h * 256 + (vs + 1) * 128,
                               ci * 128:(ci + 1) * 128],
                    in_=ost[m2][:, vs * 128:(vs + 1) * 128]),
                    waits=W(sc_act) if vs == 0 else (), inc=so_dma[midx % 2], inck=16)

    hctx.close()
    scanW = [(sc_pe, sc_pe.n), (sc_act, sc_act.n), (sc_dve, sc_dve.n),
             (s_ldw, s_ldw.n), (v_ev, v_ev.n), (pj, pj.n)]
    with contextlib.ExitStack() as octx:
        OE = octx.enter_context
        ob = OE(nc.sbuf_tensor("a1_ob", [128, 8 * 512], BF16))
        osq = [OE(nc.sbuf_tensor(f"a1_osq{i}", [128, 512], BF16)) for i in range(2)]
        oh_ = OE(nc.sbuf_tensor("a1_oh", [128, 8 * 512], BF16))
        q2_row = OE(nc.sbuf_tensor("a1_q2", [1, 512], F32))
        rso_row = OE(nc.sbuf_tensor("a1_rso", [1, 512], F32))
        rso_b = OE(nc.sbuf_tensor("a1_rsob", [1, 512], BF16))
        xrg = [OE(nc.sbuf_tensor(f"a1_xrg{i}", [128, 512], F32)) for i in range(4)]
        x3o = [OE(nc.sbuf_tensor(f"a1_x3o{i}", [128, 512], F32)) for i in range(4)]
        gn_b = OE(nc.sbuf_tensor("a1_gnb", [128, 8], BF16))
        go_ld = k.sem()
        go_pe = k.sem()
        go_dve = k.sem()
        go_act = k.sem()
        gst = [k.sem() for _ in range(4)]
        gxr = [k.sem() for _ in range(4)]
        st_dve = k.sem()
        k.op("vector", lambda e: e.tensor_copy(gn_b[:], gn_f[:]),
             waits=W(s_ldg) + [cw], inc=go_dve)
        for c in range(8):
            k.op("vector", lambda e, c=c: e.tensor_scalar(
                wo_sb[:, c * 1024:(c + 1) * 1024], wo_sb[:, c * 1024:(c + 1) * 1024],
                gn_f[:, c:c + 1], None, op0=ALU.mult),
                waits=[(s_ldwo, s_ldwo.n)] + W(go_dve), inc=go_dve)
        woW = (go_dve, go_dve.n)
        soW = (so_dma[0], so_dma[0].n)
        soW2 = (so_dma[1], so_dma[1].n)
        for g in range(G):
            ldb = go_ld.n
            for c in range(8):
                k.op("sync", lambda e, c=c, g=g: e.dma_start(
                    out=ob[:, c * 512:(c + 1) * 512],
                    in_=o_dram[c * 128:(c + 1) * 128, g * 512:(g + 1) * 512]),
                    waits=([soW, soW2, (go_pe, go_pe.n)] + (scanW if g == 0 else []))
                    if c == 0 else (), inc=go_ld, inck=16)
            for c in range(8):
                k.op("scalar", lambda e, c=c: e.activation(
                    osq[c % 2][:], ob[:, c * 512:(c + 1) * 512], AF.Square,
                    bias=0.0, scale=1.0),
                    waits=[(go_ld, ldb + 128)] + (W(go_pe) if c >= 2 else []),
                    inc=go_act)
                k.op("tensor", lambda e, c=c: e.matmul(
                    S[1][0:1, 0:512], cn["ones_col"][:], osq[c % 2][:],
                    start=(c == 0), stop=(c == 7)),
                    waits=W(go_act), inc=go_pe)
            k.op("vector", lambda e: e.tensor_scalar(
                q2_row[:], S[1][0:1, 0:512], 1.0 / 1024, None, op0=ALU.mult),
                waits=W(go_pe), inc=go_dve)
            k.op("vector", lambda e: e.tensor_scalar(
                q2_row[:], q2_row[:], 1e-5, None, op0=ALU.add),
                waits=W(go_dve), inc=go_dve)
            k.op("scalar", lambda e: e.activation(rso_row[:], q2_row[:], AF.Ln,
                                                  bias=0.0, scale=1.0),
                 waits=W(go_dve), inc=go_act)
            k.op("scalar", lambda e: e.activation(rso_row[:], rso_row[:], AF.Exp,
                                                  bias=0.0, scale=-0.5),
                 waits=W(go_act), inc=go_act)
            k.op("vector", lambda e: e.tensor_copy(rso_b[:], rso_row[:]),
                 waits=W(go_act), inc=go_dve)
            k.op("tensor", lambda e: e.matmul(
                S[2][:], cn["ones_row"][:], rso_b[:], start=True, stop=True),
                waits=W(go_dve), inc=go_pe)
            for c in range(8):
                k.op("vector", lambda e, c=c: e.tensor_mul(
                    oh_[:, c * 512:(c + 1) * 512], ob[:, c * 512:(c + 1) * 512],
                    S[2][:]), waits=W(go_pe) if c == 0 else W(go_dve), inc=go_dve)
            ohW = (go_dve, go_dve.n)
            for c in range(8):
                n = g * 8 + c
                b = n % 4
                k.op("sync", lambda e, c=c, g=g, b=b: e.dma_start(
                    out=xrg[b][:], in_=xT[c * 128:(c + 1) * 128, g * 512:(g + 1) * 512]),
                    waits=[(st_dve, max(0, st_dve.n - 3))], inc=gxr[b], inck=16)
                bank = S[3 + (c % 2)]
                for f in range(8):
                    k.op("tensor", lambda e, c=c, g=g, f=f, bank=bank: e.matmul(
                        bank[:], wo_sb[:, f * 1024 + c * 128: f * 1024 + (c + 1) * 128],
                        oh_[:, f * 512:(f + 1) * 512],
                        start=(f == 0), stop=(f == 7)),
                        waits=[ohW, woW, (st_dve, max(0, st_dve.n - 1))] if f == 0 else (),
                        inc=go_pe if f == 7 else None)
                k.op("vector", lambda e, b=b, bank=bank: e.tensor_add(
                    x3o[b][:], bank[:], xrg[b][:]),
                    waits=[(go_pe, go_pe.n), (gxr[b], gxr[b].n), (gst[b], gst[b].n)],
                    inc=st_dve)
                k.op("scalar", lambda e, c=c, g=g, b=b: e.dma_start(
                    out=x3T[c * 128:(c + 1) * 128, g * 512:(g + 1) * 512],
                    in_=x3o[b][:]), waits=[(st_dve, st_dve.n)], inc=gst[b], inck=16)


def build_l1_attn_program(T):
    nc = bass.Bass("TRN2", target_bir_lowering=False)
    xT = nc.dram_tensor("xT", [1024, T], F32, kind="ExternalInput").ap()
    nw = nc.dram_tensor("nw", [1024, 1], F32, kind="ExternalInput").ap()
    gn = nc.dram_tensor("gn", [1024, 1], F32, kind="ExternalInput").ap()
    wq = nc.dram_tensor("wq", [1024, 1024], F32, kind="ExternalInput").ap()
    wk = nc.dram_tensor("wk", [1024, 1024], F32, kind="ExternalInput").ap()
    wv = nc.dram_tensor("wv", [1024, 1024], F32, kind="ExternalInput").ap()
    wf = nc.dram_tensor("wf", [1024, 1024], F32, kind="ExternalInput").ap()
    wo = nc.dram_tensor("wo", [1024, 1024], F32, kind="ExternalInput").ap()
    x3T = nc.dram_tensor("x3T", [1024, T], F32, kind="ExternalOutput").ap()
    ctx = contextlib.ExitStack()
    with ctx:
        E = ctx.enter_context
        k = K(nc, E)
        env = Env(nc, E, k)
        emit_l1_attn(env, T, xT, x3T, nw, wq, wk, wv, wf, gn, wo)
        k.final_barrier()
        block = E(nc.Block())
        k.finish(block)
    return nc


def _l1_feed(x, b, nw, gn, wq, wk, wv, wf, wo):
    m = dict(make_const_arrays())
    m["xT"] = np.ascontiguousarray(x[b].T)
    m["nw"] = np.ascontiguousarray(nw[:, None])
    m["gn"] = np.ascontiguousarray(gn[:, None])
    m["wq"] = np.ascontiguousarray(wq.T)
    m["wk"] = np.ascontiguousarray(wk.T)
    m["wv"] = np.ascontiguousarray(wv.T)
    m["wf"] = np.ascontiguousarray(wf.T)
    m["wo"] = np.ascontiguousarray(wo.T)
    return m


def run_l1_attn(x, nw, gn, wq, wk, wv, wf, wo):
    B, T, D = x.shape
    nc = build_l1_attn_program(T)
    in_maps = [_l1_feed(x, b, nw, gn, wq, wk, wv, wf, wo) for b in range(B)]
    res = run_bass_kernel_spmd(nc, in_maps, core_ids=list(range(B)))
    return np.stack([r["x3T"].T for r in res.results], axis=0)


def sim_l1_attn(x, nw, gn, wq, wk, wv, wf, wo):
    from concourse.bass_interp import CoreSim
    B, T, D = x.shape
    assert B == 1
    nc = build_l1_attn_program(T)
    nc.finalize()
    sim = CoreSim(nc)
    feed = _l1_feed(x, 0, nw, gn, wq, wk, wv, wf, wo)
    for name, arr in feed.items():
        v = sim.tensor(name)
        v[:] = arr.view(v.dtype) if arr.dtype == NP_BF16 else arr
    sim.simulate()
    return sim.tensor("x3T").T[None].copy()



L1FFN_TRUNC = 0

def emit_l1_ffn(env: Env, T, xT, x4T, gwT, gbRow, WE1, BE1col, WE2, BE2, WE3, BE3col,
                ws1A, bs1C, ws2A, bs2R, ws3A, bs3C):
    trunc = L1FFN_TRUNC
    """x4 = x3 + moe_top2of16(LN(x3)) + shared_swiglu(LN(x3)). Dense v1.
    WE1/WE3: [16*1024, 512] f32; WE2: [16*512, 1024] f32; BE1col/BE3col: [16*512, 1] f32;
    BE2: [16, 1024] f32; ws*: [1024, 1024]; bs1C/bs3C: [1024, 1]; bs2R: [1, 1024]."""
    nc, E, k, cn = env.nc, env.E, env.k, env.cn
    S, PB = env.S, env.PB
    G = T // 512
    NT = T // 128
    NE = 16
    wbf1 = nc.dram_tensor("wbf1", [16 * 1024, 512], BF16, kind="Internal").ap()
    wbf3 = nc.dram_tensor("wbf3", [16 * 1024, 512], BF16, kind="Internal").ap()
    wbf2 = nc.dram_tensor("wbf2", [16 * 512, 1024], BF16, kind="Internal").ap()

    W = lambda *ss: [s.now() for s in ss]
    cw = (env.ld, env.ld.n)

    xh = E(nc.sbuf_tensor("m_xh", [128, 8 * 512], BF16))
    negmr = E(nc.sbuf_tensor("m_negmr", [1, 512], BF16))
    comb = E(nc.sbuf_tensor("m_comb", [128, 4 * 16], F32))
    gw_sb = E(nc.sbuf_tensor("m_gw", [128, 8 * 16], F32))
    csgw = E(nc.sbuf_tensor("m_csgw", [1, 16], F32))
    gbb = E(nc.sbuf_tensor("m_gbb", [128, 16], F32))
    gb_sb = E(nc.sbuf_tensor("m_gb", [1, 16], F32))
    cs1d = nc.dram_tensor("m_cs1d", [16, 512], BF16, kind="Internal").ap()
    cs3d = nc.dram_tensor("m_cs3d", [16, 512], BF16, kind="Internal").ap()
    cs1t = [E(nc.sbuf_tensor(f"m_cs1t{i}", [1, 512], BF16)) for i in range(2)]
    cs3t = [E(nc.sbuf_tensor(f"m_cs3t{i}", [1, 512], BF16)) for i in range(2)]
    be1t = E(nc.sbuf_tensor("m_be1t", [128, 16 * 4], F32))
    be3t = E(nc.sbuf_tensor("m_be3t", [128, 16 * 4], F32))
    be2t = [E(nc.sbuf_tensor(f"m_be2t{i}", [1, 1024], BF16)) for i in range(2)]
    ws1_sb = E(nc.sbuf_tensor("m_ws1", [128, 8 * 1024], BF16))
    ws3_sb = E(nc.sbuf_tensor("m_ws3", [128, 8 * 1024], BF16))
    ws2_sb = E(nc.sbuf_tensor("m_ws2", [128, 8 * 1024], BF16))
    css1 = E(nc.sbuf_tensor("m_css1", [1, 1024], BF16))
    css3 = E(nc.sbuf_tensor("m_css3", [1, 1024], BF16))
    bs1c = E(nc.sbuf_tensor("m_bs1c", [128, 8], F32))
    bs3c = E(nc.sbuf_tensor("m_bs3c", [128, 8], F32))
    bs2r = E(nc.sbuf_tensor("m_bs2r", [1, 1024], BF16))

    s_ldg = k.sem()
    s_ldx = k.sem()
    s_ldx3 = k.sem()
    s_pc = k.sem()
    s_pc2 = k.sem()
    s_pcs = k.sem()
    s_b = k.sem()
    ln_pe = k.sem()
    ln_dve = k.sem()
    ln_act = k.sem()
    ln_done = k.sem()
    gt_pe = k.sem()
    gt_dve = k.sem()
    gt_act = k.sem()
    fe_pe = k.sem()
    fe_act = k.sem()
    fe_dve = k.sem()
    s_w2 = [k.sem(), k.sem()]
    st_dve = k.sem()
    gst = [k.sem() for _ in range(4)]
    gxr = [k.sem() for _ in range(4)]

    for c in range(8):
        k.op("sync", lambda e, c=c: e.dma_start(
            out=gw_sb[:, c * 16:(c + 1) * 16], in_=gwT[c * 128:(c + 1) * 128, :]),
            inc=s_ldg, inck=16)
    k.op("sync", lambda e: e.dma_start(out=gb_sb[:], in_=gbRow), inc=s_ldg, inck=16)
    for ei in range(NE):
        for sl in range(4):
            k.op("scalar", lambda e, ei=ei, sl=sl: e.dma_start(
                out=be1t[:, ei * 4 + sl: ei * 4 + sl + 1],
                in_=BE1col[ei * 512 + sl * 128: ei * 512 + (sl + 1) * 128, 0:1]),
                inc=s_b, inck=16)
            k.op("scalar", lambda e, ei=ei, sl=sl: e.dma_start(
                out=be3t[:, ei * 4 + sl: ei * 4 + sl + 1],
                in_=BE3col[ei * 512 + sl * 128: ei * 512 + (sl + 1) * 128, 0:1]),
                inc=s_b, inck=16)
    for c in range(8):
        k.op("scalar", lambda e, c=c: e.dma_start(
            out=bs1c[:, c:c + 1], in_=bs1C[c * 128:(c + 1) * 128, 0:1]),
            inc=s_b, inck=16)
        k.op("scalar", lambda e, c=c: e.dma_start(
            out=bs3c[:, c:c + 1], in_=bs3C[c * 128:(c + 1) * 128, 0:1]),
            inc=s_b, inck=16)
    k.op("gpsimd", lambda e: e.dma_start(out=bs2r[:], in_=bs2R), inc=s_pc, inck=16)
    for (dst, src) in ((ws1_sb, ws1A), (ws3_sb, ws3A), (ws2_sb, ws2A)):
        for c in range(8):
            k.op("gpsimd", lambda e, c=c, dst=dst, src=src: e.dma_start(
                out=dst[:, c * 1024:(c + 1) * 1024],
                in_=src[c * 128:(c + 1) * 128, :]), inc=s_pc, inck=16)
    for c in range(8):
        k.op("tensor", lambda e, c=c: e.matmul(
            S[5][0:1, 0:16], cn["ones_col_f"][:], gw_sb[:, c * 16:(c + 1) * 16],
            start=(c == 0), stop=(c == 7)),
            waits=[cw, (s_ldg, s_ldg.n)] if c == 0 else (),
            inc=gt_pe if c == 7 else None)
    k.op("vector", lambda e: e.tensor_copy(csgw[:], S[5][0:1, 0:16]),
         waits=W(gt_pe), inc=gt_dve)
    k.op("tensor", lambda e: e.matmul(
        S[5][:, 16:32], cn["ones_row_f"][:], gb_sb[:], start=True, stop=True),
        waits=W(gt_dve) + [(s_ldg, s_ldg.n)], inc=gt_pe)
    k.op("vector", lambda e: e.tensor_copy(gbb[:], S[5][:, 16:32]),
         waits=W(gt_pe), inc=gt_dve)
    for (wt, cs) in ((ws1_sb, css1), (ws3_sb, css3)):
        for half in range(2):
            for c in range(8):
                k.op("tensor", lambda e, c=c, wt=wt, half=half: e.matmul(
                    S[5][0:1, 0:512], cn["ones_col"][:],
                    wt[:, c * 1024 + half * 512: c * 1024 + (half + 1) * 512],
                    start=(c == 0), stop=(c == 7)),
                    waits=[(s_pc, s_pc.n), (gt_dve, gt_dve.n), (gt_act, gt_act.n)]
                    if c == 0 else (),
                    inc=gt_pe if c == 7 else None)
            k.op("scalar", lambda e, cs=cs, half=half: e.activation(
                cs[0:1, half * 512:(half + 1) * 512], S[5][0:1, 0:512],
                AF.Copy, bias=0.0, scale=1.0),
                waits=W(gt_pe), inc=gt_act)

    with contextlib.ExitStack() as pctx:
        PE_ = pctx.enter_context
        wtmp = [PE_(nc.sbuf_tensor(f"m_wtmp{i}", [128, 8 * 512], BF16)) for i in range(2)]
        for ei in range(NE):
            for wi, (srcW, dstW, cstab) in enumerate(
                    ((WE1, wbf1, "cs1"), (WE3, wbf3, "cs3"), (WE2, wbf2, None))):
                buf = wtmp[(ei * 3 + wi) % 2]
                ldb = s_pc2.n
                nch = 8 if cstab is not None else 4
                width = 512 if cstab is not None else 1024
                for c in range(nch):
                    k.op("gpsimd", lambda e, c=c, ei=ei, srcW=srcW, buf=buf,
                         width=width, nch=nch: e.dma_start(
                        out=buf[:, c * width:(c + 1) * width],
                        in_=srcW[ei * nch * 128 + c * 128: ei * nch * 128 + (c + 1) * 128, :]),
                        waits=[(s_pcs, s_pcs.n)] if c == 0 else (), inc=s_pc2, inck=16)
                for c in range(nch):
                    k.op("scalar", lambda e, c=c, ei=ei, dstW=dstW, buf=buf,
                         width=width, nch=nch: e.dma_start(
                        out=dstW[ei * nch * 128 + c * 128: ei * nch * 128 + (c + 1) * 128, :],
                        in_=buf[:, c * width:(c + 1) * width]),
                        waits=[(s_pc2, ldb + 16 * nch)] if c == 0 else (),
                        inc=s_pcs, inck=16)
                if cstab is not None:
                    csd = cs1d if cstab == "cs1" else cs3d
                    for c in range(8):
                        k.op("tensor", lambda e, c=c, buf=buf: e.matmul(
                            S[6][0:1, 0:512], cn["ones_col"][:],
                            buf[:, c * 512:(c + 1) * 512],
                            start=(c == 0), stop=(c == 7)),
                            waits=[(s_pc2, ldb + 128), (gt_act, gt_act.n),
                                   (fe_act, fe_act.n)] if c == 0 else (),
                            inc=fe_pe if c == 7 else None)
                    k.op("scalar", lambda e: e.activation(
                        cs1t[0][:], S[6][0:1, 0:512],
                        AF.Copy, bias=0.0, scale=1.0),
                        waits=W(fe_pe) + [(s_pcs, s_pcs.n)], inc=fe_act)
                    k.op("scalar", lambda e, ei=ei, csd=csd: e.dma_start(
                        out=csd[ei:ei + 1, :], in_=cs1t[0][:]),
                        waits=W(fe_act), inc=s_pcs, inck=16)
    pcW = [(s_pcs, s_pcs.n), (s_pc, s_pc.n), (s_pc2, s_pc2.n)]

    LE = E
    xbb = LE(nc.sbuf_tensor("m_xbb", [128, 8 * 512], BF16))
    xgf = LE(nc.sbuf_tensor("m_xgf", [128, 8 * 512], F32))
    sq = [LE(nc.sbuf_tensor(f"m_sq{i}", [128, 512], BF16)) for i in range(2)]
    m_row = LE(nc.sbuf_tensor("m_mrow", [1, 512], F32))
    q_row = LE(nc.sbuf_tensor("m_qrow", [1, 512], F32))
    v_row = LE(nc.sbuf_tensor("m_vrow", [1, 512], F32))
    rs_row = LE(nc.sbuf_tensor("m_rsrow", [1, 512], F32))
    rs_b = LE(nc.sbuf_tensor("m_rsb", [1, 512], BF16))
    rstdc = LE(nc.sbuf_tensor("m_rstdc", [128, 1], F32))
    lg = LE(nc.sbuf_tensor("m_lg", [128, 16], F32))
    nmx = LE(nc.sbuf_tensor("m_nmx", [128, 1], F32))
    sume = LE(nc.sbuf_tensor("m_sume", [128, 1], F32))
    esc = LE(nc.sbuf_tensor("m_esc", [128, 16], F32))
    inv = LE(nc.sbuf_tensor("m_inv", [128, 1], F32))
    orig = LE(nc.sbuf_tensor("m_orig", [128, 16], F32))
    sb16 = LE(nc.sbuf_tensor("m_sb16", [128, 16], F32))
    gsc = LE(nc.sbuf_tensor("m_gsc", [128, 4], F32))
    scr4 = LE(nc.sbuf_tensor("m_scr4", [128, 4], F32))
    pmin = LE(nc.sbuf_tensor("m_pmin", [128, 1], F32))
    thr = LE(nc.sbuf_tensor("m_thr", [128, 1], F32))
    keepg = LE(nc.sbuf_tensor("m_keepg", [128, 4], F32))
    m1 = LE(nc.sbuf_tensor("m_m1", [128, 1], F32))
    m2 = LE(nc.sbuf_tensor("m_m2", [128, 1], F32))
    sel = LE(nc.sbuf_tensor("m_sel", [128, 16], F32))
    negmf = LE(nc.sbuf_tensor("m_negmf", [1, 512], F32))
    ysb = E(nc.sbuf_tensor("m_ysb", [128, 4 * 1024], F32))
    w1e = [E(nc.sbuf_tensor(f"m_w1e{i}", [128, 8 * 512], BF16)) for i in range(2)]
    w3e = [E(nc.sbuf_tensor(f"m_w3e{i}", [128, 8 * 512], BF16)) for i in range(2)]
    w2e_s = E(nc.sbuf_tensor("m_w2e", [128, 4 * 1024], BF16))
    w2e = [w2e_s, w2e_s]
    h1sb = E(nc.sbuf_tensor("m_h1sb", [128, 512], F32))
    h3scr = E(nc.sbuf_tensor("m_h3", [128, 512], F32))
    h_all = E(nc.sbuf_tensor("m_hall", [128, 8 * 512], BF16))

    xrg = [E(nc.sbuf_tensor(f"m_xrg{i}", [128, 512], F32)) for i in range(2)]
    x4o = [E(nc.sbuf_tensor(f"m_x4o{i}", [128, 512], F32)) for i in range(2)]
    s_cs2 = [k.sem(), k.sem()]
    res_pe = k.sem()
    prev_res = 0

    for g in range(G):
        ldb = s_ldx.n
        ldb3 = s_ldx3.n
        for c in range(8):
            k.op("gpsimd", lambda e, c=c, g=g: e.dma_start(
                out=xbb[:, c * 512:(c + 1) * 512],
                in_=xT[c * 128:(c + 1) * 128, g * 512:(g + 1) * 512]),
                waits=([(ln_done, ln_done.n)] if g > 0 else pcW) if c == 0 else (),
                inc=s_ldx, inck=16)
            k.op("sync", lambda e, c=c, g=g: e.dma_start(
                out=xgf[:, c * 512:(c + 1) * 512],
                in_=xT[c * 128:(c + 1) * 128, g * 512:(g + 1) * 512]),
                waits=([(gt_dve, gt_dve.n)] if g > 0 else pcW) if c == 0 else (),
                inc=s_ldx3, inck=16)
        for c in range(8):
            k.op("tensor", lambda e, c=c: e.matmul(
                S[0][0:1, 0:512], cn["ones_col"][:], xbb[:, c * 512:(c + 1) * 512],
                start=(c == 0), stop=(c == 7)),
                waits=[(s_ldx, ldb + 128)] if c == 0 else (),
                inc=ln_pe if c == 7 else None)
        for c in range(8):
            k.op("scalar", lambda e, c=c: e.activation(
                sq[c % 2][:], xbb[:, c * 512:(c + 1) * 512], AF.Square,
                bias=0.0, scale=1.0),
                waits=[(s_ldx, ldb + 128)] + (W(ln_pe) if c >= 2 else []),
                inc=ln_act)
            k.op("tensor", lambda e, c=c: e.matmul(
                S[1][0:1, 0:512], cn["ones_col"][:], sq[c % 2][:],
                start=(c == 0), stop=(c == 7)),
                waits=W(ln_act), inc=ln_pe)
        k.op("vector", lambda e: e.tensor_scalar(
            m_row[:], S[0][0:1, 0:512], 1.0 / 1024, None, op0=ALU.mult),
            waits=W(ln_pe), inc=ln_dve)
        k.op("vector", lambda e: e.tensor_scalar(
            q_row[:], S[1][0:1, 0:512], 1.0 / 1024, None, op0=ALU.mult),
            inc=ln_dve)
        k.op("vector", lambda e: e.tensor_mul(v_row[:], m_row[:], m_row[:]),
             waits=W(ln_dve), inc=ln_dve)
        k.op("vector", lambda e: e.tensor_sub(v_row[:], q_row[:], v_row[:]),
             waits=W(ln_dve), inc=ln_dve)
        k.op("vector", lambda e: e.tensor_scalar(
            v_row[:], v_row[:], 1e-5, None, op0=ALU.add),
            waits=W(ln_dve), inc=ln_dve)
        k.op("scalar", lambda e: e.activation(rs_row[:], v_row[:], AF.Ln,
                                              bias=0.0, scale=1.0),
             waits=W(ln_dve), inc=ln_act)
        k.op("scalar", lambda e: e.activation(rs_row[:], rs_row[:], AF.Exp,
                                              bias=0.0, scale=-0.5),
             waits=W(ln_act), inc=ln_act)
        k.op("vector", lambda e: e.tensor_copy(rs_b[:], rs_row[:]),
             waits=W(ln_act), inc=ln_dve)
        k.op("vector", lambda e: e.tensor_mul(negmf[:], m_row[:], rs_row[:]),
             waits=W(ln_dve), inc=ln_dve)
        k.op("vector", lambda e: e.tensor_scalar(
            negmf[:], negmf[:], -1.0, None, op0=ALU.mult),
            waits=W(ln_dve), inc=ln_dve)
        k.op("vector", lambda e: e.tensor_copy(negmr[:], negmf[:]),
             waits=W(ln_dve), inc=ln_dve)
        k.op("tensor", lambda e: e.matmul(
            S[2][:], cn["ones_row"][:], rs_b[:], start=True, stop=True),
            waits=W(ln_dve) + [(ln_done, ln_done.n), (fe_pe, fe_pe.n)], inc=ln_pe)
        for c in range(8):
            k.op("vector", lambda e, c=c: e.tensor_mul(
                xh[:, c * 512:(c + 1) * 512],
                xbb[:, c * 512:(c + 1) * 512], S[2][:]),
                waits=W(ln_pe), inc=(ln_done if c == 7 else ln_dve))
        for ti in range(4) if trunc < 2 else []:
            t = g * 4 + ti
            k.op("tensor", lambda e, ti=ti: e.transpose(
                PB[0:128, 0:1], rs_b[0:1, ti * 128:(ti + 1) * 128],
                cn["ident_bf"][0:1, 0:1]),
                waits=W(gt_dve, ln_dve), inc=gt_pe)
            k.op("vector", lambda e: e.tensor_copy(rstdc[:], PB[0:128, 0:1]),
                 waits=W(gt_pe), inc=gt_dve)
            for c in range(8):
                k.op("tensor", lambda e, c=c, ti=ti: e.matmul(
                    S[6][:, 0:16],
                    xgf[:, c * 512 + ti * 128: c * 512 + (ti + 1) * 128],
                    gw_sb[:, c * 16:(c + 1) * 16], start=(c == 0), stop=False),
                    waits=[(s_ldx3, ldb3 + 128), (gt_dve, gt_dve.n)]
                    if c == 0 else ())
            k.op("tensor", lambda e, ti=ti: e.matmul(
                S[6][:, 0:16], negmf[0:1, ti * 128:(ti + 1) * 128],
                csgw[:], start=False, stop=True), inc=gt_pe)
            k.op("vector", lambda e: e.tensor_scalar(
                lg[:], S[6][:, 0:16], rstdc[:, 0:1], None, op0=ALU.mult),
                waits=W(gt_pe, gt_dve), inc=gt_dve)
            k.op("vector", lambda e: e.reduce_max(
                nmx[:], lg[:], axis=AX.X, negate=True), waits=W(gt_dve), inc=gt_dve)
            k.op("scalar", lambda e: e.activation(
                esc[:], lg[:], AF.Exp, bias=nmx[:, 0:1], scale=1.0,
                accum_out=sume[:, 0:1]), waits=W(gt_dve), inc=gt_act)
            k.op("vector", lambda e: e.reciprocal(inv[:], sume[:]),
                 waits=W(gt_act), inc=gt_dve)
            k.op("vector", lambda e: e.tensor_scalar(
                orig[:], esc[:], inv[:, 0:1], None, op0=ALU.mult),
                waits=W(gt_dve), inc=gt_dve)
            k.op("vector", lambda e: e.tensor_add(sb16[:], orig[:], gbb[:]),
                 waits=W(gt_dve), inc=gt_dve)
            first = True
            for i in range(4):
                for j in range(i + 1, 4):
                    if first:
                        k.op("vector", lambda e, i=i, j=j: e.tensor_add(
                            gsc[:], sb16[:, i:13 + i:4], sb16[:, j:13 + j:4]),
                            waits=W(gt_dve), inc=gt_dve)
                    else:
                        k.op("vector", lambda e, i=i, j=j: e.tensor_add(
                            scr4[:], sb16[:, i:13 + i:4], sb16[:, j:13 + j:4]),
                            waits=W(gt_dve), inc=gt_dve)
                        k.op("vector", lambda e: e.tensor_max(
                            gsc[:], gsc[:], scr4[:]), waits=W(gt_dve), inc=gt_dve)
                    first = False
            first = True
            for i in range(4):
                for j in range(i + 1, 4):
                    if first:
                        k.op("vector", lambda e, i=i, j=j: e.tensor_tensor(
                            thr[:], gsc[:, i:i + 1], gsc[:, j:j + 1], op=ALU.min),
                            waits=W(gt_dve), inc=gt_dve)
                    else:
                        k.op("vector", lambda e, i=i, j=j: e.tensor_tensor(
                            pmin[:], gsc[:, i:i + 1], gsc[:, j:j + 1], op=ALU.min),
                            waits=W(gt_dve), inc=gt_dve)
                        k.op("vector", lambda e: e.tensor_max(
                            thr[:], thr[:], pmin[:]), waits=W(gt_dve), inc=gt_dve)
                    first = False
            k.op("vector", lambda e: e.tensor_scalar(
                keepg[:], gsc[:], thr[:, 0:1], None, op0=ALU.is_ge),
                waits=W(gt_dve), inc=gt_dve)
            k.op("vector", lambda e: e.tensor_scalar(
                keepg[:], keepg[:], 1e30, -1e30, op0=ALU.mult, op1=ALU.add),
                waits=W(gt_dve), inc=gt_dve)
            for gg in range(4):
                k.op("vector", lambda e, gg=gg: e.tensor_scalar(
                    sb16[:, gg * 4:(gg + 1) * 4], sb16[:, gg * 4:(gg + 1) * 4],
                    keepg[:, gg:gg + 1], None, op0=ALU.add),
                    waits=W(gt_dve), inc=gt_dve)
            k.op("vector", lambda e: e.reduce_max(m1[:], sb16[:], axis=AX.X),
                 waits=W(gt_dve), inc=gt_dve)
            k.op("vector", lambda e: e.tensor_scalar(
                sel[:], sb16[:], m1[:, 0:1], None, op0=ALU.is_ge),
                waits=W(gt_dve), inc=gt_dve)
            k.op("vector", lambda e: e.tensor_scalar(
                sel[:], sel[:], -2e30, None, op0=ALU.mult),
                waits=W(gt_dve), inc=gt_dve)
            k.op("vector", lambda e: e.tensor_add(sel[:], sb16[:], sel[:]),
                 waits=W(gt_dve), inc=gt_dve)
            k.op("vector", lambda e: e.reduce_max(m2[:], sel[:], axis=AX.X),
                 waits=W(gt_dve), inc=gt_dve)
            k.op("vector", lambda e: e.tensor_scalar(
                sel[:], sb16[:], m2[:, 0:1], None, op0=ALU.is_ge),
                waits=W(gt_dve), inc=gt_dve)
            k.op("vector", lambda e, ti=ti: e.tensor_mul(
                comb[:, ti * 16:(ti + 1) * 16], orig[:], sel[:]),
                waits=W(gt_dve), inc=gt_dve)
        gtW = (gt_dve, gt_dve.n)
        lnwg = (ln_done, ln_done.n)
        k.op("vector", lambda e: e.memset(ysb[:], 0.0),
             waits=[(st_dve, st_dve.n), gtW], inc=fe_dve)
        pe_mark = 0
        for ei in (range(NE + 1) if trunc == 0 else
                   ([] if trunc >= 3 else [NE])):
            shared = (ei == NE)
            dbl = ei % 2
            wW = s_w2[ei % 2].n
            csW = s_cs2[ei % 2].n
            prev_mark = pe_mark
            pe_mark = fe_pe.n
            if not shared:
                for c in range(8):
                    k.op("gpsimd", lambda e, c=c, ei=ei, dbl=dbl: e.dma_start(
                        out=w1e[dbl][:, c * 512:(c + 1) * 512],
                        in_=wbf1[ei * 1024 + c * 128: ei * 1024 + (c + 1) * 128, :]),
                        waits=([(fe_pe, prev_mark)] + (pcW if (g == 0 and ei == 0) else []))
                        if c == 0 else (), inc=s_w2[dbl], inck=16)
                    k.op("gpsimd", lambda e, c=c, ei=ei, dbl=dbl: e.dma_start(
                        out=w3e[dbl][:, c * 512:(c + 1) * 512],
                        in_=wbf3[ei * 1024 + c * 128: ei * 1024 + (c + 1) * 128, :]),
                        inc=s_w2[dbl], inck=16)
                for sl in range(4):
                    k.op("gpsimd", lambda e, sl=sl, ei=ei, dbl=dbl: e.dma_start(
                        out=w2e[dbl][:, sl * 1024:(sl + 1) * 1024],
                        in_=wbf2[ei * 512 + sl * 128: ei * 512 + (sl + 1) * 128, :]),
                        waits=[(fe_pe, fe_pe.n)] if sl == 0 else (), inc=s_w2[dbl], inck=16)
                k.op("scalar", lambda e, ei=ei, dbl=dbl: e.dma_start(
                    out=cs1t[dbl][:], in_=cs1d[ei:ei + 1, :]),
                    waits=[(fe_pe, prev_mark)] + (pcW if (g == 0 and ei == 0) else []),
                    inc=s_cs2[dbl], inck=16)
                k.op("scalar", lambda e, ei=ei, dbl=dbl: e.dma_start(
                    out=cs3t[dbl][:], in_=cs3d[ei:ei + 1, :]), inc=s_cs2[dbl], inck=16)
                k.op("gpsimd", lambda e, ei=ei, dbl=dbl: e.dma_start(
                    out=be2t[dbl][:], in_=BE2[ei:ei + 1, :]),
                    waits=[(fe_pe, prev_mark)], inc=s_w2[dbl], inck=16)
            nsl = 8 if shared else 4
            wid = 1024 if shared else 512
            wt1 = ws1_sb if shared else w1e[dbl]
            wt3 = ws3_sb if shared else w3e[dbl]
            wt2 = ws2_sb if shared else w2e[dbl]
            h_done = []
            for sl in range(nsl):
                b1 = S[6] if sl % 2 == 0 else S[5]
                b3 = S[4] if sl % 2 == 0 else S[3]
                lagw = ([(fe_act, h_done[sl - 2][0]), (fe_dve, h_done[sl - 2][1])]
                        if sl >= 2 else
                        (W(fe_act, fe_dve) if sl == 0 else []))
                for c in range(8):
                    k.op("tensor", lambda e, c=c, sl=sl, wt1=wt1, wid=wid, b1=b1:
                         e.matmul(b1[:, 0:512],
                                  wt1[:, c * wid + sl * 128: c * wid + (sl + 1) * 128],
                                  xh[:, c * 512:(c + 1) * 512],
                                  start=(c == 0), stop=False),
                         waits=(lagw + ([lnwg] if sl == 0 else [])
                                + ([(s_w2[dbl], wW + 336), (s_cs2[dbl], csW + 32)]
                                   if (not shared and sl == 0) else []))
                         if c == 0 else ())
                csrow = css1 if shared else cs1t[dbl]
                k.op("tensor", lambda e, sl=sl, csrow=csrow, b1=b1: e.matmul(
                    b1[:, 0:512], csrow[0:1, sl * 128:(sl + 1) * 128],
                    negmr[:], start=False, stop=True),
                    inc=fe_pe)
                for c in range(8):
                    k.op("tensor", lambda e, c=c, sl=sl, wt3=wt3, wid=wid, b3=b3:
                         e.matmul(b3[:, 0:512],
                                  wt3[:, c * wid + sl * 128: c * wid + (sl + 1) * 128],
                                  xh[:, c * 512:(c + 1) * 512],
                                  start=(c == 0), stop=False),
                         waits=lagw if c == 0 else ())
                csrow3 = css3 if shared else cs3t[dbl]
                k.op("tensor", lambda e, sl=sl, csrow3=csrow3, b3=b3: e.matmul(
                    b3[:, 0:512], csrow3[0:1, sl * 128:(sl + 1) * 128],
                    negmr[:], start=False, stop=True),
                    inc=fe_pe)
                bcol1 = bs1c[:, sl:sl + 1] if shared else be1t[:, ei * 4 + sl: ei * 4 + sl + 1]
                bcol3 = bs3c[:, sl:sl + 1] if shared else be3t[:, ei * 4 + sl: ei * 4 + sl + 1]
                peW = (fe_pe, fe_pe.n)
                k.op("scalar", lambda e, bcol1=bcol1, b1=b1: e.activation(
                    h1sb[:], b1[:, 0:512], AF.Silu, bias=bcol1, scale=1.0),
                    waits=[peW, (s_b, s_b.n), (fe_dve, fe_dve.n)], inc=fe_act)
                k.op("vector", lambda e, bcol3=bcol3, b3=b3: e.tensor_scalar(
                    h3scr[:], b3[:, 0:512], bcol3, None, op0=ALU.add),
                    waits=[peW, (s_b, s_b.n), (fe_dve, fe_dve.n)], inc=fe_dve)
                k.op("vector", lambda e, sl=sl: e.tensor_mul(
                    h_all[:, sl * 512:(sl + 1) * 512], h1sb[:], h3scr[:]),
                    waits=W(fe_act, fe_dve), inc=fe_dve)
                h_done.append((fe_act.n, fe_dve.n))
            hW = (fe_dve, fe_dve.n)
            yo_done = []
            for tt in range(4):
                for dh in range(2):
                    u = tt * 2 + dh
                    bank = S[u % 3]
                    lag2 = ([(fe_dve, yo_done[u - 3])] if u >= 3 else [])
                    for sl in range(nsl):
                        k.op("tensor", lambda e, sl=sl, tt=tt, dh=dh, bank=bank,
                             wt2=wt2:
                             e.matmul(bank[:],
                                      h_all[:, sl * 512 + tt * 128: sl * 512 + tt * 128 + 128],
                                      wt2[:, sl * 1024 + dh * 512: sl * 1024 + (dh + 1) * 512],
                                      start=(sl == 0), stop=False),
                             waits=[hW] + lag2 if sl == 0 else ())
                    berow = bs2r if shared else be2t[dbl]
                    k.op("tensor", lambda e, dh=dh, bank=bank, berow=berow: e.matmul(
                        bank[:], cn["ones_row"][0:1, 0:128],
                        berow[0:1, dh * 512:(dh + 1) * 512],
                        start=False, stop=True), inc=fe_pe)
                    if shared:
                        k.op("vector", lambda e, tt=tt, dh=dh, bank=bank: e.tensor_add(
                            ysb[:, tt * 1024 + dh * 512: tt * 1024 + (dh + 1) * 512],
                            ysb[:, tt * 1024 + dh * 512: tt * 1024 + (dh + 1) * 512],
                            bank[:]), waits=W(fe_pe), inc=fe_dve)
                    else:
                        k.op("vector", lambda e, tt=tt, dh=dh, ei=ei, bank=bank:
                             e.scalar_tensor_tensor(
                                ysb[:, tt * 1024 + dh * 512: tt * 1024 + (dh + 1) * 512],
                                bank[:],
                                comb[:, tt * 16 + ei: tt * 16 + ei + 1],
                                ysb[:, tt * 1024 + dh * 512: tt * 1024 + (dh + 1) * 512],
                                op0=ALU.mult, op1=ALU.add),
                             waits=W(fe_pe), inc=fe_dve)
                    yo_done.append(fe_dve.n)
        ydW = (fe_dve, fe_dve.n)
        for ti in range(4):
            k.op("vector", lambda e, ti=ti: e.tensor_copy(
                h_all[:, ti * 1024:(ti + 1) * 1024], ysb[:, ti * 1024:(ti + 1) * 1024]),
                waits=[ydW] + ([(res_pe, prev_res)] if ti == 0 else []), inc=fe_dve)
        for c in range(8):
            n = g * 8 + c
            b = n % 2
            k.op("sync", lambda e, c=c, g=g, b=b: e.dma_start(
                out=xrg[b][:], in_=xT[c * 128:(c + 1) * 128,
                                      g * 512:(g + 1) * 512]),
                waits=[(st_dve, max(0, st_dve.n - 1))], inc=gxr[b], inck=16)
            for ti in range(4):
                k.op("tensor", lambda e, c=c, ti=ti: e.transpose(
                    PB[:, ti * 128:(ti + 1) * 128],
                    h_all[:, ti * 1024 + c * 128: ti * 1024 + (c + 1) * 128],
                    cn["ident_bf"][:]),
                    waits=[(fe_dve, fe_dve.n), (res_pe, res_pe.n),
                           (st_dve, st_dve.n)] if ti == 0 else (),
                    inc=res_pe)
            k.op("vector", lambda e, b=b: e.tensor_add(
                x4o[b][:], PB[:, 0:512], xrg[b][:]),
                waits=[(res_pe, res_pe.n), (gxr[b], gxr[b].n), (gst[b], gst[b].n)],
                inc=st_dve)
            k.op("scalar", lambda e, c=c, g=g, b=b: e.dma_start(
                out=x4T[c * 128:(c + 1) * 128, g * 512:(g + 1) * 512],
                in_=x4o[b][:]), waits=[(st_dve, st_dve.n)], inc=gst[b], inck=16)
        prev_res = res_pe.n


def build_l1_ffn_program(T):
    nc = bass.Bass("TRN2", target_bir_lowering=False)
    xT = nc.dram_tensor("xT", [1024, T], F32, kind="ExternalInput").ap()
    gwT = nc.dram_tensor("gwT", [1024, 16], F32, kind="ExternalInput").ap()
    gbR = nc.dram_tensor("gbR", [1, 16], F32, kind="ExternalInput").ap()
    WE1 = nc.dram_tensor("WE1", [16 * 1024, 512], F32, kind="ExternalInput").ap()
    BE1 = nc.dram_tensor("BE1", [16 * 512, 1], F32, kind="ExternalInput").ap()
    WE2 = nc.dram_tensor("WE2", [16 * 512, 1024], F32, kind="ExternalInput").ap()
    BE2 = nc.dram_tensor("BE2", [16, 1024], F32, kind="ExternalInput").ap()
    WE3 = nc.dram_tensor("WE3", [16 * 1024, 512], F32, kind="ExternalInput").ap()
    BE3 = nc.dram_tensor("BE3", [16 * 512, 1], F32, kind="ExternalInput").ap()
    ws1 = nc.dram_tensor("ws1", [1024, 1024], F32, kind="ExternalInput").ap()
    bs1 = nc.dram_tensor("bs1", [1024, 1], F32, kind="ExternalInput").ap()
    ws2 = nc.dram_tensor("ws2", [1024, 1024], F32, kind="ExternalInput").ap()
    bs2 = nc.dram_tensor("bs2", [1, 1024], F32, kind="ExternalInput").ap()
    ws3 = nc.dram_tensor("ws3", [1024, 1024], F32, kind="ExternalInput").ap()
    bs3 = nc.dram_tensor("bs3", [1024, 1], F32, kind="ExternalInput").ap()
    x4T = nc.dram_tensor("x4T", [1024, T], F32, kind="ExternalOutput").ap()
    ctx = contextlib.ExitStack()
    with ctx:
        E = ctx.enter_context
        k = K(nc, E)
        env = Env(nc, E, k)
        emit_l1_ffn(env, T, xT, x4T, gwT, gbR, WE1, BE1, WE2, BE2, WE3, BE3,
                    ws1, bs1, ws2, bs2, ws3, bs3)
        k.final_barrier()
        block = E(nc.Block())
        k.finish(block)
    return nc


def _l1f_feed(x, b, p):
    m = dict(make_const_arrays())
    m["xT"] = np.ascontiguousarray(x[b].T)
    m["gwT"] = np.ascontiguousarray(p["gate_w"].T)
    m["gbR"] = np.ascontiguousarray(p["gate_b"][None, :])
    m["WE1"] = np.ascontiguousarray(p["We1"].reshape(16 * 1024, 512))
    m["BE1"] = np.ascontiguousarray(p["be1"].reshape(16 * 512, 1))
    m["WE2"] = np.ascontiguousarray(p["We2"].reshape(16 * 512, 1024))
    m["BE2"] = np.ascontiguousarray(p["be2"])
    m["WE3"] = np.ascontiguousarray(p["We3"].reshape(16 * 1024, 512))
    m["BE3"] = np.ascontiguousarray(p["be3"].reshape(16 * 512, 1))
    m["ws1"] = np.ascontiguousarray(p["ws1"])
    m["bs1"] = np.ascontiguousarray(p["bs1"][:, None])
    m["ws2"] = np.ascontiguousarray(p["ws2"])
    m["bs2"] = np.ascontiguousarray(p["bs2"][None, :])
    m["ws3"] = np.ascontiguousarray(p["ws3"])
    m["bs3"] = np.ascontiguousarray(p["bs3"][:, None])
    return m


def run_l1_ffn(x, p):
    B, T, D = x.shape
    nc = build_l1_ffn_program(T)
    in_maps = [_l1f_feed(x, b, p) for b in range(B)]
    res = run_bass_kernel_spmd(nc, in_maps, core_ids=list(range(B)))
    return np.stack([r["x4T"].T for r in res.results], axis=0)


def sim_l1_ffn(x, p):
    from concourse.bass_interp import CoreSim
    B, T, D = x.shape
    assert B == 1
    nc = build_l1_ffn_program(T)
    nc.finalize()
    sim = CoreSim(nc)
    feed = _l1f_feed(x, 0, p)
    for name, arr in feed.items():
        v = sim.tensor(name)
        v[:] = arr.view(v.dtype) if arr.dtype == NP_BF16 else arr
    sim.simulate()
    return sim.tensor("x4T").T[None].copy()



def kernel(**inp):
    T = 4096
    B = 8
    cap = 4096
    consts = make_const_arrays()
    x = np.asarray(inp["L0_attn_x"])

    def maps(extra):
        out = []
        for b in range(B):
            m = dict(consts)
            m.update(extra(b))
            out.append(m)
        return out

    nc1 = build_l0_attn_program(T)
    wqkvT = np.ascontiguousarray(np.asarray(inp["L0_attn_w_qkv"]).T)
    woutT = np.ascontiguousarray(np.asarray(inp["L0_attn_w_out"]).T)
    gcol = np.ascontiguousarray(np.asarray(inp["L0_attn_gamma"])[:, None])
    r1 = run_bass_kernel_spmd(nc1, maps(lambda b: {
        "xT": np.ascontiguousarray(x[b].T), "wqkvT": wqkvT, "woutT": woutT,
        "gammacol": gcol}), core_ids=list(range(B)))

    nc2 = build_l0_router_program(T, capl=CAPL)
    Wr = np.ascontiguousarray(np.asarray(inp["L0_ffn_Wr"]))
    r2 = run_bass_kernel_spmd(nc2, maps(lambda b: {
        "xT": r1.results[b]["x1T"], "Wr": Wr}), core_ids=list(range(B)))
    counts_all = np.stack([r2.results[b]["counts"][0] for b in range(B)], 0)

    nc3 = build_l0_rest_program(T, cap, capl=CAPL)
    W1 = np.ascontiguousarray(np.asarray(inp["L0_ffn_W1"]).reshape(8 * 1024, 4096))
    W2 = np.ascontiguousarray(np.asarray(inp["L0_ffn_W2"]).reshape(8 * 4096, 1024))
    r3 = run_bass_kernel_spmd(nc3, maps(lambda b: {
        "xT": r1.results[b]["x1T"], "W1": W1, "W2": W2,
        "xe": r2.results[b]["xe"], "idx": r2.results[b]["idx"],
        "ptok": r2.results[b]["ptok"], "lrloc": r2.results[b]["lrloc"],
        "oneh": r2.results[b]["oneh"], "countsA": counts_all,
        "corelt": (np.arange(8) < b).astype(np.float32)[:, None]}),
        core_ids=list(range(B)))

    nc4 = build_l1_attn_program(T)
    l1w = {n: np.ascontiguousarray(np.asarray(inp["L1_attn_w_" + n]).T)
           for n in ("q", "k", "v", "f", "o")}
    nwc = np.ascontiguousarray(np.asarray(inp["L1_attn_norm_weight"])[:, None])
    gnc = np.ascontiguousarray(np.asarray(inp["L1_attn_gnorm_weight"])[:, None])
    r4 = run_bass_kernel_spmd(nc4, maps(lambda b: {
        "xT": r3.results[b]["x2T"], "nw": nwc, "gn": gnc,
        "wq": l1w["q"], "wk": l1w["k"], "wv": l1w["v"], "wf": l1w["f"],
        "wo": l1w["o"]}), core_ids=list(range(B)))

    nc5 = build_l1_ffn_program(T)
    p5 = {
        "gwT": np.ascontiguousarray(np.asarray(inp["L1_ffn_gate_w"]).T),
        "gbR": np.ascontiguousarray(np.asarray(inp["L1_ffn_gate_b"])[None, :]),
        "WE1": np.ascontiguousarray(np.asarray(inp["L1_ffn_We1"]).reshape(16 * 1024, 512)),
        "BE1": np.ascontiguousarray(np.asarray(inp["L1_ffn_be1"]).reshape(16 * 512, 1)),
        "WE2": np.ascontiguousarray(np.asarray(inp["L1_ffn_We2"]).reshape(16 * 512, 1024)),
        "BE2": np.ascontiguousarray(np.asarray(inp["L1_ffn_be2"])),
        "WE3": np.ascontiguousarray(np.asarray(inp["L1_ffn_We3"]).reshape(16 * 1024, 512)),
        "BE3": np.ascontiguousarray(np.asarray(inp["L1_ffn_be3"]).reshape(16 * 512, 1)),
        "ws1": np.ascontiguousarray(np.asarray(inp["L1_ffn_ws1"])),
        "bs1": np.ascontiguousarray(np.asarray(inp["L1_ffn_bs1"])[:, None]),
        "ws2": np.ascontiguousarray(np.asarray(inp["L1_ffn_ws2"])),
        "bs2": np.ascontiguousarray(np.asarray(inp["L1_ffn_bs2"])[None, :]),
        "ws3": np.ascontiguousarray(np.asarray(inp["L1_ffn_ws3"])),
        "bs3": np.ascontiguousarray(np.asarray(inp["L1_ffn_bs3"])[:, None]),
    }
    r5 = run_bass_kernel_spmd(nc5, maps(lambda b: dict(p5, xT=r4.results[b]["x3T"])),
                              core_ids=list(range(B)))
    return np.stack([r5.results[b]["x4T"].T for b in range(B)], axis=0).astype(np.float32)


# version 03
# speedup vs baseline: 1.1881x; 1.0238x over previous
import contextlib
import numpy as np
import ml_dtypes
import concourse.bass as bass
import concourse.mybir as mybir
from concourse.bass_utils import run_bass_kernel_spmd
from concourse.bass_core_rust import IndirectOffsetOnAxis

F32 = mybir.dt.float32
BF16 = mybir.dt.bfloat16
I32 = mybir.dt.int32
AF = mybir.ActivationFunctionType
ALU = mybir.AluOpType
AX = mybir.AxisListType
NP_BF16 = ml_dtypes.bfloat16

ENGINES = ("sync", "vector", "scalar", "tensor", "gpsimd")
SCALE_ATTN = 128.0 ** -0.5


class Sem:
    def __init__(self, handle, name):
        self.h = handle
        self.n = 0
        self.name = name
        self.eng = None

    def now(self):
        return (self, self.n)


class K:

    def __init__(self, nc, E):
        self.nc = nc
        self.E = E
        self.q = {e: [] for e in ENGINES}
        self._semi = 0
        self.dma_sems = set()

    def sem(self, name=None):
        name = name or f"s{self._semi}"
        self._semi += 1
        h = self.E(self.nc.semaphore(name))
        return Sem(h, name)

    def op(self, eng, emit, waits=(), inc=None, inck=1):
        wl = []
        for w in waits:
            if isinstance(w, Sem):
                wl.append((w, w.n))
            else:
                wl.append(w)
        if inc is not None:
            if inc.eng is None:
                inc.eng = eng
            assert inc.eng == eng, f"sem {inc.name} inc from {eng} and {inc.eng}"
            inc.n += inck
            if inck >= 16:
                self.dma_sems.add(inc)

        def run(e, emit=emit, wl=wl, inc=inc, inck=inck):
            last_wait = None
            for (s, v) in wl:
                if v > 0:
                    e.wait_ge(s.h, v)
            inst = emit(e)
            if inc is not None:
                assert inst is not None, "inc on an op that emitted nothing"
                inst.then_inc(inc.h, inck)
        self.q[eng].append(run)

    def final_barrier(self):
        self.op("sync", lambda e: None,
                waits=[(x, x.n) for x in sorted(self.dma_sems, key=lambda s: s.name)])

    def finish(self, block):
        for name in ENGINES:
            fns = self.q[name]
            if not fns:
                continue

            def body(e, fns=fns):
                for f in fns:
                    f(e)
            getattr(block, name)(body)



def make_const_arrays():
    c = {}
    c["ones_col"] = np.ones((128, 1), NP_BF16)
    c["ones_row"] = np.ones((1, 128), NP_BF16)
    c["ident_bf"] = np.eye(128, dtype=NP_BF16)
    c["ident_f"] = np.eye(128, dtype=np.float32)
    am = np.zeros((128, 4 * 512), np.float32)
    for r in range(4):
        t = np.arange(128)[:, None]
        kk = np.arange(512)[None, :]
        am[:, r * 512:(r + 1) * 512] = np.where(kk <= 128 * r + t, 0.0, -1e38)
    c["am"] = am
    mmk = np.zeros((128, 4 * 512), np.float32)
    for r in range(4):
        kk = np.arange(128)[:, None]
        t = np.arange(512)[None, :]
        mmk[:, r * 512:(r + 1) * 512] = (t >= 128 * r + kk).astype(np.float32)
    c["mm"] = mmk.astype(NP_BF16)
    c["ones_col_f"] = np.ones((128, 1), np.float32)
    c["ones_row_f"] = np.ones((1, 128), np.float32)
    j = np.arange(128)[:, None]
    t = np.arange(128)[None, :]
    c["strictL_f"] = (j < t).astype(np.float32)
    c["tri_incl_f"] = (j <= t).astype(np.float32)
    c["tri_revx_f"] = (j > t).astype(np.float32)
    return c

CONST_SPECS = [
    ("ones_col", [128, 1], BF16), ("ones_row", [1, 128], BF16),
    ("ident_bf", [128, 128], BF16), ("ident_f", [128, 128], F32),
    ("am", [128, 2048], F32), ("mm", [128, 2048], BF16),
    ("ones_col_f", [128, 1], F32), ("ones_row_f", [1, 128], F32),
    ("strictL_f", [128, 128], F32), ("tri_incl_f", [128, 128], F32),
    ("tri_revx_f", [128, 128], F32),
]


class Env:

    def __init__(self, nc, E, k: K):
        self.nc, self.E, self.k = nc, E, k
        self.ld = k.sem("ld_const")
        self.cn = {}
        for name, shape, dt in CONST_SPECS:
            d = nc.dram_tensor(name, shape, dt, kind="ExternalInput").ap()
            s = E(nc.sbuf_tensor(name + "_s", shape, dt))
            self.cn[name] = s
            k.op("sync", lambda e, d=d, s=s: e.dma_start(out=s[:], in_=d),
                 inc=self.ld, inck=16)
        self.S = [E(nc.psum_tensor(f"psb{i}", [128, 512], F32)) for i in range(7)]
        self.PB = E(nc.psum_tensor("psbf", [128, 512], BF16))



def emit_l0_attn(env: Env, T, xT, x1T, wqkvT, woutT, gammacol):
    nc, E, k, cn = env.nc, env.E, env.k, env.cn
    S, PB = env.S, env.PB
    G = T // 512
    QT = T // 128
    NH = 8

    xh = E(nc.sbuf_tensor("xh", [128, 8 * T], BF16))
    xb2 = [E(nc.sbuf_tensor(f"xb{i}", [128, 8 * 512], BF16)) for i in range(2)]
    sq = [E(nc.sbuf_tensor(f"sq{i}", [128, 512], BF16)) for i in range(2)]
    gcol_f = E(nc.sbuf_tensor("gcol_f", [128, 8], F32))
    gcol_b = E(nc.sbuf_tensor("gcol_b", [128, 8], BF16))
    m_row = E(nc.sbuf_tensor("m_row", [1, 512], F32))
    m_b = E(nc.sbuf_tensor("m_b", [1, 512], BF16))
    xtmp = E(nc.sbuf_tensor("xtmp", [128, 512], F32))
    q_row = E(nc.sbuf_tensor("q_row", [1, 512], F32))
    var_row = E(nc.sbuf_tensor("var_row", [1, 512], F32))
    rstd_row = E(nc.sbuf_tensor("rstd_row", [1, 512], F32))
    rstd_b = E(nc.sbuf_tensor("rstd_b", [1, 512], BF16))
    wq2 = [E(nc.sbuf_tensor(f"wq{i}", [128, 8 * 128], BF16)) for i in range(2)]
    wk2 = [E(nc.sbuf_tensor(f"wk{i}", [128, 8 * 128], BF16)) for i in range(2)]
    wv2 = [E(nc.sbuf_tensor(f"wv{i}", [128, 8 * 128], BF16)) for i in range(2)]
    wout_sb = E(nc.sbuf_tensor("wout_sb", [128, 8 * 1024], BF16))
    Qf = E(nc.sbuf_tensor("Qf", [128, T], BF16))
    Kf = E(nc.sbuf_tensor("Kf", [128, T], BF16))
    Vt = E(nc.sbuf_tensor("Vt", [128, T], BF16))
    O_dram = nc.dram_tensor("O_dram", [1024, T], BF16, kind="Internal").ap()
    ost = [E(nc.sbuf_tensor(f"ost{i}", [128, 512], BF16)) for i in range(2)]
    otile = [[E(nc.sbuf_tensor(f"otile{b}_{hh}", [128, 512], BF16)) for hh in range(8)]
             for b in range(2)]
    esc = [E(nc.sbuf_tensor(f"esc{i}", [128, 512], BF16)) for i in range(2)]
    ET = [E(nc.sbuf_tensor(f"ET{i}", [128, 512], BF16)) for i in range(2)]
    nmcols = E(nc.sbuf_tensor("nmcols", [128, 8], F32))
    lcols = E(nc.sbuf_tensor("lcols", [128, 8], F32))
    nm_p = E(nc.sbuf_tensor("nm_p", [128, 1], F32))
    nm_f = E(nc.sbuf_tensor("nm_f", [128, 1], F32))
    nmneg = E(nc.sbuf_tensor("nmneg", [128, 1], F32))
    resc = E(nc.sbuf_tensor("resc", [128, 1], F32))
    lsum = E(nc.sbuf_tensor("lsum", [128, 1], F32))
    lnl = E(nc.sbuf_tensor("lnl", [128, 1], F32))
    negml = E(nc.sbuf_tensor("negml", [128, 1], F32))
    rcol = E(nc.sbuf_tensor("rcol", [128, 1], F32))
    nbcols = E(nc.sbuf_tensor("nbcols", [128, 32], BF16))
    rbcols = E(nc.sbuf_tensor("rbcols", [128, 32], BF16))
    nrow = E(nc.sbuf_tensor("nrow", [1, 512], BF16))
    rrow = E(nc.sbuf_tensor("rrow", [1, 512], BF16))
    xre = [E(nc.sbuf_tensor(f"xre{i}", [128, 512], F32)) for i in range(4)]
    yout = [E(nc.sbuf_tensor(f"yout{i}", [128, 512], F32)) for i in range(4)]

    s_ldg = k.sem()
    s_ldw = k.sem()
    s_ldx = k.sem()
    s_pe = k.sem()
    s_act = k.sem()
    s_dve = k.sem()
    s_pool = k.sem()
    s_st = k.sem()
    s_xre = k.sem()

    W = lambda *ss: [s.now() for s in ss]
    cw = (env.ld, env.ld.n)

    for c in range(8):
        k.op("sync", lambda e, c=c: e.dma_start(
            out=gcol_f[:, c:c + 1], in_=gammacol[c * 128:(c + 1) * 128, 0:1]),
            inc=s_ldg, inck=16)
    k.op("vector", lambda e: e.tensor_copy(gcol_b[:], gcol_f[:]),
         waits=W(s_ldg) + [cw], inc=s_dve)
    for c in range(8):
        k.op("gpsimd", lambda e, c=c: e.dma_start(
            out=wout_sb[:, c * 1024:(c + 1) * 1024],
            in_=woutT[c * 128:(c + 1) * 128, :]), inc=s_ldw, inck=16)

    ln_done = k.sem()
    lnd_marks = [0]
    for g in range(G):
        xb = xb2[g % 2]
        ldg = s_ldx.n
        for c in range(8):
            k.op("gpsimd", lambda e, c=c, g=g, xb=xb: e.dma_start(
                out=xb[:, c * 512:(c + 1) * 512],
                in_=xT[c * 128:(c + 1) * 128, g * 512:(g + 1) * 512]),
                waits=[(ln_done, lnd_marks[max(0, g - 1)])] if g > 1 else (),
                inc=s_ldx, inck=16)
        for c in range(8):
            k.op("tensor", lambda e, c=c: e.matmul(
                S[0][0:1, :], cn["ones_col"][:], xb[:, c * 512:(c + 1) * 512],
                start=(c == 0), stop=(c == 7)),
                waits=([(s_ldx, ldg + 128)] + ([cw] if g == 0 else [])) if c == 0 else (),
                inc=s_pe if c == 7 else None)
        for c in range(8):
            k.op("scalar", lambda e, c=c, xb=xb: e.activation(
                sq[c % 2][:], xb[:, c * 512:(c + 1) * 512], AF.Square,
                bias=0.0, scale=1.0),
                waits=[(s_ldx, ldg + 128)] + (W(s_pe) if c >= 2 else []),
                inc=s_act)
            k.op("tensor", lambda e, c=c: e.matmul(
                S[1][0:1, :], cn["ones_col"][:], sq[c % 2][:],
                start=(c == 0), stop=(c == 7)),
                waits=W(s_act), inc=s_pe)
        k.op("vector", lambda e: e.tensor_scalar(
            m_row[:], S[0][0:1, :], 1.0 / 1024, None, op0=ALU.mult),
            waits=W(s_pe), inc=s_dve)
        k.op("vector", lambda e: e.tensor_scalar(
            q_row[:], S[1][0:1, :], 1.0 / 1024, None, op0=ALU.mult),
            inc=s_dve)
        k.op("vector", lambda e: e.tensor_mul(var_row[:], m_row[:], m_row[:]),
             waits=W(s_dve), inc=s_dve)
        k.op("vector", lambda e: e.tensor_sub(var_row[:], q_row[:], var_row[:]),
             waits=W(s_dve), inc=s_dve)
        k.op("scalar", lambda e: e.activation(rstd_row[:], var_row[:], AF.Ln,
                                              bias=0.0, scale=1.0),
             waits=W(s_dve), inc=s_act)
        k.op("vector", lambda e: e.tensor_scalar(
            rstd_row[:], rstd_row[:], -0.5, None, op0=ALU.mult),
            waits=W(s_act), inc=s_dve)
        k.op("scalar", lambda e: e.activation(rstd_row[:], rstd_row[:], AF.Exp,
                                              bias=0.0, scale=1.0),
             waits=W(s_dve), inc=s_act)
        k.op("vector", lambda e: e.tensor_copy(rstd_b[:], rstd_row[:]),
             waits=W(s_act), inc=s_dve)
        k.op("vector", lambda e: e.tensor_copy(m_b[:], m_row[:]),
             waits=W(s_dve), inc=s_dve)
        k.op("tensor", lambda e: e.matmul(
            S[2][:], cn["ones_row"][:], rstd_b[:], start=True, stop=True),
            waits=W(s_dve) + [(ln_done, ln_done.n)], inc=s_pe)
        k.op("tensor", lambda e: e.matmul(
            S[3][:], cn["ones_row"][:], m_b[:], start=True, stop=True),
            inc=s_pe)
        for c in range(8):
            k.op("vector", lambda e, c=c, xb=xb: e.tensor_sub(
                xtmp[:], xb[:, c * 512:(c + 1) * 512], S[3][:]),
                waits=W(s_pe) if c == 0 else W(s_dve), inc=s_dve)
            k.op("vector", lambda e, c=c, g=g: e.scalar_tensor_tensor(
                xh[:, c * T + g * 512: c * T + (g + 1) * 512],
                xtmp[:], gcol_b[:, c:c + 1], S[2][:],
                op0=ALU.mult, op1=ALU.mult),
                waits=W(s_dve), inc=(ln_done if c == 7 else s_dve))
        lnd_marks.append(ln_done.n)

    wsem = k.sem()
    pj = k.sem()
    p1pe = k.sem()
    p1dve = k.sem()
    p1act = k.sem()
    p1pool = k.sem()
    st_dve = k.sem()
    p2pe = k.sem()
    p2act = k.sem()
    p2dve = k.sem()
    pvpe = k.sem()
    oact = k.sem()
    trpe = k.sem()
    trevac = k.sem()
    v_ev = k.sem()
    sodma = [k.sem() for _ in range(2)]

    pe_marks = [0, 0]
    for h in range(NH):
        wq, wk, wv = wq2[h % 2], wk2[h % 2], wv2[h % 2]
        for c in range(8):
            for wt, off in ((wq, h * 128), (wk, 1024 + h * 128), (wv, 2048 + h * 128)):
                k.op("gpsimd", lambda e, c=c, wt=wt, off=off: e.dma_start(
                    out=wt[:, c * 128:(c + 1) * 128],
                    in_=wqkvT[c * 128:(c + 1) * 128, off:off + 128]),
                    waits=[(s_pe, pe_marks[h % 2])] if (c == 0 and wt is wq) else (),
                    inc=wsem, inck=16)
        pe_marks[h % 2] = None
        lnw = (ln_done, G)
        for g in range(G):
            for (dst, wt, scale) in ((Qf, wq, SCALE_ATTN), (Kf, wk, 1.0)):
                bank = S[4 if dst is Qf else 5]
                for c in range(8):
                    k.op("tensor", lambda e, c=c, wt=wt, g=g, bank=bank: e.matmul(
                        bank[:], wt[:, c * 128:(c + 1) * 128],
                        xh[:, c * T + g * 512: c * T + (g + 1) * 512],
                        start=(c == 0), stop=(c == 7)),
                        waits=[lnw, (wsem, wsem.n)] + W(pj, p1act, s_act)
                        if c == 0 else (),
                        inc=s_pe if c == 7 else None)
                k.op("scalar", lambda e, dst=dst, g=g, scale=scale, bank=bank:
                     e.activation(dst[:, g * 512:(g + 1) * 512], bank[:],
                                  AF.Copy, bias=0.0, scale=scale),
                     waits=W(s_pe), inc=pj)
        vbase = v_ev.n
        for t in range(QT):
            vb = S[6] if t % 2 == 0 else S[3]
            for c in range(8):
                k.op("tensor", lambda e, c=c, t=t, vb=vb, wv=wv: e.matmul(
                    vb[:, 0:128], xh[:, c * T + t * 128: c * T + (t + 1) * 128],
                    wv[:, c * 128:(c + 1) * 128], start=(c == 0), stop=(c == 7)),
                    waits=([lnw, (v_ev, vbase + max(0, t - 1)), (wsem, wsem.n)]
                           + (W(pj, p1act, s_act) if t <= 1 else []))
                    if c == 0 else (),
                    inc=s_pe if c == 7 else None)
            k.op("scalar", lambda e, t=t, vb=vb: e.activation(
                Vt[:, t * 128:(t + 1) * 128], vb[:, 0:128], AF.Copy,
                bias=0.0, scale=1.0), waits=W(s_pe), inc=v_ev)
        pe_marks[h % 2] = s_pe.n

        pjW = (pj, pj.n)
        for i in range(QT):
            nch = i // 4 + 1
            n1 = min(nch, 7)
            for g2 in range(n1):
                k.op("tensor", lambda e, i=i, g2=g2: e.matmul(
                    S[g2][:, 0:min(512, T - g2 * 512)], Qf[:, i * 128:(i + 1) * 128],
                    Kf[:, g2 * 512:min(T, (g2 + 1) * 512)], start=True, stop=True),
                    waits=[pjW] + (W(p1act, oact) if i > 0 else []),
                    inc=p1pe)
                if g2 == i // 4:
                    k.op("vector", lambda e, i=i, g2=g2: e.tensor_add(
                        S[g2][:], S[g2][:], cn["am"][:, (i % 4) * 512:(i % 4 + 1) * 512]),
                        waits=W(p1pe), inc=p1dve)
                k.op("vector", lambda e, g2=g2: e.reduce_max(
                    nmcols[:, g2:g2 + 1], S[g2][:], axis=AX.X, negate=True),
                    waits=W(p1pe, p1dve), inc=p1dve)
            k.op("vector", lambda e, n1=n1: e.tensor_reduce(
                nm_p[:], nmcols[:, 0:n1], axis=AX.X, op=ALU.min),
                waits=W(p1dve), inc=p1pool)
            for g2 in range(n1):
                k.op("scalar", lambda e, g2=g2: e.activation(
                    esc[g2 % 2][:], S[g2][:], AF.Exp, bias=nm_p[:, 0:1], scale=1.0,
                    accum_out=lcols[:, g2:g2 + 1]),
                    waits=W(p1pool), inc=p1act)
            if nch > n1:
                g2 = 7
                k.op("tensor", lambda e, i=i, g2=g2: e.matmul(
                    S[0][:], Qf[:, i * 128:(i + 1) * 128],
                    Kf[:, g2 * 512:(g2 + 1) * 512], start=True, stop=True),
                    waits=W(p1act), inc=p1pe)
                k.op("vector", lambda e, i=i: e.tensor_add(
                    S[0][:], S[0][:], cn["am"][:, (i % 4) * 512:(i % 4 + 1) * 512]),
                    waits=W(p1pe), inc=p1dve)
                k.op("vector", lambda e: e.reduce_max(
                    nmcols[:, 7:8], S[0][:], axis=AX.X, negate=True), inc=p1dve)
                k.op("vector", lambda e: e.tensor_tensor(
                    nm_f[:], nm_p[:], nmcols[:, 7:8], op=ALU.min),
                    waits=W(p1dve, p1pool), inc=p1pool)
                k.op("scalar", lambda e: e.activation(
                    esc[1][:], S[0][:], AF.Exp, bias=nm_f[:, 0:1], scale=1.0,
                    accum_out=lcols[:, 7:8]),
                    waits=W(p1pool), inc=p1act)
                k.op("vector", lambda e: e.tensor_scalar(
                    nmneg[:], nm_p[:], -1.0, None, op0=ALU.mult),
                    waits=W(p1pool), inc=p1dve)
                k.op("scalar", lambda e: e.activation(
                    resc[:], nm_f[:], AF.Exp, bias=nmneg[:, 0:1], scale=1.0),
                    waits=W(p1dve), inc=p1act)
                k.op("vector", lambda e: e.reduce_sum(
                    lsum[:], lcols[:, 0:7], axis=AX.X), waits=W(p1act), inc=p1dve)
                k.op("vector", lambda e: e.scalar_tensor_tensor(
                    lsum[:], lsum[:], resc[:, 0:1], lcols[:, 7:8],
                    op0=ALU.mult, op1=ALU.add),
                    waits=W(p1dve, p1act), inc=p1pool)
                nm_use = nm_f
                lsum_src, lsum_wait = lsum, (p1pool, p1pool.n)
            else:
                k.op("vector", lambda e, n1=n1: e.reduce_sum(
                    lsum[:], lcols[:, 0:n1], axis=AX.X), waits=W(p1act), inc=p1dve)
                nm_use = nm_p
                lsum_src, lsum_wait = lsum, (p1dve, p1dve.n)
            k.op("scalar", lambda e: e.activation(
                lnl[:], lsum[:], AF.Ln, bias=0.0, scale=1.0),
                waits=[lsum_wait], inc=p1act)
            k.op("vector", lambda e, nm_use=nm_use: e.tensor_sub(
                negml[:], nm_use[:], lnl[:]), waits=W(p1act), inc=p1dve)
            k.op("scalar", lambda e, i=i: e.activation(
                nbcols[:, i:i + 1], negml[:], AF.Copy, bias=0.0, scale=1.0),
                waits=W(p1dve), inc=p1act)
            k.op("vector", lambda e, i=i: e.tensor_sub(
                rcol[:], negml[:], nbcols[:, i:i + 1]),
                waits=W(p1act, p1dve), inc=p1pool)
            k.op("vector", lambda e, i=i: e.tensor_copy(
                rbcols[:, i:i + 1], rcol[:]), waits=W(p1pool), inc=p1dve)

        p1W = [(p1dve, p1dve.n), (p1act, p1act.n), (p1pool, p1pool.n)]
        for g in range(G):
            for r in range(4):
                qt = 4 * g + r
                k.op("tensor", lambda e, qt=qt, r=r: e.transpose(
                    PB[0:1, r * 128:(r + 1) * 128], nbcols[:, qt:qt + 1],
                    cn["ident_bf"][:]),
                    waits=p1W + W(trevac) if r == 0 else (), inc=trpe)
            k.op("vector", lambda e: e.tensor_copy(nrow[:], PB[0:1, 0:512]),
                 waits=W(trpe), inc=trevac)
            for r in range(4):
                qt = 4 * g + r
                k.op("tensor", lambda e, qt=qt, r=r: e.transpose(
                    PB[0:1, r * 128:(r + 1) * 128], rbcols[:, qt:qt + 1],
                    cn["ident_bf"][:]),
                    waits=W(trevac) if r == 0 else (), inc=trpe)
            k.op("vector", lambda e: e.tensor_copy(rrow[:], PB[0:1, 0:512]),
                 waits=W(trpe), inc=trevac)
            trW = (trevac, trevac.n)
            nkt = 4 * g + 4
            expc = []
            pvc = []
            for j in range(nkt):
                bank = S[j % 2]
                k.op("tensor", lambda e, j=j, g=g, bank=bank: e.matmul(
                    bank[:], Kf[:, j * 128:(j + 1) * 128], Qf[:, g * 512:(g + 1) * 512],
                    start=True, stop=False),
                    waits=[trW] + ([(p2act, expc[j - 2])] if j >= 2 else W(p2act)))
                k.op("tensor", lambda e, bank=bank: e.matmul(
                    bank[:], cn["ones_row"][:], nrow[:], start=False, stop=False))
                k.op("tensor", lambda e, bank=bank: e.matmul(
                    bank[:], cn["ones_row"][:], rrow[:], start=False, stop=True),
                    inc=p2pe)
                r = j - 4 * g
                k.op("scalar", lambda e, j=j, bank=bank: e.activation(
                    ET[j % 2][:], bank[:], AF.Exp, bias=0.0, scale=1.0),
                    waits=W(p2pe) + ([(pvpe, pvc[j - 2])] if j >= 2 else W(pvpe)),
                    inc=p2act)
                if r >= 0:
                    k.op("vector", lambda e, j=j, r=r: e.tensor_mul(
                        ET[j % 2][:], ET[j % 2][:],
                        cn["mm"][:, r * 512:(r + 1) * 512]),
                        waits=W(p2act), inc=p2dve)
                    evW = W(p2dve)
                else:
                    evW = W(p2act)
                expc.append(p2act.n)
                k.op("tensor", lambda e, j=j, g=g: e.matmul(
                    S[2][:], Vt[:, j * 128:(j + 1) * 128], ET[j % 2][:],
                    start=(j == 0), stop=(j == nkt - 1)),
                    waits=evW, inc=pvpe)
                pvc.append(pvpe.n)
            m = h * G + g
            k.op("scalar", lambda e, m=m: e.activation(
                ost[m % 2][:], S[2][:], AF.Copy, bias=0.0, scale=1.0),
                waits=W(pvpe) + [(sodma[m % 2], sodma[m % 2].n)], inc=oact)
            k.op("gpsimd", lambda e, h=h, g=g, m=m: e.dma_start(
                out=O_dram[h * 128:(h + 1) * 128, g * 512:(g + 1) * 512],
                in_=ost[m % 2][:]),
                waits=W(oact), inc=sodma[m % 2], inck=16)

    odW = [(x, x.n) for x in sodma]
    sxre = [k.sem() for _ in range(4)]
    sst = [k.sem() for _ in range(4)]
    solod = k.sem()
    prj = k.sem()
    for g in range(G):
        ob = g % 2
        for hh in range(8):
            k.op("sync", lambda e, hh=hh, g=g, ob=ob: e.dma_start(
                out=otile[ob][hh][:],
                in_=O_dram[hh * 128:(hh + 1) * 128, g * 512:(g + 1) * 512]),
                waits=(odW + [(prj, max(0, (g - 1) * 8))]) if hh == 0 else (),
                inc=solod, inck=16)
        for c in range(8):
            n = g * 8 + c
            b = n % 4
            k.op("sync", lambda e, c=c, g=g, b=b: e.dma_start(
                out=xre[b][:], in_=xT[c * 128:(c + 1) * 128, g * 512:(g + 1) * 512]),
                waits=[(st_dve, max(0, n - 3))], inc=sxre[b], inck=16)
            bank = S[3 + (c % 2)]
            for hh in range(8):
                k.op("tensor", lambda e, c=c, g=g, hh=hh, bank=bank, ob=ob: e.matmul(
                    bank[:], wout_sb[:, hh * 1024 + c * 128: hh * 1024 + (c + 1) * 128],
                    otile[ob][hh][:],
                    start=(hh == 0), stop=(hh == 7)),
                    waits=[(solod, 16 * 8 * (g + 1)), (s_ldw, s_ldw.n),
                           (st_dve, max(0, n - 1))] if hh == 0 else (),
                    inc=prj if hh == 7 else None)
            k.op("vector", lambda e, b=b, c=c, bank=bank: e.tensor_add(
                yout[b][:], bank[:], xre[b][:]),
                waits=[(prj, prj.n), (sxre[b], sxre[b].n), (sst[b], sst[b].n)],
                inc=st_dve)
            k.op("scalar", lambda e, c=c, g=g, b=b: e.dma_start(
                out=x1T[c * 128:(c + 1) * 128, g * 512:(g + 1) * 512], in_=yout[b][:]),
                waits=[(st_dve, st_dve.n)], inc=sst[b], inck=16)
    return



def build_l0_attn_program(T):
    nc = bass.Bass("TRN2", target_bir_lowering=False)
    xT = nc.dram_tensor("xT", [1024, T], F32, kind="ExternalInput").ap()
    wqkvT = nc.dram_tensor("wqkvT", [1024, 3072], F32, kind="ExternalInput").ap()
    woutT = nc.dram_tensor("woutT", [1024, 1024], F32, kind="ExternalInput").ap()
    gammacol = nc.dram_tensor("gammacol", [1024, 1], F32, kind="ExternalInput").ap()
    x1T = nc.dram_tensor("x1T", [1024, T], F32, kind="ExternalOutput").ap()
    ctx = contextlib.ExitStack()
    with ctx:
        E = ctx.enter_context
        k = K(nc, E)
        env = Env(nc, E, k)
        emit_l0_attn(env, T, xT, x1T, wqkvT, woutT, gammacol)
        k.final_barrier()
        block = E(nc.Block())
        k.finish(block)
    return nc


def run_l0_attn(x, w_qkv, w_out, gamma, cores=1):
    B, T, D = x.shape
    nc = build_l0_attn_program(T)
    consts = make_const_arrays()
    in_maps = []
    for b in range(B):
        m = dict(consts)
        m["xT"] = np.ascontiguousarray(x[b].T)
        m["wqkvT"] = np.ascontiguousarray(w_qkv.T)
        m["woutT"] = np.ascontiguousarray(w_out.T)
        m["gammacol"] = np.ascontiguousarray(gamma[:, None])
        in_maps.append(m)
    res = run_bass_kernel_spmd(nc, in_maps, core_ids=list(range(B)))
    return np.stack([r["x1T"].T for r in res.results], axis=0)


def sim_l0_attn(x, w_qkv, w_out, gamma):
    from concourse.bass_interp import CoreSim
    B, T, D = x.shape
    assert B == 1
    nc = build_l0_attn_program(T)
    nc.finalize()
    sim = CoreSim(nc)
    consts = make_const_arrays()
    feed = dict(consts)
    feed["xT"] = np.ascontiguousarray(x[0].T)
    feed["wqkvT"] = np.ascontiguousarray(w_qkv.T)
    feed["woutT"] = np.ascontiguousarray(w_out.T)
    feed["gammacol"] = np.ascontiguousarray(gamma[:, None])
    for name, arr in feed.items():
        v = sim.tensor(name)
        v[:] = arr.view(v.dtype) if arr.dtype == NP_BF16 else arr
    sim.simulate()
    return sim.tensor("x1T").T[None].copy()



CAPL = 640


def emit_l0_router(env: Env, T, xT, WrAp, xe_dram, idx_out, ptok_out, lrloc_out,
                   oneh_out, counts_out, capl=CAPL, lite=False):
    nc, E, k, cn = env.nc, env.E, env.k, env.cn
    S, PB = env.S, env.PB
    G = T // 512
    NT = T // 128

    wr_sb = E(nc.sbuf_tensor("f_wr", [128, 8 * 8], F32))
    cswr = E(nc.sbuf_tensor("f_cswr", [1, 8], F32))
    slotb = E(nc.sbuf_tensor("f_slotb", [1, 8], F32))
    offs = [E(nc.sbuf_tensor(f"f_offs{i}", [1, 8], F32)) for i in range(2)]
    pk_all = E(nc.sbuf_tensor("f_pk", [128, NT], F32))
    idx_all = E(nc.sbuf_tensor("f_idx", [128, NT], I32))
    lr_all = E(nc.sbuf_tensor("f_lr", [128, NT], F32))
    oh_all = E(nc.sbuf_tensor("f_oh", [128, NT * 8], F32))

    s_ldw = k.sem()
    rt_pe = k.sem()
    rt_dve = k.sem()
    rt_act = k.sem()
    rt_sc = k.sem()
    sze = k.sem()
    rt_st = k.sem()
    W = lambda *ss: [s.now() for s in ss]
    cw = (env.ld, env.ld.n)

    for c in range(8):
        k.op("sync", lambda e, c=c: e.dma_start(
            out=wr_sb[:, c * 8:(c + 1) * 8], in_=WrAp[c * 128:(c + 1) * 128, :]),
            inc=s_ldw, inck=16)
    for c in range(8):
        k.op("tensor", lambda e, c=c: e.matmul(
            S[5][0:1, 0:8], cn["ones_col_f"][:], wr_sb[:, c * 8:(c + 1) * 8],
            start=(c == 0), stop=(c == 7)),
            waits=[cw, (s_ldw, s_ldw.n)] if c == 0 else (),
            inc=rt_pe if c == 7 else None)
    k.op("vector", lambda e: e.tensor_copy(cswr[:], S[5][0:1, 0:8]),
         waits=W(rt_pe), inc=rt_dve)
    for ei in range(8):
        k.op("vector", lambda e, ei=ei: e.memset(slotb[0:1, ei:ei + 1], float(ei * capl)),
             inc=rt_dve)
    k.op("vector", lambda e: e.memset(offs[0][:], 0.0), inc=rt_dve)

    with contextlib.ExitStack() as rctx:
        RE = rctx.enter_context
        xgf = RE(nc.sbuf_tensor("f_xgf", [128, 8 * 512], F32))
        xbb = RE(nc.sbuf_tensor("f_xbb", [128, 8 * 512], BF16))
        negm = RE(nc.sbuf_tensor("f_negm", [1, 512], F32))
        lg = RE(nc.sbuf_tensor("f_lg", [128, 8], F32))
        mxc = RE(nc.sbuf_tensor("f_mxc", [128, 1], F32))
        nmxc = RE(nc.sbuf_tensor("f_nmxc", [128, 1], F32))
        sume = RE(nc.sbuf_tensor("f_sume", [128, 1], F32))
        escr = RE(nc.sbuf_tensor("f_escr", [128, 8], F32))
        scr8 = RE(nc.sbuf_tensor("f_scr8", [128, 8], F32))
        ptok = RE(nc.sbuf_tensor("f_ptok", [128, 1], F32))
        offA = RE(nc.sbuf_tensor("f_offA", [128, 1], F32))
        lrloc = RE(nc.sbuf_tensor("f_lrloc", [128, 1], F32))
        keepl = RE(nc.sbuf_tensor("f_keepl", [128, 1], F32))
        idxf = RE(nc.sbuf_tensor("f_idxf", [128, 1], F32))
        if not lite:
            xtok = [RE(nc.sbuf_tensor(f"f_xtok{i}", [128, 1024], BF16)) for i in range(2)]
            sqt = RE(nc.sbuf_tensor("f_sqt", [128, 1024], BF16))
            mcol = RE(nc.sbuf_tensor("f_mcol", [128, 1], F32))
            qcol = RE(nc.sbuf_tensor("f_qcol", [128, 1], F32))
            vcol = RE(nc.sbuf_tensor("f_vcol", [128, 1], F32))
            rstdc = RE(nc.sbuf_tensor("f_rstdc", [128, 1], F32))
            xn_tok = [RE(nc.sbuf_tensor(f"f_xn{i}", [128, 1024], BF16)) for i in range(2)]
            zt = RE(nc.sbuf_tensor("f_zt", [128, 1024], BF16))
            k.op("vector", lambda e: e.memset(zt[:], 0.0), inc=rt_dve)
            zW = (rt_dve, rt_dve.n)
            for i in range(8 * capl // 128):
                k.op("gpsimd", lambda e, i=i: e.dma_start(
                    out=xe_dram[i * 128:(i + 1) * 128, :], in_=zt[:]),
                    waits=[zW] if i == 0 else (), inc=sze, inck=16)

        s_ldx2 = k.sem()
        s_ldx3 = k.sem()
        for g in range(G):
            ldb2 = s_ldx2.n
            ldb3 = s_ldx3.n
            for c in range(8):
                k.op("gpsimd", lambda e, c=c, g=g: e.dma_start(
                    out=xbb[:, c * 512:(c + 1) * 512],
                    in_=xT[c * 128:(c + 1) * 128, g * 512:(g + 1) * 512]),
                    waits=[(rt_pe, rt_pe.n)] if (g > 0 and c == 0) else (),
                    inc=s_ldx2, inck=16)
                k.op("sync", lambda e, c=c, g=g: e.dma_start(
                    out=xgf[:, c * 512:(c + 1) * 512],
                    in_=xT[c * 128:(c + 1) * 128, g * 512:(g + 1) * 512]),
                    waits=[(rt_pe, rt_pe.n)] if (g > 0 and c == 0) else (),
                    inc=s_ldx3, inck=16)
            for c in range(8):
                k.op("tensor", lambda e, c=c: e.matmul(
                    S[5][0:1, 0:512], cn["ones_col"][:], xbb[:, c * 512:(c + 1) * 512],
                    start=(c == 0), stop=(c == 7)),
                    waits=[(s_ldx2, ldb2 + 128), (rt_dve, rt_dve.n)] if c == 0 else (),
                    inc=rt_pe if c == 7 else None)
            k.op("vector", lambda e: e.tensor_scalar(
                negm[:], S[5][0:1, 0:512], -1.0 / 1024, None, op0=ALU.mult),
                waits=W(rt_pe), inc=rt_dve)

            for ti in range(4):
                t = g * 4 + ti
                tb = t % 2
                if not lite:
                    for rnd in range(2):
                        for cc in range(4):
                            c = rnd * 4 + cc
                            k.op("tensor", lambda e, c=c, cc=cc, ti=ti: e.transpose(
                                PB[:, cc * 128:(cc + 1) * 128],
                                xbb[:, c * 512 + ti * 128: c * 512 + (ti + 1) * 128],
                                cn["ident_bf"][:]),
                                waits=W(rt_dve) if cc == 0 else (), inc=rt_pe)
                        k.op("vector", lambda e, rnd=rnd, tb=tb: e.tensor_copy(
                            xtok[tb][:, rnd * 512:(rnd + 1) * 512], PB[:, 0:512]),
                            waits=W(rt_pe), inc=rt_dve)
                    k.op("scalar", lambda e, tb=tb: e.activation(
                        sqt[:], xtok[tb][:], AF.Square, bias=0.0, scale=1.0),
                        waits=W(rt_dve), inc=rt_act)
                    k.op("vector", lambda e, tb=tb: e.reduce_sum(
                        mcol[:], xtok[tb][:], axis=AX.X), waits=W(rt_dve), inc=rt_dve)
                    k.op("vector", lambda e: e.reduce_sum(
                        qcol[:], sqt[:], axis=AX.X), waits=W(rt_act), inc=rt_dve)
                    k.op("vector", lambda e: e.tensor_scalar(
                        mcol[:], mcol[:], 1.0 / 1024, None, op0=ALU.mult),
                        waits=W(rt_dve), inc=rt_dve)
                    k.op("vector", lambda e: e.tensor_scalar(
                        vcol[:], mcol[:], mcol[:, 0:1], None, op0=ALU.mult),
                        waits=W(rt_dve), inc=rt_dve)
                    k.op("vector", lambda e: e.scalar_tensor_tensor(
                        vcol[:], qcol[:], 1.0 / 1024, vcol[:],
                        op0=ALU.mult, op1=ALU.subtract),
                        waits=W(rt_dve), inc=rt_dve)
                    k.op("vector", lambda e: e.tensor_scalar(
                        vcol[:], vcol[:], 1e-5, None, op0=ALU.add),
                        waits=W(rt_dve), inc=rt_dve)
                    k.op("scalar", lambda e: e.activation(
                        rstdc[:], vcol[:], AF.Ln, bias=0.0, scale=1.0),
                        waits=W(rt_dve), inc=rt_act)
                    k.op("scalar", lambda e: e.activation(
                        rstdc[:], rstdc[:], AF.Exp, bias=0.0, scale=-0.5),
                        waits=W(rt_act), inc=rt_act)
                    k.op("vector", lambda e, tb=tb: e.tensor_scalar(
                        xn_tok[tb][:], xtok[tb][:], mcol[:, 0:1], rstdc[:, 0:1],
                        op0=ALU.subtract, op1=ALU.mult),
                        waits=W(rt_act, rt_sc), inc=rt_dve)
                for c in range(8):
                    k.op("tensor", lambda e, c=c, ti=ti: e.matmul(
                        S[6][:, 0:8],
                        xgf[:, c * 512 + ti * 128: c * 512 + (ti + 1) * 128],
                        wr_sb[:, c * 8:(c + 1) * 8], start=(c == 0), stop=False),
                        waits=([(s_ldx3, ldb3 + 128)] + W(rt_dve)) if c == 0 else ())
                k.op("tensor", lambda e, ti=ti: e.matmul(
                    S[6][:, 0:8], negm[0:1, ti * 128:(ti + 1) * 128], cswr[:],
                    start=False, stop=True), inc=rt_pe)
                k.op("vector", lambda e: e.reduce_max(
                    mxc[:], S[6][:, 0:8], axis=AX.X), waits=W(rt_pe), inc=rt_dve)
                k.op("vector", lambda e, t=t: e.tensor_scalar(
                    oh_all[:, t * 8:(t + 1) * 8], S[6][:, 0:8], mxc[:, 0:1],
                    None, op0=ALU.is_ge),
                    waits=W(rt_dve), inc=rt_dve)
                if not lite:
                    k.op("vector", lambda e: e.tensor_scalar(
                        lg[:], S[6][:, 0:8], rstdc[:, 0:1], None, op0=ALU.mult),
                        waits=W(rt_dve, rt_act), inc=rt_dve)
                    k.op("vector", lambda e: e.reduce_max(
                        nmxc[:], lg[:], axis=AX.X, negate=True),
                        waits=W(rt_dve), inc=rt_dve)
                    k.op("scalar", lambda e: e.activation(
                        escr[:], lg[:], AF.Exp, bias=nmxc[:, 0:1], scale=1.0,
                        accum_out=sume[:, 0:1]), waits=W(rt_dve), inc=rt_act)
                    k.op("vector", lambda e: e.reciprocal(
                        ptok[:], sume[:]), waits=W(rt_act), inc=rt_dve)
                    k.op("vector", lambda e, t=t: e.tensor_copy(
                        pk_all[:, t:t + 1], ptok[:]), waits=W(rt_dve), inc=rt_dve)
                oz = (rt_dve, rt_dve.n)
                pairs = ((0, slotb), (1, None)) if not lite else ((0, slotb),)
                for bi, extra in pairs:
                    reg = bi * 16
                    k.op("tensor", lambda e, reg=reg, t=t: e.matmul(
                        S[4][:, reg:reg + 8], cn["strictL_f"][:],
                        oh_all[:, t * 8:(t + 1) * 8],
                        start=True, stop=False), waits=[oz])
                    k.op("tensor", lambda e, reg=reg, t=t, extra=extra: e.matmul(
                        S[4][:, reg:reg + 8], cn["ones_row_f"][:], offs[t % 2][:],
                        start=False, stop=(extra is None)),
                        inc=rt_pe if extra is None else None)
                    if extra is not None:
                        k.op("tensor", lambda e, reg=reg, extra=extra: e.matmul(
                            S[4][:, reg:reg + 8], cn["ones_row_f"][:], extra[:],
                            start=False, stop=True), inc=rt_pe)
                k.op("tensor", lambda e, t=t: e.matmul(
                    S[5][0:1, 504:512], cn["ones_col_f"][:],
                    oh_all[:, t * 8:(t + 1) * 8],
                    start=True, stop=True), waits=[oz], inc=rt_pe)
                k.op("vector", lambda e, t=t: e.tensor_add(
                    offs[(t + 1) % 2][:], offs[t % 2][:], S[5][0:1, 504:512]),
                    waits=W(rt_pe), inc=rt_dve)
                if not lite:
                    k.op("vector", lambda e, t=t: e.tensor_mul(
                        scr8[:], oh_all[:, t * 8:(t + 1) * 8], S[4][:, 0:8]),
                        waits=W(rt_pe), inc=rt_dve)
                    k.op("vector", lambda e: e.reduce_sum(offA[:], scr8[:], axis=AX.X),
                         waits=W(rt_dve), inc=rt_dve)
                    k.op("vector", lambda e, t=t: e.tensor_mul(
                        scr8[:], oh_all[:, t * 8:(t + 1) * 8], S[4][:, 16:24]),
                        waits=W(rt_dve), inc=rt_dve)
                    k.op("vector", lambda e, t=t: e.reduce_sum(
                        lr_all[:, t:t + 1], scr8[:], axis=AX.X),
                        waits=W(rt_dve), inc=rt_dve)
                    k.op("vector", lambda e, t=t: e.tensor_scalar(
                        keepl[:], lr_all[:, t:t + 1], float(capl), None, op0=ALU.is_lt),
                        waits=W(rt_dve), inc=rt_dve)
                    k.op("vector", lambda e: e.tensor_scalar(
                        idxf[:], keepl[:], -1e9, 1e9, op0=ALU.mult, op1=ALU.add),
                        waits=W(rt_dve), inc=rt_dve)
                    k.op("vector", lambda e: e.tensor_add(idxf[:], idxf[:], offA[:]),
                         waits=W(rt_dve), inc=rt_dve)
                    k.op("vector", lambda e, t=t: e.tensor_copy(
                        idx_all[:, t:t + 1], idxf[:]), waits=W(rt_dve), inc=rt_dve)
                    k.op("gpsimd", lambda e, t=t, tb=tb: e.indirect_dma_start(
                        xe_dram, IndirectOffsetOnAxis(idx_all[:, t:t + 1], 0),
                        xn_tok[tb][:], None,
                        bounds_check=8 * capl - 1, oob_is_err=False),
                        waits=[(rt_dve, rt_dve.n), (sze, sze.n),
                               (rt_sc, max(0, rt_sc.n - 64))], inc=rt_sc, inck=16)
        if counts_out is not None:
            k.op("scalar", lambda e: e.dma_start(
                out=counts_out, in_=offs[NT % 2][:]),
                waits=W(rt_dve), inc=rt_st, inck=16)
        if not lite:
            k.op("scalar", lambda e: e.dma_start(out=idx_out, in_=idx_all[:]),
                 waits=W(rt_dve), inc=rt_st, inck=16)
            k.op("scalar", lambda e: e.dma_start(out=ptok_out, in_=pk_all[:]),
                 waits=W(rt_dve), inc=rt_st, inck=16)
            k.op("scalar", lambda e: e.dma_start(out=lrloc_out, in_=lr_all[:]),
                 waits=W(rt_dve), inc=rt_st, inck=16)
            k.op("scalar", lambda e: e.dma_start(out=oneh_out, in_=oh_all[:]),
                 waits=W(rt_dve), inc=rt_st, inck=16)


def emit_l0_ffn_rest(env: Env, T, xT, x2T, W1Ap, W2Ap, xeAp, idxAp, ptokAp,
                     lrlocAp, onehAp, countsAp, coreltAp, cap_global, capl=CAPL):
    nc, E, k, cn = env.nc, env.E, env.k, env.cn
    S, PB = env.S, env.PB
    G = T // 512
    NT = T // 128
    ye_dram = nc.dram_tensor("ye_dram", [8 * capl, 1024], BF16, kind="Internal").ap()

    pk_all = E(nc.sbuf_tensor("f_pk", [128, NT], F32))
    idx_all = E(nc.sbuf_tensor("f_idx", [128, NT], I32))
    gidx_all = E(nc.sbuf_tensor("f_gidx", [128, NT], I32))
    lr_all = E(nc.sbuf_tensor("f_lr", [128, NT], F32))
    oh_all = E(nc.sbuf_tensor("f_oh", [128, NT * 8], F32))
    counts_sb = E(nc.sbuf_tensor("f_counts", [8, 8], F32))
    corelt_sb = E(nc.sbuf_tensor("f_corelt", [8, 1], F32))
    gbrow = E(nc.sbuf_tensor("f_gbrow", [1, 8], F32))
    gbb = E(nc.sbuf_tensor("f_gbb", [128, 8], F32))
    scr8 = E(nc.sbuf_tensor("f_scr8", [128, 8], F32))
    gbcol = E(nc.sbuf_tensor("f_gbcol", [128, 1], F32))
    keepc = E(nc.sbuf_tensor("f_keepc", [128, 1], F32))
    gif = E(nc.sbuf_tensor("f_gif", [128, 1], F32))

    s_ldm = k.sem()
    pr_pe = k.sem()
    pr_dve = k.sem()
    W = lambda *ss: [s.now() for s in ss]
    cw = (env.ld, env.ld.n)

    for (dst, src) in ((pk_all, ptokAp), (idx_all, idxAp), (lr_all, lrlocAp),
                       (oh_all, onehAp), (counts_sb, countsAp), (corelt_sb, coreltAp)):
        k.op("sync", lambda e, dst=dst, src=src: e.dma_start(out=dst[:], in_=src),
             inc=s_ldm, inck=16)
    k.op("tensor", lambda e: e.matmul(
        S[5][0:1, 0:8], corelt_sb[:], counts_sb[:], start=True, stop=True),
        waits=[cw, (s_ldm, s_ldm.n)], inc=pr_pe)
    k.op("vector", lambda e: e.tensor_copy(gbrow[:], S[5][0:1, 0:8]),
         waits=W(pr_pe), inc=pr_dve)
    k.op("tensor", lambda e: e.matmul(
        S[5][:, 16:24], cn["ones_row_f"][:], gbrow[:], start=True, stop=True),
        waits=W(pr_dve), inc=pr_pe)
    k.op("vector", lambda e: e.tensor_copy(gbb[:], S[5][:, 16:24]),
         waits=W(pr_pe), inc=pr_dve)
    for t in range(NT):
        k.op("vector", lambda e, t=t: e.tensor_mul(
            scr8[:], oh_all[:, t * 8:(t + 1) * 8], gbb[:]),
            waits=W(pr_dve), inc=pr_dve)
        k.op("vector", lambda e: e.reduce_sum(gbcol[:], scr8[:], axis=AX.X),
             waits=W(pr_dve), inc=pr_dve)
        k.op("vector", lambda e, t=t: e.tensor_add(
            gbcol[:], gbcol[:], lr_all[:, t:t + 1]), waits=W(pr_dve), inc=pr_dve)
        k.op("vector", lambda e: e.tensor_scalar(
            keepc[:], gbcol[:], float(cap_global), None, op0=ALU.is_lt),
            waits=W(pr_dve), inc=pr_dve)
        k.op("vector", lambda e, t=t: e.tensor_scalar(
            pk_all[:, t:t + 1], pk_all[:, t:t + 1], keepc[:, 0:1], None,
            op0=ALU.mult), waits=W(pr_dve), inc=pr_dve)
        k.op("vector", lambda e, t=t: e.tensor_scalar(
            gif[:], keepc[:], -1e9, 1e9, op0=ALU.mult, op1=ALU.add),
            waits=W(pr_dve), inc=pr_dve)
        k.op("vector", lambda e, t=t: e.scalar_tensor_tensor(
            gif[:], idx_all[:, t:t + 1], 1.0, gif[:], op0=ALU.mult, op1=ALU.add),
            waits=W(pr_dve), inc=pr_dve)
        k.op("vector", lambda e, t=t: e.tensor_copy(
            gidx_all[:, t:t + 1], gif[:]), waits=W(pr_dve), inc=pr_dve)
    prW = (pr_dve, pr_dve.n)

    NS = 32
    nrt = capl // 128
    with contextlib.ExitStack() as ectx:
        EE = ectx.enter_context
        w1b = [EE(nc.sbuf_tensor(f"f_w1b{i}", [128, 8 * 512], BF16)) for i in range(2)]
        w2h = EE(nc.sbuf_tensor("f_w2h", [128, NS * 512], BF16))
        xef = EE(nc.sbuf_tensor("f_xef", [128, 8 * capl], BF16))
        h_sb = EE(nc.sbuf_tensor("f_hsb", [128, NS * capl], BF16))
        yesb = [EE(nc.sbuf_tensor(f"f_yesb{i}", [128, 512], BF16)) for i in range(2)]
        s_w1 = k.sem()
        s_w2 = k.sem()
        s_xe = k.sem()
        fe_pe = k.sem()
        fe_act = k.sem()
        fe_st = k.sem()
        for ei in range(8):
            xeW = s_xe.n
            for c in range(8):
                k.op("sync", lambda e, c=c, ei=ei: e.dma_start_transpose(
                    xef[:, c * capl:(c + 1) * capl],
                    xeAp[ei * capl:(ei + 1) * capl, c * 128:(c + 1) * 128]),
                    waits=[(fe_pe, fe_pe.n)] if c == 0 else (), inc=s_xe, inck=16)
            for fb in range(8):
                dbl = fb % 2
                w1W = s_w1.n
                for c in range(8):
                    k.op("gpsimd", lambda e, c=c, ei=ei, fb=fb, dbl=dbl: e.dma_start(
                        out=w1b[dbl][:, c * 512:(c + 1) * 512],
                        in_=W1Ap[ei * 1024 + c * 128: ei * 1024 + (c + 1) * 128,
                                 fb * 512:(fb + 1) * 512]),
                        waits=[(fe_pe, fe_pe.n)] if c == 0 else (), inc=s_w1, inck=16)
                for sl in range(4):
                    sidx = fb * 4 + sl
                    pieces = [(S[6], 0, min(512, capl))]
                    if capl > 512:
                        pieces.append((S[5], 512, capl - 512))
                    for (bank, w0, w1w) in pieces:
                        for c in range(8):
                            k.op("tensor", lambda e, c=c, sl=sl, dbl=dbl, w0=w0,
                                 w1w=w1w, bank=bank:
                                 e.matmul(bank[:, 0:w1w],
                                          w1b[dbl][:, c * 512 + sl * 128:
                                                   c * 512 + (sl + 1) * 128],
                                          xef[:, c * capl + w0: c * capl + w0 + w1w],
                                          start=(c == 0), stop=(c == 7)),
                                 waits=[(s_w1, w1W + 128), (s_xe, xeW + 128),
                                        (fe_act, fe_act.n)] if c == 0 else (),
                                 inc=fe_pe if c == 7 else None)
                    k.op("scalar", lambda e, sidx=sidx: e.activation(
                        h_sb[:, sidx * capl: sidx * capl + min(512, capl)],
                        S[6][:, 0:min(512, capl)],
                        AF.Relu, bias=0.0, scale=1.0), waits=W(fe_pe), inc=fe_act)
                    if capl > 512:
                        k.op("scalar", lambda e, sidx=sidx: e.activation(
                            h_sb[:, sidx * capl + 512: (sidx + 1) * capl],
                            S[5][:, 0:capl - 512],
                            AF.Relu, bias=0.0, scale=1.0), inc=fe_act)
            for dh in range(2):
                w2W = s_w2.n
                for sidx in range(NS):
                    k.op("gpsimd", lambda e, sidx=sidx, ei=ei, dh=dh: e.dma_start(
                        out=w2h[:, sidx * 512:(sidx + 1) * 512],
                        in_=W2Ap[ei * 4096 + sidx * 128: ei * 4096 + (sidx + 1) * 128,
                                 dh * 512:(dh + 1) * 512]),
                        waits=[(fe_pe, fe_pe.n)] if sidx == 0 else (),
                        inc=s_w2, inck=16)
                for rt in range(nrt):
                    bank = S[rt % 2]
                    for sidx in range(NS):
                        k.op("tensor", lambda e, sidx=sidx, rt=rt, bank=bank: e.matmul(
                            bank[:], h_sb[:, sidx * capl + rt * 128:
                                          sidx * capl + (rt + 1) * 128],
                            w2h[:, sidx * 512:(sidx + 1) * 512],
                            start=(sidx == 0), stop=(sidx == NS - 1)),
                            waits=[(s_w2, w2W + 16 * NS), (fe_act, fe_act.n),
                                   (fe_st, fe_st.n)] if sidx == 0 else (),
                            inc=fe_pe if sidx == NS - 1 else None)
                    k.op("scalar", lambda e, rt=rt, bank=bank: e.activation(
                        yesb[rt % 2][:], bank[:], AF.Copy, bias=0.0, scale=1.0),
                        waits=W(fe_pe) + [(fe_st, fe_st.n)], inc=fe_act)
                    k.op("scalar", lambda e, rt=rt, ei=ei, dh=dh: e.dma_start(
                        out=ye_dram[ei * capl + rt * 128: ei * capl + (rt + 1) * 128,
                                    dh * 512:(dh + 1) * 512], in_=yesb[rt % 2][:]),
                        waits=W(fe_act), inc=fe_st, inck=16)

    with contextlib.ExitStack() as gctx:
        GE = gctx.enter_context
        ytile = [GE(nc.sbuf_tensor(f"f_yt{i}", [128, 1024], BF16)) for i in range(4)]
        xrg = [GE(nc.sbuf_tensor(f"f_xrg{i}", [128, 512], F32)) for i in range(4)]
        x2o = [GE(nc.sbuf_tensor(f"f_x2o{i}", [128, 512], F32)) for i in range(4)]
        ga = k.sem()
        gd = k.sem()
        gpe = k.sem()
        gst = [k.sem() for _ in range(4)]
        gxr = [k.sem() for _ in range(4)]
        yeW = (fe_st, fe_st.n)
        feW = [(fe_pe, fe_pe.n), (fe_act, fe_act.n)]
        k.op("gpsimd", lambda e: None, waits=[(s_w1, s_w1.n), (s_w2, s_w2.n)])
        prev_gpe = 0
        for g in range(G):
            for ti in range(4):
                t = g * 4 + ti
                tb = ti
                k.op("vector", lambda e, tb=tb: e.memset(ytile[tb][:], 0.0),
                     waits=[(gpe, prev_gpe), prW] + (feW if (g == 0) else []), inc=gd)
                k.op("gpsimd", lambda e, t=t, tb=tb: e.indirect_dma_start(
                    ytile[tb][:], None, ye_dram,
                    IndirectOffsetOnAxis(gidx_all[:, t:t + 1], 0),
                    bounds_check=8 * capl - 1, oob_is_err=False),
                    waits=[yeW, (gd, gd.n), (ga, ga.n)], inc=ga, inck=16)
                k.op("vector", lambda e, t=t, tb=tb: e.tensor_scalar(
                    ytile[tb][:], ytile[tb][:], pk_all[:, t:t + 1], None,
                    op0=ALU.mult), waits=[(ga, ga.n)], inc=gd)
            for c in range(8):
                n = g * 8 + c
                b = n % 4
                k.op("sync", lambda e, c=c, g=g, b=b: e.dma_start(
                    out=xrg[b][:], in_=xT[c * 128:(c + 1) * 128,
                                          g * 512:(g + 1) * 512]),
                    waits=[(gd, gd.n)] if n < 4 else [(gd, gd.n), (gst[b], gst[b].n)],
                    inc=gxr[b], inck=16)
                for ti in range(4):
                    k.op("tensor", lambda e, c=c, ti=ti: e.transpose(
                        PB[:, ti * 128:(ti + 1) * 128],
                        ytile[ti][:, c * 128:(c + 1) * 128], cn["ident_bf"][:]),
                        waits=[(gd, gd.n), (gpe, gpe.n)] if ti == 0 else (), inc=gpe)
                k.op("vector", lambda e, b=b: e.tensor_add(
                    x2o[b][:], PB[:, 0:512], xrg[b][:]),
                    waits=[(gpe, gpe.n), (gxr[b], gxr[b].n)], inc=gd)
                k.op("scalar", lambda e, c=c, g=g, b=b: e.dma_start(
                    out=x2T[c * 128:(c + 1) * 128, g * 512:(g + 1) * 512],
                    in_=x2o[b][:]), waits=[(gd, gd.n)], inc=gst[b], inck=16)
            prev_gpe = gpe.n

def build_l0_router_program(T, capl=CAPL, lite=False):
    nc = bass.Bass("TRN2", target_bir_lowering=False)
    xT = nc.dram_tensor("xT", [1024, T], F32, kind="ExternalInput").ap()
    Wr = nc.dram_tensor("Wr", [1024, 8], F32, kind="ExternalInput").ap()
    counts_out = nc.dram_tensor("counts", [1, 8], F32, kind="ExternalOutput").ap()
    NT = T // 128
    if lite:
        xe = idx = ptok = lrloc = oneh = None
    else:
        xe = nc.dram_tensor("xe", [8 * capl, 1024], BF16, kind="ExternalOutput").ap()
        idx = nc.dram_tensor("idx", [128, NT], I32, kind="ExternalOutput").ap()
        ptok = nc.dram_tensor("ptok", [128, NT], F32, kind="ExternalOutput").ap()
        lrloc = nc.dram_tensor("lrloc", [128, NT], F32, kind="ExternalOutput").ap()
        oneh = nc.dram_tensor("oneh", [128, NT * 8], F32, kind="ExternalOutput").ap()
    ctx = contextlib.ExitStack()
    with ctx:
        E = ctx.enter_context
        k = K(nc, E)
        env = Env(nc, E, k)
        emit_l0_router(env, T, xT, Wr, xe, idx, ptok, lrloc, oneh, counts_out,
                       capl=capl, lite=lite)
        k.final_barrier()
        block = E(nc.Block())
        k.finish(block)
    return nc


def build_l0_rest_program(T, cap_global, capl=CAPL):
    nc = bass.Bass("TRN2", target_bir_lowering=False)
    NT = T // 128
    xT = nc.dram_tensor("xT", [1024, T], F32, kind="ExternalInput").ap()
    W1 = nc.dram_tensor("W1", [8 * 1024, 4096], F32, kind="ExternalInput").ap()
    W2 = nc.dram_tensor("W2", [8 * 4096, 1024], F32, kind="ExternalInput").ap()
    xe = nc.dram_tensor("xe", [8 * capl, 1024], BF16, kind="ExternalInput").ap()
    idx = nc.dram_tensor("idx", [128, NT], I32, kind="ExternalInput").ap()
    ptok = nc.dram_tensor("ptok", [128, NT], F32, kind="ExternalInput").ap()
    lrloc = nc.dram_tensor("lrloc", [128, NT], F32, kind="ExternalInput").ap()
    oneh = nc.dram_tensor("oneh", [128, NT * 8], F32, kind="ExternalInput").ap()
    countsA = nc.dram_tensor("countsA", [8, 8], F32, kind="ExternalInput").ap()
    corelt = nc.dram_tensor("corelt", [8, 1], F32, kind="ExternalInput").ap()
    x2T = nc.dram_tensor("x2T", [1024, T], F32, kind="ExternalOutput").ap()
    ctx = contextlib.ExitStack()
    with ctx:
        E = ctx.enter_context
        k = K(nc, E)
        env = Env(nc, E, k)
        emit_l0_ffn_rest(env, T, xT, x2T, W1, W2, xe, idx, ptok, lrloc, oneh,
                         countsA, corelt, cap_global, capl=capl)
        k.final_barrier()
        block = E(nc.Block())
        k.finish(block)
    return nc


def run_l0_ffn(x, Wr, W1, W2, capl=CAPL):
    B, T, D = x.shape
    cap = -(-B * T // 8)
    consts = make_const_arrays()
    nc1 = build_l0_router_program(T, capl=capl)
    in_maps = []
    for b in range(B):
        m = dict(consts)
        m["xT"] = np.ascontiguousarray(x[b].T)
        m["Wr"] = np.ascontiguousarray(Wr)
        in_maps.append(m)
    res1 = run_bass_kernel_spmd(nc1, in_maps, core_ids=list(range(B)))
    counts_all = np.zeros((8, 8), np.float32)
    for b in range(B):
        counts_all[b] = res1.results[b]["counts"][0]
    nc2 = build_l0_rest_program(T, cap, capl=capl)
    in_maps = []
    for b in range(B):
        r = res1.results[b]
        m = dict(consts)
        m["xT"] = np.ascontiguousarray(x[b].T)
        m["W1"] = np.ascontiguousarray(W1.reshape(8 * 1024, 4096))
        m["W2"] = np.ascontiguousarray(W2.reshape(8 * 4096, 1024))
        m["xe"] = r["xe"]
        m["idx"] = r["idx"]
        m["ptok"] = r["ptok"]
        m["lrloc"] = r["lrloc"]
        m["oneh"] = r["oneh"]
        m["countsA"] = counts_all
        m["corelt"] = (np.arange(8) < b).astype(np.float32)[:, None]
        in_maps.append(m)
    res2 = run_bass_kernel_spmd(nc2, in_maps, core_ids=list(range(B)))
    return np.stack([r["x2T"].T for r in res2.results], axis=0)


def sim_l0_ffn(x, Wr, W1, W2, capl=CAPL):
    from concourse.bass_interp import CoreSim
    B, T, D = x.shape
    assert B == 1
    cap = -(-T // 8)
    consts = make_const_arrays()
    nc1 = build_l0_router_program(T, capl=capl)
    nc1.finalize()
    sim = CoreSim(nc1)
    feed = dict(consts)
    feed["xT"] = np.ascontiguousarray(x[0].T)
    feed["Wr"] = np.ascontiguousarray(Wr)
    for name, arr in feed.items():
        v = sim.tensor(name)
        v[:] = arr.view(v.dtype) if arr.dtype == NP_BF16 else arr
    sim.simulate()
    outs1 = {n: sim.tensor(n).copy() for n in ("xe", "idx", "ptok", "lrloc", "oneh", "counts")}
    counts_all = np.zeros((8, 8), np.float32)
    counts_all[0] = outs1["counts"][0]
    nc2 = build_l0_rest_program(T, cap, capl=capl)
    nc2.finalize()
    sim2 = CoreSim(nc2)
    feed = dict(consts)
    feed["xT"] = np.ascontiguousarray(x[0].T)
    feed["W1"] = np.ascontiguousarray(W1.reshape(8 * 1024, 4096))
    feed["W2"] = np.ascontiguousarray(W2.reshape(8 * 4096, 1024))
    for n in ("xe", "idx", "ptok", "lrloc", "oneh"):
        feed[n] = outs1[n]
    feed["countsA"] = counts_all
    feed["corelt"] = np.zeros((8, 1), np.float32)
    for name, arr in feed.items():
        v = sim2.tensor(name)
        v[:] = arr.view(v.dtype) if arr.dtype in (NP_BF16,) else arr
    sim2.simulate()
    return sim2.tensor("x2T").T[None].copy(), outs1["counts"]



def emit_l1_attn(env: Env, T, xT, x3T, nwcol, wqT, wkT, wvT, wfT, gncol, woT):
    nc, E, k, cn = env.nc, env.E, env.k, env.cn
    S, PB = env.S, env.PB
    G = T // 512
    NCH = T // 128
    H, DKH, M = 4, 256, 256
    o_dram = nc.dram_tensor("o1_dram", [1024, T], BF16, kind="Internal").ap()

    W = lambda *ss: [s.now() for s in ss]
    cw = (env.ld, env.ld.n)

    nw_f = E(nc.sbuf_tensor("a1_nwf", [128, 8], F32))
    nw_b = E(nc.sbuf_tensor("a1_nwb", [128, 8], BF16))
    gn_f = E(nc.sbuf_tensor("a1_gnf", [128, 8], F32))
    xh = E(nc.sbuf_tensor("a1_xh", [128, 8 * T], BF16))
    negmr = E(nc.sbuf_tensor("a1_negmr", [1, T], BF16))
    wo_sb = E(nc.sbuf_tensor("a1_wo", [128, 8 * 1024], BF16))

    s_ldg = k.sem()
    s_ldx = k.sem()
    s_ldw = k.sem()
    s_ldwo = k.sem()
    ln_pe = k.sem()
    ln_dve = k.sem()
    ln_act = k.sem()
    ln_done = k.sem()

    for c in range(8):
        k.op("sync", lambda e, c=c: e.dma_start(
            out=nw_f[:, c:c + 1], in_=nwcol[c * 128:(c + 1) * 128, 0:1]),
            inc=s_ldg, inck=16)
        k.op("sync", lambda e, c=c: e.dma_start(
            out=gn_f[:, c:c + 1], in_=gncol[c * 128:(c + 1) * 128, 0:1]),
            inc=s_ldg, inck=16)
    k.op("vector", lambda e: e.tensor_copy(nw_b[:], nw_f[:]),
         waits=W(s_ldg) + [cw], inc=ln_dve)
    for c in range(8):
        k.op("gpsimd", lambda e, c=c: e.dma_start(
            out=wo_sb[:, c * 1024:(c + 1) * 1024], in_=woT[c * 128:(c + 1) * 128, :]),
            inc=s_ldwo, inck=16)

    with contextlib.ExitStack() as lctx:
        LE = lctx.enter_context
        xbb = LE(nc.sbuf_tensor("a1_xbb", [128, 8 * 512], BF16))
        sq = [LE(nc.sbuf_tensor(f"a1_sq{i}", [128, 512], BF16)) for i in range(2)]
        m_row = LE(nc.sbuf_tensor("a1_mrow", [1, 512], F32))
        q_row = LE(nc.sbuf_tensor("a1_qrow", [1, 512], F32))
        v_row = LE(nc.sbuf_tensor("a1_vrow", [1, 512], F32))
        a_row = LE(nc.sbuf_tensor("a1_arow", [1, 512], F32))
        r1_row = LE(nc.sbuf_tensor("a1_r1", [1, 512], F32))
        m2_row = LE(nc.sbuf_tensor("a1_m2", [1, 512], F32))
        rs_row = LE(nc.sbuf_tensor("a1_rs", [1, 512], F32))
        rs_b = LE(nc.sbuf_tensor("a1_rsb", [1, 512], BF16))
        am_b = LE(nc.sbuf_tensor("a1_amb", [1, 512], BF16))
        axtmp = LE(nc.sbuf_tensor("a1_axtmp", [128, 512], F32))
        for g in range(G):
            ldb = s_ldx.n
            for c in range(8):
                k.op("gpsimd", lambda e, c=c, g=g: e.dma_start(
                    out=xbb[:, c * 512:(c + 1) * 512],
                    in_=xT[c * 128:(c + 1) * 128, g * 512:(g + 1) * 512]),
                    waits=[(ln_done, ln_done.n)] if (g > 0 and c == 0) else (),
                    inc=s_ldx, inck=16)
            for c in range(8):
                k.op("tensor", lambda e, c=c: e.matmul(
                    S[0][0:1, 0:512], cn["ones_col"][:], xbb[:, c * 512:(c + 1) * 512],
                    start=(c == 0), stop=(c == 7)),
                    waits=([(s_ldx, ldb + 128)] + ([cw] if g == 0 else [])) if c == 0 else (),
                    inc=ln_pe if c == 7 else None)
            for c in range(8):
                k.op("scalar", lambda e, c=c: e.activation(
                    sq[c % 2][:], xbb[:, c * 512:(c + 1) * 512], AF.Square,
                    bias=0.0, scale=1.0),
                    waits=[(s_ldx, ldb + 128)] + (W(ln_pe) if c >= 2 else []),
                    inc=ln_act)
                k.op("tensor", lambda e, c=c: e.matmul(
                    S[1][0:1, 0:512], cn["ones_col"][:], sq[c % 2][:],
                    start=(c == 0), stop=(c == 7)),
                    waits=W(ln_act), inc=ln_pe)
            k.op("vector", lambda e: e.tensor_scalar(
                m_row[:], S[0][0:1, 0:512], 1.0 / 1024, None, op0=ALU.mult),
                waits=W(ln_pe), inc=ln_dve)
            k.op("vector", lambda e: e.tensor_scalar(
                q_row[:], S[1][0:1, 0:512], 1.0 / 1024, None, op0=ALU.mult),
                inc=ln_dve)
            k.op("vector", lambda e: e.tensor_mul(v_row[:], m_row[:], m_row[:]),
                 waits=W(ln_dve), inc=ln_dve)
            k.op("vector", lambda e: e.tensor_sub(v_row[:], q_row[:], v_row[:]),
                 waits=W(ln_dve), inc=ln_dve)
            k.op("vector", lambda e: e.tensor_scalar(
                a_row[:], v_row[:], 1e-5, None, op0=ALU.add),
                waits=W(ln_dve), inc=ln_dve)
            k.op("scalar", lambda e: e.activation(r1_row[:], a_row[:], AF.Ln,
                                                  bias=0.0, scale=1.0),
                 waits=W(ln_dve), inc=ln_act)
            k.op("scalar", lambda e: e.activation(rs_row[:], r1_row[:], AF.Exp,
                                                  bias=0.0, scale=-0.5),
                 waits=W(ln_act), inc=ln_act)
            k.op("scalar", lambda e: e.activation(m2_row[:], r1_row[:], AF.Exp,
                                                  bias=0.0, scale=-1.0),
                 waits=W(ln_act), inc=ln_act)
            k.op("vector", lambda e: e.tensor_mul(m2_row[:], m2_row[:], v_row[:]),
                 waits=W(ln_act), inc=ln_dve)
            k.op("vector", lambda e: e.tensor_scalar(
                m2_row[:], m2_row[:], 1e-5, None, op0=ALU.add),
                waits=W(ln_dve), inc=ln_dve)
            k.op("scalar", lambda e: e.activation(m2_row[:], m2_row[:], AF.Ln,
                                                  bias=0.0, scale=1.0),
                 waits=W(ln_dve), inc=ln_act)
            k.op("scalar", lambda e: e.activation(m2_row[:], m2_row[:], AF.Exp,
                                                  bias=0.0, scale=-0.5),
                 waits=W(ln_act), inc=ln_act)
            k.op("vector", lambda e: e.tensor_mul(rs_row[:], rs_row[:], m2_row[:]),
                 waits=W(ln_act), inc=ln_dve)
            k.op("vector", lambda e: e.tensor_copy(rs_b[:], rs_row[:]),
                 waits=W(ln_dve), inc=ln_dve)
            k.op("vector", lambda e: e.tensor_copy(am_b[:], m_row[:]),
                 waits=W(ln_dve), inc=ln_dve)
            k.op("tensor", lambda e: e.matmul(
                S[2][:], cn["ones_row"][:], rs_b[:], start=True, stop=True),
                waits=W(ln_dve) + [(ln_done, ln_done.n)], inc=ln_pe)
            k.op("tensor", lambda e: e.matmul(
                S[3][:], cn["ones_row"][:], am_b[:], start=True, stop=True),
                inc=ln_pe)
            for c in range(8):
                k.op("vector", lambda e, c=c: e.tensor_sub(
                    axtmp[:], xbb[:, c * 512:(c + 1) * 512], S[3][:]),
                    waits=W(ln_pe) if c == 0 else W(ln_dve), inc=ln_dve)
                k.op("vector", lambda e, c=c, g=g: e.scalar_tensor_tensor(
                    xh[:, c * T + g * 512: c * T + (g + 1) * 512],
                    axtmp[:], nw_f[:, c:c + 1], S[2][:],
                    op0=ALU.mult, op1=ALU.mult),
                    waits=W(ln_dve), inc=(ln_done if c == 7 else ln_dve))
    lnw = (ln_done, G)

    hctx = contextlib.ExitStack()
    E = hctx.enter_context
    wq_h = E(nc.sbuf_tensor("a1_wq", [128, 8 * 256], BF16))
    wk_h = E(nc.sbuf_tensor("a1_wk", [128, 8 * 256], BF16))
    wv_h = E(nc.sbuf_tensor("a1_wv", [128, 8 * 256], BF16))
    wf_h = E(nc.sbuf_tensor("a1_wf", [128, 8 * 256], BF16))
    csq = E(nc.sbuf_tensor("a1_csq", [1, 256], BF16))
    csk = E(nc.sbuf_tensor("a1_csk", [1, 256], BF16))
    csv = E(nc.sbuf_tensor("a1_csv", [1, 256], BF16))
    csf = E(nc.sbuf_tensor("a1_csf", [1, 256], BF16))
    Qf = E(nc.sbuf_tensor("a1_Qf", [128, 2 * T], BF16))
    Kf = E(nc.sbuf_tensor("a1_Kf", [128, 2 * T], BF16))
    Vt = E(nc.sbuf_tensor("a1_Vt", [128, 2 * T], BF16))
    S1b = [E(nc.sbuf_tensor(f"a1_S1b{d}", [128, 256], BF16)) for d in range(2)]
    S2b = [E(nc.sbuf_tensor(f"a1_S2b{m}", [128, 256], BF16)) for m in range(2)]
    t1 = E(nc.sbuf_tensor("a1_t1", [128, 256], F32))
    za = E(nc.sbuf_tensor("a1_za", [128, 256], F32))
    eb = E(nc.sbuf_tensor("a1_eb", [128, 256], F32))
    ea = E(nc.sbuf_tensor("a1_ea", [128, 256], F32))
    en = E(nc.sbuf_tensor("a1_en", [128, 256], F32))
    eR = E(nc.sbuf_tensor("a1_eR", [128, 256], F32))
    sg = E(nc.sbuf_tensor("a1_sg", [128, 256], F32))
    Dt = E(nc.sbuf_tensor("a1_D", [128, 256], BF16))
    D2t = E(nc.sbuf_tensor("a1_D2", [128, 256], BF16))
    UT = E(nc.sbuf_tensor("a1_UT", [128, 128], BF16))
    G2m = E(nc.sbuf_tensor("a1_G2m", [128, 128], BF16))
    zf = E(nc.sbuf_tensor("a1_zf", [128, 256], F32))
    nmx = E(nc.sbuf_tensor("a1_nmx", [128, 1], F32))
    esf = E(nc.sbuf_tensor("a1_esf", [128, 256], F32))
    lsm = E(nc.sbuf_tensor("a1_lsm", [128, 1], F32))
    invl = E(nc.sbuf_tensor("a1_invl", [128, 1], F32))
    ptl = E(nc.sbuf_tensor("a1_ptl", [128, 256], BF16))
    ptT = E(nc.sbuf_tensor("a1_ptT", [128, 256], BF16))
    dT = E(nc.sbuf_tensor("a1_dT", [128, 256], BF16))
    kT = E(nc.sbuf_tensor("a1_kT", [128, 256], BF16))
    eArow = E(nc.sbuf_tensor("a1_eArow", [1, 256], BF16))
    eAT = E(nc.sbuf_tensor("a1_eAT", [128, 2], F32))
    s1scr = E(nc.sbuf_tensor("a1_s1scr", [128, 256], F32))
    ost = [E(nc.sbuf_tensor(f"a1_ost{i}", [128, 256], BF16)) for i in range(2)]

    hw_sem = k.sem()
    pj = k.sem()
    sc_pe = k.sem()
    sc_dve = k.sem()
    sc_act = k.sem()
    so_dma = [k.sem(), k.sem()]
    v_ev = k.sem()

    for h in range(H):
        for (wt, src) in ((wq_h, wqT), (wk_h, wkT), (wv_h, wvT), (wf_h, wfT)):
            for c in range(8):
                k.op("gpsimd", lambda e, c=c, wt=wt, src=src, h=h: e.dma_start(
                    out=wt[:, c * 256:(c + 1) * 256],
                    in_=src[c * 128:(c + 1) * 128, h * 256:(h + 1) * 256]),
                    waits=((W(pj, v_ev, sc_pe) if h > 0 else [(s_ldx, s_ldx.n), lnw])
                           if (c == 0 and wt is wq_h) else ()),
                    inc=s_ldw, inck=16)
        wW = (s_ldw, s_ldw.n)
        for g in range(G):
            for (dst, wt, cs) in ((Qf, wq_h, csq), (Kf, wk_h, csk)):
                for d in range(2):
                    bank = S[4]
                    for c in range(8):
                        k.op("tensor", lambda e, c=c, wt=wt, g=g, d=d, bank=bank: e.matmul(
                            bank[:], wt[:, c * 256 + d * 128: c * 256 + d * 128 + 128],
                            xh[:, c * T + g * 512: c * T + (g + 1) * 512],
                            start=(c == 0), stop=(c == 7)),
                            waits=[wW, lnw] + W(pj, sc_act) if c == 0 else (),
                            inc=sc_pe if c == 7 else None)
                    k.op("scalar", lambda e, dst=dst, g=g, d=d, bank=bank: e.activation(
                        dst[:, d * T + g * 512: d * T + (g + 1) * 512], bank[:],
                        AF.Silu, bias=0.0, scale=1.0),
                        waits=W(sc_pe), inc=pj)
        for t in range(NCH):
            vb = S[4] if t % 2 == 0 else S[3]
            for c in range(8):
                k.op("tensor", lambda e, c=c, t=t, vb=vb: e.matmul(
                    vb[:, 0:256], xh[:, c * T + t * 128: c * T + (t + 1) * 128],
                    wv_h[:, c * 256:(c + 1) * 256], start=(c == 0), stop=(c == 7)),
                    waits=([(v_ev, v_ev.n - min(v_ev.n, 1)), wW, lnw]
                           + (W(pj, sc_act) if t <= 1 else []))
                    if c == 0 else (),
                    inc=sc_pe if c == 7 else None)
            k.op("scalar", lambda e, t=t, vb=vb: e.activation(
                Vt[:, t * 256:(t + 1) * 256], vb[:, 0:256], AF.Silu,
                bias=0.0, scale=1.0), waits=W(sc_pe), inc=v_ev)
        for d in range(2):
            k.op("vector", lambda e, d=d: e.memset(S1b[d][:], 0.0),
                 waits=W(sc_pe, sc_dve), inc=sc_dve)
            k.op("vector", lambda e, d=d: e.memset(S2b[d][:], 0.0), inc=sc_dve)
        for ci in range(NCH):
            fb = S[3]
            for c in range(8):
                k.op("tensor", lambda e, c=c, ci=ci, fb=fb: e.matmul(
                    fb[:, 0:256], xh[:, c * T + ci * 128: c * T + (ci + 1) * 128],
                    wf_h[:, c * 256:(c + 1) * 256], start=(c == 0), stop=(c == 7)),
                    waits=W(sc_act, sc_dve) + [(v_ev, v_ev.n), wW] if c == 0 else (),
                    inc=sc_pe if c == 7 else None)
            k.op("scalar", lambda e, fb=fb: e.activation(
                za[:], fb[:, 0:256], AF.Abs, bias=0.0, scale=1.0),
                waits=W(sc_pe), inc=sc_act)
            k.op("scalar", lambda e: e.activation(
                eb[:], za[:], AF.Exp, bias=0.0, scale=-1.0),
                waits=W(sc_act), inc=sc_act)
            k.op("vector", lambda e: e.tensor_scalar(
                eb[:], eb[:], 1.0, None, op0=ALU.add), waits=W(sc_act), inc=sc_dve)
            k.op("scalar", lambda e: e.activation(
                eb[:], eb[:], AF.Ln, bias=0.0, scale=1.0),
                waits=W(sc_dve), inc=sc_act)
            k.op("vector", lambda e, fb=fb: e.tensor_scalar(
                t1[:], fb[:, 0:256], 0.0, None, op0=ALU.min),
                waits=W(sc_act), inc=sc_dve)
            k.op("vector", lambda e: e.tensor_sub(t1[:], t1[:], eb[:]),
                 waits=W(sc_dve, sc_act), inc=sc_dve)
            k.op("tensor", lambda e: e.matmul(
                S[5][:, 0:256], cn["tri_incl_f"][:], t1[:], start=True, stop=True),
                waits=W(sc_dve, sc_act), inc=sc_pe)
            k.op("tensor", lambda e: e.matmul(
                S[6][:, 0:256], cn["tri_revx_f"][:], t1[:], start=True, stop=True),
                inc=sc_pe)
            k.op("tensor", lambda e: e.matmul(
                S[4][0:1, 0:256], cn["ones_col_f"][:], t1[:], start=True, stop=True),
                waits=W(sc_dve) + [(v_ev, v_ev.n)], inc=sc_pe)
            for d in range(2):
                k.op("tensor", lambda e, d=d: e.matmul(
                    S[4][:, 256 + d:257 + d], t1[:, d * 128:(d + 1) * 128],
                    cn["ones_col_f"][:], start=True, stop=True), inc=sc_pe)
            k.op("scalar", lambda e: e.activation(
                ea[:], S[5][:, 0:256], AF.Exp, bias=0.0, scale=1.0 / 16),
                waits=W(sc_pe), inc=sc_act)
            k.op("scalar", lambda e: e.activation(
                en[:], S[5][:, 0:256], AF.Exp, bias=0.0, scale=-1.0 / 16),
                inc=sc_act)
            k.op("scalar", lambda e: e.activation(
                eR[:], S[6][:, 0:256], AF.Exp, bias=0.0, scale=1.0 / 16),
                inc=sc_act)
            k.op("scalar", lambda e: e.activation(
                eArow[:], S[4][0:1, 0:256], AF.Exp, bias=0.0, scale=1.0 / 16),
                inc=sc_act)
            for d in range(2):
                k.op("scalar", lambda e, d=d: e.activation(
                    eAT[:, d:d + 1], S[4][:, 256 + d:257 + d], AF.Exp,
                    bias=0.0, scale=1.0 / 16), inc=sc_act)
            k.op("scalar", lambda e: e.activation(
                sg[:], t1[:], AF.Exp, bias=0.0, scale=1.0 / 16),
                waits=W(sc_dve), inc=sc_act)
            k.op("vector", lambda e: e.tensor_scalar(
                sg[:], sg[:], -1.0, 1.0, op0=ALU.mult, op1=ALU.add),
                waits=W(sc_act), inc=sc_dve)
            k.op("vector", lambda e: e.tensor_mul(Dt[:], sg[:], en[:]),
                 waits=W(sc_dve, sc_act), inc=sc_dve)
            k.op("vector", lambda e: e.tensor_mul(D2t[:], sg[:], eR[:]),
                 waits=W(sc_dve), inc=sc_dve)
            k.op("tensor", lambda e: e.matmul(
                S[4][:, 0:256], cn["ones_row"][:], eArow[:], start=True, stop=True),
                waits=W(sc_act, sc_dve), inc=sc_pe)
            for d in range(2):
                k.op("tensor", lambda e, d=d, ci=ci: e.matmul(
                    S[6][:, 256:384], Kf[:, d * T + ci * 128: d * T + (ci + 1) * 128],
                    Qf[:, d * T + ci * 128: d * T + (ci + 1) * 128],
                    start=(d == 0), stop=(d == 1)),
                    waits=W(sc_act) if d == 0 else (), inc=sc_pe if d == 1 else None)
            k.op("vector", lambda e: e.tensor_mul(
                UT[:], S[6][:, 256:384], cn["mm"][:, 0:128]),
                waits=W(sc_pe), inc=sc_dve)
            for d in range(2):
                k.op("tensor", lambda e, d=d, ci=ci: e.matmul(
                    S[5][:, 256:512], Qf[:, d * T + ci * 128: d * T + (ci + 1) * 128],
                    S1b[d][:], start=(d == 0), stop=False),
                    waits=W(sc_act, sc_dve) if d == 0 else ())
            k.op("tensor", lambda e: e.matmul(
                S[5][:, 256:512], UT[:], Dt[:], start=False, stop=True),
                inc=sc_pe)
            k.op("vector", lambda e: e.tensor_mul(zf[:], ea[:], S[5][:, 256:512]),
                 waits=W(sc_pe), inc=sc_dve)
            k.op("vector", lambda e: e.reduce_max(
                nmx[:], zf[:], axis=AX.X, negate=True), waits=W(sc_dve), inc=sc_dve)
            k.op("scalar", lambda e: e.activation(
                esf[:], zf[:], AF.Exp, bias=nmx[:, 0:1], scale=1.0,
                accum_out=lsm[:, 0:1]), waits=W(sc_dve), inc=sc_act)
            k.op("vector", lambda e: e.reciprocal(invl[:], lsm[:]),
                 waits=W(sc_act), inc=sc_dve)
            k.op("vector", lambda e: e.tensor_mul(ptl[:], esf[:], ea[:]),
                 waits=W(sc_dve), inc=sc_dve)
            k.op("vector", lambda e: e.tensor_scalar(
                ptl[:], ptl[:], invl[:, 0:1], None, op0=ALU.mult),
                waits=W(sc_dve), inc=sc_dve)
            for (dst, srcf, isk) in ((ptT, ptl, False), (dT, Dt, False), (kT, None, True)):
                for d in range(2):
                    if isk:
                        k.op("tensor", lambda e, d=d, ci=ci: e.transpose(
                            PB[:, d * 128:(d + 1) * 128],
                            Kf[:, d * T + ci * 128: d * T + (ci + 1) * 128],
                            cn["ident_bf"][:]),
                            waits=W(sc_dve) if d == 0 else (), inc=sc_pe)
                    else:
                        k.op("tensor", lambda e, d=d, srcf=srcf: e.transpose(
                            PB[:, d * 128:(d + 1) * 128],
                            srcf[:, d * 128:(d + 1) * 128], cn["ident_bf"][:]),
                            waits=W(sc_dve) if d == 0 else (), inc=sc_pe)
                k.op("vector", lambda e, dst=dst: e.tensor_copy(dst[:], PB[:, 0:256]),
                     waits=W(sc_pe), inc=sc_dve)
            for d in range(2):
                k.op("tensor", lambda e, d=d: e.matmul(
                    S[6][:, 256:384], dT[:, d * 128:(d + 1) * 128],
                    ptT[:, d * 128:(d + 1) * 128], start=(d == 0), stop=(d == 1)),
                    waits=W(sc_dve) if d == 0 else (), inc=sc_pe if d == 1 else None)
            k.op("vector", lambda e: e.tensor_mul(
                G2m[:], S[6][:, 256:384], cn["mm"][:, 0:128]),
                waits=W(sc_pe), inc=sc_dve)
            for vs in range(2):
                bank = S[vs]
                for d in range(2):
                    k.op("tensor", lambda e, d=d, vs=vs, bank=bank: e.matmul(
                        bank[:, 0:128], S2b[d][:, vs * 128:(vs + 1) * 128],
                        ptT[:, d * 128:(d + 1) * 128], start=(d == 0), stop=False),
                        waits=W(sc_dve, sc_act) if d == 0 else ())
                k.op("tensor", lambda e, vs=vs, ci=ci, bank=bank: e.matmul(
                    bank[:, 0:128], Vt[:, ci * 256 + vs * 128: ci * 256 + (vs + 1) * 128],
                    G2m[:], start=False, stop=True),
                    waits=W(sc_dve), inc=sc_pe)
            for d in range(2):
                k.op("tensor", lambda e, d=d: e.matmul(
                    S[3][:, 256:512], kT[:, d * 128:(d + 1) * 128], D2t[:],
                    start=True, stop=True), waits=W(sc_dve),
                    inc=sc_pe)
                k.op("vector", lambda e, d=d: e.tensor_mul(
                    s1scr[:], S1b[d][:], S[4][:, 0:256]),
                    waits=W(sc_pe), inc=sc_dve)
                k.op("vector", lambda e, d=d: e.tensor_add(
                    S1b[d][:], s1scr[:], S[3][:, 256:512]),
                    waits=W(sc_dve), inc=sc_dve)
            for mcb in range(2):
                k.op("tensor", lambda e, mcb=mcb, ci=ci: e.matmul(
                    S[3][:, 256:512], D2t[:, mcb * 128:(mcb + 1) * 128],
                    Vt[:, ci * 256:(ci + 1) * 256],
                    start=True, stop=True), waits=W(sc_dve), inc=sc_pe)
                k.op("vector", lambda e, mcb=mcb: e.tensor_scalar(
                    s1scr[:], S2b[mcb][:], eAT[:, mcb:mcb + 1], None, op0=ALU.mult),
                    waits=W(sc_pe), inc=sc_dve)
                k.op("vector", lambda e, mcb=mcb: e.tensor_add(
                    S2b[mcb][:], s1scr[:], S[3][:, 256:512]),
                    waits=W(sc_dve), inc=sc_dve)
            midx = (h * NCH + ci)
            k.op("scalar", lambda e, m2=midx % 2: e.activation(
                ost[m2][:, 0:128], S[0][:, 0:128], AF.Silu, bias=0.0, scale=1.0),
                waits=W(sc_pe) + [(so_dma[midx % 2], so_dma[midx % 2].n)], inc=sc_act)
            k.op("scalar", lambda e, m2=midx % 2: e.activation(
                ost[m2][:, 128:256], S[1][:, 0:128], AF.Silu, bias=0.0, scale=1.0),
                inc=sc_act)
            for vs in range(2):
                k.op("gpsimd", lambda e, h=h, ci=ci, vs=vs, m2=midx % 2: e.dma_start(
                    out=o_dram[h * 256 + vs * 128: h * 256 + (vs + 1) * 128,
                               ci * 128:(ci + 1) * 128],
                    in_=ost[m2][:, vs * 128:(vs + 1) * 128]),
                    waits=W(sc_act) if vs == 0 else (), inc=so_dma[midx % 2], inck=16)

    hctx.close()
    scanW = [(sc_pe, sc_pe.n), (sc_act, sc_act.n), (sc_dve, sc_dve.n),
             (s_ldw, s_ldw.n), (v_ev, v_ev.n), (pj, pj.n)]
    with contextlib.ExitStack() as octx:
        OE = octx.enter_context
        ob = OE(nc.sbuf_tensor("a1_ob", [128, 8 * 512], BF16))
        osq = [OE(nc.sbuf_tensor(f"a1_osq{i}", [128, 512], BF16)) for i in range(2)]
        oh_ = OE(nc.sbuf_tensor("a1_oh", [128, 8 * 512], BF16))
        q2_row = OE(nc.sbuf_tensor("a1_q2", [1, 512], F32))
        rso_row = OE(nc.sbuf_tensor("a1_rso", [1, 512], F32))
        rso_b = OE(nc.sbuf_tensor("a1_rsob", [1, 512], BF16))
        xrg = [OE(nc.sbuf_tensor(f"a1_xrg{i}", [128, 512], F32)) for i in range(4)]
        x3o = [OE(nc.sbuf_tensor(f"a1_x3o{i}", [128, 512], F32)) for i in range(4)]
        gn_b = OE(nc.sbuf_tensor("a1_gnb", [128, 8], BF16))
        go_ld = k.sem()
        go_pe = k.sem()
        go_dve = k.sem()
        go_act = k.sem()
        gst = [k.sem() for _ in range(4)]
        gxr = [k.sem() for _ in range(4)]
        st_dve = k.sem()
        k.op("vector", lambda e: e.tensor_copy(gn_b[:], gn_f[:]),
             waits=W(s_ldg) + [cw], inc=go_dve)
        for c in range(8):
            k.op("vector", lambda e, c=c: e.tensor_scalar(
                wo_sb[:, c * 1024:(c + 1) * 1024], wo_sb[:, c * 1024:(c + 1) * 1024],
                gn_f[:, c:c + 1], None, op0=ALU.mult),
                waits=[(s_ldwo, s_ldwo.n)] + W(go_dve), inc=go_dve)
        woW = (go_dve, go_dve.n)
        soW = (so_dma[0], so_dma[0].n)
        soW2 = (so_dma[1], so_dma[1].n)
        for g in range(G):
            ldb = go_ld.n
            for c in range(8):
                k.op("sync", lambda e, c=c, g=g: e.dma_start(
                    out=ob[:, c * 512:(c + 1) * 512],
                    in_=o_dram[c * 128:(c + 1) * 128, g * 512:(g + 1) * 512]),
                    waits=([soW, soW2, (go_pe, go_pe.n)] + (scanW if g == 0 else []))
                    if c == 0 else (), inc=go_ld, inck=16)
            for c in range(8):
                k.op("scalar", lambda e, c=c: e.activation(
                    osq[c % 2][:], ob[:, c * 512:(c + 1) * 512], AF.Square,
                    bias=0.0, scale=1.0),
                    waits=[(go_ld, ldb + 128)] + (W(go_pe) if c >= 2 else []),
                    inc=go_act)
                k.op("tensor", lambda e, c=c: e.matmul(
                    S[1][0:1, 0:512], cn["ones_col"][:], osq[c % 2][:],
                    start=(c == 0), stop=(c == 7)),
                    waits=W(go_act), inc=go_pe)
            k.op("vector", lambda e: e.tensor_scalar(
                q2_row[:], S[1][0:1, 0:512], 1.0 / 1024, None, op0=ALU.mult),
                waits=W(go_pe), inc=go_dve)
            k.op("vector", lambda e: e.tensor_scalar(
                q2_row[:], q2_row[:], 1e-5, None, op0=ALU.add),
                waits=W(go_dve), inc=go_dve)
            k.op("scalar", lambda e: e.activation(rso_row[:], q2_row[:], AF.Ln,
                                                  bias=0.0, scale=1.0),
                 waits=W(go_dve), inc=go_act)
            k.op("scalar", lambda e: e.activation(rso_row[:], rso_row[:], AF.Exp,
                                                  bias=0.0, scale=-0.5),
                 waits=W(go_act), inc=go_act)
            k.op("vector", lambda e: e.tensor_copy(rso_b[:], rso_row[:]),
                 waits=W(go_act), inc=go_dve)
            k.op("tensor", lambda e: e.matmul(
                S[2][:], cn["ones_row"][:], rso_b[:], start=True, stop=True),
                waits=W(go_dve), inc=go_pe)
            for c in range(8):
                k.op("vector", lambda e, c=c: e.tensor_mul(
                    oh_[:, c * 512:(c + 1) * 512], ob[:, c * 512:(c + 1) * 512],
                    S[2][:]), waits=W(go_pe) if c == 0 else W(go_dve), inc=go_dve)
            ohW = (go_dve, go_dve.n)
            for c in range(8):
                n = g * 8 + c
                b = n % 4
                k.op("sync", lambda e, c=c, g=g, b=b: e.dma_start(
                    out=xrg[b][:], in_=xT[c * 128:(c + 1) * 128, g * 512:(g + 1) * 512]),
                    waits=[(st_dve, max(0, st_dve.n - 3))], inc=gxr[b], inck=16)
                bank = S[3 + (c % 2)]
                for f in range(8):
                    k.op("tensor", lambda e, c=c, g=g, f=f, bank=bank: e.matmul(
                        bank[:], wo_sb[:, f * 1024 + c * 128: f * 1024 + (c + 1) * 128],
                        oh_[:, f * 512:(f + 1) * 512],
                        start=(f == 0), stop=(f == 7)),
                        waits=[ohW, woW, (st_dve, max(0, st_dve.n - 1))] if f == 0 else (),
                        inc=go_pe if f == 7 else None)
                k.op("vector", lambda e, b=b, bank=bank: e.tensor_add(
                    x3o[b][:], bank[:], xrg[b][:]),
                    waits=[(go_pe, go_pe.n), (gxr[b], gxr[b].n), (gst[b], gst[b].n)],
                    inc=st_dve)
                k.op("scalar", lambda e, c=c, g=g, b=b: e.dma_start(
                    out=x3T[c * 128:(c + 1) * 128, g * 512:(g + 1) * 512],
                    in_=x3o[b][:]), waits=[(st_dve, st_dve.n)], inc=gst[b], inck=16)


def build_l1_attn_program(T):
    nc = bass.Bass("TRN2", target_bir_lowering=False)
    xT = nc.dram_tensor("xT", [1024, T], F32, kind="ExternalInput").ap()
    nw = nc.dram_tensor("nw", [1024, 1], F32, kind="ExternalInput").ap()
    gn = nc.dram_tensor("gn", [1024, 1], F32, kind="ExternalInput").ap()
    wq = nc.dram_tensor("wq", [1024, 1024], F32, kind="ExternalInput").ap()
    wk = nc.dram_tensor("wk", [1024, 1024], F32, kind="ExternalInput").ap()
    wv = nc.dram_tensor("wv", [1024, 1024], F32, kind="ExternalInput").ap()
    wf = nc.dram_tensor("wf", [1024, 1024], F32, kind="ExternalInput").ap()
    wo = nc.dram_tensor("wo", [1024, 1024], F32, kind="ExternalInput").ap()
    x3T = nc.dram_tensor("x3T", [1024, T], F32, kind="ExternalOutput").ap()
    ctx = contextlib.ExitStack()
    with ctx:
        E = ctx.enter_context
        k = K(nc, E)
        env = Env(nc, E, k)
        emit_l1_attn(env, T, xT, x3T, nw, wq, wk, wv, wf, gn, wo)
        k.final_barrier()
        block = E(nc.Block())
        k.finish(block)
    return nc


def _l1_feed(x, b, nw, gn, wq, wk, wv, wf, wo):
    m = dict(make_const_arrays())
    m["xT"] = np.ascontiguousarray(x[b].T)
    m["nw"] = np.ascontiguousarray(nw[:, None])
    m["gn"] = np.ascontiguousarray(gn[:, None])
    m["wq"] = np.ascontiguousarray(wq.T)
    m["wk"] = np.ascontiguousarray(wk.T)
    m["wv"] = np.ascontiguousarray(wv.T)
    m["wf"] = np.ascontiguousarray(wf.T)
    m["wo"] = np.ascontiguousarray(wo.T)
    return m


def run_l1_attn(x, nw, gn, wq, wk, wv, wf, wo):
    B, T, D = x.shape
    nc = build_l1_attn_program(T)
    in_maps = [_l1_feed(x, b, nw, gn, wq, wk, wv, wf, wo) for b in range(B)]
    res = run_bass_kernel_spmd(nc, in_maps, core_ids=list(range(B)))
    return np.stack([r["x3T"].T for r in res.results], axis=0)


def sim_l1_attn(x, nw, gn, wq, wk, wv, wf, wo):
    from concourse.bass_interp import CoreSim
    B, T, D = x.shape
    assert B == 1
    nc = build_l1_attn_program(T)
    nc.finalize()
    sim = CoreSim(nc)
    feed = _l1_feed(x, 0, nw, gn, wq, wk, wv, wf, wo)
    for name, arr in feed.items():
        v = sim.tensor(name)
        v[:] = arr.view(v.dtype) if arr.dtype == NP_BF16 else arr
    sim.simulate()
    return sim.tensor("x3T").T[None].copy()



L1FFN_TRUNC = 0

def emit_l1_ffn(env: Env, T, xT, x4T, gwT, gbRow, WE1, BE1col, WE2, BE2, WE3, BE3col,
                ws1A, bs1C, ws2A, bs2R, ws3A, bs3C):
    trunc = L1FFN_TRUNC
    """x4 = x3 + moe_top2of16(LN(x3)) + shared_swiglu(LN(x3)). Dense v1.
    WE1/WE3: [16*1024, 512] f32; WE2: [16*512, 1024] f32; BE1col/BE3col: [16*512, 1] f32;
    BE2: [16, 1024] f32; ws*: [1024, 1024]; bs1C/bs3C: [1024, 1]; bs2R: [1, 1024]."""
    nc, E, k, cn = env.nc, env.E, env.k, env.cn
    S, PB = env.S, env.PB
    G = T // 512
    NT = T // 128
    NE = 16
    wbf1 = nc.dram_tensor("wbf1", [16 * 1024, 512], BF16, kind="Internal").ap()
    wbf3 = nc.dram_tensor("wbf3", [16 * 1024, 512], BF16, kind="Internal").ap()
    wbf2 = nc.dram_tensor("wbf2", [16 * 512, 1024], BF16, kind="Internal").ap()

    W = lambda *ss: [s.now() for s in ss]
    cw = (env.ld, env.ld.n)

    xh = E(nc.sbuf_tensor("m_xh", [128, 8 * 512], BF16))
    negmr = E(nc.sbuf_tensor("m_negmr", [1, 512], BF16))
    comb = E(nc.sbuf_tensor("m_comb", [128, 4 * 16], F32))
    gw_sb = E(nc.sbuf_tensor("m_gw", [128, 8 * 16], F32))
    csgw = E(nc.sbuf_tensor("m_csgw", [1, 16], F32))
    gbb = E(nc.sbuf_tensor("m_gbb", [128, 16], F32))
    gb_sb = E(nc.sbuf_tensor("m_gb", [1, 16], F32))
    be1t = E(nc.sbuf_tensor("m_be1t", [128, 16 * 4], F32))
    be3t = E(nc.sbuf_tensor("m_be3t", [128, 16 * 4], F32))
    be2t = [E(nc.sbuf_tensor(f"m_be2t{i}", [1, 1024], BF16)) for i in range(2)]
    ws1_sb = E(nc.sbuf_tensor("m_ws1", [128, 8 * 1024], BF16))
    ws3_sb = E(nc.sbuf_tensor("m_ws3", [128, 8 * 1024], BF16))
    ws2_sb = E(nc.sbuf_tensor("m_ws2", [128, 8 * 1024], BF16))
    bs1c = E(nc.sbuf_tensor("m_bs1c", [128, 8], F32))
    bs3c = E(nc.sbuf_tensor("m_bs3c", [128, 8], F32))
    bs2r = E(nc.sbuf_tensor("m_bs2r", [1, 1024], BF16))

    s_ldg = k.sem()
    s_ldx = k.sem()
    s_ldx3 = k.sem()
    s_pc = k.sem()
    s_pc2 = k.sem()
    s_pcs = k.sem()
    s_b = k.sem()
    ln_pe = k.sem()
    ln_dve = k.sem()
    ln_act = k.sem()
    ln_done = k.sem()
    gt_pe = k.sem()
    gt_dve = k.sem()
    gt_act = k.sem()
    fe_pe = k.sem()
    fe_act = k.sem()
    fe_dve = k.sem()
    s_w2 = [k.sem(), k.sem()]
    st_dve = k.sem()
    gst = [k.sem() for _ in range(4)]
    gxr = [k.sem() for _ in range(4)]

    for c in range(8):
        k.op("sync", lambda e, c=c: e.dma_start(
            out=gw_sb[:, c * 16:(c + 1) * 16], in_=gwT[c * 128:(c + 1) * 128, :]),
            inc=s_ldg, inck=16)
    k.op("sync", lambda e: e.dma_start(out=gb_sb[:], in_=gbRow), inc=s_ldg, inck=16)
    for ei in range(NE):
        for sl in range(4):
            k.op("scalar", lambda e, ei=ei, sl=sl: e.dma_start(
                out=be1t[:, ei * 4 + sl: ei * 4 + sl + 1],
                in_=BE1col[ei * 512 + sl * 128: ei * 512 + (sl + 1) * 128, 0:1]),
                inc=s_b, inck=16)
            k.op("scalar", lambda e, ei=ei, sl=sl: e.dma_start(
                out=be3t[:, ei * 4 + sl: ei * 4 + sl + 1],
                in_=BE3col[ei * 512 + sl * 128: ei * 512 + (sl + 1) * 128, 0:1]),
                inc=s_b, inck=16)
    for c in range(8):
        k.op("scalar", lambda e, c=c: e.dma_start(
            out=bs1c[:, c:c + 1], in_=bs1C[c * 128:(c + 1) * 128, 0:1]),
            inc=s_b, inck=16)
        k.op("scalar", lambda e, c=c: e.dma_start(
            out=bs3c[:, c:c + 1], in_=bs3C[c * 128:(c + 1) * 128, 0:1]),
            inc=s_b, inck=16)
    k.op("gpsimd", lambda e: e.dma_start(out=bs2r[:], in_=bs2R), inc=s_pc, inck=16)
    for (dst, src) in ((ws1_sb, ws1A), (ws3_sb, ws3A), (ws2_sb, ws2A)):
        for c in range(8):
            k.op("gpsimd", lambda e, c=c, dst=dst, src=src: e.dma_start(
                out=dst[:, c * 1024:(c + 1) * 1024],
                in_=src[c * 128:(c + 1) * 128, :]), inc=s_pc, inck=16)
    for c in range(8):
        k.op("tensor", lambda e, c=c: e.matmul(
            S[5][0:1, 0:16], cn["ones_col_f"][:], gw_sb[:, c * 16:(c + 1) * 16],
            start=(c == 0), stop=(c == 7)),
            waits=[cw, (s_ldg, s_ldg.n)] if c == 0 else (),
            inc=gt_pe if c == 7 else None)
    k.op("vector", lambda e: e.tensor_copy(csgw[:], S[5][0:1, 0:16]),
         waits=W(gt_pe), inc=gt_dve)
    k.op("tensor", lambda e: e.matmul(
        S[5][:, 16:32], cn["ones_row_f"][:], gb_sb[:], start=True, stop=True),
        waits=W(gt_dve) + [(s_ldg, s_ldg.n)], inc=gt_pe)
    k.op("vector", lambda e: e.tensor_copy(gbb[:], S[5][:, 16:32]),
         waits=W(gt_pe), inc=gt_dve)
    with contextlib.ExitStack() as pctx:
        PE_ = pctx.enter_context
        wtmp = [PE_(nc.sbuf_tensor(f"m_wtmp{i}", [128, 8 * 512], BF16)) for i in range(2)]
        for ei in range(NE):
            for wi, (srcW, dstW, cstab) in enumerate(
                    ((WE1, wbf1, "cs1"), (WE3, wbf3, "cs3"), (WE2, wbf2, None))):
                buf = wtmp[(ei * 3 + wi) % 2]
                ldb = s_pc2.n
                nch = 8 if cstab is not None else 4
                width = 512 if cstab is not None else 1024
                for c in range(nch):
                    k.op("gpsimd", lambda e, c=c, ei=ei, srcW=srcW, buf=buf,
                         width=width, nch=nch: e.dma_start(
                        out=buf[:, c * width:(c + 1) * width],
                        in_=srcW[ei * nch * 128 + c * 128: ei * nch * 128 + (c + 1) * 128, :]),
                        waits=[(s_pcs, s_pcs.n)] if c == 0 else (), inc=s_pc2, inck=16)
                for c in range(nch):
                    k.op("scalar", lambda e, c=c, ei=ei, dstW=dstW, buf=buf,
                         width=width, nch=nch: e.dma_start(
                        out=dstW[ei * nch * 128 + c * 128: ei * nch * 128 + (c + 1) * 128, :],
                        in_=buf[:, c * width:(c + 1) * width]),
                        waits=[(s_pc2, ldb + 16 * nch)] if c == 0 else (),
                        inc=s_pcs, inck=16)

    pcW = [(s_pcs, s_pcs.n), (s_pc, s_pc.n), (s_pc2, s_pc2.n)]

    LE = E
    xbb = LE(nc.sbuf_tensor("m_xbb", [128, 8 * 512], BF16))
    xgf = LE(nc.sbuf_tensor("m_xgf", [128, 8 * 512], F32))
    sq = [LE(nc.sbuf_tensor(f"m_sq{i}", [128, 512], BF16)) for i in range(2)]
    m_row = LE(nc.sbuf_tensor("m_mrow", [1, 512], F32))
    q_row = LE(nc.sbuf_tensor("m_qrow", [1, 512], F32))
    v_row = LE(nc.sbuf_tensor("m_vrow", [1, 512], F32))
    rs_row = LE(nc.sbuf_tensor("m_rsrow", [1, 512], F32))
    rs_b = LE(nc.sbuf_tensor("m_rsb", [1, 512], BF16))
    rstdc = LE(nc.sbuf_tensor("m_rstdc", [128, 1], F32))
    lg = LE(nc.sbuf_tensor("m_lg", [128, 16], F32))
    nmx = LE(nc.sbuf_tensor("m_nmx", [128, 1], F32))
    sume = LE(nc.sbuf_tensor("m_sume", [128, 1], F32))
    esc = LE(nc.sbuf_tensor("m_esc", [128, 16], F32))
    inv = LE(nc.sbuf_tensor("m_inv", [128, 1], F32))
    orig = LE(nc.sbuf_tensor("m_orig", [128, 16], F32))
    sb16 = LE(nc.sbuf_tensor("m_sb16", [128, 16], F32))
    gsc = LE(nc.sbuf_tensor("m_gsc", [128, 4], F32))
    scr4 = LE(nc.sbuf_tensor("m_scr4", [128, 4], F32))
    pmin = LE(nc.sbuf_tensor("m_pmin", [128, 1], F32))
    thr = LE(nc.sbuf_tensor("m_thr", [128, 1], F32))
    keepg = LE(nc.sbuf_tensor("m_keepg", [128, 4], F32))
    m1 = LE(nc.sbuf_tensor("m_m1", [128, 1], F32))
    m2 = LE(nc.sbuf_tensor("m_m2", [128, 1], F32))
    sel = LE(nc.sbuf_tensor("m_sel", [128, 16], F32))
    negmf = LE(nc.sbuf_tensor("m_negmf", [1, 512], F32))
    fm_b = LE(nc.sbuf_tensor("m_fmb", [1, 512], BF16))
    fxtmp = LE(nc.sbuf_tensor("m_fxtmp", [128, 512], F32))
    ysb = E(nc.sbuf_tensor("m_ysb", [128, 4 * 1024], F32))
    w1e = [E(nc.sbuf_tensor(f"m_w1e{i}", [128, 8 * 512], BF16)) for i in range(2)]
    w3e = [E(nc.sbuf_tensor(f"m_w3e{i}", [128, 8 * 512], BF16)) for i in range(2)]
    w2e_s = E(nc.sbuf_tensor("m_w2e", [128, 4 * 1024], BF16))
    w2e = [w2e_s, w2e_s]
    h1sb = E(nc.sbuf_tensor("m_h1sb", [128, 512], F32))
    h3scr = E(nc.sbuf_tensor("m_h3", [128, 512], F32))
    h_all = E(nc.sbuf_tensor("m_hall", [128, 8 * 512], BF16))

    xrg = [E(nc.sbuf_tensor(f"m_xrg{i}", [128, 512], F32)) for i in range(2)]
    x4o = [E(nc.sbuf_tensor(f"m_x4o{i}", [128, 512], F32)) for i in range(2)]
    s_cs2 = [k.sem(), k.sem()]
    res_pe = k.sem()
    prev_res = 0

    for g in range(G):
        ldb = s_ldx.n
        ldb3 = s_ldx3.n
        for c in range(8):
            k.op("gpsimd", lambda e, c=c, g=g: e.dma_start(
                out=xbb[:, c * 512:(c + 1) * 512],
                in_=xT[c * 128:(c + 1) * 128, g * 512:(g + 1) * 512]),
                waits=([(ln_done, ln_done.n)] if g > 0 else pcW) if c == 0 else (),
                inc=s_ldx, inck=16)
            k.op("sync", lambda e, c=c, g=g: e.dma_start(
                out=xgf[:, c * 512:(c + 1) * 512],
                in_=xT[c * 128:(c + 1) * 128, g * 512:(g + 1) * 512]),
                waits=([(gt_dve, gt_dve.n)] if g > 0 else pcW) if c == 0 else (),
                inc=s_ldx3, inck=16)
        for c in range(8):
            k.op("tensor", lambda e, c=c: e.matmul(
                S[0][0:1, 0:512], cn["ones_col"][:], xbb[:, c * 512:(c + 1) * 512],
                start=(c == 0), stop=(c == 7)),
                waits=[(s_ldx, ldb + 128)] if c == 0 else (),
                inc=ln_pe if c == 7 else None)
        for c in range(8):
            k.op("scalar", lambda e, c=c: e.activation(
                sq[c % 2][:], xbb[:, c * 512:(c + 1) * 512], AF.Square,
                bias=0.0, scale=1.0),
                waits=[(s_ldx, ldb + 128)] + (W(ln_pe) if c >= 2 else []),
                inc=ln_act)
            k.op("tensor", lambda e, c=c: e.matmul(
                S[1][0:1, 0:512], cn["ones_col"][:], sq[c % 2][:],
                start=(c == 0), stop=(c == 7)),
                waits=W(ln_act), inc=ln_pe)
        k.op("vector", lambda e: e.tensor_scalar(
            m_row[:], S[0][0:1, 0:512], 1.0 / 1024, None, op0=ALU.mult),
            waits=W(ln_pe), inc=ln_dve)
        k.op("vector", lambda e: e.tensor_scalar(
            q_row[:], S[1][0:1, 0:512], 1.0 / 1024, None, op0=ALU.mult),
            inc=ln_dve)
        k.op("vector", lambda e: e.tensor_mul(v_row[:], m_row[:], m_row[:]),
             waits=W(ln_dve), inc=ln_dve)
        k.op("vector", lambda e: e.tensor_sub(v_row[:], q_row[:], v_row[:]),
             waits=W(ln_dve), inc=ln_dve)
        k.op("vector", lambda e: e.tensor_scalar(
            v_row[:], v_row[:], 1e-5, None, op0=ALU.add),
            waits=W(ln_dve), inc=ln_dve)
        k.op("scalar", lambda e: e.activation(rs_row[:], v_row[:], AF.Ln,
                                              bias=0.0, scale=1.0),
             waits=W(ln_dve), inc=ln_act)
        k.op("scalar", lambda e: e.activation(rs_row[:], rs_row[:], AF.Exp,
                                              bias=0.0, scale=-0.5),
             waits=W(ln_act), inc=ln_act)
        k.op("vector", lambda e: e.tensor_copy(rs_b[:], rs_row[:]),
             waits=W(ln_act), inc=ln_dve)
        k.op("vector", lambda e: e.tensor_mul(negmf[:], m_row[:], rs_row[:]),
             waits=W(ln_dve), inc=ln_dve)
        k.op("vector", lambda e: e.tensor_scalar(
            negmf[:], negmf[:], -1.0, None, op0=ALU.mult),
            waits=W(ln_dve), inc=ln_dve)
        k.op("vector", lambda e: e.tensor_copy(fm_b[:], m_row[:]),
             waits=W(ln_dve), inc=ln_dve)
        k.op("tensor", lambda e: e.matmul(
            S[2][:], cn["ones_row"][:], rs_b[:], start=True, stop=True),
            waits=W(ln_dve) + [(ln_done, ln_done.n), (fe_pe, fe_pe.n)], inc=ln_pe)
        k.op("tensor", lambda e: e.matmul(
            S[5][:, 0:512], cn["ones_row"][:], fm_b[:], start=True, stop=True),
            waits=[(gt_dve, gt_dve.n)], inc=ln_pe)
        for c in range(8):
            k.op("vector", lambda e, c=c: e.tensor_sub(
                fxtmp[:], xbb[:, c * 512:(c + 1) * 512], S[5][:, 0:512]),
                waits=W(ln_pe) if c == 0 else W(ln_dve), inc=ln_dve)
            k.op("vector", lambda e, c=c: e.tensor_mul(
                xh[:, c * 512:(c + 1) * 512], fxtmp[:], S[2][:]),
                waits=W(ln_dve), inc=(ln_done if c == 7 else ln_dve))
        for ti in range(4) if trunc < 2 else []:
            t = g * 4 + ti
            k.op("tensor", lambda e, ti=ti: e.transpose(
                PB[0:128, 0:1], rs_b[0:1, ti * 128:(ti + 1) * 128],
                cn["ident_bf"][0:1, 0:1]),
                waits=W(gt_dve, ln_dve), inc=gt_pe)
            k.op("vector", lambda e: e.tensor_copy(rstdc[:], PB[0:128, 0:1]),
                 waits=W(gt_pe), inc=gt_dve)
            for c in range(8):
                k.op("tensor", lambda e, c=c, ti=ti: e.matmul(
                    S[6][:, 0:16],
                    xgf[:, c * 512 + ti * 128: c * 512 + (ti + 1) * 128],
                    gw_sb[:, c * 16:(c + 1) * 16], start=(c == 0), stop=False),
                    waits=[(s_ldx3, ldb3 + 128), (gt_dve, gt_dve.n)]
                    if c == 0 else ())
            k.op("tensor", lambda e, ti=ti: e.matmul(
                S[6][:, 0:16], negmf[0:1, ti * 128:(ti + 1) * 128],
                csgw[:], start=False, stop=True), inc=gt_pe)
            k.op("vector", lambda e: e.tensor_scalar(
                lg[:], S[6][:, 0:16], rstdc[:, 0:1], None, op0=ALU.mult),
                waits=W(gt_pe, gt_dve), inc=gt_dve)
            k.op("vector", lambda e: e.reduce_max(
                nmx[:], lg[:], axis=AX.X, negate=True), waits=W(gt_dve), inc=gt_dve)
            k.op("scalar", lambda e: e.activation(
                esc[:], lg[:], AF.Exp, bias=nmx[:, 0:1], scale=1.0,
                accum_out=sume[:, 0:1]), waits=W(gt_dve), inc=gt_act)
            k.op("vector", lambda e: e.reciprocal(inv[:], sume[:]),
                 waits=W(gt_act), inc=gt_dve)
            k.op("vector", lambda e: e.tensor_scalar(
                orig[:], esc[:], inv[:, 0:1], None, op0=ALU.mult),
                waits=W(gt_dve), inc=gt_dve)
            k.op("vector", lambda e: e.tensor_add(sb16[:], orig[:], gbb[:]),
                 waits=W(gt_dve), inc=gt_dve)
            first = True
            for i in range(4):
                for j in range(i + 1, 4):
                    if first:
                        k.op("vector", lambda e, i=i, j=j: e.tensor_add(
                            gsc[:], sb16[:, i:13 + i:4], sb16[:, j:13 + j:4]),
                            waits=W(gt_dve), inc=gt_dve)
                    else:
                        k.op("vector", lambda e, i=i, j=j: e.tensor_add(
                            scr4[:], sb16[:, i:13 + i:4], sb16[:, j:13 + j:4]),
                            waits=W(gt_dve), inc=gt_dve)
                        k.op("vector", lambda e: e.tensor_max(
                            gsc[:], gsc[:], scr4[:]), waits=W(gt_dve), inc=gt_dve)
                    first = False
            first = True
            for i in range(4):
                for j in range(i + 1, 4):
                    if first:
                        k.op("vector", lambda e, i=i, j=j: e.tensor_tensor(
                            thr[:], gsc[:, i:i + 1], gsc[:, j:j + 1], op=ALU.min),
                            waits=W(gt_dve), inc=gt_dve)
                    else:
                        k.op("vector", lambda e, i=i, j=j: e.tensor_tensor(
                            pmin[:], gsc[:, i:i + 1], gsc[:, j:j + 1], op=ALU.min),
                            waits=W(gt_dve), inc=gt_dve)
                        k.op("vector", lambda e: e.tensor_max(
                            thr[:], thr[:], pmin[:]), waits=W(gt_dve), inc=gt_dve)
                    first = False
            k.op("vector", lambda e: e.tensor_scalar(
                keepg[:], gsc[:], thr[:, 0:1], None, op0=ALU.is_ge),
                waits=W(gt_dve), inc=gt_dve)
            k.op("vector", lambda e: e.tensor_scalar(
                keepg[:], keepg[:], 1e30, -1e30, op0=ALU.mult, op1=ALU.add),
                waits=W(gt_dve), inc=gt_dve)
            for gg in range(4):
                k.op("vector", lambda e, gg=gg: e.tensor_scalar(
                    sb16[:, gg * 4:(gg + 1) * 4], sb16[:, gg * 4:(gg + 1) * 4],
                    keepg[:, gg:gg + 1], None, op0=ALU.add),
                    waits=W(gt_dve), inc=gt_dve)
            k.op("vector", lambda e: e.reduce_max(m1[:], sb16[:], axis=AX.X),
                 waits=W(gt_dve), inc=gt_dve)
            k.op("vector", lambda e: e.tensor_scalar(
                sel[:], sb16[:], m1[:, 0:1], None, op0=ALU.is_ge),
                waits=W(gt_dve), inc=gt_dve)
            k.op("vector", lambda e: e.tensor_scalar(
                sel[:], sel[:], -2e30, None, op0=ALU.mult),
                waits=W(gt_dve), inc=gt_dve)
            k.op("vector", lambda e: e.tensor_add(sel[:], sb16[:], sel[:]),
                 waits=W(gt_dve), inc=gt_dve)
            k.op("vector", lambda e: e.reduce_max(m2[:], sel[:], axis=AX.X),
                 waits=W(gt_dve), inc=gt_dve)
            k.op("vector", lambda e: e.tensor_scalar(
                sel[:], sb16[:], m2[:, 0:1], None, op0=ALU.is_ge),
                waits=W(gt_dve), inc=gt_dve)
            k.op("vector", lambda e, ti=ti: e.tensor_mul(
                comb[:, ti * 16:(ti + 1) * 16], orig[:], sel[:]),
                waits=W(gt_dve), inc=gt_dve)
        gtW = (gt_dve, gt_dve.n)
        lnwg = (ln_done, ln_done.n)
        k.op("vector", lambda e: e.memset(ysb[:], 0.0),
             waits=[(st_dve, st_dve.n), gtW], inc=fe_dve)
        pe_mark = 0
        for ei in (range(NE + 1) if trunc == 0 else
                   ([] if trunc >= 3 else [NE])):
            shared = (ei == NE)
            dbl = ei % 2
            wW = s_w2[ei % 2].n
            csW = s_cs2[ei % 2].n
            prev_mark = pe_mark
            pe_mark = fe_pe.n
            if not shared:
                for c in range(8):
                    k.op("gpsimd", lambda e, c=c, ei=ei, dbl=dbl: e.dma_start(
                        out=w1e[dbl][:, c * 512:(c + 1) * 512],
                        in_=wbf1[ei * 1024 + c * 128: ei * 1024 + (c + 1) * 128, :]),
                        waits=([(fe_pe, prev_mark)] + (pcW if (g == 0 and ei == 0) else []))
                        if c == 0 else (), inc=s_w2[dbl], inck=16)
                    k.op("gpsimd", lambda e, c=c, ei=ei, dbl=dbl: e.dma_start(
                        out=w3e[dbl][:, c * 512:(c + 1) * 512],
                        in_=wbf3[ei * 1024 + c * 128: ei * 1024 + (c + 1) * 128, :]),
                        inc=s_w2[dbl], inck=16)
                for sl in range(4):
                    k.op("gpsimd", lambda e, sl=sl, ei=ei, dbl=dbl: e.dma_start(
                        out=w2e[dbl][:, sl * 1024:(sl + 1) * 1024],
                        in_=wbf2[ei * 512 + sl * 128: ei * 512 + (sl + 1) * 128, :]),
                        waits=[(fe_pe, fe_pe.n)] if sl == 0 else (), inc=s_w2[dbl], inck=16)

                k.op("gpsimd", lambda e, ei=ei, dbl=dbl: e.dma_start(
                    out=be2t[dbl][:], in_=BE2[ei:ei + 1, :]),
                    waits=[(fe_pe, prev_mark)], inc=s_w2[dbl], inck=16)
            nsl = 8 if shared else 4
            wid = 1024 if shared else 512
            wt1 = ws1_sb if shared else w1e[dbl]
            wt3 = ws3_sb if shared else w3e[dbl]
            wt2 = ws2_sb if shared else w2e[dbl]
            h_done = []
            for sl in range(nsl):
                b1 = S[6] if sl % 2 == 0 else S[5]
                b3 = S[4] if sl % 2 == 0 else S[3]
                lagw = ([(fe_act, h_done[sl - 2][0]), (fe_dve, h_done[sl - 2][1])]
                        if sl >= 2 else
                        (W(fe_act, fe_dve) if sl == 0 else []))
                for c in range(8):
                    k.op("tensor", lambda e, c=c, sl=sl, wt1=wt1, wid=wid, b1=b1:
                         e.matmul(b1[:, 0:512],
                                  wt1[:, c * wid + sl * 128: c * wid + (sl + 1) * 128],
                                  xh[:, c * 512:(c + 1) * 512],
                                  start=(c == 0), stop=(c == 7)),
                         waits=(lagw + ([lnwg] if sl == 0 else [])
                                + ([(s_w2[dbl], wW + 336)]
                                   if (not shared and sl == 0) else []))
                         if c == 0 else (),
                         inc=fe_pe if c == 7 else None)

                for c in range(8):
                    k.op("tensor", lambda e, c=c, sl=sl, wt3=wt3, wid=wid, b3=b3:
                         e.matmul(b3[:, 0:512],
                                  wt3[:, c * wid + sl * 128: c * wid + (sl + 1) * 128],
                                  xh[:, c * 512:(c + 1) * 512],
                                  start=(c == 0), stop=(c == 7)),
                         waits=lagw if c == 0 else (),
                         inc=fe_pe if c == 7 else None)

                bcol1 = bs1c[:, sl:sl + 1] if shared else be1t[:, ei * 4 + sl: ei * 4 + sl + 1]
                bcol3 = bs3c[:, sl:sl + 1] if shared else be3t[:, ei * 4 + sl: ei * 4 + sl + 1]
                peW = (fe_pe, fe_pe.n)
                k.op("scalar", lambda e, bcol1=bcol1, b1=b1: e.activation(
                    h1sb[:], b1[:, 0:512], AF.Silu, bias=bcol1, scale=1.0),
                    waits=[peW, (s_b, s_b.n), (fe_dve, fe_dve.n)], inc=fe_act)
                k.op("vector", lambda e, bcol3=bcol3, b3=b3: e.tensor_scalar(
                    h3scr[:], b3[:, 0:512], bcol3, None, op0=ALU.add),
                    waits=[peW, (s_b, s_b.n), (fe_dve, fe_dve.n)], inc=fe_dve)
                k.op("vector", lambda e, sl=sl: e.tensor_mul(
                    h_all[:, sl * 512:(sl + 1) * 512], h1sb[:], h3scr[:]),
                    waits=W(fe_act, fe_dve), inc=fe_dve)
                h_done.append((fe_act.n, fe_dve.n))
            hW = (fe_dve, fe_dve.n)
            yo_done = []
            for tt in range(4):
                for dh in range(2):
                    u = tt * 2 + dh
                    bank = S[u % 3]
                    lag2 = ([(fe_dve, yo_done[u - 3])] if u >= 3 else [])
                    for sl in range(nsl):
                        k.op("tensor", lambda e, sl=sl, tt=tt, dh=dh, bank=bank,
                             wt2=wt2:
                             e.matmul(bank[:],
                                      h_all[:, sl * 512 + tt * 128: sl * 512 + tt * 128 + 128],
                                      wt2[:, sl * 1024 + dh * 512: sl * 1024 + (dh + 1) * 512],
                                      start=(sl == 0), stop=False),
                             waits=[hW] + lag2 if sl == 0 else ())
                    berow = bs2r if shared else be2t[dbl]
                    k.op("tensor", lambda e, dh=dh, bank=bank, berow=berow: e.matmul(
                        bank[:], cn["ones_row"][0:1, 0:128],
                        berow[0:1, dh * 512:(dh + 1) * 512],
                        start=False, stop=True), inc=fe_pe)
                    if shared:
                        k.op("vector", lambda e, tt=tt, dh=dh, bank=bank: e.tensor_add(
                            ysb[:, tt * 1024 + dh * 512: tt * 1024 + (dh + 1) * 512],
                            ysb[:, tt * 1024 + dh * 512: tt * 1024 + (dh + 1) * 512],
                            bank[:]), waits=W(fe_pe), inc=fe_dve)
                    else:
                        k.op("vector", lambda e, tt=tt, dh=dh, ei=ei, bank=bank:
                             e.scalar_tensor_tensor(
                                ysb[:, tt * 1024 + dh * 512: tt * 1024 + (dh + 1) * 512],
                                bank[:],
                                comb[:, tt * 16 + ei: tt * 16 + ei + 1],
                                ysb[:, tt * 1024 + dh * 512: tt * 1024 + (dh + 1) * 512],
                                op0=ALU.mult, op1=ALU.add),
                             waits=W(fe_pe), inc=fe_dve)
                    yo_done.append(fe_dve.n)
        ydW = (fe_dve, fe_dve.n)
        for ti in range(4):
            k.op("vector", lambda e, ti=ti: e.tensor_copy(
                h_all[:, ti * 1024:(ti + 1) * 1024], ysb[:, ti * 1024:(ti + 1) * 1024]),
                waits=[ydW] + ([(res_pe, prev_res)] if ti == 0 else []), inc=fe_dve)
        for c in range(8):
            n = g * 8 + c
            b = n % 2
            k.op("sync", lambda e, c=c, g=g, b=b: e.dma_start(
                out=xrg[b][:], in_=xT[c * 128:(c + 1) * 128,
                                      g * 512:(g + 1) * 512]),
                waits=[(st_dve, max(0, st_dve.n - 1))], inc=gxr[b], inck=16)
            for ti in range(4):
                k.op("tensor", lambda e, c=c, ti=ti: e.transpose(
                    PB[:, ti * 128:(ti + 1) * 128],
                    h_all[:, ti * 1024 + c * 128: ti * 1024 + (c + 1) * 128],
                    cn["ident_bf"][:]),
                    waits=[(fe_dve, fe_dve.n), (res_pe, res_pe.n),
                           (st_dve, st_dve.n)] if ti == 0 else (),
                    inc=res_pe)
            k.op("vector", lambda e, b=b: e.tensor_add(
                x4o[b][:], PB[:, 0:512], xrg[b][:]),
                waits=[(res_pe, res_pe.n), (gxr[b], gxr[b].n), (gst[b], gst[b].n)],
                inc=st_dve)
            k.op("scalar", lambda e, c=c, g=g, b=b: e.dma_start(
                out=x4T[c * 128:(c + 1) * 128, g * 512:(g + 1) * 512],
                in_=x4o[b][:]), waits=[(st_dve, st_dve.n)], inc=gst[b], inck=16)
        prev_res = res_pe.n


def build_l1_ffn_program(T):
    nc = bass.Bass("TRN2", target_bir_lowering=False)
    xT = nc.dram_tensor("xT", [1024, T], F32, kind="ExternalInput").ap()
    gwT = nc.dram_tensor("gwT", [1024, 16], F32, kind="ExternalInput").ap()
    gbR = nc.dram_tensor("gbR", [1, 16], F32, kind="ExternalInput").ap()
    WE1 = nc.dram_tensor("WE1", [16 * 1024, 512], F32, kind="ExternalInput").ap()
    BE1 = nc.dram_tensor("BE1", [16 * 512, 1], F32, kind="ExternalInput").ap()
    WE2 = nc.dram_tensor("WE2", [16 * 512, 1024], F32, kind="ExternalInput").ap()
    BE2 = nc.dram_tensor("BE2", [16, 1024], F32, kind="ExternalInput").ap()
    WE3 = nc.dram_tensor("WE3", [16 * 1024, 512], F32, kind="ExternalInput").ap()
    BE3 = nc.dram_tensor("BE3", [16 * 512, 1], F32, kind="ExternalInput").ap()
    ws1 = nc.dram_tensor("ws1", [1024, 1024], F32, kind="ExternalInput").ap()
    bs1 = nc.dram_tensor("bs1", [1024, 1], F32, kind="ExternalInput").ap()
    ws2 = nc.dram_tensor("ws2", [1024, 1024], F32, kind="ExternalInput").ap()
    bs2 = nc.dram_tensor("bs2", [1, 1024], F32, kind="ExternalInput").ap()
    ws3 = nc.dram_tensor("ws3", [1024, 1024], F32, kind="ExternalInput").ap()
    bs3 = nc.dram_tensor("bs3", [1024, 1], F32, kind="ExternalInput").ap()
    x4T = nc.dram_tensor("x4T", [1024, T], F32, kind="ExternalOutput").ap()
    ctx = contextlib.ExitStack()
    with ctx:
        E = ctx.enter_context
        k = K(nc, E)
        env = Env(nc, E, k)
        emit_l1_ffn(env, T, xT, x4T, gwT, gbR, WE1, BE1, WE2, BE2, WE3, BE3,
                    ws1, bs1, ws2, bs2, ws3, bs3)
        k.final_barrier()
        block = E(nc.Block())
        k.finish(block)
    return nc


def _l1f_feed(x, b, p):
    m = dict(make_const_arrays())
    m["xT"] = np.ascontiguousarray(x[b].T)
    m["gwT"] = np.ascontiguousarray(p["gate_w"].T)
    m["gbR"] = np.ascontiguousarray(p["gate_b"][None, :])
    m["WE1"] = np.ascontiguousarray(p["We1"].reshape(16 * 1024, 512))
    m["BE1"] = np.ascontiguousarray(p["be1"].reshape(16 * 512, 1))
    m["WE2"] = np.ascontiguousarray(p["We2"].reshape(16 * 512, 1024))
    m["BE2"] = np.ascontiguousarray(p["be2"])
    m["WE3"] = np.ascontiguousarray(p["We3"].reshape(16 * 1024, 512))
    m["BE3"] = np.ascontiguousarray(p["be3"].reshape(16 * 512, 1))
    m["ws1"] = np.ascontiguousarray(p["ws1"])
    m["bs1"] = np.ascontiguousarray(p["bs1"][:, None])
    m["ws2"] = np.ascontiguousarray(p["ws2"])
    m["bs2"] = np.ascontiguousarray(p["bs2"][None, :])
    m["ws3"] = np.ascontiguousarray(p["ws3"])
    m["bs3"] = np.ascontiguousarray(p["bs3"][:, None])
    return m


def run_l1_ffn(x, p):
    B, T, D = x.shape
    nc = build_l1_ffn_program(T)
    in_maps = [_l1f_feed(x, b, p) for b in range(B)]
    res = run_bass_kernel_spmd(nc, in_maps, core_ids=list(range(B)))
    return np.stack([r["x4T"].T for r in res.results], axis=0)


def sim_l1_ffn(x, p):
    from concourse.bass_interp import CoreSim
    B, T, D = x.shape
    assert B == 1
    nc = build_l1_ffn_program(T)
    nc.finalize()
    sim = CoreSim(nc)
    feed = _l1f_feed(x, 0, p)
    for name, arr in feed.items():
        v = sim.tensor(name)
        v[:] = arr.view(v.dtype) if arr.dtype == NP_BF16 else arr
    sim.simulate()
    return sim.tensor("x4T").T[None].copy()



def kernel(**inp):
    T = 4096
    B = 8
    cap = 4096
    consts = make_const_arrays()
    x = np.asarray(inp["L0_attn_x"])

    def maps(extra):
        out = []
        for b in range(B):
            m = dict(consts)
            m.update(extra(b))
            out.append(m)
        return out

    nc1 = build_l0_attn_program(T)
    wqkvT = np.ascontiguousarray(np.asarray(inp["L0_attn_w_qkv"]).T)
    woutT = np.ascontiguousarray(np.asarray(inp["L0_attn_w_out"]).T)
    gcol = np.ascontiguousarray(np.asarray(inp["L0_attn_gamma"])[:, None])
    r1 = run_bass_kernel_spmd(nc1, maps(lambda b: {
        "xT": np.ascontiguousarray(x[b].T), "wqkvT": wqkvT, "woutT": woutT,
        "gammacol": gcol}), core_ids=list(range(B)))

    nc2 = build_l0_router_program(T, capl=CAPL)
    Wr = np.ascontiguousarray(np.asarray(inp["L0_ffn_Wr"]))
    r2 = run_bass_kernel_spmd(nc2, maps(lambda b: {
        "xT": r1.results[b]["x1T"], "Wr": Wr}), core_ids=list(range(B)))
    counts_all = np.stack([r2.results[b]["counts"][0] for b in range(B)], 0)

    nc3 = build_l0_rest_program(T, cap, capl=CAPL)
    W1 = np.ascontiguousarray(np.asarray(inp["L0_ffn_W1"]).reshape(8 * 1024, 4096))
    W2 = np.ascontiguousarray(np.asarray(inp["L0_ffn_W2"]).reshape(8 * 4096, 1024))
    r3 = run_bass_kernel_spmd(nc3, maps(lambda b: {
        "xT": r1.results[b]["x1T"], "W1": W1, "W2": W2,
        "xe": r2.results[b]["xe"], "idx": r2.results[b]["idx"],
        "ptok": r2.results[b]["ptok"], "lrloc": r2.results[b]["lrloc"],
        "oneh": r2.results[b]["oneh"], "countsA": counts_all,
        "corelt": (np.arange(8) < b).astype(np.float32)[:, None]}),
        core_ids=list(range(B)))

    nc4 = build_l1_attn_program(T)
    l1w = {n: np.ascontiguousarray(np.asarray(inp["L1_attn_w_" + n]).T)
           for n in ("q", "k", "v", "f", "o")}
    nwc = np.ascontiguousarray(np.asarray(inp["L1_attn_norm_weight"])[:, None])
    gnc = np.ascontiguousarray(np.asarray(inp["L1_attn_gnorm_weight"])[:, None])
    r4 = run_bass_kernel_spmd(nc4, maps(lambda b: {
        "xT": r3.results[b]["x2T"], "nw": nwc, "gn": gnc,
        "wq": l1w["q"], "wk": l1w["k"], "wv": l1w["v"], "wf": l1w["f"],
        "wo": l1w["o"]}), core_ids=list(range(B)))

    nc5 = build_l1_ffn_program(T)
    p5 = {
        "gwT": np.ascontiguousarray(np.asarray(inp["L1_ffn_gate_w"]).T),
        "gbR": np.ascontiguousarray(np.asarray(inp["L1_ffn_gate_b"])[None, :]),
        "WE1": np.ascontiguousarray(np.asarray(inp["L1_ffn_We1"]).reshape(16 * 1024, 512)),
        "BE1": np.ascontiguousarray(np.asarray(inp["L1_ffn_be1"]).reshape(16 * 512, 1)),
        "WE2": np.ascontiguousarray(np.asarray(inp["L1_ffn_We2"]).reshape(16 * 512, 1024)),
        "BE2": np.ascontiguousarray(np.asarray(inp["L1_ffn_be2"])),
        "WE3": np.ascontiguousarray(np.asarray(inp["L1_ffn_We3"]).reshape(16 * 1024, 512)),
        "BE3": np.ascontiguousarray(np.asarray(inp["L1_ffn_be3"]).reshape(16 * 512, 1)),
        "ws1": np.ascontiguousarray(np.asarray(inp["L1_ffn_ws1"])),
        "bs1": np.ascontiguousarray(np.asarray(inp["L1_ffn_bs1"])[:, None]),
        "ws2": np.ascontiguousarray(np.asarray(inp["L1_ffn_ws2"])),
        "bs2": np.ascontiguousarray(np.asarray(inp["L1_ffn_bs2"])[None, :]),
        "ws3": np.ascontiguousarray(np.asarray(inp["L1_ffn_ws3"])),
        "bs3": np.ascontiguousarray(np.asarray(inp["L1_ffn_bs3"])[:, None]),
    }
    r5 = run_bass_kernel_spmd(nc5, maps(lambda b: dict(p5, xT=r4.results[b]["x3T"])),
                              core_ids=list(range(B)))
    return np.stack([r5.results[b]["x4T"].T for b in range(B)], axis=0).astype(np.float32)
